# version 27
# baseline (speedup 1.0000x reference)
import numpy as np
import ml_dtypes
import concourse.bass as bass
import concourse.mybir as mybir
from concourse.bass_utils import run_bass_kernel_spmd
from contextlib import ExitStack

F32 = mybir.dt.float32
BF16 = mybir.dt.bfloat16
AF = mybir.ActivationFunctionType
ALU = mybir.AluOpType
AX = mybir.AxisListType

B, S, D, H, NK, F = 2, 2048, 1024, 16, 21, 4096
TW = 4095
EXPW = 4224
MAGIC = 1.5 * 2.0 ** 23
EPS = 1e-5
NTT = S // 128
ENGS = ("sync", "tensor", "vector", "scalar", "gpsimd")


class Ctr:
    def __init__(self, nc, stack, name):
        self.h = stack.enter_context(nc.semaphore(name))
        self.n = 0

    def mark(self, instr, k=1):
        instr.then_inc(self.h, k)
        self.n += k
        return self.n


class _DummyInstr:
    def then_inc(self, sem, k=1, **kw):
        return self


class _DummyEng:
    def __getattr__(self, name):
        def f(*a, **k):
            return _DummyInstr()
        return f


class KB:
    def __init__(self, taps=(), stop_after=None):
        self.pending = None
        self.stop_after = stop_after
        self.taps = set(taps)
        self.tap_outs = {}
        nc = self.nc = bass.Bass("TRN2", target_bir_lowering=False)
        self.x_in = nc.dram_tensor("x", [B, S, D], F32, kind="ExternalInput").ap()
        self.w_in = nc.dram_tensor("w_in", [D, 3 * D], F32, kind="ExternalInput").ap()
        self.w_gate_in = nc.dram_tensor("w_gate", [D, 2 * D], F32, kind="ExternalInput").ap()
        self.b_gate_in = nc.dram_tensor("b_gate", [2 * D], F32, kind="ExternalInput").ap()
        self.amp_in = nc.dram_tensor("amp", [H, NK], F32, kind="ExternalInput").ap()
        self.sharp_in = nc.dram_tensor("sharp", [H, NK], F32, kind="ExternalInput").ap()
        self.off_in = nc.dram_tensor("off", [H, NK], F32, kind="ExternalInput").ap()
        self.w1_in = nc.dram_tensor("w1", [F, D], F32, kind="ExternalInput").ap()
        self.b1_in = nc.dram_tensor("b1", [F], F32, kind="ExternalInput").ap()
        self.w2_in = nc.dram_tensor("w2", [D, F], F32, kind="ExternalInput").ap()
        self.b2_in = nc.dram_tensor("b2", [D], F32, kind="ExternalInput").ap()
        self.out = nc.dram_tensor("out", [B, S, D], F32, kind="ExternalOutput").ap()

        self.d_expb = nc.dram_tensor("d_expb", [H, 128 * EXPW], BF16, kind="Internal").ap()
        self.d_winb = nc.dram_tensor("d_winb", [D, 3 * D], BF16, kind="Internal").ap()
        self.d_wgb = nc.dram_tensor("d_wgb", [D, 2 * D], BF16, kind="Internal").ap()
        self.d_wq1 = nc.dram_tensor("d_wq1", [D, F], BF16, kind="Internal").ap()
        self.d_wq2 = nc.dram_tensor("d_wq2", [F, D], BF16, kind="Internal").ap()
        self.d_xq = nc.dram_tensor("d_xq", [B, 8, 128, S], BF16, kind="Internal").ap()
        self.d_abr = nc.dram_tensor("d_abr", [B, S, D], F32, kind="Internal").ap()

        ident = np.eye(128)
        self.ident_bf_d = nc.inline_tensor(ident.astype(ml_dtypes.bfloat16), "identd_bf")
        sel = np.zeros((63, 65), dtype=np.float32)
        for p in range(63):
            sel[p, 32 * (p // 21)] = 1.0
        self.selG_d = nc.inline_tensor(sel, "selG_d")

    def tap(self, name, shape, dtype=F32):
        if name in self.taps and name not in self.tap_outs:
            self.tap_outs[name] = self.nc.dram_tensor(
                "tap_" + name, shape, dtype, kind="ExternalOutput").ap()
        return self.tap_outs.get(name)

    def on(self, engine_name, fn):
        if self.pending is not None:
            self.pending.append((engine_name, fn))
            fn(_DummyEng())
        else:
            getattr(self.block, engine_name)(fn)

    def grouped(self, phase_fn, *args):
        ctrs = (*self.SQ, *self.SP, self.sPE, self.sDV, self.sAC, self.sPL)
        snap = [(c, c.n) for c in ctrs]
        self.pending = []
        phase_fn(*args)
        pend = self.pending
        self.pending = None
        for c, v in snap:
            c.n = v
        for en in ENGS:
            fns = [f for e, f in pend if e == en]
            if fns:
                def sect(eng, fns=fns):
                    for f in fns:
                        f(eng)
                getattr(self.block, en)(sect)

    def dma(self, ctr, instr, key=None):
        instr.then_inc(ctr.h, 16)
        ctr.n += 16
        if key is not None:
            self.ev[key] = (ctr, ctr.n)
        return (ctr, ctr.n)

    def wt(self, eng, key):
        c, v = self.ev[key]
        eng.wait_ge(c.h, v)

    def bar(self):
        vals = [(c, c.n) for c in (*self.SQ, *self.SP, self.sPE, self.sDV, self.sAC, self.sPL)]
        for en in ENGS:
            def f(eng, vals=vals):
                for c, v in vals:
                    if v > 0:
                        eng.wait_ge(c.h, v)
            self.on(en, f)

    def build(self):
        nc = self.nc
        st = ExitStack()
        with st:
            s = self

            def sbufP(name, shape, dt, side=None):
                return st.enter_context(nc.sbuf_tensor(name, shape, dt, side=side))

            s.ident_bf = sbufP("ident_bf", [128, 128], BF16)
            s.ones1_bf = sbufP("ones1_bf", [1, 128], BF16)
            s.ones1_f = sbufP("ones1_f", [1, 128], F32)
            s.ones21 = sbufP("ones21", [NK, 1], F32)
            s.ones128 = sbufP("ones128", [128, 1], F32)
            s.onesp64 = sbufP("onesp64", [128, 64], BF16)
            s.bgate_bf = sbufP("bgate_bf", [1, 2 * D], BF16)
            s.b1b = sbufP("b1b", [128, F], BF16)
            s.b2b = sbufP("b2b", [128, D], BF16)
            s.s1col = sbufP("s1col", [128, 1], F32)
            s.s2col = sbufP("s2col", [128, 1], F32)
            s.dq1c = sbufP("dq1c", [128, B * NTT], F32)
            s.cb = sbufP("cb", [128, 40], F32)
            s.epscol = sbufP("epscol", [128, 1], F32)
            s.negmag = sbufP("negmag", [128, 1], F32)
            s.scal = sbufP("scal", [1, 8], F32)
            s.stg = sbufP("stg", [128, 2 * 128], BF16)
            s.p1cb = sbufP("p1cb_buf", [128, 20 * NTT], F32) \
                if any(t.startswith("p1cb") for t in self.taps) else None
            s.ftmp = sbufP("ftmp", [128, 2 * 512], F32)

            s.pA = st.enter_context(nc.psum_tensor("pA", [128, 512], F32))
            s.pB = st.enter_context(nc.psum_tensor("pB", [128, 512], F32))
            s.pC = st.enter_context(nc.psum_tensor("pC", [128, 512], F32))
            s.pD = st.enter_context(nc.psum_tensor("pD", [128, 512], F32))
            s.pO = st.enter_context(nc.psum_tensor("pO", [128, 512], F32))
            s.pO2 = st.enter_context(nc.psum_tensor("pO2", [128, 512], F32))
            s.pT1 = st.enter_context(nc.psum_tensor("pT1", [128, 128], BF16))
            s.pT2 = st.enter_context(nc.psum_tensor("pT2", [128, 128], BF16))

            s.SQ = [Ctr(nc, st, f"SQ{i}") for i in range(8)]
            s.SP = [Ctr(nc, st, f"SP{i}") for i in range(6)]
            s.sPE = Ctr(nc, st, "sPE")
            s.sDV = Ctr(nc, st, "sDV")
            s.sAC = Ctr(nc, st, "sAC")
            s.sPL = Ctr(nc, st, "sPL")

            s.block = st.enter_context(nc.Block())
            self.ev = {}

            stop = self.stop_after
            with ExitStack() as ph0:
                s.wtile = ph0.enter_context(nc.sbuf_tensor("wtile", [128, 3 * F], F32))
                s.wqtile = ph0.enter_context(nc.sbuf_tensor("wqtile", [128, 3 * F], BF16))
                s.stg8 = ph0.enter_context(nc.sbuf_tensor("stg8", [128, 2 * F], BF16))
                s.iotaf = ph0.enter_context(nc.sbuf_tensor("iotaf", [128, TW], F32))
                s.relrev = ph0.enter_context(nc.sbuf_tensor("relrev", [128, TW], F32))
                s.gwork = ph0.enter_context(nc.sbuf_tensor("gwork", [128, TW], F32))
                s.hkG = ph0.enter_context(nc.sbuf_tensor("hkG", [128, 18], F32))
                s.selG = ph0.enter_context(nc.sbuf_tensor("selG", [63, 65], F32))
                s.onesbc = ph0.enter_context(nc.sbuf_tensor("onesbc", [128, 128], BF16))
                s.rowexp = ph0.enter_context(nc.sbuf_tensor("rowexp", [65, TW], BF16))
                s.bgrow = ph0.enter_context(nc.sbuf_tensor("bgrow", [1, 2 * D], F32))
                s.b1row = ph0.enter_context(nc.sbuf_tensor("b1row", [1, F], F32))
                s.b2row = ph0.enter_context(nc.sbuf_tensor("b2row", [1, D], F32))
                s.bch = ph0.enter_context(nc.sbuf_tensor("bch", [128, TW], BF16))
                self.grouped(self.phase0)
            self.bar()

            for b in (range(B) if stop != "ph0" else []):
                with ExitStack() as gL1:
                    s.xn_fm = gL1.enter_context(
                        nc.sbuf_tensor(f"xn_fm_{b}", [128, 8 * S], BF16, side="left"))
                    s.a_fm = s.xn_fm
                    with ExitStack() as gR1:
                        s.k_fm = gR1.enter_context(
                            nc.sbuf_tensor(f"k_fm_{b}", [128, 8 * S], BF16, side="right"))
                        s.q_fm = gR1.enter_context(
                            nc.sbuf_tensor(f"q_fm_{b}", [128, 8 * S], BF16, side="right"))
                        s.v_tm = gR1.enter_context(
                            nc.sbuf_tensor(f"v_tm_{b}", [128, NTT * 1040], BF16, side="right"))
                        with ExitStack() as p1s:
                            s.xt = p1s.enter_context(nc.sbuf_tensor(f"xt_{b}", [128, 2 * D], F32))
                            s.xnt = p1s.enter_context(nc.sbuf_tensor(f"xnt_{b}", [128, D], F32))
                            s.xnbt = p1s.enter_context(nc.sbuf_tensor(f"xnbt_{b}", [128, D], BF16))
                            s.xqbt = p1s.enter_context(nc.sbuf_tensor(f"xqbt_{b}", [128, D], BF16))
                            s.xqs = p1s.enter_context(nc.sbuf_tensor(f"xqs_{b}", [128, 8 * 128], BF16))
                            s.dump = p1s.enter_context(nc.sbuf_tensor(f"dump_{b}", [128, D], F32))
                            s.rtmp = p1s.enter_context(nc.sbuf_tensor(f"rtmp_{b}", [128, D], F32))
                            self.grouped(self.p1_ln_quant, b)
                        self.bar()
                        if stop == "p1":
                            self.finale()
                            return nc
                        with ExitStack() as p2s:
                            s.wcolb = p2s.enter_context(
                                nc.sbuf_tensor(f"wcolb_{b}", [128, 2 * 1024], BF16))
                            s.wvb = p2s.enter_context(
                                nc.sbuf_tensor(f"wvb_{b}", [128, 2 * 8 * 512], BF16))
                            self.grouped(self.p2_proj, b)
                        self.bar()
                        if stop == "p2":
                            self.finale()
                            return nc
                        with ExitStack() as p3s:
                            s.skw = p3s.enter_context(
                                nc.sbuf_tensor(f"skw_{b}", [128, 3 * 3968], BF16, side="left"))
                            s.eraw = p3s.enter_context(
                                nc.sbuf_tensor(f"eraw_{b}", [128, 3 * 512], BF16, side="left"))
                            s.emul = p3s.enter_context(
                                nc.sbuf_tensor(f"emul_{b}", [128, 20 * 512], BF16, side="left"))
                            s.rdbuf = p3s.enter_context(
                                nc.sbuf_tensor(f"rdbuf_{b}", [128, 512], BF16, side="left"))
                            s.rbs = p3s.enter_context(
                                nc.sbuf_tensor(f"rbs_{b}", [64, 2 * 512], BF16, side="left"))
                            self.grouped(self.p3_attn, b)
                        self.bar()
                        if stop == "p3":
                            self.finale()
                            return nc
                        gR1.close()
                        with ExitStack() as p4s:
                            s.wgs = p4s.enter_context(
                                nc.sbuf_tensor(f"wgs_{b}", [128, 8 * 2048], BF16, side="right"))
                            s.sgb = p4s.enter_context(
                                nc.sbuf_tensor(f"sgb_{b}", [128, 512], F32, side="right"))
                            s.abb = p4s.enter_context(
                                nc.sbuf_tensor(f"abb_{b}", [128, 2 * 1024], F32, side="right"))
                            self.grouped(self.p4_gate, b)
                        self.bar()
                        if stop == "p4":
                            self.finale()
                            return nc
                with ExitStack() as p56s:
                    s.wq1s = p56s.enter_context(nc.sbuf_tensor(f"wq1s_{b}", [128, 8 * F], BF16))
                    s.wq2s = p56s.enter_context(nc.sbuf_tensor(f"wq2s_{b}", [128, 32 * D], BF16))
                    s.gbuf = p56s.enter_context(nc.sbuf_tensor(f"gbuf_{b}", [128, F], F32))
                    s.hqt = p56s.enter_context(nc.sbuf_tensor(f"hqt_{b}", [128, 2 * F], BF16))
                    s.gsqt = p56s.enter_context(nc.sbuf_tensor(f"gsqt_{b}", [128, 512], F32))
                    s.hqf = s.hqt
                    s.xqt = p56s.enter_context(nc.sbuf_tensor(f"xqt_{b}", [128, 2 * 1024], BF16))
                    s.abrt = p56s.enter_context(nc.sbuf_tensor(f"abrt_{b}", [128, 2 * D], F32))
                    s.xat = p56s.enter_context(nc.sbuf_tensor(f"xat_{b}", [128, 2 * D], F32))
                    self.grouped(self.p56_ffn, b)
                self.bar()
                if stop == "p56":
                    self.finale()
                    return nc
            self.finale()
        return nc

    def phase0(self):
        s, ev = self, self.ev
        SQ, SP = s.SQ, s.SP
        sPE, sDV, sAC, sPL = s.sPE, s.sDV, s.sAC, s.sPL

        def l3(n, i):
            return ((n - 1 - i) // 3) * 3 + i

        def loads(sync):
            for dst, src in [(s.ident_bf[:], s.ident_bf_d.ap()),
                             (s.selG[:], s.selG_d.ap()),
                             (s.b1row[:], s.b1_in[None, :]),
                             (s.b2row[:], s.b2_in[None, :]),
                             (s.bgrow[:], s.b_gate_in[None, :])]:
                self.dma(SQ[0], sync.dma_start(out=dst, in_=src))
            for g in range(6):
                nr = 63 if g < 5 else 21
                for k, t in enumerate((s.amp_in, s.sharp_in, s.off_in)):
                    flat = t.rearrange("h k -> (h k)")
                    self.dma(SQ[0], sync.dma_start(
                        out=s.hkG[0:nr, 3 * g + k:3 * g + k + 1],
                        in_=flat[g * 63:g * 63 + nr][:, None],
                        allow_slow_non_contiguous=True))
            ev["p0_loads"] = (SQ[0], SQ[0].n)
            ev["hkg"] = (SQ[0], SQ[0].n)
        self.on("sync", loads)

        def iota0(gpsimd):
            ev["iota"] = sPL.mark(gpsimd.iota(
                out=s.iotaf[:], pattern=[[1, TW]], base=0, channel_multiplier=0,
                allow_small_or_imprecise_dtypes=True))
        self.on("gpsimd", iota0)

        def consts(vector):
            self.wt(vector, "p0_loads")
            vector.memset(s.ones1_bf[:], 1.0)
            vector.memset(s.ones1_f[:], 1.0)
            vector.memset(s.ones128[:], 1.0)
            vector.memset(s.onesp64[64:65, :], 1.0)
            vector.memset(s.epscol[:], EPS)
            vector.memset(s.negmag[:], -MAGIC)
            vector.memset(s.onesbc[0:1, :], 1.0)
            vector.memset(s.onesbc[32:33, :], 1.0)
            vector.memset(s.onesbc[64:65, :], 1.0)
            vector.tensor_copy(out=s.bgate_bf[:], in_=s.bgrow[:])
            vector.wait_ge(sPL.h, ev["iota"])
            ev["relrev"] = sDV.mark(vector.tensor_scalar(
                out=s.relrev[:], in0=s.iotaf[:], scalar1=-1.0, scalar2=2047.0,
                op0=ALU.mult, op1=ALU.add))
        self.on("vector", consts)

        act0 = sAC.n
        dvb0 = sDV.n

        def rexp_ev(g, c, act0=act0):
            return act0 + 9 * g + 2 + c

        def bce_ev(h, c, dvb0=dvb0):
            return dvb0 + 6 + 12 + 8 * h + c + 1

        def hkg_neg(vector):
            self.wt(vector, "hkg")
            for g in range(6):
                c = 3 * g + 1
                ev[f"hkgn_{g}"] = sDV.mark(vector.scalar_tensor_tensor(
                    out=s.hkG[0:63, c:c + 1], in0=s.hkG[0:63, c:c + 1], scalar=-1.0,
                    in1=s.hkG[0:63, c:c + 1], op0=ALU.mult, op1=ALU.min))
        self.on("vector", hkg_neg)

        for g in range(6):
            nh = 3 if g < 5 else 1
            nr = nh * 21

            def u_step(vector, g=g, nr=nr):
                vector.wait_ge(sDV.h, ev[f"hkgn_{g}"])
                if g == 0:
                    vector.wait_ge(sDV.h, ev["relrev"])
                else:
                    vector.wait_ge(sPE.h, ev[f"tsum_{g-1}"])
                ev[f"u_{g}"] = sDV.mark(vector.tensor_scalar(
                    out=s.gwork[0:nr, :], in0=s.relrev[0:nr, :],
                    scalar1=s.hkG[0:nr, 3 * g + 2:3 * g + 3], scalar2=None,
                    op0=ALU.subtract))
            self.on("vector", u_step)

            def exp_step(scalar, g=g, nr=nr):
                scalar.wait_ge(sDV.h, ev[f"u_{g}"])
                scalar.activation(out=s.gwork[0:nr, :], in_=s.gwork[0:nr, :], func=AF.Square)
                ev[f"gexp_{g}"] = sAC.mark(scalar.activation(
                    out=s.gwork[0:nr, :], in_=s.gwork[0:nr, :], func=AF.Exp,
                    scale=s.hkG[0:nr, 3 * g + 1:3 * g + 2]))
            self.on("scalar", exp_step)

            def amp_step(vector, g=g, nr=nr):
                vector.wait_ge(sAC.h, ev[f"gexp_{g}"])
                ev[f"ge_{g}"] = sDV.mark(vector.tensor_scalar(
                    out=s.gwork[0:nr, :], in0=s.gwork[0:nr, :],
                    scalar1=s.hkG[0:nr, 3 * g:3 * g + 1], scalar2=None, op0=ALU.mult))
            self.on("vector", amp_step)

            def sum_step(tensor, g=g, nr=nr, nh=nh):
                tensor.wait_ge(sDV.h, ev[f"ge_{g}"])
                for c in range(8):
                    n = min(512, TW - c * 512)
                    if g > 0 or c > 0:
                        pv = rexp_ev(g, c - 1) if c > 0 else rexp_ev(g - 1, 7)
                        tensor.wait_ge(sAC.h, pv)
                    ev[f"tsum_{g}_{c}"] = sPE.mark(tensor.matmul(
                        out=s.pC[0:(32 * (nh - 1) + 1), 0:n], lhsT=s.selG[0:nr, 0:(32 * (nh - 1) + 1)],
                        rhs=s.gwork[0:nr, c * 512:c * 512 + n], start=True, stop=True))
                ev[f"tsum_{g}"] = sPE.n
            self.on("tensor", sum_step)

            def rexp_step(scalar, g=g, nh=nh):
                for c in range(8):
                    n = min(512, TW - c * 512)
                    scalar.wait_ge(sPE.h, ev[f"tsum_{g}_{c}"])
                    for m in range(nh):
                        ins = scalar.activation(
                            out=s.rowexp[32 * m:32 * m + 1, c * 512:c * 512 + n],
                            in_=s.pC[32 * m:32 * m + 1, 0:n], func=AF.Exp)
                    v = sAC.mark(ins)
                    ev[f"rexp_{g}_{c}"] = v
                    assert v == rexp_ev(g, c), (v, g, c)
            self.on("scalar", rexp_step)

        for h in range(H):
            g, m = divmod(h, 3)

            def bc_step(tensor, h=h, g=g, m=m):
                for c in range(8):
                    n = min(512, TW - c * 512)
                    tensor.wait_ge(sAC.h, rexp_ev(g, c))
                    pb = s.pA if c % 2 == 0 else s.pB
                    if h > 0 or c >= 2:
                        pv = bce_ev(h, c - 2) if c >= 2 else bce_ev(h - 1, 6 + c)
                        tensor.wait_ge(sDV.h, pv)
                    ev[f"bcm_{h}_{c}"] = sPE.mark(tensor.matmul(
                        out=pb[:, 0:n], lhsT=s.onesbc[32 * m:32 * m + 1, :],
                        rhs=s.rowexp[32 * m:32 * m + 1, c * 512:c * 512 + n],
                        start=True, stop=True, tile_position=(32 * m, 0)))
            self.on("tensor", bc_step)

            def bce_step(vector, h=h):
                for c in range(8):
                    n = min(512, TW - c * 512)
                    pb = s.pA if c % 2 == 0 else s.pB
                    vector.wait_ge(sPE.h, ev[f"bcm_{h}_{c}"])
                    if h > 0 and c == 0:
                        self.wt(vector, f"expw_{h-1}")
                    v = sDV.mark(vector.tensor_copy(
                        out=s.bch[:, c * 512:c * 512 + n], in_=pb[:, 0:n]))
                    ev[f"bce_{h}_{c}"] = v
                    assert v == bce_ev(h, c), (v, bce_ev(h, c), h, c)
                ev[f"bce_{h}"] = sDV.n
            self.on("vector", bce_step)

            def expw_step(sync, h=h):
                sync.wait_ge(sDV.h, ev[f"bce_{h}"])
                dst = bass.AP(s.d_expb.tensor, h * 128 * EXPW, [[EXPW, 128], [1, TW]])
                self.dma(SP[h % 2], sync.dma_start(out=dst, in_=s.bch[:, 0:TW]), f"expw_{h}")
            self.on("sync", expw_step)

        for wname, src_w, dst_w, ncols in (("win", s.w_in, s.d_winb, 3 * D),
                                           ("wg", s.w_gate_in, s.d_wgb, 2 * D)):
            for i in range(8):
                def wld(sync, wname=wname, i=i, src_w=src_w, ncols=ncols):
                    if i >= 3:
                        sync.wait_ge(sDV.h, ev[f"cast_{wname}_{i-3}"])
                    elif wname == "wg":
                        sync.wait_ge(sDV.h, ev[f"cast_win_{l3(8, i)}"])
                    h2 = ncols // 2
                    self.dma(SQ[1 + i % 3], sync.dma_start(
                        out=s.wtile[:, (i % 3) * F:(i % 3) * F + h2],
                        in_=src_w[i * 128:(i + 1) * 128, 0:h2]), f"wld_{wname}_{i}")
                    self.dma(SQ[4 + i % 3], sync.dma_start(
                        out=s.wtile[:, (i % 3) * F + h2:(i % 3) * F + ncols],
                        in_=src_w[i * 128:(i + 1) * 128, h2:ncols]), f"wldR_{wname}_{i}")
                self.on("sync", wld)

                def wcast(vector, wname=wname, i=i, ncols=ncols):
                    self.wt(vector, f"wld_{wname}_{i}")
                    self.wt(vector, f"wldR_{wname}_{i}")
                    if i >= 3:
                        self.wt(vector, f"wst_{wname}_{i-3}")
                    if wname == "wg" and i < 3:
                        self.wt(vector, f"wst_win_{l3(8, i)}")
                    ev[f"cast_{wname}_{i}"] = sDV.mark(vector.tensor_copy(
                        out=s.wqtile[:, (i % 3) * F:(i % 3) * F + ncols],
                        in_=s.wtile[:, (i % 3) * F:(i % 3) * F + ncols]))
                self.on("vector", wcast)

                def wst(sync, wname=wname, i=i - 2, dst_w=dst_w, ncols=ncols):
                    if i < 0:
                        return
                    sync.wait_ge(sDV.h, ev[f"cast_{wname}_{i}"])
                    self.dma(SP[2 + i % 3], sync.dma_start(
                        out=dst_w[i * 128:(i + 1) * 128, :],
                        in_=s.wqtile[:, (i % 3) * F:(i % 3) * F + ncols]),
                        f"wst_{wname}_{i}")
                self.on("sync", wst)
                if i == 7:
                    for it in (6, 7):
                        def wstt(sync, wname=wname, i=it, dst_w=dst_w, ncols=ncols):
                            sync.wait_ge(sDV.h, ev[f"cast_{wname}_{i}"])
                            self.dma(SP[2 + i % 3], sync.dma_start(
                                out=dst_w[i * 128:(i + 1) * 128, :],
                                in_=s.wqtile[:, (i % 3) * F:(i % 3) * F + ncols]),
                                f"wst_{wname}_{i}")
                        self.on("sync", wstt)

        for wname, src_w, dst_w, ntile, ncols, scol in (
                ("w1", s.w1_in, s.d_wq1, 32, D, s.s1col),
                ("w2", s.w2_in, s.d_wq2, 8, F, s.s2col)):
            nelem = float(ntile * 128 * ncols)
            nblk = ncols // 128
            for i in range(ntile):
                def ald(sync, wname=wname, i=i, src_w=src_w, ncols=ncols):
                    if i >= 3:
                        sync.wait_ge(sDV.h, ev[f"asum_{wname}_{i-3}"])
                    if i < 3 and wname == "w1":
                        sync.wait_ge(sDV.h, ev[f"cast_wg_{l3(8, i)}"])
                    if i < 3 and wname == "w2":
                        sync.wait_ge(sDV.h, ev[f"q_w1_{l3(32, i)}"])
                    h2 = ncols // 2
                    self.dma(SQ[1 + i % 3], sync.dma_start(
                        out=s.wtile[:, (i % 3) * F:(i % 3) * F + h2],
                        in_=src_w[i * 128:(i + 1) * 128, 0:h2]), f"ald_{wname}_{i}")
                    self.dma(SQ[4 + i % 3], sync.dma_start(
                        out=s.wtile[:, (i % 3) * F + h2:(i % 3) * F + ncols],
                        in_=src_w[i * 128:(i + 1) * 128, h2:ncols]), f"aldR_{wname}_{i}")
                self.on("sync", ald)

                def ared(vector, wname=wname, i=i, ncols=ncols):
                    self.wt(vector, f"ald_{wname}_{i}")
                    self.wt(vector, f"aldR_{wname}_{i}")
                    v = sDV.mark(vector.tensor_reduce(
                        out=s.cb[:, 0:1], in_=s.wtile[:, (i % 3) * F:(i % 3) * F + ncols],
                        axis=AX.X, op=ALU.add, apply_absolute_value=True))
                    vector.wait_ge(sDV.h, v)
                    if i == 0:
                        ev[f"asum_{wname}_{i}"] = sDV.mark(
                            vector.tensor_copy(out=s.cb[:, 1:2], in_=s.cb[:, 0:1]))
                    else:
                        ev[f"asum_{wname}_{i}"] = sDV.mark(vector.tensor_tensor(
                            out=s.cb[:, 1:2], in0=s.cb[:, 1:2], in1=s.cb[:, 0:1], op=ALU.add))
                self.on("vector", ared)

            def tot(tensor, wname=wname, ntile=ntile):
                tensor.wait_ge(sDV.h, ev[f"asum_{wname}_{ntile-1}"])
                tensor.wait_ge(sAC.h, rexp_ev(5, 7))
                ev[f"tot_{wname}"] = sPE.mark(tensor.matmul(
                    out=s.pC[0:1, 0:1], lhsT=s.ones128[:], rhs=s.cb[:, 1:2],
                    start=True, stop=True))
            self.on("tensor", tot)

            def sv(vector, wname=wname, nelem=nelem):
                vector.wait_ge(sPE.h, ev[f"tot_{wname}"])
                v = sDV.mark(vector.tensor_scalar(
                    out=s.scal[0:1, 0:1], in0=s.pC[0:1, 0:1], scalar1=1.0 / nelem,
                    scalar2=1e-5, op0=ALU.mult, op1=ALU.max))
                vector.wait_ge(sDV.h, v)
                ev[f"sval_{wname}"] = sDV.mark(
                    vector.reciprocal(out=s.scal[0:1, 1:2], in_=s.scal[0:1, 0:1]))
            self.on("vector", sv)

            def sbc(tensor, wname=wname):
                tensor.wait_ge(sDV.h, ev[f"sval_{wname}"])
                ev[f"sbc_{wname}"] = sPE.mark(tensor.matmul(
                    out=s.pC[:, 0:1], lhsT=s.ones1_f[:], rhs=s.scal[0:1, 1:2],
                    start=True, stop=True))
            self.on("tensor", sbc)

            def sce(vector, wname=wname, scol=scol):
                vector.wait_ge(sPE.h, ev[f"sbc_{wname}"])
                ev[f"scol_{wname}"] = sDV.mark(vector.tensor_copy(out=scol[:], in_=s.pC[:, 0:1]))
            self.on("vector", sce)

            for i in range(ntile):
                def qld(sync, wname=wname, i=i, src_w=src_w, ncols=ncols, ntile=ntile):
                    if i >= 3:
                        sync.wait_ge(sDV.h, ev[f"q_{wname}_{i-3}"])
                    if i < 3:
                        sync.wait_ge(sDV.h, ev[f"asum_{wname}_{l3(ntile, i)}"])
                    h2 = ncols // 2
                    self.dma(SQ[1 + i % 3], sync.dma_start(
                        out=s.wtile[:, (i % 3) * F:(i % 3) * F + h2],
                        in_=src_w[i * 128:(i + 1) * 128, 0:h2]), f"qld_{wname}_{i}")
                    self.dma(SQ[4 + i % 3], sync.dma_start(
                        out=s.wtile[:, (i % 3) * F + h2:(i % 3) * F + ncols],
                        in_=src_w[i * 128:(i + 1) * 128, h2:ncols]), f"qldR_{wname}_{i}")
                self.on("sync", qld)
                def wqst(sync, wname=wname, i=i - 2, dst_w=dst_w, ncols=ncols, nblk=nblk):
                    if i < 0:
                        return
                    sync.wait_ge(sAC.h, ev[f"tev_{wname}_{i}_{nblk-1}"])
                    if wname == "w1" and i < 3:
                        self.wt(sync, f"wst_wg_{l3(8, i)}")
                    if wname == "w2" and i < 3:
                        self.wt(sync, f"wqst_w1_{l3(32, i)}")
                    pitch = F if wname == "w1" else D
                    dst = bass.AP(dst_w.tensor, i * 128,
                                  [[pitch, 128], [128 * pitch, nblk], [1, 128]])
                    self.dma(SP[2 + i % 3], sync.dma_start(
                        out=dst, in_=s.stg8[:, (i % 2) * ncols:(i % 2) * ncols + ncols].rearrange(
                            "p (j n) -> p j n", j=nblk)), f"wqst_{wname}_{i}")
                self.on("sync", wqst)

                def qnt1(vector, wname=wname, i=i, ncols=ncols, scol=scol):
                    self.wt(vector, f"qld_{wname}_{i}")
                    self.wt(vector, f"qldR_{wname}_{i}")
                    if i == 0:
                        vector.wait_ge(sDV.h, ev[f"scol_{wname}"])
                    vector.tensor_scalar(
                        out=s.wtile[:, (i % 3) * F:(i % 3) * F + ncols],
                        in0=s.wtile[:, (i % 3) * F:(i % 3) * F + ncols],
                        scalar1=scol[:, 0:1], scalar2=MAGIC,
                        op0=ALU.mult, op1=ALU.add)
                    ev[f"q1_{wname}_{i}"] = sDV.mark(vector.tensor_scalar(
                        out=s.wtile[:, (i % 3) * F:(i % 3) * F + ncols],
                        in0=s.wtile[:, (i % 3) * F:(i % 3) * F + ncols],
                        scalar1=MAGIC, scalar2=1.0,
                        op0=ALU.subtract, op1=ALU.min))
                self.on("vector", qnt1)

                def qnt2(vector, wname=wname, i=i, ncols=ncols, nblk=nblk):
                    if i >= 3:
                        vector.wait_ge(sPE.h, ev[f"trp_{wname}_{i-3}_{nblk-1}"])
                    ev[f"q_{wname}_{i}"] = sDV.mark(vector.tensor_scalar(
                        out=s.wqtile[:, (i % 3) * ncols:(i % 3) * ncols + ncols],
                        in0=s.wtile[:, (i % 3) * F:(i % 3) * F + ncols],
                        scalar1=-1.0, scalar2=None, op0=ALU.max))
                self.on("vector", qnt2)

                nblk = ncols // 128
                for rr in range((nblk + 1) // 2):
                    def trp(tensor, wname=wname, i=i, rr=rr, nblk=nblk, ncols=ncols):
                        for half in range(2):
                            j = rr * 2 + half
                            if j >= nblk:
                                continue
                            if rr == 0 and half == 0:
                                tensor.wait_ge(sDV.h, ev[f"q_{wname}_{i}"])
                            pb = s.pT1 if j % 2 == 0 else s.pT2
                            if i > 0 or j >= 2:
                                key = (f"tev_{wname}_{i}_{j-2}" if j >= 2
                                       else f"tev_{wname}_{i-1}_{nblk - 2 + j}")
                                tensor.wait_ge(sAC.h, ev[key])
                            ev[f"trp_{wname}_{i}_{j}"] = sPE.mark(tensor.transpose(
                                out=pb[:],
                                in_=s.wqtile[:, (i % 3) * ncols + j * 128:(i % 3) * ncols + (j + 1) * 128],
                                identity=s.ident_bf[:]))
                    self.on("tensor", trp)

                    def tev(scalar, wname=wname, i=i, rr=rr, nblk=nblk, ncols=ncols):
                        for half in range(2):
                            j = rr * 2 + half
                            if j >= nblk:
                                continue
                            pb = s.pT1 if j % 2 == 0 else s.pT2
                            scalar.wait_ge(sPE.h, ev[f"trp_{wname}_{i}_{j}"])
                            if j == 0 and i >= 2:
                                self.wt(scalar, f"wqst_{wname}_{i-2}")
                            ev[f"tev_{wname}_{i}_{j}"] = sAC.mark(scalar.activation(
                                out=s.stg8[:, (i % 2) * ncols + j * 128:(i % 2) * ncols + (j + 1) * 128],
                                in_=pb[:], func=AF.Copy))
                    self.on("scalar", tev)

                if i == ntile - 1:
                    for it in (ntile - 2, ntile - 1):
                        def wqstt(sync, wname=wname, i=it, dst_w=dst_w, ncols=ncols, nblk=nblk):
                            sync.wait_ge(sAC.h, ev[f"tev_{wname}_{i}_{nblk-1}"])
                            pitch = F if wname == "w1" else D
                            dst = bass.AP(dst_w.tensor, i * 128,
                                          [[pitch, 128], [128 * pitch, nblk], [1, 128]])
                            self.dma(SP[2 + i % 3], sync.dma_start(
                                out=dst, in_=s.stg8[:, (i % 2) * ncols:(i % 2) * ncols + ncols].rearrange(
                                    "p (j n) -> p j n", j=nblk)), f"wqst_{wname}_{i}")
                        self.on("sync", wqstt)



        dvb0 = sDV.n

        def bbe_ev(c, dvb0=dvb0):
            return dvb0 + c + 1

        def bbm(tensor):
            tensor.wait_ge(sDV.h, ev[f"bce_{H-1}"])
            for c in range(10):
                pb = s.pA if c % 2 == 0 else s.pB
                if c >= 2:
                    tensor.wait_ge(sDV.h, bbe_ev(c - 2))
                rhs = (s.b1row[0:1, c * 512:(c + 1) * 512] if c < 8
                       else s.b2row[0:1, (c - 8) * 512:(c - 7) * 512])
                ev[f"bbm_{c}"] = sPE.mark(tensor.matmul(
                    out=pb[:], lhsT=s.ones1_f[:], rhs=rhs, start=True, stop=True))
        self.on("tensor", bbm)

        def bbe(vector):
            for c in range(10):
                pb = s.pA if c % 2 == 0 else s.pB
                vector.wait_ge(sPE.h, ev[f"bbm_{c}"])
                dst = (s.b1b[:, c * 512:(c + 1) * 512] if c < 8
                       else s.b2b[:, (c - 8) * 512:(c - 7) * 512])
                v = sDV.mark(vector.tensor_copy(out=dst, in_=pb[:]))
                ev[f"bbe_{c}"] = v
                assert v == bbe_ev(c), (v, c)
        self.on("vector", bbe)

    def p1_ln_quant(self, b):
        s, ev = self, self.ev
        SQ, SP = s.SQ, s.SP
        sPE, sDV, sAC, sPL = s.sPE, s.sDV, s.sAC, s.sPL

        tpcb = self.tap(f"p1cb_b{b}", [128, 20 * NTT], F32)

        for tt in range(NTT):
            def xld(sync, b=b, tt=tt):
                if tt >= 2:
                    sync.wait_ge(sDV.h, ev[f"xnt_{b}_{tt-2}"])
                self.dma(SQ[1 + tt % 2], sync.dma_start(
                    out=s.xt[:, (tt % 2) * D:(tt % 2) * D + D],
                    in_=s.x_in[b, tt * 128:(tt + 1) * 128, :]), f"xld_{b}_{tt}")
            self.on("sync", xld)
            def xq_w(sync, b=b, tt=tt - 1):
                if tt < 0:
                    return
                sync.wait_ge(sAC.h, ev[f"xqe_{b}_{tt}"])
                dst = s.d_xq[b, :, :, tt * 128:(tt + 1) * 128]
                src = s.xqs[:].rearrange("p (dc n) -> p dc n", dc=8)
                self.dma(SP[tt % 2], sync.dma_start(out=dst.transpose([1, 0, 2]), in_=src),
                         f"xqw_{b}_{tt}")
            self.on("sync", xq_w)

            def stats_a(gpsimd, b=b, tt=tt):
                self.wt(gpsimd, f"xld_{b}_{tt}")
                if tt >= 1:
                    gpsimd.wait_ge(sDV.h, ev[f"rden_{b}_{tt-1}"])
                ev[f"ss_{b}_{tt}"] = sPL.mark(gpsimd.tensor_tensor(
                    out=s.dump[:], in0=s.xt[:, (tt % 2) * D:(tt % 2) * D + D],
                    in1=s.xt[:, (tt % 2) * D:(tt % 2) * D + D], op=ALU.mult))
            self.on("gpsimd", stats_a)

            def stats_v(vector, b=b, tt=tt):
                self.wt(vector, f"xld_{b}_{tt}")
                v = sDV.mark(vector.tensor_reduce(out=s.cb[:, 0:1], in_=s.xt[:, (tt % 2) * D:(tt % 2) * D + D],
                                                  axis=AX.X, op=ALU.add))
                vector.wait_ge(sPL.h, ev[f"ss_{b}_{tt}"])
                v = sDV.mark(vector.tensor_reduce(out=s.cb[:, 1:2], in_=s.dump[:],
                                                  axis=AX.X, op=ALU.add))
                vector.wait_ge(sDV.h, v)
                v = sDV.mark(vector.tensor_scalar(out=s.cb[:, 2:3], in0=s.cb[:, 0:1],
                                                  scalar1=1.0 / D, scalar2=None, op0=ALU.mult))
                vector.wait_ge(sDV.h, v)
                v = sDV.mark(vector.tensor_scalar(out=s.cb[:, 3:4], in0=s.cb[:, 2:3],
                                                  scalar1=s.cb[:, 2:3], scalar2=None,
                                                  op0=ALU.mult))
                vector.wait_ge(sDV.h, v)
                ev[f"var_{b}_{tt}"] = sDV.mark(vector.scalar_tensor_tensor(
                    out=s.cb[:, 4:5], in0=s.cb[:, 1:2], scalar=1.0 / D, in1=s.cb[:, 3:4],
                    op0=ALU.mult, op1=ALU.subtract))
            self.on("vector", stats_v)

            def rstd_a(scalar, b=b, tt=tt):
                scalar.wait_ge(sDV.h, ev[f"var_{b}_{tt}"])
                ev[f"sqv_{b}_{tt}"] = sAC.mark(scalar.activation(
                    out=s.cb[:, 5:6], in_=s.cb[:, 4:5], func=AF.Sqrt, bias=s.epscol[:, 0:1]))
            self.on("scalar", rstd_a)

            def ln_ap(vector, b=b, tt=tt):
                vector.wait_ge(sAC.h, ev[f"sqv_{b}_{tt}"])
                v = sDV.mark(vector.reciprocal(out=s.cb[:, 6:7], in_=s.cb[:, 5:6]))
                vector.wait_ge(sDV.h, v)
                if tt >= 1:
                    vector.wait_ge(sAC.h, ev[f"xnc_{b}_{tt-1}"])
                    vector.wait_ge(sPL.h, ev[f"xnq_{b}_{tt-1}"])
                ev[f"xnt_{b}_{tt}"] = sDV.mark(vector.tensor_scalar(
                    out=s.xnt[:], in0=s.xt[:, (tt % 2) * D:(tt % 2) * D + D], scalar1=s.cb[:, 2:3],
                    scalar2=s.cb[:, 6:7], op0=ALU.subtract, op1=ALU.mult))
            self.on("vector", ln_ap)

            def xn_a(scalar, b=b, tt=tt):
                scalar.wait_ge(sDV.h, ev[f"xnt_{b}_{tt}"])
                if tt >= 1:
                    scalar.wait_ge(sPE.h, ev[f"xntr_{b}_{tt-1}"])
                ev[f"xnc_{b}_{tt}"] = sAC.mark(
                    scalar.activation(out=s.xnbt[:], in_=s.xnt[:], func=AF.Copy))
            self.on("scalar", xn_a)

            def xn_sq(gpsimd, b=b, tt=tt):
                gpsimd.wait_ge(sDV.h, ev[f"xnt_{b}_{tt}"])
                ev[f"xnq_{b}_{tt}"] = sPL.mark(gpsimd.tensor_tensor(
                    out=s.dump[:], in0=s.xnt[:], in1=s.xnt[:], op=ALU.mult))
            self.on("gpsimd", xn_sq)

            def q_cols(vector, b=b, tt=tt):
                vector.wait_ge(sPL.h, ev[f"xnq_{b}_{tt}"])
                v = sDV.mark(vector.tensor_reduce(out=s.cb[:, 8:9], in_=s.xnt[:], axis=AX.X,
                                                  op=ALU.max, apply_absolute_value=True))
                v = sDV.mark(vector.tensor_reduce(out=s.cb[:, 7:8], in_=s.dump[:],
                                                  axis=AX.X, op=ALU.add))
                vector.wait_ge(sDV.h, v)
                v = sDV.mark(vector.tensor_scalar(out=s.cb[:, 9:10], in0=s.cb[:, 7:8],
                                                  scalar1=1.0 / D, scalar2=EPS,
                                                  op0=ALU.mult, op1=ALU.add))
                ev[f"rden_{b}_{tt}"] = v
            self.on("vector", q_cols)

            def rsq_a(scalar, b=b, tt=tt):
                scalar.wait_ge(sDV.h, ev[f"rden_{b}_{tt}"])
                ev[f"rsq_{b}_{tt}"] = sAC.mark(scalar.activation(
                    out=s.cb[:, 10:11], in_=s.cb[:, 9:10], func=AF.Sqrt))
            self.on("scalar", rsq_a)

            def q_fin(vector, b=b, tt=tt):
                vector.wait_ge(sAC.h, ev[f"rsq_{b}_{tt}"])
                v = sDV.mark(vector.reciprocal(out=s.cb[:, 11:12], in_=s.cb[:, 10:11]))
                vector.wait_ge(sDV.h, v)
                v = sDV.mark(vector.tensor_scalar(out=s.cb[:, 12:13], in0=s.cb[:, 8:9],
                                                  scalar1=s.cb[:, 11:12], scalar2=1e-5,
                                                  op0=ALU.mult, op1=ALU.max))
                vector.wait_ge(sDV.h, v)
                v = sDV.mark(vector.reciprocal(out=s.cb[:, 13:14], in_=s.cb[:, 12:13]))
                vector.wait_ge(sDV.h, v)
                v = sDV.mark(vector.tensor_scalar(out=s.cb[:, 14:15], in0=s.cb[:, 13:14],
                                                  scalar1=127.0, scalar2=None, op0=ALU.mult))
                vector.wait_ge(sDV.h, v)
                v = sDV.mark(vector.tensor_scalar(out=s.cb[:, 15:16], in0=s.cb[:, 11:12],
                                                  scalar1=s.cb[:, 14:15], scalar2=None,
                                                  op0=ALU.mult))
                vector.wait_ge(sDV.h, v)
                v = sDV.mark(vector.tensor_scalar(out=s.cb[:, 16:17], in0=s.cb[:, 14:15],
                                                  scalar1=s.s1col[:, 0:1], scalar2=None,
                                                  op0=ALU.mult))
                vector.wait_ge(sDV.h, v)
                sDV.mark(vector.reciprocal(out=s.dq1c[:, b * NTT + tt:b * NTT + tt + 1],
                                           in_=s.cb[:, 16:17]))
                if tt >= 1:
                    vector.wait_ge(sPE.h, ev[f"xqtr_{b}_{tt-1}"])
                vector.tensor_scalar(out=s.rtmp[:], in0=s.xnt[:], scalar1=s.cb[:, 15:16],
                                     scalar2=MAGIC, op0=ALU.mult, op1=ALU.add)
                ev[f"p1q_{b}_{tt}"] = sDV.mark(vector.tensor_scalar(
                    out=s.xqbt[:], in0=s.rtmp[:], scalar1=MAGIC, scalar2=None,
                    op0=ALU.subtract))
                if getattr(s, "p1cb", None) is not None:
                    ev[f"p1cb_{b}_{tt}"] = sDV.mark(vector.tensor_copy(
                        out=s.p1cb[:][:, tt * 20:tt * 20 + 20], in_=s.cb[:, 0:20]))
            self.on("vector", q_fin)

            for rnd in range(8):
                def tr_p(tensor, b=b, tt=tt, rnd=rnd):
                    for half in range(2):
                        blk = rnd * 2 + half
                        pb = s.pT1 if blk % 2 == 0 else s.pT2
                        if blk == 0:
                            tensor.wait_ge(sAC.h, ev[f"xnc_{b}_{tt}"])
                        if blk == 8:
                            tensor.wait_ge(sDV.h, ev[f"p1q_{b}_{tt}"])
                        if tt > 0 or blk >= 2:
                            key = (f"trev_{b}_{tt}_{blk-2}" if blk >= 2
                                   else f"trev_{b}_{tt-1}_{14 + blk}")
                            tensor.wait_ge(sAC.h, ev[key])
                        src = (s.xnbt[:, blk * 128:(blk + 1) * 128] if blk < 8
                               else s.xqbt[:, (blk - 8) * 128:(blk - 7) * 128])
                        ev[f"tr_{b}_{tt}_{blk}"] = sPE.mark(tensor.transpose(
                            out=pb[:], in_=src, identity=s.ident_bf[:]))
                    if rnd == 7:
                        ev[f"xqtr_{b}_{tt}"] = sPE.n
                        ev[f"xntr_{b}_{tt}"] = sPE.n
                self.on("tensor", tr_p)

                def tr_e(scalar, b=b, tt=tt, rnd=rnd):
                    for half in range(2):
                        blk = rnd * 2 + half
                        pb = s.pT1 if blk % 2 == 0 else s.pT2
                        scalar.wait_ge(sPE.h, ev[f"tr_{b}_{tt}_{blk}"])
                        if blk < 8:
                            dst = s.xn_fm[:, blk * S + tt * 128:blk * S + (tt + 1) * 128]
                        else:
                            if blk == 8 and tt > 0:
                                self.wt(scalar, f"xqw_{b}_{tt-1}")
                            dst = s.xqs[:, (blk - 8) * 128:(blk - 7) * 128]
                        ev[f"trev_{b}_{tt}_{blk}"] = sAC.mark(scalar.activation(
                            out=dst, in_=pb[:], func=AF.Copy))
                    if rnd == 7:
                        ev[f"xqe_{b}_{tt}"] = sAC.n
                        ev[f"xne_{b}_{tt}"] = sAC.n
                self.on("scalar", tr_e)



        def xq_wl(sync, b=b, tt=NTT - 1):
            sync.wait_ge(sAC.h, ev[f"xqe_{b}_{tt}"])
            dst = s.d_xq[b, :, :, tt * 128:(tt + 1) * 128]
            src = s.xqs[:].rearrange("p (dc n) -> p dc n", dc=8)
            self.dma(SP[tt % 2], sync.dma_start(out=dst.transpose([1, 0, 2]), in_=src),
                     f"xqw_{b}_{tt}")
        self.on("sync", xq_wl)

        tp = self.tap(f"xn_fm_b{b}", [128, 8 * S], BF16)
        if tp is not None:
            def tapw(gpsimd, b=b, tp=tp):
                gpsimd.wait_ge(s.sAC.h, ev[f"xqe_{b}_{NTT-1}"])
                self.dma(s.SP[4], gpsimd.dma_start(out=tp, in_=s.xn_fm[:]))
            self.on("gpsimd", tapw)
        tpcb2 = self.tap_outs.get(f"p1cb_b{b}")
        if tpcb2 is not None:
            def tapw2(gpsimd, b=b, tpcb2=tpcb2):
                gpsimd.wait_ge(s.sDV.h, ev[f"p1cb_{b}_{NTT-1}"])
                self.dma(s.SP[4], gpsimd.dma_start(out=tpcb2, in_=s.p1cb[:]))
            self.on("gpsimd", tapw2)

    def p2_proj(self, b):
        s, ev = self, self.ev
        SQ, SP = s.SQ, s.SP
        sPE, sDV, sAC, sPL = s.sPE, s.sDV, s.sAC, s.sPL

        tpq1 = self.tap("wq1t", [D, F], BF16)
        if tpq1 is not None:
            def tapq1(gpsimd):
                self.wt(gpsimd, "wqst_w1_31")
                self.dma(SP[4], gpsimd.dma_start(out=tpq1, in_=s.d_wq1))
            self.on("gpsimd", tapq1)

        tpw = self.tap("winb", [D, 3 * D], BF16)
        if tpw is not None and b == 0:
            def tapwb(gpsimd):
                self.wt(gpsimd, "wst_win_7")
                self.dma(SP[4], gpsimd.dma_start(out=tpw, in_=s.d_winb))
            self.on("gpsimd", tapwb)

        ac2 = sAC.n

        def pje_ev(fs, tc):
            return ac2 + fs * 4 + tc + 1

        for fs in range(16):
            wcol_off = fs * 128 if fs < 8 else 2 * D + (fs - 8) * 128

            def wcl(sync, b=b, fs=fs, wcol_off=wcol_off):
                if fs >= 2:
                    sync.wait_ge(sPE.h, ev[f"pjm_{b}_{fs-2}_3"])
                src = bass.AP(s.d_winb.tensor, wcol_off,
                              [[3 * D, 128], [128 * 3 * D, 8], [1, 128]])
                self.dma(SQ[1 + fs % 2], sync.dma_start(
                    out=s.wcolb[:, (fs % 2) * 1024:(fs % 2) * 1024 + 1024].rearrange("p (dc n) -> p dc n", dc=8),
                    in_=src), f"wcl_{b}_{fs}")
            self.on("sync", wcl)

            def pj(tensor, b=b, fs=fs):
                self.wt(tensor, f"wcl_{b}_{fs}")
                for tc in range(4):
                    pb = s.pA if tc % 2 == 0 else s.pB
                    pv = (pje_ev(fs, tc - 2) if tc >= 2
                          else (pje_ev(fs - 1, 2 + tc) if fs >= 1 else None))
                    if pv is not None:
                        tensor.wait_ge(sAC.h, pv)
                    for dc in range(8):
                        ins = tensor.matmul(
                            out=pb[:], lhsT=s.wcolb[:, (fs % 2) * 1024 + dc * 128:(fs % 2) * 1024 + (dc + 1) * 128],
                            rhs=s.xn_fm[:, dc * S + tc * 512:dc * S + (tc + 1) * 512],
                            start=(dc == 0), stop=(dc == 7))
                    ev[f"pjm_{b}_{fs}_{tc}"] = sPE.mark(ins)
            self.on("tensor", pj)

            def pje(scalar, b=b, fs=fs):
                for tc in range(4):
                    pb = s.pA if tc % 2 == 0 else s.pB
                    scalar.wait_ge(sPE.h, ev[f"pjm_{b}_{fs}_{tc}"])
                    if fs < 8:
                        dst = s.k_fm[:, fs * S + tc * 512:fs * S + (tc + 1) * 512]
                        sc = 1.0
                    else:
                        dst = s.q_fm[:, (fs - 8) * S + tc * 512:(fs - 8) * S + (tc + 1) * 512]
                        sc = 0.125
                    v = sAC.mark(scalar.activation(
                        out=dst, in_=pb[:], func=AF.Copy, scale=sc))
                    ev[f"pje_{b}_{fs}_{tc}"] = v
                    assert v == pje_ev(fs, tc), (v, fs, tc)
            self.on("scalar", pje)

        def vones(vector, b=b):
            ev[f"vones_{b}"] = sDV.mark(vector.memset(s.v_tm[:], 1.0))
        self.on("vector", vones)

        def wvl(sync, b=b):
            src = bass.AP(s.d_winb.tensor, D,
                          [[3 * D, 128], [128 * 3 * D, 8], [1, 1024]])
            self.dma(SQ[3], sync.dma_start(
                out=s.wvb[:].rearrange("p (dc n) -> p dc n", dc=8), in_=src), f"wvl_{b}")
        self.on("sync", wvl)

        for tt in range(NTT):
            def vpj(tensor, b=b, tt=tt):
                if tt == 0:
                    self.wt(tensor, f"wvl_{b}")
                for vc in range(2):
                    pb = s.pC if vc == 0 else s.pD
                    if tt >= 1:
                        tensor.wait_ge(sDV.h, ev[f"vpe_{b}_{tt-1}_{vc}"])
                    for dc in range(8):
                        ins = tensor.matmul(
                            out=pb[:], lhsT=s.xn_fm[:, dc * S + tt * 128:dc * S + (tt + 1) * 128],
                            rhs=s.wvb[:, dc * 1024 + vc * 512:dc * 1024 + (vc + 1) * 512],
                            start=(dc == 0), stop=(dc == 7))
                    ev[f"vpm_{b}_{tt}_{vc}"] = sPE.mark(ins)
            self.on("tensor", vpj)

            def vpe(vector, b=b, tt=tt):
                if tt == 0:
                    vector.wait_ge(sDV.h, ev[f"vones_{b}"])
                for vc in range(2):
                    pb = s.pC if vc == 0 else s.pD
                    vector.wait_ge(sPE.h, ev[f"vpm_{b}_{tt}_{vc}"])
                    base = tt * 1040 + vc * 520
                    dst = s.v_tm[:, base:base + 520].rearrange(
                        "p (g c) -> p g c", g=8, c=65)[:, :, 0:64]
                    src = pb[:].rearrange("p (g c) -> p g c", g=8, c=64)
                    ev[f"vpe_{b}_{tt}_{vc}"] = sDV.mark(vector.tensor_copy(out=dst, in_=src))
            self.on("vector", vpe)

        tpk = self.tap(f"k_fm_b{b}", [128, 8 * S], BF16)
        tpq = self.tap(f"q_fm_b{b}", [128, 8 * S], BF16)
        tpv = self.tap(f"v_tm_b{b}", [128, NTT * 1040], BF16)
        if tpk is not None:
            def tapw(gpsimd, b=b):
                gpsimd.wait_ge(s.sAC.h, ev[f"pje_{b}_15_3"])
                gpsimd.wait_ge(s.sDV.h, ev[f"vpe_{b}_{NTT-1}_1"])
                self.dma(s.SP[4], gpsimd.dma_start(out=tpk, in_=s.k_fm[:]))
                self.dma(s.SP[4], gpsimd.dma_start(out=tpq, in_=s.q_fm[:]))
                self.dma(s.SP[4], gpsimd.dma_start(out=tpv, in_=s.v_tm[:]))
            self.on("gpsimd", tapw)

    def p3_attn(self, b):
        s, ev = self, self.ev
        SQ, SP = s.SQ, s.SP
        sPE, sDV, sAC, sPL = s.sPE, s.sDV, s.sAC, s.sPL

        SKW = 3968

        NG = 4 * H
        NS = 20
        qk_ev, pv_ev, rbm_ev = {}, {}, {}
        n = sPE.n
        for m in range(NG):
            if m >= 2:
                n += 1
                rbm_ev[m - 2] = n
            for step in range(4):
                for j in range(4):
                    n += 1
                    qk_ev[(m, 4 * step + j)] = n
                if m >= 1:
                    for j in range(4):
                        n += 1
                        pv_ev[(m - 1, 4 * step + j)] = n
        for step in range(4):
            for j in range(4):
                n += 1
                pv_ev[(NG - 1, 4 * step + j)] = n
        n += 1
        rbm_ev[NG - 2] = n
        n += 1
        rbm_ev[NG - 1] = n

        exp_ev, dc_ev = {}, {}
        n = sAC.n
        for m in range(NG):
            for kt in range(16):
                n += 1
                exp_ev[(m, kt)] = n
            if m >= 1:
                n += 1
                dc_ev[m - 1] = n
        n += 1
        dc_ev[NG - 1] = n

        pool_ev, rd_ev, ae_ev = {}, {}, {}
        n = sDV.n
        for m in range(NG):
            if m >= 2:
                n += 1
                rd_ev[m - 2] = n
                n += 1
                ae_ev[m - 2] = n
            for kt in range(16):
                n += 1
                pool_ev[(m, kt)] = n
        for g in (NG - 2, NG - 1):
            n += 1
            rd_ev[g] = n
            n += 1
            ae_ev[g] = n

        psq = [s.pA, s.pB, s.pC]

        def bank(m, kt):
            return (m + kt) % 3

        def hof(g):
            h, qc = g // 4, g % 4
            return (h % 2) * 64, (h // 2) * S, qc

        def pOx(g):
            return s.pO if g % 2 == 0 else s.pO2

        def pe_all(tensor, b=b):
            def qk(m, kt):
                hh, hc, qc = hof(m)
                T = 16 * m + kt
                if T >= 3:
                    pm, pk = (m, kt - 3) if kt >= 3 else (m - 1, 13 + kt)
                    tensor.wait_ge(sAC.h, exp_ev[(pm, pk)])
                v = sPE.mark(tensor.matmul(
                    out=psq[bank(m, kt)][:],
                    lhsT=s.k_fm[hh:hh + 64, hc + kt * 128:hc + (kt + 1) * 128],
                    rhs=s.q_fm[hh:hh + 64, hc + qc * 512:hc + (qc + 1) * 512],
                    start=True, stop=True))
                assert v == qk_ev[(m, kt)]

            def pv(g, kt):
                hh, hc, qc = hof(g)
                h = g // 4
                if kt == 0 and g >= 2:
                    tensor.wait_ge(sDV.h, ae_ev[g - 2])
                    tensor.wait_ge(sAC.h, dc_ev[g - 2])
                tensor.wait_ge(sDV.h, pool_ev[(g, kt)])
                v = sPE.mark(tensor.matmul(
                    out=pOx(g)[0:65, :],
                    lhsT=s.v_tm[:, kt * 1040 + h * 65:kt * 1040 + h * 65 + 65],
                    rhs=s.emul[:, ((16 * g + kt) % NS) * 512:((16 * g + kt) % NS) * 512 + 512],
                    start=(kt == 0), stop=(kt == 15)))
                assert v == pv_ev[(g, kt)]

            def rbm(g):
                tensor.wait_ge(sAC.h, dc_ev[g])
                if g >= 1:
                    tensor.wait_ge(sDV.h, rd_ev[g - 1])
                v = sPE.mark(tensor.matmul(
                    out=s.pD[0:64, :], lhsT=s.onesp64[64:65, :],
                    rhs=s.rdbuf[64:65, :], start=True, stop=True))
                assert v == rbm_ev[g]

            for m in range(NG):
                if m >= 2:
                    rbm(m - 2)
                for step in range(4):
                    for j in range(4):
                        qk(m, 4 * step + j)
                    if m >= 1:
                        for j in range(4):
                            pv(m - 1, 4 * step + j)
            for step in range(4):
                for j in range(4):
                    pv(NG - 1, 4 * step + j)
            rbm(NG - 2)
            rbm(NG - 1)
        self.on("tensor", pe_all)

        def act_all(scalar, b=b):
            def dcopy(g):
                scalar.wait_ge(sPE.h, pv_ev[(g, 15)])
                if g >= 1:
                    scalar.wait_ge(sPE.h, rbm_ev[g - 1])
                v = sAC.mark(scalar.activation(
                    out=s.rdbuf[64:65, :], in_=pOx(g)[64:65, :], func=AF.Copy))
                assert v == dc_ev[g]

            for m in range(NG):
                for kt in range(16):
                    scalar.wait_ge(sPE.h, qk_ev[(m, kt)])
                    T = 16 * m + kt
                    if T >= 3:
                        pm, pk = (m, kt - 3) if kt >= 3 else (m - 1, 13 + kt)
                        scalar.wait_ge(sDV.h, pool_ev[(pm, pk)])
                    v = sAC.mark(scalar.activation(
                        out=s.eraw[:, bank(m, kt) * 512:bank(m, kt) * 512 + 512],
                        in_=psq[bank(m, kt)][:], func=AF.Exp))
                    assert v == exp_ev[(m, kt)]
                if m >= 1:
                    dcopy(m - 1)
            dcopy(NG - 1)
        self.on("scalar", act_all)

        def sync_all(sync, b=b):
            for h in range(H):
                if h >= 3:
                    sync.wait_ge(sDV.h, pool_ev[(4 * (h - 3) + 3, 15)])
                src = bass.AP(s.d_expb.tensor, h * 128 * EXPW + 127,
                              [[EXPW - 1, 128], [1, SKW]])
                self.dma(SP[2 + h % 3], sync.dma_start(
                    out=s.skw[:, (h % 3) * SKW:(h % 3) * SKW + SKW], in_=src),
                    f"skw_{b}_{h}")
        self.on("sync", sync_all)

        def dve_all(vector, b=b):
            def fin(g):
                vector.wait_ge(sPE.h, rbm_ev[g])
                with s.nc.allow_low_precision("softmax 1/denom applied to bf16 weights"):
                    v = sDV.mark(vector.reciprocal(
                        out=s.rbs[:, (g % 2) * 512:(g % 2) * 512 + 512],
                        in_=s.pD[0:64, :]))
                assert v == rd_ev[g]
                vector.wait_ge(sDV.h, v)
                hh, hc, qc = hof(g)
                v = sDV.mark(vector.tensor_tensor(
                    out=s.a_fm[hh:hh + 64, hc + qc * 512:hc + (qc + 1) * 512],
                    in0=pOx(g)[0:64, :],
                    in1=s.rbs[:, (g % 2) * 512:(g % 2) * 512 + 512], op=ALU.mult))
                assert v == ae_ev[g]
                ev[f"ae_{b}_{g}"] = v

            for m in range(NG):
                if m >= 2:
                    fin(m - 2)
                h = m // 4
                for kt in range(16):
                    if m % 4 == 0 and kt == 0:
                        self.wt(vector, f"skw_{b}_{h}")
                    vector.wait_ge(sAC.h, exp_ev[(m, kt)])
                    T = 16 * m + kt
                    if T >= NS:
                        pg, pk = divmod(T - NS, 16)
                        vector.wait_ge(sPE.h, pv_ev[(pg, pk)])
                    j0 = 1920 + (m % 4) * 512 - kt * 128
                    v = sDV.mark(vector.tensor_tensor(
                        out=s.emul[:, (T % NS) * 512:(T % NS) * 512 + 512],
                        in0=s.eraw[:, bank(m, kt) * 512:bank(m, kt) * 512 + 512],
                        in1=s.skw[:, (h % 3) * SKW + j0:(h % 3) * SKW + j0 + 512],
                        op=ALU.mult))
                    assert v == pool_ev[(m, kt)]
            fin(NG - 2)
            fin(NG - 1)
        self.on("vector", dve_all)



        tpa = self.tap(f"a_fm_b{b}", [128, 8 * S], BF16)
        if tpa is not None:
            def tapw(gpsimd, b=b, tpa=tpa):
                gpsimd.wait_ge(s.sDV.h, ev[f"ae_{b}_{4 * H - 1}"])
                self.dma(s.SP[4], gpsimd.dma_start(out=tpa, in_=s.a_fm[:]))
            self.on("gpsimd", tapw)

    def p4_gate(self, b):
        s, ev = self, self.ev
        SQ, SP = s.SQ, s.SP
        sPE, sDV, sAC, sPL = s.sPE, s.sDV, s.sAC, s.sPL
        banks = [s.pA, s.pC, s.pB, s.pD]

        def wgl(sync, b=b):
            src = bass.AP(s.d_wgb.tensor, 0, [[2 * D, 128], [128 * 2 * D, 8], [1, 2 * D]])
            self.dma(SQ[3], sync.dma_start(
                out=s.wgs[:].rearrange("p (dc n) -> p dc n", dc=8), in_=src), f"wgl_{b}")
        self.on("sync", wgl)

        for tt in range(NTT):
            def gm(tensor, b=b, tt=tt):
                if tt == 0:
                    self.wt(tensor, f"wgl_{b}")
                for gc in range(4):
                    pb = banks[gc]
                    if tt >= 1:
                        if gc in (2, 3):
                            tensor.wait_ge(sAC.h, ev[f"sg_{b}_{tt-1}_{gc-2}"])
                        else:
                            tensor.wait_ge(sDV.h, ev[f"ab_{b}_{tt-1}_{gc}"])
                    for dc in range(8):
                        ins = tensor.matmul(
                            out=pb[:], lhsT=s.a_fm[:, dc * S + tt * 128:dc * S + (tt + 1) * 128],
                            rhs=s.wgs[:, dc * 2048 + gc * 512:dc * 2048 + (gc + 1) * 512],
                            start=(dc == 0), stop=False)
                    ins = tensor.matmul(
                        out=pb[:], lhsT=s.ones1_bf[:],
                        rhs=s.bgate_bf[0:1, gc * 512:(gc + 1) * 512], start=False, stop=True)
                    ev[f"gm_{b}_{tt}_{gc}"] = sPE.mark(ins)
            self.on("tensor", gm)

            for half in range(2):
                def sgst(scalar, b=b, tt=tt, half=half):
                    scalar.wait_ge(sPE.h, ev[f"gm_{b}_{tt}_{2 + half}"])
                    if tt >= 1 or half == 1:
                        key = f"ab_{b}_{tt}_{half-1}" if half == 1 else f"ab_{b}_{tt-1}_1"
                        scalar.wait_ge(sDV.h, ev[key])
                    ev[f"sg_{b}_{tt}_{half}"] = sAC.mark(scalar.activation(
                        out=s.sgb[:], in_=banks[2 + half][:], func=AF.Sigmoid))
                self.on("scalar", sgst)

                def abst(vector, b=b, tt=tt, half=half):
                    vector.wait_ge(sAC.h, ev[f"sg_{b}_{tt}_{half}"])
                    vector.wait_ge(sPE.h, ev[f"gm_{b}_{tt}_{half}"])
                    if tt >= 2:
                        self.wt(vector, f"abw_{b}_{tt-2}")
                    ev[f"ab_{b}_{tt}_{half}"] = sDV.mark(vector.tensor_tensor(
                        out=s.abb[:, (tt % 2) * 1024 + half * 512:(tt % 2) * 1024 + (half + 1) * 512],
                        in0=banks[half][:], in1=s.sgb[:], op=ALU.mult))
                self.on("vector", abst)

            def abw(sync, b=b, tt=tt):
                sync.wait_ge(sDV.h, ev[f"ab_{b}_{tt}_1"])
                self.dma(SP[tt % 2], sync.dma_start(
                    out=s.d_abr[b, tt * 128:(tt + 1) * 128, :],
                    in_=s.abb[:, (tt % 2) * 1024:(tt % 2) * 1024 + 1024]), f"abw_{b}_{tt}")
            self.on("sync", abw)

        tpab = self.tap(f"abr_b{b}", [S, D], F32)
        tpxq = self.tap(f"dxq_b{b}", [8, 128, S], BF16)
        if tpab is not None:
            def tapab(gpsimd, b=b, tpab=tpab, tpxq=tpxq):
                self.wt(gpsimd, f"abw_{b}_{NTT-1}")
                self.dma(SP[4], gpsimd.dma_start(out=tpab, in_=s.d_abr[b]))
                if tpxq is not None:
                    self.dma(SP[4], gpsimd.dma_start(out=tpxq, in_=s.d_xq[b]))
            self.on("gpsimd", tapab)

    def p56_ffn(self, b):
        s, ev = self, self.ev
        SQ, SP = s.SQ, s.SP
        sPE, sDV, sAC, sPL = s.sPE, s.sDV, s.sAC, s.sPL

        def wql(sync, b=b):
            if b == 0:
                self.wt(sync, "wqst_w1_31")
                self.wt(sync, "wqst_w2_7")
            src1 = bass.AP(s.d_wq1.tensor, 0, [[F, 128], [128 * F, 8], [1, F]])
            self.dma(SQ[4], sync.dma_start(
                out=s.wq1s[:].rearrange("p (dc n) -> p dc n", dc=8), in_=src1), f"wq1l_{b}")
            src2 = bass.AP(s.d_wq2.tensor, 0, [[D, 128], [128 * D, 32], [1, D]])
            self.dma(SQ[5], sync.dma_start(
                out=s.wq2s[:].rearrange("p (fc n) -> p fc n", fc=32), in_=src2), f"wq2l_{b}")
        self.on("sync", wql)

        def emit_tail(tt, part="all"):
            front = part in ("all", "front")
            back = part in ("all", "back")
            for fc8 in (range(8) if front else ()):
                def rnd_q(gpsimd, b=b, tt=tt, fc8=fc8):
                    if fc8 == 0:
                        gpsimd.wait_ge(sDV.h, ev[f"scq_{b}_{tt}"])
                    ev[f"hqp_{b}_{tt}_{fc8}"] = sPL.mark(gpsimd.tensor_scalar(
                        out=s.gbuf[:, fc8 * 512:(fc8 + 1) * 512],
                        in0=s.gbuf[:, fc8 * 512:(fc8 + 1) * 512],
                        scalar1=s.cb[:, 32 + tt % 2:33 + tt % 2], scalar2=MAGIC,
                        op0=ALU.mult, op1=ALU.add))
                self.on("gpsimd", rnd_q)

                def rnd_q2(scalar, b=b, tt=tt, fc8=fc8):
                    scalar.wait_ge(sPL.h, ev[f"hqp_{b}_{tt}_{fc8}"])
                    ev[f"hqc_{b}_{tt}_{fc8}"] = sAC.mark(scalar.activation(
                        out=s.hqt[:, (tt % 2) * F + fc8 * 512:(tt % 2) * F + (fc8 + 1) * 512],
                        in_=s.gbuf[:, fc8 * 512:(fc8 + 1) * 512],
                        func=AF.Identity, bias=s.negmag[:, 0:1]))
                self.on("scalar", rnd_q2)

            for fc8 in (range(8) if back else ()):
                for pr in range(2):
                    def htr(tensor, b=b, tt=tt, fc8=fc8, pr=pr):
                        if fc8 == 0 and pr == 0:
                            pass
                        if pr == 0:
                            tensor.wait_ge(sAC.h, ev[f"hqc_{b}_{tt}_{fc8}"])
                        for half in range(2):
                            fc = fc8 * 4 + pr * 2 + half
                            pb = s.pT1 if fc % 2 == 0 else s.pT2
                            if tt > 0 or fc >= 2:
                                key = (f"hte_{b}_{tt}_{fc-2}" if fc >= 2
                                       else f"hte_{b}_{tt-1}_{30 + fc}")
                                tensor.wait_ge(sAC.h, ev[key])
                            ev[f"htr_{b}_{tt}_{fc}"] = sPE.mark(tensor.transpose(
                                out=pb[:],
                                in_=s.hqt[:, (tt % 2) * F + fc * 128:(tt % 2) * F + (fc + 1) * 128],
                                identity=s.ident_bf[:]))
                    self.on("tensor", htr)

                    def hte(scalar, b=b, tt=tt, fc8=fc8, pr=pr):
                        for half in range(2):
                            fc = fc8 * 4 + pr * 2 + half
                            pb = s.pT1 if fc % 2 == 0 else s.pT2
                            scalar.wait_ge(sPE.h, ev[f"htr_{b}_{tt}_{fc}"])
                            ev[f"hte_{b}_{tt}_{fc}"] = sAC.mark(scalar.activation(
                                out=s.hqf[:, (tt % 2) * F + fc * 128:(tt % 2) * F + (fc + 1) * 128],
                                in_=pb[:], func=AF.Copy))
                    self.on("scalar", hte)

            if not back:
                return

            def f2m(tensor, b=b, tt=tt):
                if tt == 0:
                    self.wt(tensor, f"wq2l_{b}")
                for d2 in range(2):
                    pb = s.pC if d2 == 0 else s.pD
                    if tt >= 1:
                        tensor.wait_ge(sDV.h, ev[f"mg_{b}_{tt-1}_{d2}"])
                    for fc in range(32):
                        if d2 == 0:
                            tensor.wait_ge(sAC.h, ev[f"hte_{b}_{tt}_{fc}"])
                        ins = tensor.matmul(
                            out=pb[:],
                            lhsT=s.hqf[:, (tt % 2) * F + fc * 128:(tt % 2) * F + (fc + 1) * 128],
                            rhs=s.wq2s[:, fc * D + d2 * 512:fc * D + (d2 + 1) * 512],
                            start=(fc == 0), stop=(fc == 31))
                    ev[f"f2m_{b}_{tt}_{d2}"] = sPE.mark(ins)
            self.on("tensor", f2m)

            for d2 in range(2):
                def mg(vector, b=b, tt=tt, d2=d2):
                    vector.wait_ge(sPE.h, ev[f"f2m_{b}_{tt}_{d2}"])
                    vector.wait_ge(sPL.h, ev[f"xa_{b}_{tt}"])
                    if tt >= 2:
                        self.wt(vector, f"ow_{b}_{tt-2}")
                    pb = s.pC if d2 == 0 else s.pD
                    v = sDV.mark(vector.scalar_tensor_tensor(
                        out=s.ftmp[:, d2 * 512:(d2 + 1) * 512], in0=pb[:],
                        scalar=s.cb[:, 34 + tt % 2:35 + tt % 2],
                        in1=s.b2b[:, d2 * 512:(d2 + 1) * 512],
                        op0=ALU.mult, op1=ALU.add))
                    ev[f"mg_{b}_{tt}_{d2}"] = sDV.mark(vector.tensor_tensor(
                        out=s.xat[:, (tt % 2) * D + d2 * 512:(tt % 2) * D + (d2 + 1) * 512],
                        in0=s.ftmp[:, d2 * 512:(d2 + 1) * 512],
                        in1=s.xat[:, (tt % 2) * D + d2 * 512:(tt % 2) * D + (d2 + 1) * 512],
                        op=ALU.add))
                self.on("vector", mg)

            def ow(sync, b=b, tt=tt):
                sync.wait_ge(sDV.h, ev[f"mg_{b}_{tt}_0"])
                sync.wait_ge(sDV.h, ev[f"mg_{b}_{tt}_1"])
                self.dma(SP[tt % 2], sync.dma_start(
                    out=s.out[b, tt * 128:(tt + 1) * 128, :],
                    in_=s.xat[:, (tt % 2) * D:(tt % 2) * D + D]), f"ow_{b}_{tt}")
            self.on("sync", ow)

        for tt in range(NTT):
            if tt >= 2:
                emit_tail(tt - 2, "back")

            def lds(sync, b=b, tt=tt):
                if tt >= 2:
                    sync.wait_ge(sPE.h, ev[f"f1m_{b}_{tt-2}_7"])
                    sync.wait_ge(sPL.h, ev[f"xa_{b}_{tt-2}"])
                    self.wt(sync, f"ow_{b}_{tt-2}")
                src = s.d_xq[b, :, :, tt * 128:(tt + 1) * 128]
                self.dma(SQ[1 + tt % 2], sync.dma_start(
                    out=s.xqt[:, (tt % 2) * 1024:(tt % 2) * 1024 + 1024].rearrange("p (dc n) -> p dc n", dc=8),
                    in_=src.transpose([1, 0, 2])), f"xql_{b}_{tt}")
                self.dma(SQ[6 + tt % 2], sync.dma_start(
                    out=s.abrt[:, (tt % 2) * D:(tt % 2) * D + D],
                    in_=s.d_abr[b, tt * 128:(tt + 1) * 128, :]), f"abl_{b}_{tt}")
                self.dma(SQ[3 - 3 * (tt % 2)], sync.dma_start(
                    out=s.xat[:, (tt % 2) * D:(tt % 2) * D + D],
                    in_=s.x_in[b, tt * 128:(tt + 1) * 128, :]), f"x2l_{b}_{tt}")
            self.on("sync", lds)

            def xadd(gpsimd, b=b, tt=tt):
                self.wt(gpsimd, f"x2l_{b}_{tt}")
                gpsimd.wait_ge(SQ[6 + tt % 2].h, ev[f"abl_{b}_{tt}"][1])
                ev[f"xa_{b}_{tt}"] = sPL.mark(gpsimd.tensor_tensor(
                    out=s.xat[:, (tt % 2) * D:(tt % 2) * D + D],
                    in0=s.xat[:, (tt % 2) * D:(tt % 2) * D + D],
                    in1=s.abrt[:, (tt % 2) * D:(tt % 2) * D + D], op=ALU.add))
            self.on("gpsimd", xadd)

            if tt >= 1:
                emit_tail(tt - 1, "front")

            for fc8 in range(8):
                def f1m(tensor, b=b, tt=tt, fc8=fc8):
                    if tt == 0 and fc8 == 0:
                        self.wt(tensor, f"wq1l_{b}")
                    if fc8 == 0:
                        self.wt(tensor, f"xql_{b}_{tt}")
                    pb = [s.pA, s.pB, s.pO, s.pO2][fc8 % 4]
                    key = (f"dq_{b}_{tt}_{fc8-4}" if fc8 >= 4
                           else (f"dq_{b}_{tt-1}_{4 + fc8}" if tt >= 1 else None))
                    if key is not None:
                        tensor.wait_ge(sDV.h, ev[key])
                    for dc in range(8):
                        ins = tensor.matmul(
                            out=pb[:], lhsT=s.xqt[:, (tt % 2) * 1024 + dc * 128:(tt % 2) * 1024 + (dc + 1) * 128],
                            rhs=s.wq1s[:, dc * F + fc8 * 512:dc * F + (fc8 + 1) * 512],
                            start=(dc == 0), stop=(dc == 7))
                    ev[f"f1m_{b}_{tt}_{fc8}"] = sPE.mark(ins)
                self.on("tensor", f1m)

                def dq(vector, b=b, tt=tt, fc8=fc8):
                    vector.wait_ge(sPE.h, ev[f"f1m_{b}_{tt}_{fc8}"])
                    if tt >= 1:
                        vector.wait_ge(sAC.h, ev[f"hqc_{b}_{tt-1}_{fc8}"])
                    ev[f"dq_{b}_{tt}_{fc8}"] = sDV.mark(vector.scalar_tensor_tensor(
                        out=s.gbuf[:, fc8 * 512:(fc8 + 1) * 512],
                        in0=[s.pA, s.pB, s.pO, s.pO2][fc8 % 4][:],
                        scalar=s.dq1c[:, b * NTT + tt:b * NTT + tt + 1],
                        in1=s.b1b[:, fc8 * 512:(fc8 + 1) * 512],
                        op0=ALU.mult, op1=ALU.add))
                self.on("vector", dq)

                def gl(scalar, b=b, tt=tt, fc8=fc8):
                    scalar.wait_ge(sDV.h, ev[f"dq_{b}_{tt}_{fc8}"])
                    ev[f"gl_{b}_{tt}_{fc8}"] = sAC.mark(scalar.activation(
                        out=s.gbuf[:, fc8 * 512:(fc8 + 1) * 512],
                        in_=s.gbuf[:, fc8 * 512:(fc8 + 1) * 512], func=AF.Gelu))
                self.on("scalar", gl)

                def glsq(gpsimd, b=b, tt=tt, fc8=fc8):
                    gpsimd.wait_ge(sAC.h, ev[f"gl_{b}_{tt}_{fc8}"])
                    if fc8 > 0:
                        gpsimd.wait_ge(sDV.h, ev[f"gsq_{b}_{tt}_{fc8-1}"])
                    ev[f"glsq_{b}_{tt}_{fc8}"] = sPL.mark(gpsimd.tensor_tensor(
                        out=s.gsqt[:], in0=s.gbuf[:, fc8 * 512:(fc8 + 1) * 512],
                        in1=s.gbuf[:, fc8 * 512:(fc8 + 1) * 512], op=ALU.mult))
                self.on("gpsimd", glsq)

                def gsq(vector, b=b, tt=tt, fc8=fc8):
                    vector.wait_ge(sPL.h, ev[f"glsq_{b}_{tt}_{fc8}"])
                    vector.tensor_reduce(out=s.cb[:, fc8:fc8 + 1], in_=s.gsqt[:],
                                         axis=AX.X, op=ALU.add)
                    ev[f"gsq_{b}_{tt}_{fc8}"] = sDV.mark(vector.tensor_reduce(
                        out=s.cb[:, 24 + fc8:25 + fc8],
                        in_=s.gbuf[:, fc8 * 512:(fc8 + 1) * 512], axis=AX.X,
                        op=ALU.max, apply_absolute_value=True))
                self.on("vector", gsq)

            def quantA(vector, b=b, tt=tt):
                vector.wait_ge(sDV.h, ev[f"gsq_{b}_{tt}_7"])
                v = sDV.mark(vector.tensor_reduce(out=s.cb[:, 17:18], in_=s.cb[:, 24:32],
                                                  axis=AX.X, op=ALU.max))
                ev[f"amax_{b}_{tt}"] = v
                v2 = sDV.mark(vector.tensor_reduce(out=s.cb[:, 8:9], in_=s.cb[:, 0:8],
                                                   axis=AX.X, op=ALU.add))
                vector.wait_ge(sDV.h, v2)
                v2 = sDV.mark(vector.tensor_scalar(out=s.cb[:, 9:10], in0=s.cb[:, 8:9],
                                                   scalar1=1.0 / F, scalar2=EPS,
                                                   op0=ALU.mult, op1=ALU.add))
                ev[f"rden2_{b}_{tt}"] = v2
            self.on("vector", quantA)

            def sq2(scalar, b=b, tt=tt):
                scalar.wait_ge(sDV.h, ev[f"rden2_{b}_{tt}"])
                ev[f"sq2_{b}_{tt}"] = sAC.mark(scalar.activation(
                    out=s.cb[:, 10:11], in_=s.cb[:, 9:10], func=AF.Sqrt))
            self.on("scalar", sq2)

            def quantB(vector, b=b, tt=tt):
                vector.wait_ge(sDV.h, ev[f"amax_{b}_{tt}"])
                v = sDV.mark(vector.tensor_scalar(out=s.cb[:, 18:19], in0=s.cb[:, 17:18],
                                                  scalar1=1e-5, scalar2=None, op0=ALU.max))
                vector.wait_ge(sDV.h, v)
                v = sDV.mark(vector.reciprocal(out=s.cb[:, 19:20], in_=s.cb[:, 18:19]))
                vector.wait_ge(sDV.h, v)
                v = sDV.mark(vector.tensor_scalar(out=s.cb[:, 20:21], in0=s.cb[:, 19:20],
                                                  scalar1=127.0, scalar2=None, op0=ALU.mult))
                vector.wait_ge(sDV.h, v)
                v = sDV.mark(vector.tensor_scalar(out=s.cb[:, 21:22], in0=s.cb[:, 20:21],
                                                  scalar1=s.s2col[:, 0:1], scalar2=None,
                                                  op0=ALU.mult))
                vector.wait_ge(sDV.h, v)
                v = sDV.mark(vector.reciprocal(out=s.cb[:, 22:23], in_=s.cb[:, 21:22]))
                vector.wait_ge(sDV.h, v)
                vector.wait_ge(sAC.h, ev[f"sq2_{b}_{tt}"])
                v = sDV.mark(vector.reciprocal(out=s.cb[:, 11:12], in_=s.cb[:, 10:11]))
                vector.wait_ge(sDV.h, v)
                v = sDV.mark(vector.tensor_scalar(out=s.cb[:, 22:23], in0=s.cb[:, 22:23],
                                                  scalar1=s.cb[:, 11:12], scalar2=None,
                                                  op0=ALU.mult))
                vector.wait_ge(sDV.h, v)
                p = tt % 2
                v = sDV.mark(vector.tensor_copy(out=s.cb[:, 32 + p:33 + p],
                                                in_=s.cb[:, 20:21]))
                vector.wait_ge(sDV.h, v)
                v = sDV.mark(vector.tensor_copy(out=s.cb[:, 34 + p:35 + p],
                                                in_=s.cb[:, 22:23]))
                vector.wait_ge(sDV.h, v)
                ev[f"scq_{b}_{tt}"] = v
            self.on("vector", quantB)

        emit_tail(NTT - 2, "back")
        emit_tail(NTT - 1, "front")
        emit_tail(NTT - 1, "back")

    def finale(self):
        s = self

        def fin(gpsimd):
            for c in (*s.SP, *s.SQ):
                if c.n > 0:
                    gpsimd.wait_ge(c.h, c.n)
        self.on("gpsimd", fin)

        def fin2(sync):
            for c in (*s.SP, *s.SQ):
                if c.n > 0:
                    sync.wait_ge(c.h, c.n)
        self.on("sync", fin2)


def build_core_program(taps=(), stop_after=None):
    kb = KB(taps, stop_after=stop_after)
    nc = kb.build()
    return nc, kb


def kernel(L0_attn_x, L0_attn_w_in, L0_attn_w_gate, L0_attn_b_gate, L0_attn_amplitudes,
           L0_attn_sharpness, L0_attn_offsets, L0_ffn_w1, L0_ffn_b1, L0_ffn_w2, L0_ffn_b2):
    x = np.ascontiguousarray(L0_attn_x)
    nc, _ = build_core_program()
    in_maps = []
    for c in range(8):
        in_maps.append({
            "x": np.ascontiguousarray(x[2 * c:2 * c + 2]),
            "w_in": np.ascontiguousarray(L0_attn_w_in),
            "w_gate": np.ascontiguousarray(L0_attn_w_gate),
            "b_gate": np.ascontiguousarray(L0_attn_b_gate),
            "amp": np.ascontiguousarray(L0_attn_amplitudes),
            "sharp": np.ascontiguousarray(L0_attn_sharpness),
            "off": np.ascontiguousarray(L0_attn_offsets),
            "w1": np.ascontiguousarray(L0_ffn_w1),
            "b1": np.ascontiguousarray(L0_ffn_b1),
            "w2": np.ascontiguousarray(L0_ffn_w2),
            "b2": np.ascontiguousarray(L0_ffn_b2),
        })
    res = run_bass_kernel_spmd(nc, in_maps, core_ids=list(range(8)))
    return np.concatenate([r["out"] for r in res.results], axis=0)


# version 28
# speedup vs baseline: 1.0139x; 1.0139x over previous
import numpy as np
import ml_dtypes
import concourse.bass as bass
import concourse.mybir as mybir
from concourse.bass_utils import run_bass_kernel_spmd
from contextlib import ExitStack

F32 = mybir.dt.float32
BF16 = mybir.dt.bfloat16
AF = mybir.ActivationFunctionType
ALU = mybir.AluOpType
AX = mybir.AxisListType

B, S, D, H, NK, F = 2, 2048, 1024, 16, 21, 4096
TW = 4095
EXPW = 4224
MAGIC = 1.5 * 2.0 ** 23
EPS = 1e-5
NTT = S // 128
ENGS = ("sync", "tensor", "vector", "scalar", "gpsimd")


class Ctr:
    def __init__(self, nc, stack, name):
        self.h = stack.enter_context(nc.semaphore(name))
        self.n = 0

    def mark(self, instr, k=1):
        instr.then_inc(self.h, k)
        self.n += k
        return self.n


class _DummyInstr:
    def then_inc(self, sem, k=1, **kw):
        return self


class _DummyEng:
    def __getattr__(self, name):
        def f(*a, **k):
            return _DummyInstr()
        return f


class KB:
    def __init__(self, taps=(), stop_after=None):
        self.pending = None
        self.stop_after = stop_after
        self.taps = set(taps)
        self.tap_outs = {}
        nc = self.nc = bass.Bass("TRN2", target_bir_lowering=False)
        self.x_in = nc.dram_tensor("x", [B, S, D], F32, kind="ExternalInput").ap()
        self.w_in = nc.dram_tensor("w_in", [D, 3 * D], F32, kind="ExternalInput").ap()
        self.w_gate_in = nc.dram_tensor("w_gate", [D, 2 * D], F32, kind="ExternalInput").ap()
        self.b_gate_in = nc.dram_tensor("b_gate", [2 * D], F32, kind="ExternalInput").ap()
        self.amp_in = nc.dram_tensor("amp", [H, NK], F32, kind="ExternalInput").ap()
        self.sharp_in = nc.dram_tensor("sharp", [H, NK], F32, kind="ExternalInput").ap()
        self.off_in = nc.dram_tensor("off", [H, NK], F32, kind="ExternalInput").ap()
        self.w1_in = nc.dram_tensor("w1", [F, D], F32, kind="ExternalInput").ap()
        self.b1_in = nc.dram_tensor("b1", [F], F32, kind="ExternalInput").ap()
        self.w2_in = nc.dram_tensor("w2", [D, F], F32, kind="ExternalInput").ap()
        self.b2_in = nc.dram_tensor("b2", [D], F32, kind="ExternalInput").ap()
        self.out = nc.dram_tensor("out", [B, S, D], F32, kind="ExternalOutput").ap()

        self.d_expb = nc.dram_tensor("d_expb", [H, 128 * EXPW], BF16, kind="Internal").ap()
        self.d_winb = nc.dram_tensor("d_winb", [D, 3 * D], BF16, kind="Internal").ap()
        self.d_wgb = nc.dram_tensor("d_wgb", [D, 2 * D], BF16, kind="Internal").ap()
        self.d_wq1 = nc.dram_tensor("d_wq1", [D, F], BF16, kind="Internal").ap()
        self.d_wq2 = nc.dram_tensor("d_wq2", [F, D], BF16, kind="Internal").ap()
        self.d_xq = nc.dram_tensor("d_xq", [B, 8, 128, S], BF16, kind="Internal").ap()
        self.d_abr = nc.dram_tensor("d_abr", [B, S, D], F32, kind="Internal").ap()

        ident = np.eye(128)
        self.ident_bf_d = nc.inline_tensor(ident.astype(ml_dtypes.bfloat16), "identd_bf")
        sel = np.zeros((63, 65), dtype=np.float32)
        for p in range(63):
            sel[p, 32 * (p // 21)] = 1.0
        self.selG_d = nc.inline_tensor(sel, "selG_d")

    def tap(self, name, shape, dtype=F32):
        if name in self.taps and name not in self.tap_outs:
            self.tap_outs[name] = self.nc.dram_tensor(
                "tap_" + name, shape, dtype, kind="ExternalOutput").ap()
        return self.tap_outs.get(name)

    def on(self, engine_name, fn):
        if self.pending is not None:
            self.pending.append((engine_name, fn))
            fn(_DummyEng())
        else:
            getattr(self.block, engine_name)(fn)

    def grouped(self, phase_fn, *args):
        ctrs = (*self.SQ, *self.SP, self.sPE, self.sDV, self.sAC, self.sPL)
        snap = [(c, c.n) for c in ctrs]
        self.pending = []
        phase_fn(*args)
        pend = self.pending
        self.pending = None
        for c, v in snap:
            c.n = v
        for en in ENGS:
            fns = [f for e, f in pend if e == en]
            if fns:
                def sect(eng, fns=fns):
                    for f in fns:
                        f(eng)
                getattr(self.block, en)(sect)

    def dma(self, ctr, instr, key=None):
        instr.then_inc(ctr.h, 16)
        ctr.n += 16
        if key is not None:
            self.ev[key] = (ctr, ctr.n)
        return (ctr, ctr.n)

    def wt(self, eng, key):
        c, v = self.ev[key]
        eng.wait_ge(c.h, v)

    def bar(self):
        vals = [(c, c.n) for c in (*self.SQ, *self.SP, self.sPE, self.sDV, self.sAC, self.sPL)]
        for en in ENGS:
            def f(eng, vals=vals):
                for c, v in vals:
                    if v > 0:
                        eng.wait_ge(c.h, v)
            self.on(en, f)

    def build(self):
        nc = self.nc
        st = ExitStack()
        with st:
            s = self

            def sbufP(name, shape, dt, side=None):
                return st.enter_context(nc.sbuf_tensor(name, shape, dt, side=side))

            s.ident_bf = sbufP("ident_bf", [128, 128], BF16)
            s.ones1_bf = sbufP("ones1_bf", [1, 128], BF16)
            s.ones1_f = sbufP("ones1_f", [1, 128], F32)
            s.ones21 = sbufP("ones21", [NK, 1], F32)
            s.ones128 = sbufP("ones128", [128, 1], F32)
            s.onesp64 = sbufP("onesp64", [128, 64], BF16)
            s.bgate_bf = sbufP("bgate_bf", [1, 2 * D], BF16)
            s.b1b = sbufP("b1b", [128, F], BF16)
            s.b2b = sbufP("b2b", [128, D], BF16)
            s.s1col = sbufP("s1col", [128, 1], F32)
            s.s2col = sbufP("s2col", [128, 1], F32)
            s.dq1c = sbufP("dq1c", [128, B * NTT], F32)
            s.cb = sbufP("cb", [128, 40], F32)
            s.epscol = sbufP("epscol", [128, 1], F32)
            s.negmag = sbufP("negmag", [128, 1], F32)
            s.scal = sbufP("scal", [1, 8], F32)
            s.stg = sbufP("stg", [128, 2 * 128], BF16)
            s.p1cb = sbufP("p1cb_buf", [128, 20 * NTT], F32) \
                if any(t.startswith("p1cb") for t in self.taps) else None
            s.ftmp = sbufP("ftmp", [128, 2 * 512], F32)

            s.pA = st.enter_context(nc.psum_tensor("pA", [128, 512], F32))
            s.pB = st.enter_context(nc.psum_tensor("pB", [128, 512], F32))
            s.pC = st.enter_context(nc.psum_tensor("pC", [128, 512], F32))
            s.pD = st.enter_context(nc.psum_tensor("pD", [128, 512], F32))
            s.pO = st.enter_context(nc.psum_tensor("pO", [128, 512], F32))
            s.pO2 = st.enter_context(nc.psum_tensor("pO2", [128, 512], F32))
            s.pT1 = st.enter_context(nc.psum_tensor("pT1", [128, 128], BF16))
            s.pT2 = st.enter_context(nc.psum_tensor("pT2", [128, 128], BF16))

            s.SQ = [Ctr(nc, st, f"SQ{i}") for i in range(8)]
            s.SP = [Ctr(nc, st, f"SP{i}") for i in range(6)]
            s.sPE = Ctr(nc, st, "sPE")
            s.sDV = Ctr(nc, st, "sDV")
            s.sAC = Ctr(nc, st, "sAC")
            s.sPL = Ctr(nc, st, "sPL")

            s.block = st.enter_context(nc.Block())
            self.ev = {}

            stop = self.stop_after
            with ExitStack() as ph0:
                s.wtile = ph0.enter_context(nc.sbuf_tensor("wtile", [128, 3 * F], F32))
                s.wqtile = ph0.enter_context(nc.sbuf_tensor("wqtile", [128, 3 * F], BF16))
                s.stg8 = ph0.enter_context(nc.sbuf_tensor("stg8", [128, 2 * F], BF16))
                s.iotaf = ph0.enter_context(nc.sbuf_tensor("iotaf", [128, TW], F32))
                s.relrev = ph0.enter_context(nc.sbuf_tensor("relrev", [128, TW], F32))
                s.gwork = ph0.enter_context(nc.sbuf_tensor("gwork", [128, TW], F32))
                s.hkG = ph0.enter_context(nc.sbuf_tensor("hkG", [128, 18], F32))
                s.selG = ph0.enter_context(nc.sbuf_tensor("selG", [63, 65], F32))
                s.onesbc = ph0.enter_context(nc.sbuf_tensor("onesbc", [128, 128], BF16))
                s.rowexp = ph0.enter_context(nc.sbuf_tensor("rowexp", [65, TW], BF16))
                s.bgrow = ph0.enter_context(nc.sbuf_tensor("bgrow", [1, 2 * D], F32))
                s.b1row = ph0.enter_context(nc.sbuf_tensor("b1row", [1, F], F32))
                s.b2row = ph0.enter_context(nc.sbuf_tensor("b2row", [1, D], F32))
                s.bch = ph0.enter_context(nc.sbuf_tensor("bch", [128, TW], BF16))
                self.grouped(self.phase0)
            self.bar()

            for b in (range(B) if stop != "ph0" else []):
                with ExitStack() as gL1:
                    s.xn_fm = gL1.enter_context(
                        nc.sbuf_tensor(f"xn_fm_{b}", [128, 8 * S], BF16, side="left"))
                    s.a_fm = s.xn_fm
                    with ExitStack() as gR1:
                        s.k_fm = gR1.enter_context(
                            nc.sbuf_tensor(f"k_fm_{b}", [128, 8 * S], BF16, side="right"))
                        s.q_fm = gR1.enter_context(
                            nc.sbuf_tensor(f"q_fm_{b}", [128, 8 * S], BF16, side="right"))
                        s.v_tm = gR1.enter_context(
                            nc.sbuf_tensor(f"v_tm_{b}", [128, NTT * 1040], BF16, side="right"))
                        with ExitStack() as p1s:
                            s.xt = p1s.enter_context(nc.sbuf_tensor(f"xt_{b}", [128, 2 * D], F32))
                            s.xnt = p1s.enter_context(nc.sbuf_tensor(f"xnt_{b}", [128, D], F32))
                            s.xnbt = p1s.enter_context(nc.sbuf_tensor(f"xnbt_{b}", [128, D], BF16))
                            s.xqbt = p1s.enter_context(nc.sbuf_tensor(f"xqbt_{b}", [128, D], BF16))
                            s.xqs = p1s.enter_context(nc.sbuf_tensor(f"xqs_{b}", [128, 8 * 128], BF16))
                            s.dump = p1s.enter_context(nc.sbuf_tensor(f"dump_{b}", [128, D], F32))
                            s.rtmp = p1s.enter_context(nc.sbuf_tensor(f"rtmp_{b}", [128, D], F32))
                            self.grouped(self.p1_ln_quant, b)
                        self.bar()
                        if stop == "p1":
                            self.finale()
                            return nc
                        with ExitStack() as p2s:
                            s.wcolb = p2s.enter_context(
                                nc.sbuf_tensor(f"wcolb_{b}", [128, 2 * 1024], BF16))
                            s.wvb = p2s.enter_context(
                                nc.sbuf_tensor(f"wvb_{b}", [128, 2 * 8 * 512], BF16))
                            self.grouped(self.p2_proj, b)
                        self.bar()
                        if stop == "p2":
                            self.finale()
                            return nc
                        with ExitStack() as p3s:
                            s.skw = p3s.enter_context(
                                nc.sbuf_tensor(f"skw_{b}", [128, 3 * 3968], BF16, side="left"))
                            s.eraw = p3s.enter_context(
                                nc.sbuf_tensor(f"eraw_{b}", [128, 3 * 512], BF16, side="left"))
                            s.emul = p3s.enter_context(
                                nc.sbuf_tensor(f"emul_{b}", [128, 20 * 512], BF16, side="left"))
                            s.rdbuf = p3s.enter_context(
                                nc.sbuf_tensor(f"rdbuf_{b}", [128, 512], BF16, side="left"))
                            s.rbs = p3s.enter_context(
                                nc.sbuf_tensor(f"rbs_{b}", [64, 2 * 512], BF16, side="left"))
                            self.grouped(self.p3_attn, b)
                        self.bar()
                        if stop == "p3":
                            self.finale()
                            return nc
                        gR1.close()
                        with ExitStack() as p4s:
                            s.wgs = p4s.enter_context(
                                nc.sbuf_tensor(f"wgs_{b}", [128, 8 * 2048], BF16, side="right"))
                            s.sgb = p4s.enter_context(
                                nc.sbuf_tensor(f"sgb_{b}", [128, 512], F32, side="right"))
                            s.abb = p4s.enter_context(
                                nc.sbuf_tensor(f"abb_{b}", [128, 2 * 1024], F32, side="right"))
                            self.grouped(self.p4_gate, b)
                        self.bar()
                        if stop == "p4":
                            self.finale()
                            return nc
                with ExitStack() as p56s:
                    s.wq1s = p56s.enter_context(nc.sbuf_tensor(f"wq1s_{b}", [128, 8 * F], BF16))
                    s.wq2s = p56s.enter_context(nc.sbuf_tensor(f"wq2s_{b}", [128, 32 * D], BF16))
                    s.gbuf = p56s.enter_context(nc.sbuf_tensor(f"gbuf_{b}", [128, F], F32))
                    s.hqt = p56s.enter_context(nc.sbuf_tensor(f"hqt_{b}", [128, 2 * F], BF16))
                    s.gsqt = p56s.enter_context(nc.sbuf_tensor(f"gsqt_{b}", [128, 512], F32))
                    s.hqf = s.hqt
                    s.xqt = p56s.enter_context(nc.sbuf_tensor(f"xqt_{b}", [128, 2 * 1024], BF16))
                    s.abrt = p56s.enter_context(nc.sbuf_tensor(f"abrt_{b}", [128, 2 * D], F32))
                    s.xat = p56s.enter_context(nc.sbuf_tensor(f"xat_{b}", [128, 2 * D], F32))
                    self.grouped(self.p56_ffn, b)
                self.bar()
                if stop == "p56":
                    self.finale()
                    return nc
            self.finale()
        return nc

    def phase0(self):
        s, ev = self, self.ev
        SQ, SP = s.SQ, s.SP
        sPE, sDV, sAC, sPL = s.sPE, s.sDV, s.sAC, s.sPL

        def l3(n, i):
            return ((n - 1 - i) // 3) * 3 + i

        def loads(sync):
            for dst, src in [(s.ident_bf[:], s.ident_bf_d.ap()),
                             (s.selG[:], s.selG_d.ap()),
                             (s.b1row[:], s.b1_in[None, :]),
                             (s.b2row[:], s.b2_in[None, :]),
                             (s.bgrow[:], s.b_gate_in[None, :])]:
                self.dma(SQ[0], sync.dma_start(out=dst, in_=src))
            for g in range(6):
                nr = 63 if g < 5 else 21
                for k, t in enumerate((s.amp_in, s.sharp_in, s.off_in)):
                    flat = t.rearrange("h k -> (h k)")
                    self.dma(SQ[0], sync.dma_start(
                        out=s.hkG[0:nr, 3 * g + k:3 * g + k + 1],
                        in_=flat[g * 63:g * 63 + nr][:, None],
                        allow_slow_non_contiguous=True))
            ev["p0_loads"] = (SQ[0], SQ[0].n)
            ev["hkg"] = (SQ[0], SQ[0].n)
        self.on("sync", loads)

        def iota0(gpsimd):
            ev["iota"] = sPL.mark(gpsimd.iota(
                out=s.iotaf[:], pattern=[[1, TW]], base=0, channel_multiplier=0,
                allow_small_or_imprecise_dtypes=True))
        self.on("gpsimd", iota0)

        def consts(vector):
            self.wt(vector, "p0_loads")
            vector.memset(s.ones1_bf[:], 1.0)
            vector.memset(s.ones1_f[:], 1.0)
            vector.memset(s.ones128[:], 1.0)
            vector.memset(s.onesp64[64:65, :], 1.0)
            vector.memset(s.epscol[:], EPS)
            vector.memset(s.negmag[:], -MAGIC)
            vector.memset(s.onesbc[0:1, :], 1.0)
            vector.memset(s.onesbc[32:33, :], 1.0)
            vector.memset(s.onesbc[64:65, :], 1.0)
            vector.tensor_copy(out=s.bgate_bf[:], in_=s.bgrow[:])
            vector.wait_ge(sPL.h, ev["iota"])
            ev["relrev"] = sDV.mark(vector.tensor_scalar(
                out=s.relrev[:], in0=s.iotaf[:], scalar1=-1.0, scalar2=2047.0,
                op0=ALU.mult, op1=ALU.add))
        self.on("vector", consts)

        act0 = sAC.n
        dvb0 = sDV.n

        def rexp_ev(g, c, act0=act0):
            return act0 + 9 * g + 2 + c

        def bce_ev(h, c, dvb0=dvb0):
            return dvb0 + 6 + 12 + 8 * h + c + 1

        def hkg_neg(vector):
            self.wt(vector, "hkg")
            for g in range(6):
                c = 3 * g + 1
                ev[f"hkgn_{g}"] = sDV.mark(vector.scalar_tensor_tensor(
                    out=s.hkG[0:63, c:c + 1], in0=s.hkG[0:63, c:c + 1], scalar=-1.0,
                    in1=s.hkG[0:63, c:c + 1], op0=ALU.mult, op1=ALU.min))
        self.on("vector", hkg_neg)

        for g in range(6):
            nh = 3 if g < 5 else 1
            nr = nh * 21

            def u_step(vector, g=g, nr=nr):
                vector.wait_ge(sDV.h, ev[f"hkgn_{g}"])
                if g == 0:
                    vector.wait_ge(sDV.h, ev["relrev"])
                else:
                    vector.wait_ge(sPE.h, ev[f"tsum_{g-1}"])
                ev[f"u_{g}"] = sDV.mark(vector.tensor_scalar(
                    out=s.gwork[0:nr, :], in0=s.relrev[0:nr, :],
                    scalar1=s.hkG[0:nr, 3 * g + 2:3 * g + 3], scalar2=None,
                    op0=ALU.subtract))
            self.on("vector", u_step)

            def exp_step(scalar, g=g, nr=nr):
                scalar.wait_ge(sDV.h, ev[f"u_{g}"])
                scalar.activation(out=s.gwork[0:nr, :], in_=s.gwork[0:nr, :], func=AF.Square)
                ev[f"gexp_{g}"] = sAC.mark(scalar.activation(
                    out=s.gwork[0:nr, :], in_=s.gwork[0:nr, :], func=AF.Exp,
                    scale=s.hkG[0:nr, 3 * g + 1:3 * g + 2]))
            self.on("scalar", exp_step)

            def amp_step(vector, g=g, nr=nr):
                vector.wait_ge(sAC.h, ev[f"gexp_{g}"])
                ev[f"ge_{g}"] = sDV.mark(vector.tensor_scalar(
                    out=s.gwork[0:nr, :], in0=s.gwork[0:nr, :],
                    scalar1=s.hkG[0:nr, 3 * g:3 * g + 1], scalar2=None, op0=ALU.mult))
            self.on("vector", amp_step)

            def sum_step(tensor, g=g, nr=nr, nh=nh):
                tensor.wait_ge(sDV.h, ev[f"ge_{g}"])
                for c in range(8):
                    n = min(512, TW - c * 512)
                    if g > 0 or c > 0:
                        pv = rexp_ev(g, c - 1) if c > 0 else rexp_ev(g - 1, 7)
                        tensor.wait_ge(sAC.h, pv)
                    ev[f"tsum_{g}_{c}"] = sPE.mark(tensor.matmul(
                        out=s.pC[0:(32 * (nh - 1) + 1), 0:n], lhsT=s.selG[0:nr, 0:(32 * (nh - 1) + 1)],
                        rhs=s.gwork[0:nr, c * 512:c * 512 + n], start=True, stop=True))
                ev[f"tsum_{g}"] = sPE.n
            self.on("tensor", sum_step)

            def rexp_step(scalar, g=g, nh=nh):
                for c in range(8):
                    n = min(512, TW - c * 512)
                    scalar.wait_ge(sPE.h, ev[f"tsum_{g}_{c}"])
                    for m in range(nh):
                        ins = scalar.activation(
                            out=s.rowexp[32 * m:32 * m + 1, c * 512:c * 512 + n],
                            in_=s.pC[32 * m:32 * m + 1, 0:n], func=AF.Exp)
                    v = sAC.mark(ins)
                    ev[f"rexp_{g}_{c}"] = v
                    assert v == rexp_ev(g, c), (v, g, c)
            self.on("scalar", rexp_step)

        for h in range(H):
            g, m = divmod(h, 3)

            def bc_step(tensor, h=h, g=g, m=m):
                for c in range(8):
                    n = min(512, TW - c * 512)
                    tensor.wait_ge(sAC.h, rexp_ev(g, c))
                    pb = s.pA if c % 2 == 0 else s.pB
                    if h > 0 or c >= 2:
                        pv = bce_ev(h, c - 2) if c >= 2 else bce_ev(h - 1, 6 + c)
                        tensor.wait_ge(sDV.h, pv)
                    ev[f"bcm_{h}_{c}"] = sPE.mark(tensor.matmul(
                        out=pb[:, 0:n], lhsT=s.onesbc[32 * m:32 * m + 1, :],
                        rhs=s.rowexp[32 * m:32 * m + 1, c * 512:c * 512 + n],
                        start=True, stop=True, tile_position=(32 * m, 0)))
            self.on("tensor", bc_step)

            def bce_step(vector, h=h):
                for c in range(8):
                    n = min(512, TW - c * 512)
                    pb = s.pA if c % 2 == 0 else s.pB
                    vector.wait_ge(sPE.h, ev[f"bcm_{h}_{c}"])
                    if h > 0 and c == 0:
                        self.wt(vector, f"expw_{h-1}")
                    v = sDV.mark(vector.tensor_copy(
                        out=s.bch[:, c * 512:c * 512 + n], in_=pb[:, 0:n]))
                    ev[f"bce_{h}_{c}"] = v
                    assert v == bce_ev(h, c), (v, bce_ev(h, c), h, c)
                ev[f"bce_{h}"] = sDV.n
            self.on("vector", bce_step)

            def expw_step(sync, h=h):
                sync.wait_ge(sDV.h, ev[f"bce_{h}"])
                dst = bass.AP(s.d_expb.tensor, h * 128 * EXPW, [[EXPW, 128], [1, TW]])
                self.dma(SP[h % 2], sync.dma_start(out=dst, in_=s.bch[:, 0:TW]), f"expw_{h}")
            self.on("sync", expw_step)

        for wname, src_w, dst_w, ncols in (("win", s.w_in, s.d_winb, 3 * D),
                                           ("wg", s.w_gate_in, s.d_wgb, 2 * D)):
            for i in range(8):
                def wld(sync, wname=wname, i=i, src_w=src_w, ncols=ncols):
                    if i >= 3:
                        sync.wait_ge(sDV.h, ev[f"cast_{wname}_{i-3}"])
                    elif wname == "wg":
                        sync.wait_ge(sDV.h, ev[f"cast_win_{l3(8, i)}"])
                    h2 = ncols // 2
                    self.dma(SQ[1 + i % 3], sync.dma_start(
                        out=s.wtile[:, (i % 3) * F:(i % 3) * F + h2],
                        in_=src_w[i * 128:(i + 1) * 128, 0:h2]), f"wld_{wname}_{i}")
                    self.dma(SQ[4 + i % 3], sync.dma_start(
                        out=s.wtile[:, (i % 3) * F + h2:(i % 3) * F + ncols],
                        in_=src_w[i * 128:(i + 1) * 128, h2:ncols]), f"wldR_{wname}_{i}")
                self.on("sync", wld)

                def wcast(vector, wname=wname, i=i, ncols=ncols):
                    self.wt(vector, f"wld_{wname}_{i}")
                    self.wt(vector, f"wldR_{wname}_{i}")
                    if i >= 3:
                        self.wt(vector, f"wst_{wname}_{i-3}")
                    if wname == "wg" and i < 3:
                        self.wt(vector, f"wst_win_{l3(8, i)}")
                    ev[f"cast_{wname}_{i}"] = sDV.mark(vector.tensor_copy(
                        out=s.wqtile[:, (i % 3) * F:(i % 3) * F + ncols],
                        in_=s.wtile[:, (i % 3) * F:(i % 3) * F + ncols]))
                self.on("vector", wcast)

                def wst(sync, wname=wname, i=i - 2, dst_w=dst_w, ncols=ncols):
                    if i < 0:
                        return
                    sync.wait_ge(sDV.h, ev[f"cast_{wname}_{i}"])
                    self.dma(SP[2 + i % 3], sync.dma_start(
                        out=dst_w[i * 128:(i + 1) * 128, :],
                        in_=s.wqtile[:, (i % 3) * F:(i % 3) * F + ncols]),
                        f"wst_{wname}_{i}")
                self.on("sync", wst)
                if i == 7:
                    for it in (6, 7):
                        def wstt(sync, wname=wname, i=it, dst_w=dst_w, ncols=ncols):
                            sync.wait_ge(sDV.h, ev[f"cast_{wname}_{i}"])
                            self.dma(SP[2 + i % 3], sync.dma_start(
                                out=dst_w[i * 128:(i + 1) * 128, :],
                                in_=s.wqtile[:, (i % 3) * F:(i % 3) * F + ncols]),
                                f"wst_{wname}_{i}")
                        self.on("sync", wstt)

        for wname, src_w, dst_w, ntile, ncols, scol in (
                ("w1", s.w1_in, s.d_wq1, 32, D, s.s1col),
                ("w2", s.w2_in, s.d_wq2, 8, F, s.s2col)):
            nelem = float(ntile * 128 * ncols)
            nblk = ncols // 128
            for i in range(ntile):
                def ald(sync, wname=wname, i=i, src_w=src_w, ncols=ncols):
                    if i >= 3:
                        sync.wait_ge(sDV.h, ev[f"asum_{wname}_{i-3}"])
                    if i < 3 and wname == "w1":
                        sync.wait_ge(sDV.h, ev[f"cast_wg_{l3(8, i)}"])
                    if i < 3 and wname == "w2":
                        sync.wait_ge(sDV.h, ev[f"q_w1_{l3(32, i)}"])
                    h2 = ncols // 2
                    self.dma(SQ[1 + i % 3], sync.dma_start(
                        out=s.wtile[:, (i % 3) * F:(i % 3) * F + h2],
                        in_=src_w[i * 128:(i + 1) * 128, 0:h2]), f"ald_{wname}_{i}")
                    self.dma(SQ[4 + i % 3], sync.dma_start(
                        out=s.wtile[:, (i % 3) * F + h2:(i % 3) * F + ncols],
                        in_=src_w[i * 128:(i + 1) * 128, h2:ncols]), f"aldR_{wname}_{i}")
                self.on("sync", ald)

                def ared(vector, wname=wname, i=i, ncols=ncols):
                    self.wt(vector, f"ald_{wname}_{i}")
                    self.wt(vector, f"aldR_{wname}_{i}")
                    v = sDV.mark(vector.tensor_reduce(
                        out=s.cb[:, 0:1], in_=s.wtile[:, (i % 3) * F:(i % 3) * F + ncols],
                        axis=AX.X, op=ALU.add, apply_absolute_value=True))
                    vector.wait_ge(sDV.h, v)
                    if i == 0:
                        ev[f"asum_{wname}_{i}"] = sDV.mark(
                            vector.tensor_copy(out=s.cb[:, 1:2], in_=s.cb[:, 0:1]))
                    else:
                        ev[f"asum_{wname}_{i}"] = sDV.mark(vector.tensor_tensor(
                            out=s.cb[:, 1:2], in0=s.cb[:, 1:2], in1=s.cb[:, 0:1], op=ALU.add))
                self.on("vector", ared)

            def tot(tensor, wname=wname, ntile=ntile):
                tensor.wait_ge(sDV.h, ev[f"asum_{wname}_{ntile-1}"])
                tensor.wait_ge(sAC.h, rexp_ev(5, 7))
                ev[f"tot_{wname}"] = sPE.mark(tensor.matmul(
                    out=s.pC[0:1, 0:1], lhsT=s.ones128[:], rhs=s.cb[:, 1:2],
                    start=True, stop=True))
            self.on("tensor", tot)

            def sv(vector, wname=wname, nelem=nelem):
                vector.wait_ge(sPE.h, ev[f"tot_{wname}"])
                v = sDV.mark(vector.tensor_scalar(
                    out=s.scal[0:1, 0:1], in0=s.pC[0:1, 0:1], scalar1=1.0 / nelem,
                    scalar2=1e-5, op0=ALU.mult, op1=ALU.max))
                vector.wait_ge(sDV.h, v)
                ev[f"sval_{wname}"] = sDV.mark(
                    vector.reciprocal(out=s.scal[0:1, 1:2], in_=s.scal[0:1, 0:1]))
            self.on("vector", sv)

            def sbc(tensor, wname=wname):
                tensor.wait_ge(sDV.h, ev[f"sval_{wname}"])
                ev[f"sbc_{wname}"] = sPE.mark(tensor.matmul(
                    out=s.pC[:, 0:1], lhsT=s.ones1_f[:], rhs=s.scal[0:1, 1:2],
                    start=True, stop=True))
            self.on("tensor", sbc)

            def sce(vector, wname=wname, scol=scol):
                vector.wait_ge(sPE.h, ev[f"sbc_{wname}"])
                ev[f"scol_{wname}"] = sDV.mark(vector.tensor_copy(out=scol[:], in_=s.pC[:, 0:1]))
            self.on("vector", sce)

            for i in range(ntile):
                def qld(sync, wname=wname, i=i, src_w=src_w, ncols=ncols, ntile=ntile):
                    if i >= 3:
                        sync.wait_ge(sDV.h, ev[f"q_{wname}_{i-3}"])
                    if i < 3:
                        sync.wait_ge(sDV.h, ev[f"asum_{wname}_{l3(ntile, i)}"])
                    h2 = ncols // 2
                    self.dma(SQ[1 + i % 3], sync.dma_start(
                        out=s.wtile[:, (i % 3) * F:(i % 3) * F + h2],
                        in_=src_w[i * 128:(i + 1) * 128, 0:h2]), f"qld_{wname}_{i}")
                    self.dma(SQ[4 + i % 3], sync.dma_start(
                        out=s.wtile[:, (i % 3) * F + h2:(i % 3) * F + ncols],
                        in_=src_w[i * 128:(i + 1) * 128, h2:ncols]), f"qldR_{wname}_{i}")
                self.on("sync", qld)
                def wqst(sync, wname=wname, i=i - 2, dst_w=dst_w, ncols=ncols, nblk=nblk):
                    if i < 0:
                        return
                    sync.wait_ge(sAC.h, ev[f"tev_{wname}_{i}_{nblk-1}"])
                    if wname == "w1" and i < 3:
                        self.wt(sync, f"wst_wg_{l3(8, i)}")
                    if wname == "w2" and i < 3:
                        self.wt(sync, f"wqst_w1_{l3(32, i)}")
                    pitch = F if wname == "w1" else D
                    dst = bass.AP(dst_w.tensor, i * 128,
                                  [[pitch, 128], [128 * pitch, nblk], [1, 128]])
                    self.dma(SP[2 + i % 3], sync.dma_start(
                        out=dst, in_=s.stg8[:, (i % 2) * ncols:(i % 2) * ncols + ncols].rearrange(
                            "p (j n) -> p j n", j=nblk)), f"wqst_{wname}_{i}")
                self.on("sync", wqst)

                def qnt1(vector, wname=wname, i=i, ncols=ncols, scol=scol):
                    self.wt(vector, f"qld_{wname}_{i}")
                    self.wt(vector, f"qldR_{wname}_{i}")
                    if i == 0:
                        vector.wait_ge(sDV.h, ev[f"scol_{wname}"])
                    vector.tensor_scalar(
                        out=s.wtile[:, (i % 3) * F:(i % 3) * F + ncols],
                        in0=s.wtile[:, (i % 3) * F:(i % 3) * F + ncols],
                        scalar1=scol[:, 0:1], scalar2=MAGIC,
                        op0=ALU.mult, op1=ALU.add)
                    ev[f"q1_{wname}_{i}"] = sDV.mark(vector.tensor_scalar(
                        out=s.wtile[:, (i % 3) * F:(i % 3) * F + ncols],
                        in0=s.wtile[:, (i % 3) * F:(i % 3) * F + ncols],
                        scalar1=MAGIC, scalar2=1.0,
                        op0=ALU.subtract, op1=ALU.min))
                self.on("vector", qnt1)

                def qnt2(vector, wname=wname, i=i, ncols=ncols, nblk=nblk):
                    if i >= 3:
                        vector.wait_ge(sPE.h, ev[f"trp_{wname}_{i-3}_{nblk-1}"])
                    ev[f"q_{wname}_{i}"] = sDV.mark(vector.tensor_scalar(
                        out=s.wqtile[:, (i % 3) * ncols:(i % 3) * ncols + ncols],
                        in0=s.wtile[:, (i % 3) * F:(i % 3) * F + ncols],
                        scalar1=-1.0, scalar2=None, op0=ALU.max))
                self.on("vector", qnt2)

                nblk = ncols // 128
                for rr in range((nblk + 1) // 2):
                    def trp(tensor, wname=wname, i=i, rr=rr, nblk=nblk, ncols=ncols):
                        for half in range(2):
                            j = rr * 2 + half
                            if j >= nblk:
                                continue
                            if rr == 0 and half == 0:
                                tensor.wait_ge(sDV.h, ev[f"q_{wname}_{i}"])
                            pb = s.pT1 if j % 2 == 0 else s.pT2
                            if i > 0 or j >= 2:
                                key = (f"tev_{wname}_{i}_{j-2}" if j >= 2
                                       else f"tev_{wname}_{i-1}_{nblk - 2 + j}")
                                tensor.wait_ge(sAC.h, ev[key])
                            ev[f"trp_{wname}_{i}_{j}"] = sPE.mark(tensor.transpose(
                                out=pb[:],
                                in_=s.wqtile[:, (i % 3) * ncols + j * 128:(i % 3) * ncols + (j + 1) * 128],
                                identity=s.ident_bf[:]))
                    self.on("tensor", trp)

                    def tev(scalar, wname=wname, i=i, rr=rr, nblk=nblk, ncols=ncols):
                        for half in range(2):
                            j = rr * 2 + half
                            if j >= nblk:
                                continue
                            pb = s.pT1 if j % 2 == 0 else s.pT2
                            scalar.wait_ge(sPE.h, ev[f"trp_{wname}_{i}_{j}"])
                            if j == 0 and i >= 2:
                                self.wt(scalar, f"wqst_{wname}_{i-2}")
                            ev[f"tev_{wname}_{i}_{j}"] = sAC.mark(scalar.activation(
                                out=s.stg8[:, (i % 2) * ncols + j * 128:(i % 2) * ncols + (j + 1) * 128],
                                in_=pb[:], func=AF.Copy))
                    self.on("scalar", tev)

                if i == ntile - 1:
                    for it in (ntile - 2, ntile - 1):
                        def wqstt(sync, wname=wname, i=it, dst_w=dst_w, ncols=ncols, nblk=nblk):
                            sync.wait_ge(sAC.h, ev[f"tev_{wname}_{i}_{nblk-1}"])
                            pitch = F if wname == "w1" else D
                            dst = bass.AP(dst_w.tensor, i * 128,
                                          [[pitch, 128], [128 * pitch, nblk], [1, 128]])
                            self.dma(SP[2 + i % 3], sync.dma_start(
                                out=dst, in_=s.stg8[:, (i % 2) * ncols:(i % 2) * ncols + ncols].rearrange(
                                    "p (j n) -> p j n", j=nblk)), f"wqst_{wname}_{i}")
                        self.on("sync", wqstt)



        dvb0 = sDV.n

        def bbe_ev(c, dvb0=dvb0):
            return dvb0 + c + 1

        def bbm(tensor):
            tensor.wait_ge(sDV.h, ev[f"bce_{H-1}"])
            for c in range(10):
                pb = s.pA if c % 2 == 0 else s.pB
                if c >= 2:
                    tensor.wait_ge(sDV.h, bbe_ev(c - 2))
                rhs = (s.b1row[0:1, c * 512:(c + 1) * 512] if c < 8
                       else s.b2row[0:1, (c - 8) * 512:(c - 7) * 512])
                ev[f"bbm_{c}"] = sPE.mark(tensor.matmul(
                    out=pb[:], lhsT=s.ones1_f[:], rhs=rhs, start=True, stop=True))
        self.on("tensor", bbm)

        def bbe(vector):
            for c in range(10):
                pb = s.pA if c % 2 == 0 else s.pB
                vector.wait_ge(sPE.h, ev[f"bbm_{c}"])
                dst = (s.b1b[:, c * 512:(c + 1) * 512] if c < 8
                       else s.b2b[:, (c - 8) * 512:(c - 7) * 512])
                v = sDV.mark(vector.tensor_copy(out=dst, in_=pb[:]))
                ev[f"bbe_{c}"] = v
                assert v == bbe_ev(c), (v, c)
        self.on("vector", bbe)

    def p1_ln_quant(self, b):
        s, ev = self, self.ev
        SQ, SP = s.SQ, s.SP
        sPE, sDV, sAC, sPL = s.sPE, s.sDV, s.sAC, s.sPL

        tpcb = self.tap(f"p1cb_b{b}", [128, 20 * NTT], F32)

        for tt in range(NTT):
            def xld(sync, b=b, tt=tt):
                if tt >= 2:
                    sync.wait_ge(sDV.h, ev[f"xnt_{b}_{tt-2}"])
                self.dma(SQ[1 + tt % 2], sync.dma_start(
                    out=s.xt[:, (tt % 2) * D:(tt % 2) * D + D],
                    in_=s.x_in[b, tt * 128:(tt + 1) * 128, :]), f"xld_{b}_{tt}")
            self.on("sync", xld)
            def xq_w(sync, b=b, tt=tt - 1):
                if tt < 0:
                    return
                sync.wait_ge(sAC.h, ev[f"xqe_{b}_{tt}"])
                dst = s.d_xq[b, :, :, tt * 128:(tt + 1) * 128]
                src = s.xqs[:].rearrange("p (dc n) -> p dc n", dc=8)
                self.dma(SP[tt % 2], sync.dma_start(out=dst.transpose([1, 0, 2]), in_=src),
                         f"xqw_{b}_{tt}")
            self.on("sync", xq_w)

            def stats_a(gpsimd, b=b, tt=tt):
                self.wt(gpsimd, f"xld_{b}_{tt}")
                if tt >= 1:
                    gpsimd.wait_ge(sDV.h, ev[f"rden_{b}_{tt-1}"])
                ev[f"ss_{b}_{tt}"] = sPL.mark(gpsimd.tensor_tensor(
                    out=s.dump[:], in0=s.xt[:, (tt % 2) * D:(tt % 2) * D + D],
                    in1=s.xt[:, (tt % 2) * D:(tt % 2) * D + D], op=ALU.mult))
            self.on("gpsimd", stats_a)

            def stats_v(vector, b=b, tt=tt):
                self.wt(vector, f"xld_{b}_{tt}")
                v = sDV.mark(vector.tensor_reduce(out=s.cb[:, 0:1], in_=s.xt[:, (tt % 2) * D:(tt % 2) * D + D],
                                                  axis=AX.X, op=ALU.add))
                vector.wait_ge(sPL.h, ev[f"ss_{b}_{tt}"])
                v = sDV.mark(vector.tensor_reduce(out=s.cb[:, 1:2], in_=s.dump[:],
                                                  axis=AX.X, op=ALU.add))
                vector.wait_ge(sDV.h, v)
                v = sDV.mark(vector.tensor_scalar(out=s.cb[:, 2:3], in0=s.cb[:, 0:1],
                                                  scalar1=1.0 / D, scalar2=None, op0=ALU.mult))
                vector.wait_ge(sDV.h, v)
                v = sDV.mark(vector.tensor_scalar(out=s.cb[:, 3:4], in0=s.cb[:, 2:3],
                                                  scalar1=s.cb[:, 2:3], scalar2=None,
                                                  op0=ALU.mult))
                vector.wait_ge(sDV.h, v)
                ev[f"var_{b}_{tt}"] = sDV.mark(vector.scalar_tensor_tensor(
                    out=s.cb[:, 4:5], in0=s.cb[:, 1:2], scalar=1.0 / D, in1=s.cb[:, 3:4],
                    op0=ALU.mult, op1=ALU.subtract))
            self.on("vector", stats_v)

            def rstd_a(scalar, b=b, tt=tt):
                scalar.wait_ge(sDV.h, ev[f"var_{b}_{tt}"])
                ev[f"sqv_{b}_{tt}"] = sAC.mark(scalar.activation(
                    out=s.cb[:, 5:6], in_=s.cb[:, 4:5], func=AF.Sqrt, bias=s.epscol[:, 0:1]))
            self.on("scalar", rstd_a)

            def ln_ap(vector, b=b, tt=tt):
                vector.wait_ge(sAC.h, ev[f"sqv_{b}_{tt}"])
                v = sDV.mark(vector.reciprocal(out=s.cb[:, 6:7], in_=s.cb[:, 5:6]))
                vector.wait_ge(sDV.h, v)
                if tt >= 1:
                    vector.wait_ge(sAC.h, ev[f"xnc_{b}_{tt-1}"])
                    vector.wait_ge(sPL.h, ev[f"xnq_{b}_{tt-1}"])
                ev[f"xnt_{b}_{tt}"] = sDV.mark(vector.tensor_scalar(
                    out=s.xnt[:], in0=s.xt[:, (tt % 2) * D:(tt % 2) * D + D], scalar1=s.cb[:, 2:3],
                    scalar2=s.cb[:, 6:7], op0=ALU.subtract, op1=ALU.mult))
            self.on("vector", ln_ap)

            def xn_a(scalar, b=b, tt=tt):
                scalar.wait_ge(sDV.h, ev[f"xnt_{b}_{tt}"])
                if tt >= 1:
                    scalar.wait_ge(sPE.h, ev[f"xntr_{b}_{tt-1}"])
                ev[f"xnc_{b}_{tt}"] = sAC.mark(
                    scalar.activation(out=s.xnbt[:], in_=s.xnt[:], func=AF.Copy))
            self.on("scalar", xn_a)

            def xn_sq(gpsimd, b=b, tt=tt):
                gpsimd.wait_ge(sDV.h, ev[f"xnt_{b}_{tt}"])
                ev[f"xnq_{b}_{tt}"] = sPL.mark(gpsimd.tensor_tensor(
                    out=s.dump[:], in0=s.xnt[:], in1=s.xnt[:], op=ALU.mult))
            self.on("gpsimd", xn_sq)

            def q_cols(vector, b=b, tt=tt):
                vector.wait_ge(sPL.h, ev[f"xnq_{b}_{tt}"])
                v = sDV.mark(vector.tensor_reduce(out=s.cb[:, 8:9], in_=s.xnt[:], axis=AX.X,
                                                  op=ALU.max, apply_absolute_value=True))
                v = sDV.mark(vector.tensor_reduce(out=s.cb[:, 7:8], in_=s.dump[:],
                                                  axis=AX.X, op=ALU.add))
                vector.wait_ge(sDV.h, v)
                v = sDV.mark(vector.tensor_scalar(out=s.cb[:, 9:10], in0=s.cb[:, 7:8],
                                                  scalar1=1.0 / D, scalar2=EPS,
                                                  op0=ALU.mult, op1=ALU.add))
                ev[f"rden_{b}_{tt}"] = v
            self.on("vector", q_cols)

            def rsq_a(scalar, b=b, tt=tt):
                scalar.wait_ge(sDV.h, ev[f"rden_{b}_{tt}"])
                ev[f"rsq_{b}_{tt}"] = sAC.mark(scalar.activation(
                    out=s.cb[:, 10:11], in_=s.cb[:, 9:10], func=AF.Sqrt))
            self.on("scalar", rsq_a)

            def q_fin(vector, b=b, tt=tt):
                vector.wait_ge(sAC.h, ev[f"rsq_{b}_{tt}"])
                v = sDV.mark(vector.reciprocal(out=s.cb[:, 11:12], in_=s.cb[:, 10:11]))
                vector.wait_ge(sDV.h, v)
                v = sDV.mark(vector.tensor_scalar(out=s.cb[:, 12:13], in0=s.cb[:, 8:9],
                                                  scalar1=s.cb[:, 11:12], scalar2=1e-5,
                                                  op0=ALU.mult, op1=ALU.max))
                vector.wait_ge(sDV.h, v)
                v = sDV.mark(vector.reciprocal(out=s.cb[:, 13:14], in_=s.cb[:, 12:13]))
                vector.wait_ge(sDV.h, v)
                v = sDV.mark(vector.tensor_scalar(out=s.cb[:, 14:15], in0=s.cb[:, 13:14],
                                                  scalar1=127.0, scalar2=None, op0=ALU.mult))
                vector.wait_ge(sDV.h, v)
                v = sDV.mark(vector.tensor_scalar(out=s.cb[:, 15:16], in0=s.cb[:, 11:12],
                                                  scalar1=s.cb[:, 14:15], scalar2=None,
                                                  op0=ALU.mult))
                vector.wait_ge(sDV.h, v)
                v = sDV.mark(vector.tensor_scalar(out=s.cb[:, 16:17], in0=s.cb[:, 14:15],
                                                  scalar1=s.s1col[:, 0:1], scalar2=None,
                                                  op0=ALU.mult))
                vector.wait_ge(sDV.h, v)
                sDV.mark(vector.reciprocal(out=s.dq1c[:, b * NTT + tt:b * NTT + tt + 1],
                                           in_=s.cb[:, 16:17]))
                if tt >= 1:
                    vector.wait_ge(sPE.h, ev[f"xqtr_{b}_{tt-1}"])
                vector.tensor_scalar(out=s.rtmp[:], in0=s.xnt[:], scalar1=s.cb[:, 15:16],
                                     scalar2=MAGIC, op0=ALU.mult, op1=ALU.add)
                ev[f"p1q_{b}_{tt}"] = sDV.mark(vector.tensor_scalar(
                    out=s.xqbt[:], in0=s.rtmp[:], scalar1=MAGIC, scalar2=None,
                    op0=ALU.subtract))
                if getattr(s, "p1cb", None) is not None:
                    ev[f"p1cb_{b}_{tt}"] = sDV.mark(vector.tensor_copy(
                        out=s.p1cb[:][:, tt * 20:tt * 20 + 20], in_=s.cb[:, 0:20]))
            self.on("vector", q_fin)

            for rnd in range(8):
                def tr_p(tensor, b=b, tt=tt, rnd=rnd):
                    for half in range(2):
                        blk = rnd * 2 + half
                        pb = s.pT1 if blk % 2 == 0 else s.pT2
                        if blk == 0:
                            tensor.wait_ge(sAC.h, ev[f"xnc_{b}_{tt}"])
                        if blk == 8:
                            tensor.wait_ge(sDV.h, ev[f"p1q_{b}_{tt}"])
                        if tt > 0 or blk >= 2:
                            key = (f"trev_{b}_{tt}_{blk-2}" if blk >= 2
                                   else f"trev_{b}_{tt-1}_{14 + blk}")
                            tensor.wait_ge(sAC.h, ev[key])
                        src = (s.xnbt[:, blk * 128:(blk + 1) * 128] if blk < 8
                               else s.xqbt[:, (blk - 8) * 128:(blk - 7) * 128])
                        ev[f"tr_{b}_{tt}_{blk}"] = sPE.mark(tensor.transpose(
                            out=pb[:], in_=src, identity=s.ident_bf[:]))
                    if rnd == 7:
                        ev[f"xqtr_{b}_{tt}"] = sPE.n
                        ev[f"xntr_{b}_{tt}"] = sPE.n
                self.on("tensor", tr_p)

                def tr_e(scalar, b=b, tt=tt, rnd=rnd):
                    for half in range(2):
                        blk = rnd * 2 + half
                        pb = s.pT1 if blk % 2 == 0 else s.pT2
                        scalar.wait_ge(sPE.h, ev[f"tr_{b}_{tt}_{blk}"])
                        if blk < 8:
                            dst = s.xn_fm[:, blk * S + tt * 128:blk * S + (tt + 1) * 128]
                        else:
                            if blk == 8 and tt > 0:
                                self.wt(scalar, f"xqw_{b}_{tt-1}")
                            dst = s.xqs[:, (blk - 8) * 128:(blk - 7) * 128]
                        ev[f"trev_{b}_{tt}_{blk}"] = sAC.mark(scalar.activation(
                            out=dst, in_=pb[:], func=AF.Copy))
                    if rnd == 7:
                        ev[f"xqe_{b}_{tt}"] = sAC.n
                        ev[f"xne_{b}_{tt}"] = sAC.n
                self.on("scalar", tr_e)



        def xq_wl(sync, b=b, tt=NTT - 1):
            sync.wait_ge(sAC.h, ev[f"xqe_{b}_{tt}"])
            dst = s.d_xq[b, :, :, tt * 128:(tt + 1) * 128]
            src = s.xqs[:].rearrange("p (dc n) -> p dc n", dc=8)
            self.dma(SP[tt % 2], sync.dma_start(out=dst.transpose([1, 0, 2]), in_=src),
                     f"xqw_{b}_{tt}")
        self.on("sync", xq_wl)

        tp = self.tap(f"xn_fm_b{b}", [128, 8 * S], BF16)
        if tp is not None:
            def tapw(gpsimd, b=b, tp=tp):
                gpsimd.wait_ge(s.sAC.h, ev[f"xqe_{b}_{NTT-1}"])
                self.dma(s.SP[4], gpsimd.dma_start(out=tp, in_=s.xn_fm[:]))
            self.on("gpsimd", tapw)
        tpcb2 = self.tap_outs.get(f"p1cb_b{b}")
        if tpcb2 is not None:
            def tapw2(gpsimd, b=b, tpcb2=tpcb2):
                gpsimd.wait_ge(s.sDV.h, ev[f"p1cb_{b}_{NTT-1}"])
                self.dma(s.SP[4], gpsimd.dma_start(out=tpcb2, in_=s.p1cb[:]))
            self.on("gpsimd", tapw2)

    def p2_proj(self, b):
        s, ev = self, self.ev
        SQ, SP = s.SQ, s.SP
        sPE, sDV, sAC, sPL = s.sPE, s.sDV, s.sAC, s.sPL

        tpq1 = self.tap("wq1t", [D, F], BF16)
        if tpq1 is not None:
            def tapq1(gpsimd):
                self.wt(gpsimd, "wqst_w1_31")
                self.dma(SP[4], gpsimd.dma_start(out=tpq1, in_=s.d_wq1))
            self.on("gpsimd", tapq1)

        tpw = self.tap("winb", [D, 3 * D], BF16)
        if tpw is not None and b == 0:
            def tapwb(gpsimd):
                self.wt(gpsimd, "wst_win_7")
                self.dma(SP[4], gpsimd.dma_start(out=tpw, in_=s.d_winb))
            self.on("gpsimd", tapwb)

        ac2 = sAC.n

        def pje_ev(fs, tc):
            return ac2 + fs * 4 + tc + 1

        for fs in range(16):
            wcol_off = fs * 128 if fs < 8 else 2 * D + (fs - 8) * 128

            def wcl(sync, b=b, fs=fs, wcol_off=wcol_off):
                if fs >= 2:
                    sync.wait_ge(sPE.h, ev[f"pjm_{b}_{fs-2}_3"])
                src = bass.AP(s.d_winb.tensor, wcol_off,
                              [[3 * D, 128], [128 * 3 * D, 8], [1, 128]])
                self.dma(SQ[1 + fs % 2], sync.dma_start(
                    out=s.wcolb[:, (fs % 2) * 1024:(fs % 2) * 1024 + 1024].rearrange("p (dc n) -> p dc n", dc=8),
                    in_=src), f"wcl_{b}_{fs}")
            self.on("sync", wcl)

            def pj(tensor, b=b, fs=fs):
                self.wt(tensor, f"wcl_{b}_{fs}")
                for tc in range(4):
                    pb = s.pA if tc % 2 == 0 else s.pB
                    pv = (pje_ev(fs, tc - 2) if tc >= 2
                          else (pje_ev(fs - 1, 2 + tc) if fs >= 1 else None))
                    if pv is not None:
                        tensor.wait_ge(sAC.h, pv)
                    for dc in range(8):
                        ins = tensor.matmul(
                            out=pb[:], lhsT=s.wcolb[:, (fs % 2) * 1024 + dc * 128:(fs % 2) * 1024 + (dc + 1) * 128],
                            rhs=s.xn_fm[:, dc * S + tc * 512:dc * S + (tc + 1) * 512],
                            start=(dc == 0), stop=(dc == 7))
                    ev[f"pjm_{b}_{fs}_{tc}"] = sPE.mark(ins)
            self.on("tensor", pj)

            def pje(scalar, b=b, fs=fs):
                for tc in range(4):
                    pb = s.pA if tc % 2 == 0 else s.pB
                    scalar.wait_ge(sPE.h, ev[f"pjm_{b}_{fs}_{tc}"])
                    if fs < 8:
                        dst = s.k_fm[:, fs * S + tc * 512:fs * S + (tc + 1) * 512]
                        sc = 1.0
                    else:
                        dst = s.q_fm[:, (fs - 8) * S + tc * 512:(fs - 8) * S + (tc + 1) * 512]
                        sc = 0.125
                    v = sAC.mark(scalar.activation(
                        out=dst, in_=pb[:], func=AF.Copy, scale=sc))
                    ev[f"pje_{b}_{fs}_{tc}"] = v
                    assert v == pje_ev(fs, tc), (v, fs, tc)
            self.on("scalar", pje)

        def vones(vector, b=b):
            ev[f"vones_{b}"] = sDV.mark(vector.memset(s.v_tm[:], 1.0))
        self.on("vector", vones)

        def wvl(sync, b=b):
            src = bass.AP(s.d_winb.tensor, D,
                          [[3 * D, 128], [128 * 3 * D, 8], [1, 1024]])
            self.dma(SQ[3], sync.dma_start(
                out=s.wvb[:].rearrange("p (dc n) -> p dc n", dc=8), in_=src), f"wvl_{b}")
        self.on("sync", wvl)

        for tt in range(NTT):
            def vpj(tensor, b=b, tt=tt):
                if tt == 0:
                    self.wt(tensor, f"wvl_{b}")
                for vc in range(2):
                    pb = s.pC if vc == 0 else s.pD
                    if tt >= 1:
                        tensor.wait_ge(sDV.h, ev[f"vpe_{b}_{tt-1}_{vc}"])
                    for dc in range(8):
                        ins = tensor.matmul(
                            out=pb[:], lhsT=s.xn_fm[:, dc * S + tt * 128:dc * S + (tt + 1) * 128],
                            rhs=s.wvb[:, dc * 1024 + vc * 512:dc * 1024 + (vc + 1) * 512],
                            start=(dc == 0), stop=(dc == 7))
                    ev[f"vpm_{b}_{tt}_{vc}"] = sPE.mark(ins)
            self.on("tensor", vpj)

            def vpe(vector, b=b, tt=tt):
                if tt == 0:
                    vector.wait_ge(sDV.h, ev[f"vones_{b}"])
                for vc in range(2):
                    pb = s.pC if vc == 0 else s.pD
                    vector.wait_ge(sPE.h, ev[f"vpm_{b}_{tt}_{vc}"])
                    base = tt * 1040 + vc * 520
                    dst = s.v_tm[:, base:base + 520].rearrange(
                        "p (g c) -> p g c", g=8, c=65)[:, :, 0:64]
                    src = pb[:].rearrange("p (g c) -> p g c", g=8, c=64)
                    ev[f"vpe_{b}_{tt}_{vc}"] = sDV.mark(vector.tensor_copy(out=dst, in_=src))
            self.on("vector", vpe)

        tpk = self.tap(f"k_fm_b{b}", [128, 8 * S], BF16)
        tpq = self.tap(f"q_fm_b{b}", [128, 8 * S], BF16)
        tpv = self.tap(f"v_tm_b{b}", [128, NTT * 1040], BF16)
        if tpk is not None:
            def tapw(gpsimd, b=b):
                gpsimd.wait_ge(s.sAC.h, ev[f"pje_{b}_15_3"])
                gpsimd.wait_ge(s.sDV.h, ev[f"vpe_{b}_{NTT-1}_1"])
                self.dma(s.SP[4], gpsimd.dma_start(out=tpk, in_=s.k_fm[:]))
                self.dma(s.SP[4], gpsimd.dma_start(out=tpq, in_=s.q_fm[:]))
                self.dma(s.SP[4], gpsimd.dma_start(out=tpv, in_=s.v_tm[:]))
            self.on("gpsimd", tapw)

    def p3_attn(self, b):
        s, ev = self, self.ev
        SQ, SP = s.SQ, s.SP
        sPE, sDV, sAC, sPL = s.sPE, s.sDV, s.sAC, s.sPL

        SKW = 3968

        NG = 4 * H
        NS = 20
        qk_ev, pv_ev, rbm_ev = {}, {}, {}
        n = sPE.n
        for m in range(NG):
            if m >= 2:
                n += 1
                rbm_ev[m - 2] = n
            for step in range(4):
                for j in range(4):
                    n += 1
                    qk_ev[(m, 4 * step + j)] = n
                if m >= 1:
                    for j in range(4):
                        n += 1
                        pv_ev[(m - 1, 4 * step + j)] = n
        for step in range(4):
            for j in range(4):
                n += 1
                pv_ev[(NG - 1, 4 * step + j)] = n
        n += 1
        rbm_ev[NG - 2] = n
        n += 1
        rbm_ev[NG - 1] = n

        exp_ev, dc_ev = {}, {}
        n = sAC.n
        for m in range(NG):
            for kt in range(16):
                n += 1
                exp_ev[(m, kt)] = n
            if m >= 1:
                n += 1
                dc_ev[m - 1] = n
        n += 1
        dc_ev[NG - 1] = n

        pool_ev, rd_ev, ae_ev = {}, {}, {}
        n = sDV.n
        for m in range(NG):
            if m >= 2:
                n += 1
                rd_ev[m - 2] = n
                n += 1
                ae_ev[m - 2] = n
            for kt in range(16):
                n += 1
                pool_ev[(m, kt)] = n
        for g in (NG - 2, NG - 1):
            n += 1
            rd_ev[g] = n
            n += 1
            ae_ev[g] = n

        psq = [s.pA, s.pB, s.pC]

        def bank(m, kt):
            return (m + kt) % 3

        def hof(g):
            h, qc = g // 4, g % 4
            return (h % 2) * 64, (h // 2) * S, qc

        def pOx(g):
            return s.pO if g % 2 == 0 else s.pO2

        def pe_all(tensor, b=b):
            def qk(m, kt):
                hh, hc, qc = hof(m)
                T = 16 * m + kt
                if T >= 3:
                    pm, pk = (m, kt - 3) if kt >= 3 else (m - 1, 13 + kt)
                    tensor.wait_ge(sAC.h, exp_ev[(pm, pk)])
                v = sPE.mark(tensor.matmul(
                    out=psq[bank(m, kt)][:],
                    lhsT=s.k_fm[hh:hh + 64, hc + kt * 128:hc + (kt + 1) * 128],
                    rhs=s.q_fm[hh:hh + 64, hc + qc * 512:hc + (qc + 1) * 512],
                    start=True, stop=True))
                assert v == qk_ev[(m, kt)]

            def pv(g, kt):
                hh, hc, qc = hof(g)
                h = g // 4
                if kt == 0 and g >= 2:
                    tensor.wait_ge(sDV.h, ae_ev[g - 2])
                    tensor.wait_ge(sAC.h, dc_ev[g - 2])
                tensor.wait_ge(sDV.h, pool_ev[(g, kt)])
                v = sPE.mark(tensor.matmul(
                    out=pOx(g)[0:65, :],
                    lhsT=s.v_tm[:, kt * 1040 + h * 65:kt * 1040 + h * 65 + 65],
                    rhs=s.emul[:, ((16 * g + kt) % NS) * 512:((16 * g + kt) % NS) * 512 + 512],
                    start=(kt == 0), stop=(kt == 15)))
                assert v == pv_ev[(g, kt)]

            def rbm(g):
                tensor.wait_ge(sAC.h, dc_ev[g])
                if g >= 1:
                    tensor.wait_ge(sDV.h, rd_ev[g - 1])
                v = sPE.mark(tensor.matmul(
                    out=s.pD[0:64, :], lhsT=s.onesp64[64:65, :],
                    rhs=s.rdbuf[64:65, :], start=True, stop=True))
                assert v == rbm_ev[g]

            for m in range(NG):
                if m >= 2:
                    rbm(m - 2)
                for step in range(4):
                    for j in range(4):
                        qk(m, 4 * step + j)
                    if m >= 1:
                        for j in range(4):
                            pv(m - 1, 4 * step + j)
            for step in range(4):
                for j in range(4):
                    pv(NG - 1, 4 * step + j)
            rbm(NG - 2)
            rbm(NG - 1)
        self.on("tensor", pe_all)

        def act_all(scalar, b=b):
            def dcopy(g):
                scalar.wait_ge(sPE.h, pv_ev[(g, 15)])
                if g >= 1:
                    scalar.wait_ge(sPE.h, rbm_ev[g - 1])
                v = sAC.mark(scalar.activation(
                    out=s.rdbuf[64:65, :], in_=pOx(g)[64:65, :], func=AF.Copy))
                assert v == dc_ev[g]

            for m in range(NG):
                for kt in range(16):
                    scalar.wait_ge(sPE.h, qk_ev[(m, kt)])
                    T = 16 * m + kt
                    if T >= 3:
                        pm, pk = (m, kt - 3) if kt >= 3 else (m - 1, 13 + kt)
                        scalar.wait_ge(sDV.h, pool_ev[(pm, pk)])
                    v = sAC.mark(scalar.activation(
                        out=s.eraw[:, bank(m, kt) * 512:bank(m, kt) * 512 + 512],
                        in_=psq[bank(m, kt)][:], func=AF.Exp))
                    assert v == exp_ev[(m, kt)]
                if m >= 1:
                    dcopy(m - 1)
            dcopy(NG - 1)
        self.on("scalar", act_all)

        def sync_all(sync, b=b):
            for h in range(H):
                if h >= 3:
                    sync.wait_ge(sDV.h, pool_ev[(4 * (h - 3) + 3, 15)])
                src = bass.AP(s.d_expb.tensor, h * 128 * EXPW + 127,
                              [[EXPW - 1, 128], [1, SKW]])
                self.dma(SP[2 + h % 3], sync.dma_start(
                    out=s.skw[:, (h % 3) * SKW:(h % 3) * SKW + SKW], in_=src),
                    f"skw_{b}_{h}")
        self.on("sync", sync_all)

        def dve_all(vector, b=b):
            def fin(g):
                vector.wait_ge(sPE.h, rbm_ev[g])
                with s.nc.allow_low_precision("softmax 1/denom applied to bf16 weights"):
                    v = sDV.mark(vector.reciprocal(
                        out=s.rbs[:, (g % 2) * 512:(g % 2) * 512 + 512],
                        in_=s.pD[0:64, :]))
                assert v == rd_ev[g]
                vector.wait_ge(sDV.h, v)
                hh, hc, qc = hof(g)
                v = sDV.mark(vector.tensor_tensor(
                    out=s.a_fm[hh:hh + 64, hc + qc * 512:hc + (qc + 1) * 512],
                    in0=pOx(g)[0:64, :],
                    in1=s.rbs[:, (g % 2) * 512:(g % 2) * 512 + 512], op=ALU.mult))
                assert v == ae_ev[g]
                ev[f"ae_{b}_{g}"] = v

            for m in range(NG):
                if m >= 2:
                    fin(m - 2)
                h = m // 4
                for kt in range(16):
                    if m % 4 == 0 and kt == 0:
                        self.wt(vector, f"skw_{b}_{h}")
                    vector.wait_ge(sAC.h, exp_ev[(m, kt)])
                    T = 16 * m + kt
                    if T >= NS:
                        pg, pk = divmod(T - NS, 16)
                        vector.wait_ge(sPE.h, pv_ev[(pg, pk)])
                    j0 = 1920 + (m % 4) * 512 - kt * 128
                    v = sDV.mark(vector.tensor_tensor(
                        out=s.emul[:, (T % NS) * 512:(T % NS) * 512 + 512],
                        in0=s.eraw[:, bank(m, kt) * 512:bank(m, kt) * 512 + 512],
                        in1=s.skw[:, (h % 3) * SKW + j0:(h % 3) * SKW + j0 + 512],
                        op=ALU.mult))
                    assert v == pool_ev[(m, kt)]
            fin(NG - 2)
            fin(NG - 1)
        self.on("vector", dve_all)



        tpa = self.tap(f"a_fm_b{b}", [128, 8 * S], BF16)
        if tpa is not None:
            def tapw(gpsimd, b=b, tpa=tpa):
                gpsimd.wait_ge(s.sDV.h, ev[f"ae_{b}_{4 * H - 1}"])
                self.dma(s.SP[4], gpsimd.dma_start(out=tpa, in_=s.a_fm[:]))
            self.on("gpsimd", tapw)

    def p4_gate(self, b):
        s, ev = self, self.ev
        SQ, SP = s.SQ, s.SP
        sPE, sDV, sAC, sPL = s.sPE, s.sDV, s.sAC, s.sPL
        banks = [s.pA, s.pC, s.pB, s.pD]

        def wgl(sync, b=b):
            src = bass.AP(s.d_wgb.tensor, 0, [[2 * D, 128], [128 * 2 * D, 8], [1, 2 * D]])
            self.dma(SQ[3], sync.dma_start(
                out=s.wgs[:].rearrange("p (dc n) -> p dc n", dc=8), in_=src), f"wgl_{b}")
        self.on("sync", wgl)

        for tt in range(NTT):
            def gm(tensor, b=b, tt=tt):
                if tt == 0:
                    self.wt(tensor, f"wgl_{b}")
                for gc in range(4):
                    pb = banks[gc]
                    if tt >= 1:
                        if gc in (2, 3):
                            tensor.wait_ge(sAC.h, ev[f"sg_{b}_{tt-1}_{gc-2}"])
                        else:
                            tensor.wait_ge(sDV.h, ev[f"ab_{b}_{tt-1}_{gc}"])
                    for dc in range(8):
                        ins = tensor.matmul(
                            out=pb[:], lhsT=s.a_fm[:, dc * S + tt * 128:dc * S + (tt + 1) * 128],
                            rhs=s.wgs[:, dc * 2048 + gc * 512:dc * 2048 + (gc + 1) * 512],
                            start=(dc == 0), stop=False)
                    ins = tensor.matmul(
                        out=pb[:], lhsT=s.ones1_bf[:],
                        rhs=s.bgate_bf[0:1, gc * 512:(gc + 1) * 512], start=False, stop=True)
                    ev[f"gm_{b}_{tt}_{gc}"] = sPE.mark(ins)
            self.on("tensor", gm)

            for half in range(2):
                def sgst(scalar, b=b, tt=tt, half=half):
                    scalar.wait_ge(sPE.h, ev[f"gm_{b}_{tt}_{2 + half}"])
                    if tt >= 1 or half == 1:
                        key = f"ab_{b}_{tt}_{half-1}" if half == 1 else f"ab_{b}_{tt-1}_1"
                        scalar.wait_ge(sDV.h, ev[key])
                    ev[f"sg_{b}_{tt}_{half}"] = sAC.mark(scalar.activation(
                        out=s.sgb[:], in_=banks[2 + half][:], func=AF.Sigmoid))
                self.on("scalar", sgst)

                def abst(vector, b=b, tt=tt, half=half):
                    vector.wait_ge(sAC.h, ev[f"sg_{b}_{tt}_{half}"])
                    vector.wait_ge(sPE.h, ev[f"gm_{b}_{tt}_{half}"])
                    if tt >= 2:
                        self.wt(vector, f"abw_{b}_{tt-2}")
                    ev[f"ab_{b}_{tt}_{half}"] = sDV.mark(vector.tensor_tensor(
                        out=s.abb[:, (tt % 2) * 1024 + half * 512:(tt % 2) * 1024 + (half + 1) * 512],
                        in0=banks[half][:], in1=s.sgb[:], op=ALU.mult))
                self.on("vector", abst)

            def abw(sync, b=b, tt=tt):
                sync.wait_ge(sDV.h, ev[f"ab_{b}_{tt}_1"])
                self.dma(SP[tt % 2], sync.dma_start(
                    out=s.d_abr[b, tt * 128:(tt + 1) * 128, :],
                    in_=s.abb[:, (tt % 2) * 1024:(tt % 2) * 1024 + 1024]), f"abw_{b}_{tt}")
            self.on("sync", abw)

        tpab = self.tap(f"abr_b{b}", [S, D], F32)
        tpxq = self.tap(f"dxq_b{b}", [8, 128, S], BF16)
        if tpab is not None:
            def tapab(gpsimd, b=b, tpab=tpab, tpxq=tpxq):
                self.wt(gpsimd, f"abw_{b}_{NTT-1}")
                self.dma(SP[4], gpsimd.dma_start(out=tpab, in_=s.d_abr[b]))
                if tpxq is not None:
                    self.dma(SP[4], gpsimd.dma_start(out=tpxq, in_=s.d_xq[b]))
            self.on("gpsimd", tapab)

    def p56_ffn(self, b):
        s, ev = self, self.ev
        SQ, SP = s.SQ, s.SP
        sPE, sDV, sAC, sPL = s.sPE, s.sDV, s.sAC, s.sPL

        def wql(sync, b=b):
            if b == 0:
                self.wt(sync, "wqst_w1_31")
                self.wt(sync, "wqst_w2_7")
            src1 = bass.AP(s.d_wq1.tensor, 0, [[F, 128], [128 * F, 8], [1, F]])
            self.dma(SQ[4], sync.dma_start(
                out=s.wq1s[:].rearrange("p (dc n) -> p dc n", dc=8), in_=src1), f"wq1l_{b}")
            src2 = bass.AP(s.d_wq2.tensor, 0, [[D, 128], [128 * D, 32], [1, D]])
            self.dma(SQ[5], sync.dma_start(
                out=s.wq2s[:].rearrange("p (fc n) -> p fc n", fc=32), in_=src2), f"wq2l_{b}")
        self.on("sync", wql)

        def emit_tail(tt, part="all"):
            front = part in ("all", "front")
            back = part in ("all", "back")
            for fc8 in (range(8) if front else ()):
                def rnd_q(gpsimd, b=b, tt=tt, fc8=fc8):
                    if fc8 == 0:
                        gpsimd.wait_ge(sDV.h, ev[f"scq_{b}_{tt}"])
                    ev[f"hqp_{b}_{tt}_{fc8}"] = sPL.mark(gpsimd.tensor_scalar(
                        out=s.gbuf[:, fc8 * 512:(fc8 + 1) * 512],
                        in0=s.gbuf[:, fc8 * 512:(fc8 + 1) * 512],
                        scalar1=s.cb[:, 32 + tt % 2:33 + tt % 2], scalar2=MAGIC,
                        op0=ALU.mult, op1=ALU.add))
                self.on("gpsimd", rnd_q)

                def rnd_q2(scalar, b=b, tt=tt, fc8=fc8):
                    scalar.wait_ge(sPL.h, ev[f"hqp_{b}_{tt}_{fc8}"])
                    ev[f"hqc_{b}_{tt}_{fc8}"] = sAC.mark(scalar.activation(
                        out=s.hqt[:, (tt % 2) * F + fc8 * 512:(tt % 2) * F + (fc8 + 1) * 512],
                        in_=s.gbuf[:, fc8 * 512:(fc8 + 1) * 512],
                        func=AF.Identity, bias=s.negmag[:, 0:1]))
                self.on("scalar", rnd_q2)

            for fc8 in (range(8) if back else ()):
                for pr in range(2):
                    def htr(tensor, b=b, tt=tt, fc8=fc8, pr=pr):
                        if fc8 == 0 and pr == 0:
                            pass
                        if pr == 0:
                            tensor.wait_ge(sAC.h, ev[f"hqc_{b}_{tt}_{fc8}"])
                        for half in range(2):
                            fc = fc8 * 4 + pr * 2 + half
                            pb = s.pT1 if fc % 2 == 0 else s.pT2
                            if tt > 0 or fc >= 2:
                                key = (f"hte_{b}_{tt}_{fc-2}" if fc >= 2
                                       else f"hte_{b}_{tt-1}_{30 + fc}")
                                tensor.wait_ge(sAC.h, ev[key])
                            ev[f"htr_{b}_{tt}_{fc}"] = sPE.mark(tensor.transpose(
                                out=pb[:],
                                in_=s.hqt[:, (tt % 2) * F + fc * 128:(tt % 2) * F + (fc + 1) * 128],
                                identity=s.ident_bf[:]))
                    self.on("tensor", htr)

                    def hte(scalar, b=b, tt=tt, fc8=fc8, pr=pr):
                        for half in range(2):
                            fc = fc8 * 4 + pr * 2 + half
                            pb = s.pT1 if fc % 2 == 0 else s.pT2
                            scalar.wait_ge(sPE.h, ev[f"htr_{b}_{tt}_{fc}"])
                            ev[f"hte_{b}_{tt}_{fc}"] = sAC.mark(scalar.activation(
                                out=s.hqf[:, (tt % 2) * F + fc * 128:(tt % 2) * F + (fc + 1) * 128],
                                in_=pb[:], func=AF.Copy))
                    self.on("scalar", hte)

            if not back:
                return

            def f2m(tensor, b=b, tt=tt):
                if tt == 0:
                    self.wt(tensor, f"wq2l_{b}")
                for d2 in range(2):
                    pb = s.pC if d2 == 0 else s.pD
                    if tt >= 1:
                        tensor.wait_ge(sDV.h, ev[f"mg_{b}_{tt-1}_{d2}"])
                    for fc in range(32):
                        if d2 == 0:
                            tensor.wait_ge(sAC.h, ev[f"hte_{b}_{tt}_{fc}"])
                        ins = tensor.matmul(
                            out=pb[:],
                            lhsT=s.hqf[:, (tt % 2) * F + fc * 128:(tt % 2) * F + (fc + 1) * 128],
                            rhs=s.wq2s[:, fc * D + d2 * 512:fc * D + (d2 + 1) * 512],
                            start=(fc == 0), stop=(fc == 31))
                    ev[f"f2m_{b}_{tt}_{d2}"] = sPE.mark(ins)
            self.on("tensor", f2m)

            for d2 in range(2):
                def mg(vector, b=b, tt=tt, d2=d2):
                    vector.wait_ge(sPE.h, ev[f"f2m_{b}_{tt}_{d2}"])
                    vector.wait_ge(sPL.h, ev[f"xa_{b}_{tt}"])
                    if tt >= 2:
                        self.wt(vector, f"ow_{b}_{tt-2}")
                    pb = s.pC if d2 == 0 else s.pD
                    v = sDV.mark(vector.scalar_tensor_tensor(
                        out=s.ftmp[:, d2 * 512:(d2 + 1) * 512], in0=pb[:],
                        scalar=s.cb[:, 34 + tt % 2:35 + tt % 2],
                        in1=s.b2b[:, d2 * 512:(d2 + 1) * 512],
                        op0=ALU.mult, op1=ALU.add))
                    ev[f"mg_{b}_{tt}_{d2}"] = sDV.mark(vector.tensor_tensor(
                        out=s.xat[:, (tt % 2) * D + d2 * 512:(tt % 2) * D + (d2 + 1) * 512],
                        in0=s.ftmp[:, d2 * 512:(d2 + 1) * 512],
                        in1=s.xat[:, (tt % 2) * D + d2 * 512:(tt % 2) * D + (d2 + 1) * 512],
                        op=ALU.add))
                self.on("vector", mg)

            def ow(sync, b=b, tt=tt):
                sync.wait_ge(sDV.h, ev[f"mg_{b}_{tt}_0"])
                sync.wait_ge(sDV.h, ev[f"mg_{b}_{tt}_1"])
                self.dma(SP[tt % 2], sync.dma_start(
                    out=s.out[b, tt * 128:(tt + 1) * 128, :],
                    in_=s.xat[:, (tt % 2) * D:(tt % 2) * D + D]), f"ow_{b}_{tt}")
            self.on("sync", ow)

        for tt in range(NTT):
            def lds_a(sync, b=b, tt=tt):
                if tt >= 2:
                    sync.wait_ge(sPE.h, ev[f"f1m_{b}_{tt-2}_7"])
                    sync.wait_ge(sPL.h, ev[f"xa_{b}_{tt-2}"])
                src = s.d_xq[b, :, :, tt * 128:(tt + 1) * 128]
                self.dma(SQ[1 + tt % 2], sync.dma_start(
                    out=s.xqt[:, (tt % 2) * 1024:(tt % 2) * 1024 + 1024].rearrange("p (dc n) -> p dc n", dc=8),
                    in_=src.transpose([1, 0, 2])), f"xql_{b}_{tt}")
                self.dma(SQ[6 + tt % 2], sync.dma_start(
                    out=s.abrt[:, (tt % 2) * D:(tt % 2) * D + D],
                    in_=s.d_abr[b, tt * 128:(tt + 1) * 128, :]), f"abl_{b}_{tt}")
            self.on("sync", lds_a)

            if tt >= 2:
                emit_tail(tt - 2, "back")

            def lds(sync, b=b, tt=tt):
                if tt >= 2:
                    self.wt(sync, f"ow_{b}_{tt-2}")
                self.dma(SQ[3 - 3 * (tt % 2)], sync.dma_start(
                    out=s.xat[:, (tt % 2) * D:(tt % 2) * D + D],
                    in_=s.x_in[b, tt * 128:(tt + 1) * 128, :]), f"x2l_{b}_{tt}")
            self.on("sync", lds)

            def xadd(gpsimd, b=b, tt=tt):
                self.wt(gpsimd, f"x2l_{b}_{tt}")
                gpsimd.wait_ge(SQ[6 + tt % 2].h, ev[f"abl_{b}_{tt}"][1])
                ev[f"xa_{b}_{tt}"] = sPL.mark(gpsimd.tensor_tensor(
                    out=s.xat[:, (tt % 2) * D:(tt % 2) * D + D],
                    in0=s.xat[:, (tt % 2) * D:(tt % 2) * D + D],
                    in1=s.abrt[:, (tt % 2) * D:(tt % 2) * D + D], op=ALU.add))
            self.on("gpsimd", xadd)

            if tt >= 1:
                emit_tail(tt - 1, "front")

            for fc8 in range(8):
                def f1m(tensor, b=b, tt=tt, fc8=fc8):
                    if tt == 0 and fc8 == 0:
                        self.wt(tensor, f"wq1l_{b}")
                    if fc8 == 0:
                        self.wt(tensor, f"xql_{b}_{tt}")
                    pb = [s.pA, s.pB, s.pO, s.pO2][fc8 % 4]
                    key = (f"dq_{b}_{tt}_{fc8-4}" if fc8 >= 4
                           else (f"dq_{b}_{tt-1}_{4 + fc8}" if tt >= 1 else None))
                    if key is not None:
                        tensor.wait_ge(sDV.h, ev[key])
                    for dc in range(8):
                        ins = tensor.matmul(
                            out=pb[:], lhsT=s.xqt[:, (tt % 2) * 1024 + dc * 128:(tt % 2) * 1024 + (dc + 1) * 128],
                            rhs=s.wq1s[:, dc * F + fc8 * 512:dc * F + (fc8 + 1) * 512],
                            start=(dc == 0), stop=(dc == 7))
                    ev[f"f1m_{b}_{tt}_{fc8}"] = sPE.mark(ins)
                self.on("tensor", f1m)

                def dq(vector, b=b, tt=tt, fc8=fc8):
                    vector.wait_ge(sPE.h, ev[f"f1m_{b}_{tt}_{fc8}"])
                    if tt >= 1:
                        vector.wait_ge(sAC.h, ev[f"hqc_{b}_{tt-1}_{fc8}"])
                    ev[f"dq_{b}_{tt}_{fc8}"] = sDV.mark(vector.scalar_tensor_tensor(
                        out=s.gbuf[:, fc8 * 512:(fc8 + 1) * 512],
                        in0=[s.pA, s.pB, s.pO, s.pO2][fc8 % 4][:],
                        scalar=s.dq1c[:, b * NTT + tt:b * NTT + tt + 1],
                        in1=s.b1b[:, fc8 * 512:(fc8 + 1) * 512],
                        op0=ALU.mult, op1=ALU.add))
                self.on("vector", dq)

                def gl(scalar, b=b, tt=tt, fc8=fc8):
                    scalar.wait_ge(sDV.h, ev[f"dq_{b}_{tt}_{fc8}"])
                    ev[f"gl_{b}_{tt}_{fc8}"] = sAC.mark(scalar.activation(
                        out=s.gbuf[:, fc8 * 512:(fc8 + 1) * 512],
                        in_=s.gbuf[:, fc8 * 512:(fc8 + 1) * 512], func=AF.Gelu))
                self.on("scalar", gl)

                def glsq(gpsimd, b=b, tt=tt, fc8=fc8):
                    gpsimd.wait_ge(sAC.h, ev[f"gl_{b}_{tt}_{fc8}"])
                    if fc8 > 0:
                        gpsimd.wait_ge(sDV.h, ev[f"gsq_{b}_{tt}_{fc8-1}"])
                    ev[f"glsq_{b}_{tt}_{fc8}"] = sPL.mark(gpsimd.tensor_tensor(
                        out=s.gsqt[:], in0=s.gbuf[:, fc8 * 512:(fc8 + 1) * 512],
                        in1=s.gbuf[:, fc8 * 512:(fc8 + 1) * 512], op=ALU.mult))
                self.on("gpsimd", glsq)

                def gsq(vector, b=b, tt=tt, fc8=fc8):
                    vector.wait_ge(sPL.h, ev[f"glsq_{b}_{tt}_{fc8}"])
                    vector.tensor_reduce(out=s.cb[:, fc8:fc8 + 1], in_=s.gsqt[:],
                                         axis=AX.X, op=ALU.add)
                    ev[f"gsq_{b}_{tt}_{fc8}"] = sDV.mark(vector.tensor_reduce(
                        out=s.cb[:, 24 + fc8:25 + fc8],
                        in_=s.gbuf[:, fc8 * 512:(fc8 + 1) * 512], axis=AX.X,
                        op=ALU.max, apply_absolute_value=True))
                self.on("vector", gsq)

            def quantA(vector, b=b, tt=tt):
                vector.wait_ge(sDV.h, ev[f"gsq_{b}_{tt}_7"])
                v = sDV.mark(vector.tensor_reduce(out=s.cb[:, 17:18], in_=s.cb[:, 24:32],
                                                  axis=AX.X, op=ALU.max))
                ev[f"amax_{b}_{tt}"] = v
                v2 = sDV.mark(vector.tensor_reduce(out=s.cb[:, 8:9], in_=s.cb[:, 0:8],
                                                   axis=AX.X, op=ALU.add))
                vector.wait_ge(sDV.h, v2)
                v2 = sDV.mark(vector.tensor_scalar(out=s.cb[:, 9:10], in0=s.cb[:, 8:9],
                                                   scalar1=1.0 / F, scalar2=EPS,
                                                   op0=ALU.mult, op1=ALU.add))
                ev[f"rden2_{b}_{tt}"] = v2
            self.on("vector", quantA)

            def sq2(scalar, b=b, tt=tt):
                scalar.wait_ge(sDV.h, ev[f"rden2_{b}_{tt}"])
                ev[f"sq2_{b}_{tt}"] = sAC.mark(scalar.activation(
                    out=s.cb[:, 10:11], in_=s.cb[:, 9:10], func=AF.Sqrt))
            self.on("scalar", sq2)

            def quantB(vector, b=b, tt=tt):
                vector.wait_ge(sDV.h, ev[f"amax_{b}_{tt}"])
                v = sDV.mark(vector.tensor_scalar(out=s.cb[:, 18:19], in0=s.cb[:, 17:18],
                                                  scalar1=1e-5, scalar2=None, op0=ALU.max))
                vector.wait_ge(sDV.h, v)
                v = sDV.mark(vector.reciprocal(out=s.cb[:, 19:20], in_=s.cb[:, 18:19]))
                vector.wait_ge(sDV.h, v)
                v = sDV.mark(vector.tensor_scalar(out=s.cb[:, 20:21], in0=s.cb[:, 19:20],
                                                  scalar1=127.0, scalar2=None, op0=ALU.mult))
                vector.wait_ge(sDV.h, v)
                v = sDV.mark(vector.tensor_scalar(out=s.cb[:, 21:22], in0=s.cb[:, 20:21],
                                                  scalar1=s.s2col[:, 0:1], scalar2=None,
                                                  op0=ALU.mult))
                vector.wait_ge(sDV.h, v)
                v = sDV.mark(vector.reciprocal(out=s.cb[:, 22:23], in_=s.cb[:, 21:22]))
                vector.wait_ge(sDV.h, v)
                vector.wait_ge(sAC.h, ev[f"sq2_{b}_{tt}"])
                v = sDV.mark(vector.reciprocal(out=s.cb[:, 11:12], in_=s.cb[:, 10:11]))
                vector.wait_ge(sDV.h, v)
                v = sDV.mark(vector.tensor_scalar(out=s.cb[:, 22:23], in0=s.cb[:, 22:23],
                                                  scalar1=s.cb[:, 11:12], scalar2=None,
                                                  op0=ALU.mult))
                vector.wait_ge(sDV.h, v)
                p = tt % 2
                v = sDV.mark(vector.tensor_copy(out=s.cb[:, 32 + p:33 + p],
                                                in_=s.cb[:, 20:21]))
                vector.wait_ge(sDV.h, v)
                v = sDV.mark(vector.tensor_copy(out=s.cb[:, 34 + p:35 + p],
                                                in_=s.cb[:, 22:23]))
                vector.wait_ge(sDV.h, v)
                ev[f"scq_{b}_{tt}"] = v
            self.on("vector", quantB)

        emit_tail(NTT - 2, "back")
        emit_tail(NTT - 1, "front")
        emit_tail(NTT - 1, "back")

    def finale(self):
        s = self

        def fin(gpsimd):
            for c in (*s.SP, *s.SQ):
                if c.n > 0:
                    gpsimd.wait_ge(c.h, c.n)
        self.on("gpsimd", fin)

        def fin2(sync):
            for c in (*s.SP, *s.SQ):
                if c.n > 0:
                    sync.wait_ge(c.h, c.n)
        self.on("sync", fin2)


def build_core_program(taps=(), stop_after=None):
    kb = KB(taps, stop_after=stop_after)
    nc = kb.build()
    return nc, kb


def kernel(L0_attn_x, L0_attn_w_in, L0_attn_w_gate, L0_attn_b_gate, L0_attn_amplitudes,
           L0_attn_sharpness, L0_attn_offsets, L0_ffn_w1, L0_ffn_b1, L0_ffn_w2, L0_ffn_b2):
    x = np.ascontiguousarray(L0_attn_x)
    nc, _ = build_core_program()
    in_maps = []
    for c in range(8):
        in_maps.append({
            "x": np.ascontiguousarray(x[2 * c:2 * c + 2]),
            "w_in": np.ascontiguousarray(L0_attn_w_in),
            "w_gate": np.ascontiguousarray(L0_attn_w_gate),
            "b_gate": np.ascontiguousarray(L0_attn_b_gate),
            "amp": np.ascontiguousarray(L0_attn_amplitudes),
            "sharp": np.ascontiguousarray(L0_attn_sharpness),
            "off": np.ascontiguousarray(L0_attn_offsets),
            "w1": np.ascontiguousarray(L0_ffn_w1),
            "b1": np.ascontiguousarray(L0_ffn_b1),
            "w2": np.ascontiguousarray(L0_ffn_w2),
            "b2": np.ascontiguousarray(L0_ffn_b2),
        })
    res = run_bass_kernel_spmd(nc, in_maps, core_ids=list(range(8)))
    return np.concatenate([r["out"] for r in res.results], axis=0)


# version 29
# speedup vs baseline: 1.0375x; 1.0232x over previous
import numpy as np
import ml_dtypes
import concourse.bass as bass
import concourse.mybir as mybir
from concourse.bass_utils import run_bass_kernel_spmd
from contextlib import ExitStack

F32 = mybir.dt.float32
BF16 = mybir.dt.bfloat16
AF = mybir.ActivationFunctionType
ALU = mybir.AluOpType
AX = mybir.AxisListType

B, S, D, H, NK, F = 2, 2048, 1024, 16, 21, 4096
TW = 4095
EXPW = 4224
MAGIC = 1.5 * 2.0 ** 23
EPS = 1e-5
NTT = S // 128
ENGS = ("sync", "tensor", "vector", "scalar", "gpsimd")


class Ctr:
    def __init__(self, nc, stack, name):
        self.h = stack.enter_context(nc.semaphore(name))
        self.n = 0

    def mark(self, instr, k=1):
        instr.then_inc(self.h, k)
        self.n += k
        return self.n


class _DummyInstr:
    def then_inc(self, sem, k=1, **kw):
        return self


class _DummyEng:
    def __getattr__(self, name):
        def f(*a, **k):
            return _DummyInstr()
        return f


class KB:
    def __init__(self, taps=(), stop_after=None):
        self.pending = None
        self.stop_after = stop_after
        self.taps = set(taps)
        self.tap_outs = {}
        nc = self.nc = bass.Bass("TRN2", target_bir_lowering=False)
        self.x_in = nc.dram_tensor("x", [B, S, D], F32, kind="ExternalInput").ap()
        self.w_in = nc.dram_tensor("w_in", [D, 3 * D], F32, kind="ExternalInput").ap()
        self.w_gate_in = nc.dram_tensor("w_gate", [D, 2 * D], F32, kind="ExternalInput").ap()
        self.b_gate_in = nc.dram_tensor("b_gate", [2 * D], F32, kind="ExternalInput").ap()
        self.amp_in = nc.dram_tensor("amp", [H, NK], F32, kind="ExternalInput").ap()
        self.sharp_in = nc.dram_tensor("sharp", [H, NK], F32, kind="ExternalInput").ap()
        self.off_in = nc.dram_tensor("off", [H, NK], F32, kind="ExternalInput").ap()
        self.w1_in = nc.dram_tensor("w1", [F, D], F32, kind="ExternalInput").ap()
        self.b1_in = nc.dram_tensor("b1", [F], F32, kind="ExternalInput").ap()
        self.w2_in = nc.dram_tensor("w2", [D, F], F32, kind="ExternalInput").ap()
        self.b2_in = nc.dram_tensor("b2", [D], F32, kind="ExternalInput").ap()
        self.out = nc.dram_tensor("out", [B, S, D], F32, kind="ExternalOutput").ap()

        self.d_expb = nc.dram_tensor("d_expb", [H, 128 * EXPW], BF16, kind="Internal").ap()
        self.d_winb = nc.dram_tensor("d_winb", [D, 3 * D], BF16, kind="Internal").ap()
        self.d_wgb = nc.dram_tensor("d_wgb", [D, 2 * D], BF16, kind="Internal").ap()
        self.d_wq1 = nc.dram_tensor("d_wq1", [D, F], BF16, kind="Internal").ap()
        self.d_wq2 = nc.dram_tensor("d_wq2", [F, D], BF16, kind="Internal").ap()
        self.d_xq = nc.dram_tensor("d_xq", [B, 8, 128, S], BF16, kind="Internal").ap()
        self.d_abr = nc.dram_tensor("d_abr", [B, S, D], F32, kind="Internal").ap()

        ident = np.eye(128)
        self.ident_bf_d = nc.inline_tensor(ident.astype(ml_dtypes.bfloat16), "identd_bf")
        sel = np.zeros((63, 65), dtype=np.float32)
        for p in range(63):
            sel[p, 32 * (p // 21)] = 1.0
        self.selG_d = nc.inline_tensor(sel, "selG_d")

    def tap(self, name, shape, dtype=F32):
        if name in self.taps and name not in self.tap_outs:
            self.tap_outs[name] = self.nc.dram_tensor(
                "tap_" + name, shape, dtype, kind="ExternalOutput").ap()
        return self.tap_outs.get(name)

    def on(self, engine_name, fn):
        if self.pending is not None:
            self.pending.append((engine_name, fn))
            fn(_DummyEng())
        else:
            getattr(self.block, engine_name)(fn)

    def grouped(self, phase_fn, *args):
        ctrs = (*self.SQ, *self.SP, self.sPE, self.sDV, self.sAC, self.sPL)
        snap = [(c, c.n) for c in ctrs]
        self.pending = []
        phase_fn(*args)
        pend = self.pending
        self.pending = None
        for c, v in snap:
            c.n = v
        for en in ENGS:
            fns = [f for e, f in pend if e == en]
            if fns:
                def sect(eng, fns=fns):
                    for f in fns:
                        f(eng)
                getattr(self.block, en)(sect)

    def dma(self, ctr, instr, key=None):
        instr.then_inc(ctr.h, 16)
        ctr.n += 16
        if key is not None:
            self.ev[key] = (ctr, ctr.n)
        return (ctr, ctr.n)

    def wt(self, eng, key):
        c, v = self.ev[key]
        eng.wait_ge(c.h, v)

    def bar(self):
        vals = [(c, c.n) for c in (*self.SQ, *self.SP, self.sPE, self.sDV, self.sAC, self.sPL)]
        for en in ENGS:
            def f(eng, vals=vals):
                for c, v in vals:
                    if v > 0:
                        eng.wait_ge(c.h, v)
            self.on(en, f)

    def build(self):
        nc = self.nc
        st = ExitStack()
        with st:
            s = self

            def sbufP(name, shape, dt, side=None):
                return st.enter_context(nc.sbuf_tensor(name, shape, dt, side=side))

            s.ident_bf = sbufP("ident_bf", [128, 128], BF16)
            s.ones1_bf = sbufP("ones1_bf", [1, 128], BF16)
            s.ones1_f = sbufP("ones1_f", [1, 128], F32)
            s.ones21 = sbufP("ones21", [NK, 1], F32)
            s.ones128 = sbufP("ones128", [128, 1], F32)
            s.onesp64 = sbufP("onesp64", [128, 64], BF16)
            s.bgate_bf = sbufP("bgate_bf", [1, 2 * D], BF16)
            s.b1b = sbufP("b1b", [128, F], BF16)
            s.b2b = sbufP("b2b", [128, D], BF16)
            s.s1col = sbufP("s1col", [128, 1], F32)
            s.s2col = sbufP("s2col", [128, 1], F32)
            s.dq1c = sbufP("dq1c", [128, B * NTT], F32)
            s.cb = sbufP("cb", [128, 40], F32)
            s.epscol = sbufP("epscol", [128, 1], F32)
            s.negmag = sbufP("negmag", [128, 1], F32)
            s.scal = sbufP("scal", [1, 8], F32)
            s.stg = sbufP("stg", [128, 2 * 128], BF16)
            s.p1cb = sbufP("p1cb_buf", [128, 20 * NTT], F32) \
                if any(t.startswith("p1cb") for t in self.taps) else None
            s.ftmp = sbufP("ftmp", [128, 2 * 512], F32)

            s.pA = st.enter_context(nc.psum_tensor("pA", [128, 512], F32))
            s.pB = st.enter_context(nc.psum_tensor("pB", [128, 512], F32))
            s.pC = st.enter_context(nc.psum_tensor("pC", [128, 512], F32))
            s.pD = st.enter_context(nc.psum_tensor("pD", [128, 512], F32))
            s.pO = st.enter_context(nc.psum_tensor("pO", [128, 512], F32))
            s.pO2 = st.enter_context(nc.psum_tensor("pO2", [128, 512], F32))
            s.pT1 = st.enter_context(nc.psum_tensor("pT1", [128, 128], BF16))
            s.pT2 = st.enter_context(nc.psum_tensor("pT2", [128, 128], BF16))

            s.SQ = [Ctr(nc, st, f"SQ{i}") for i in range(8)]
            s.SP = [Ctr(nc, st, f"SP{i}") for i in range(6)]
            s.sPE = Ctr(nc, st, "sPE")
            s.sDV = Ctr(nc, st, "sDV")
            s.sAC = Ctr(nc, st, "sAC")
            s.sPL = Ctr(nc, st, "sPL")

            s.block = st.enter_context(nc.Block())
            self.ev = {}

            stop = self.stop_after
            with ExitStack() as ph0:
                s.wtile = ph0.enter_context(nc.sbuf_tensor("wtile", [128, 3 * F], F32))
                s.wqtile = ph0.enter_context(nc.sbuf_tensor("wqtile", [128, 3 * F], BF16))
                s.stg8 = ph0.enter_context(nc.sbuf_tensor("stg8", [128, 2 * F], BF16))
                s.iotaf = ph0.enter_context(nc.sbuf_tensor("iotaf", [128, TW], F32))
                s.relrev = ph0.enter_context(nc.sbuf_tensor("relrev", [128, TW], F32))
                s.gwork = ph0.enter_context(nc.sbuf_tensor("gwork", [128, TW], F32))
                s.hkG = ph0.enter_context(nc.sbuf_tensor("hkG", [128, 18], F32))
                s.selG = ph0.enter_context(nc.sbuf_tensor("selG", [63, 65], F32))
                s.onesbc = ph0.enter_context(nc.sbuf_tensor("onesbc", [128, 128], BF16))
                s.rowexp = ph0.enter_context(nc.sbuf_tensor("rowexp", [65, TW], BF16))
                s.bgrow = ph0.enter_context(nc.sbuf_tensor("bgrow", [1, 2 * D], F32))
                s.b1row = ph0.enter_context(nc.sbuf_tensor("b1row", [1, F], F32))
                s.b2row = ph0.enter_context(nc.sbuf_tensor("b2row", [1, D], F32))
                s.bch = ph0.enter_context(nc.sbuf_tensor("bch", [128, TW], BF16))
                self.grouped(self.phase0)
            self.bar()

            for b in (range(B) if stop != "ph0" else []):
                with ExitStack() as gL1:
                    s.xn_fm = gL1.enter_context(
                        nc.sbuf_tensor(f"xn_fm_{b}", [128, 8 * S], BF16, side="left"))
                    s.a_fm = s.xn_fm
                    with ExitStack() as gR1:
                        s.k_fm = gR1.enter_context(
                            nc.sbuf_tensor(f"k_fm_{b}", [128, 8 * S], BF16, side="right"))
                        s.q_fm = gR1.enter_context(
                            nc.sbuf_tensor(f"q_fm_{b}", [128, 8 * S], BF16, side="right"))
                        s.v_tm = gR1.enter_context(
                            nc.sbuf_tensor(f"v_tm_{b}", [128, NTT * 1040], BF16, side="right"))
                        with ExitStack() as p1s:
                            s.xt = p1s.enter_context(nc.sbuf_tensor(f"xt_{b}", [128, 2 * D], F32))
                            s.xnt = p1s.enter_context(nc.sbuf_tensor(f"xnt_{b}", [128, D], F32))
                            s.xnbt = p1s.enter_context(nc.sbuf_tensor(f"xnbt_{b}", [128, D], BF16))
                            s.xqbt = p1s.enter_context(nc.sbuf_tensor(f"xqbt_{b}", [128, D], BF16))
                            s.xqs = p1s.enter_context(nc.sbuf_tensor(f"xqs_{b}", [128, 8 * 128], BF16))
                            s.dump = p1s.enter_context(nc.sbuf_tensor(f"dump_{b}", [128, D], F32))
                            s.rtmp = p1s.enter_context(nc.sbuf_tensor(f"rtmp_{b}", [128, D], F32))
                            self.grouped(self.p1_ln_quant, b)
                        self.bar()
                        if stop == "p1":
                            self.finale()
                            return nc
                        with ExitStack() as p2s:
                            s.wcolb = p2s.enter_context(
                                nc.sbuf_tensor(f"wcolb_{b}", [128, 2 * 1024], BF16))
                            s.wvb = p2s.enter_context(
                                nc.sbuf_tensor(f"wvb_{b}", [128, 2 * 8 * 512], BF16))
                            self.grouped(self.p2_proj, b)
                        self.bar()
                        if stop == "p2":
                            self.finale()
                            return nc
                        with ExitStack() as p3s:
                            s.skw = p3s.enter_context(
                                nc.sbuf_tensor(f"skw_{b}", [128, 3 * 3968], BF16, side="left"))
                            s.eraw = p3s.enter_context(
                                nc.sbuf_tensor(f"eraw_{b}", [128, 3 * 512], BF16, side="left"))
                            s.emul = p3s.enter_context(
                                nc.sbuf_tensor(f"emul_{b}", [128, 20 * 512], BF16, side="left"))
                            s.rdbuf = p3s.enter_context(
                                nc.sbuf_tensor(f"rdbuf_{b}", [128, 512], BF16, side="left"))
                            s.rbs = p3s.enter_context(
                                nc.sbuf_tensor(f"rbs_{b}", [64, 2 * 512], BF16, side="left"))
                            self.grouped(self.p3_attn, b)
                        self.bar()
                        if stop == "p3":
                            self.finale()
                            return nc
                        gR1.close()
                        with ExitStack() as p4s:
                            s.wgs = p4s.enter_context(
                                nc.sbuf_tensor(f"wgs_{b}", [128, 8 * 2048], BF16, side="right"))
                            s.sgb = p4s.enter_context(
                                nc.sbuf_tensor(f"sgb_{b}", [128, 512], F32, side="right"))
                            s.abb = p4s.enter_context(
                                nc.sbuf_tensor(f"abb_{b}", [128, 2 * 1024], F32, side="right"))
                            self.grouped(self.p4_gate, b)
                        self.bar()
                        if stop == "p4":
                            self.finale()
                            return nc
                with ExitStack() as p56s:
                    s.wq1s = p56s.enter_context(nc.sbuf_tensor(f"wq1s_{b}", [128, 8 * F], BF16))
                    s.wq2s = p56s.enter_context(nc.sbuf_tensor(f"wq2s_{b}", [128, 32 * D], BF16))
                    s.gbuf = p56s.enter_context(nc.sbuf_tensor(f"gbuf_{b}", [128, F], F32))
                    s.hqt = p56s.enter_context(nc.sbuf_tensor(f"hqt_{b}", [128, 2 * F], BF16))
                    s.gsqt = p56s.enter_context(nc.sbuf_tensor(f"gsqt_{b}", [128, 512], F32))
                    s.hqf = s.hqt
                    s.xqt = p56s.enter_context(nc.sbuf_tensor(f"xqt_{b}", [128, 2 * 1024], BF16))
                    s.abrt = p56s.enter_context(nc.sbuf_tensor(f"abrt_{b}", [128, 2 * D], F32))
                    s.xat = p56s.enter_context(nc.sbuf_tensor(f"xat_{b}", [128, 2 * D], F32))
                    self.grouped(self.p56_ffn, b)
                self.bar()
                if stop == "p56":
                    self.finale()
                    return nc
            self.finale()
        return nc

    def phase0(self):
        s, ev = self, self.ev
        SQ, SP = s.SQ, s.SP
        sPE, sDV, sAC, sPL = s.sPE, s.sDV, s.sAC, s.sPL

        def l3(n, i):
            return ((n - 1 - i) // 3) * 3 + i

        def loads(sync):
            for dst, src in [(s.ident_bf[:], s.ident_bf_d.ap()),
                             (s.selG[:], s.selG_d.ap()),
                             (s.b1row[:], s.b1_in[None, :]),
                             (s.b2row[:], s.b2_in[None, :]),
                             (s.bgrow[:], s.b_gate_in[None, :])]:
                self.dma(SQ[0], sync.dma_start(out=dst, in_=src))
            for g in range(6):
                nr = 63 if g < 5 else 21
                for k, t in enumerate((s.amp_in, s.sharp_in, s.off_in)):
                    flat = t.rearrange("h k -> (h k)")
                    self.dma(SQ[0], sync.dma_start(
                        out=s.hkG[0:nr, 3 * g + k:3 * g + k + 1],
                        in_=flat[g * 63:g * 63 + nr][:, None],
                        allow_slow_non_contiguous=True))
            ev["p0_loads"] = (SQ[0], SQ[0].n)
            ev["hkg"] = (SQ[0], SQ[0].n)
        self.on("sync", loads)

        def iota0(gpsimd):
            ev["iota"] = sPL.mark(gpsimd.iota(
                out=s.iotaf[:], pattern=[[1, TW]], base=0, channel_multiplier=0,
                allow_small_or_imprecise_dtypes=True))
        self.on("gpsimd", iota0)

        def consts(vector):
            self.wt(vector, "p0_loads")
            vector.memset(s.ones1_bf[:], 1.0)
            vector.memset(s.ones1_f[:], 1.0)
            vector.memset(s.ones128[:], 1.0)
            vector.memset(s.onesp64[64:65, :], 1.0)
            vector.memset(s.epscol[:], EPS)
            vector.memset(s.negmag[:], -MAGIC)
            vector.memset(s.onesbc[0:1, :], 1.0)
            vector.memset(s.onesbc[32:33, :], 1.0)
            vector.memset(s.onesbc[64:65, :], 1.0)
            vector.tensor_copy(out=s.bgate_bf[:], in_=s.bgrow[:])
            vector.wait_ge(sPL.h, ev["iota"])
            ev["relrev"] = sDV.mark(vector.tensor_scalar(
                out=s.relrev[:], in0=s.iotaf[:], scalar1=-1.0, scalar2=2047.0,
                op0=ALU.mult, op1=ALU.add))
        self.on("vector", consts)

        act0 = sAC.n
        dvb0 = sDV.n

        def rexp_ev(g, c, act0=act0):
            return act0 + 9 * g + 2 + c

        def bce_ev(h, c, dvb0=dvb0):
            return dvb0 + 6 + 12 + 8 * h + c + 1

        def hkg_neg(vector):
            self.wt(vector, "hkg")
            for g in range(6):
                c = 3 * g + 1
                ev[f"hkgn_{g}"] = sDV.mark(vector.scalar_tensor_tensor(
                    out=s.hkG[0:63, c:c + 1], in0=s.hkG[0:63, c:c + 1], scalar=-1.0,
                    in1=s.hkG[0:63, c:c + 1], op0=ALU.mult, op1=ALU.min))
        self.on("vector", hkg_neg)

        for g in range(6):
            nh = 3 if g < 5 else 1
            nr = nh * 21

            def u_step(vector, g=g, nr=nr):
                vector.wait_ge(sDV.h, ev[f"hkgn_{g}"])
                if g == 0:
                    vector.wait_ge(sDV.h, ev["relrev"])
                else:
                    vector.wait_ge(sPE.h, ev[f"tsum_{g-1}"])
                ev[f"u_{g}"] = sDV.mark(vector.tensor_scalar(
                    out=s.gwork[0:nr, :], in0=s.relrev[0:nr, :],
                    scalar1=s.hkG[0:nr, 3 * g + 2:3 * g + 3], scalar2=None,
                    op0=ALU.subtract))
            self.on("vector", u_step)

            def exp_step(scalar, g=g, nr=nr):
                scalar.wait_ge(sDV.h, ev[f"u_{g}"])
                scalar.activation(out=s.gwork[0:nr, :], in_=s.gwork[0:nr, :], func=AF.Square)
                ev[f"gexp_{g}"] = sAC.mark(scalar.activation(
                    out=s.gwork[0:nr, :], in_=s.gwork[0:nr, :], func=AF.Exp,
                    scale=s.hkG[0:nr, 3 * g + 1:3 * g + 2]))
            self.on("scalar", exp_step)

            def amp_step(vector, g=g, nr=nr):
                vector.wait_ge(sAC.h, ev[f"gexp_{g}"])
                ev[f"ge_{g}"] = sDV.mark(vector.tensor_scalar(
                    out=s.gwork[0:nr, :], in0=s.gwork[0:nr, :],
                    scalar1=s.hkG[0:nr, 3 * g:3 * g + 1], scalar2=None, op0=ALU.mult))
            self.on("vector", amp_step)

            def sum_step(tensor, g=g, nr=nr, nh=nh):
                tensor.wait_ge(sDV.h, ev[f"ge_{g}"])
                for c in range(8):
                    n = min(512, TW - c * 512)
                    if g > 0 or c > 0:
                        pv = rexp_ev(g, c - 1) if c > 0 else rexp_ev(g - 1, 7)
                        tensor.wait_ge(sAC.h, pv)
                    ev[f"tsum_{g}_{c}"] = sPE.mark(tensor.matmul(
                        out=s.pC[0:(32 * (nh - 1) + 1), 0:n], lhsT=s.selG[0:nr, 0:(32 * (nh - 1) + 1)],
                        rhs=s.gwork[0:nr, c * 512:c * 512 + n], start=True, stop=True))
                ev[f"tsum_{g}"] = sPE.n
            self.on("tensor", sum_step)

            def rexp_step(scalar, g=g, nh=nh):
                for c in range(8):
                    n = min(512, TW - c * 512)
                    scalar.wait_ge(sPE.h, ev[f"tsum_{g}_{c}"])
                    for m in range(nh):
                        ins = scalar.activation(
                            out=s.rowexp[32 * m:32 * m + 1, c * 512:c * 512 + n],
                            in_=s.pC[32 * m:32 * m + 1, 0:n], func=AF.Exp)
                    v = sAC.mark(ins)
                    ev[f"rexp_{g}_{c}"] = v
                    assert v == rexp_ev(g, c), (v, g, c)
            self.on("scalar", rexp_step)

        for h in range(H):
            g, m = divmod(h, 3)

            def bc_step(tensor, h=h, g=g, m=m):
                for c in range(8):
                    n = min(512, TW - c * 512)
                    tensor.wait_ge(sAC.h, rexp_ev(g, c))
                    pb = s.pA if c % 2 == 0 else s.pB
                    if h > 0 or c >= 2:
                        pv = bce_ev(h, c - 2) if c >= 2 else bce_ev(h - 1, 6 + c)
                        tensor.wait_ge(sDV.h, pv)
                    ev[f"bcm_{h}_{c}"] = sPE.mark(tensor.matmul(
                        out=pb[:, 0:n], lhsT=s.onesbc[32 * m:32 * m + 1, :],
                        rhs=s.rowexp[32 * m:32 * m + 1, c * 512:c * 512 + n],
                        start=True, stop=True, tile_position=(32 * m, 0)))
            self.on("tensor", bc_step)

            def bce_step(vector, h=h):
                for c in range(8):
                    n = min(512, TW - c * 512)
                    pb = s.pA if c % 2 == 0 else s.pB
                    vector.wait_ge(sPE.h, ev[f"bcm_{h}_{c}"])
                    if h > 0 and c == 0:
                        self.wt(vector, f"expw_{h-1}")
                    v = sDV.mark(vector.tensor_copy(
                        out=s.bch[:, c * 512:c * 512 + n], in_=pb[:, 0:n]))
                    ev[f"bce_{h}_{c}"] = v
                    assert v == bce_ev(h, c), (v, bce_ev(h, c), h, c)
                ev[f"bce_{h}"] = sDV.n
            self.on("vector", bce_step)

            def expw_step(sync, h=h):
                sync.wait_ge(sDV.h, ev[f"bce_{h}"])
                dst = bass.AP(s.d_expb.tensor, h * 128 * EXPW, [[EXPW, 128], [1, TW]])
                self.dma(SP[h % 2], sync.dma_start(out=dst, in_=s.bch[:, 0:TW]), f"expw_{h}")
            self.on("sync", expw_step)

        for wname, src_w, dst_w, ncols in (("win", s.w_in, s.d_winb, 3 * D),
                                           ("wg", s.w_gate_in, s.d_wgb, 2 * D)):
            for i in range(8):
                def wld(sync, wname=wname, i=i, src_w=src_w, ncols=ncols):
                    if i >= 3:
                        sync.wait_ge(sDV.h, ev[f"cast_{wname}_{i-3}"])
                    elif wname == "wg":
                        sync.wait_ge(sDV.h, ev[f"cast_win_{l3(8, i)}"])
                    h2 = ncols // 2
                    self.dma(SQ[1 + i % 3], sync.dma_start(
                        out=s.wtile[:, (i % 3) * F:(i % 3) * F + h2],
                        in_=src_w[i * 128:(i + 1) * 128, 0:h2]), f"wld_{wname}_{i}")
                    self.dma(SQ[4 + i % 3], sync.dma_start(
                        out=s.wtile[:, (i % 3) * F + h2:(i % 3) * F + ncols],
                        in_=src_w[i * 128:(i + 1) * 128, h2:ncols]), f"wldR_{wname}_{i}")
                self.on("sync", wld)

                def wcast(vector, wname=wname, i=i, ncols=ncols):
                    self.wt(vector, f"wld_{wname}_{i}")
                    self.wt(vector, f"wldR_{wname}_{i}")
                    if i >= 3:
                        self.wt(vector, f"wst_{wname}_{i-3}")
                    if wname == "wg" and i < 3:
                        self.wt(vector, f"wst_win_{l3(8, i)}")
                    ev[f"cast_{wname}_{i}"] = sDV.mark(vector.tensor_copy(
                        out=s.wqtile[:, (i % 3) * F:(i % 3) * F + ncols],
                        in_=s.wtile[:, (i % 3) * F:(i % 3) * F + ncols]))
                self.on("vector", wcast)

                def wst(sync, wname=wname, i=i - 2, dst_w=dst_w, ncols=ncols):
                    if i < 0:
                        return
                    sync.wait_ge(sDV.h, ev[f"cast_{wname}_{i}"])
                    self.dma(SP[2 + i % 3], sync.dma_start(
                        out=dst_w[i * 128:(i + 1) * 128, :],
                        in_=s.wqtile[:, (i % 3) * F:(i % 3) * F + ncols]),
                        f"wst_{wname}_{i}")
                self.on("sync", wst)
                if i == 7:
                    for it in (6, 7):
                        def wstt(sync, wname=wname, i=it, dst_w=dst_w, ncols=ncols):
                            sync.wait_ge(sDV.h, ev[f"cast_{wname}_{i}"])
                            self.dma(SP[2 + i % 3], sync.dma_start(
                                out=dst_w[i * 128:(i + 1) * 128, :],
                                in_=s.wqtile[:, (i % 3) * F:(i % 3) * F + ncols]),
                                f"wst_{wname}_{i}")
                        self.on("sync", wstt)

        for wname, src_w, dst_w, ntile, ncols, scol in (
                ("w1", s.w1_in, s.d_wq1, 32, D, s.s1col),
                ("w2", s.w2_in, s.d_wq2, 8, F, s.s2col)):
            nelem = float(ntile * 128 * ncols)
            nblk = ncols // 128
            for i in range(ntile):
                def ald(sync, wname=wname, i=i, src_w=src_w, ncols=ncols):
                    if i >= 3:
                        sync.wait_ge(sDV.h, ev[f"asum_{wname}_{i-3}"])
                    if i < 3 and wname == "w1":
                        sync.wait_ge(sDV.h, ev[f"cast_wg_{l3(8, i)}"])
                    if i < 3 and wname == "w2":
                        sync.wait_ge(sDV.h, ev[f"q_w1_{l3(32, i)}"])
                    h2 = ncols // 2
                    self.dma(SQ[1 + i % 3], sync.dma_start(
                        out=s.wtile[:, (i % 3) * F:(i % 3) * F + h2],
                        in_=src_w[i * 128:(i + 1) * 128, 0:h2]), f"ald_{wname}_{i}")
                    self.dma(SQ[4 + i % 3], sync.dma_start(
                        out=s.wtile[:, (i % 3) * F + h2:(i % 3) * F + ncols],
                        in_=src_w[i * 128:(i + 1) * 128, h2:ncols]), f"aldR_{wname}_{i}")
                self.on("sync", ald)

                def ared(vector, wname=wname, i=i, ncols=ncols):
                    self.wt(vector, f"ald_{wname}_{i}")
                    self.wt(vector, f"aldR_{wname}_{i}")
                    v = sDV.mark(vector.tensor_reduce(
                        out=s.cb[:, 0:1], in_=s.wtile[:, (i % 3) * F:(i % 3) * F + ncols],
                        axis=AX.X, op=ALU.add, apply_absolute_value=True))
                    vector.wait_ge(sDV.h, v)
                    if i == 0:
                        ev[f"asum_{wname}_{i}"] = sDV.mark(
                            vector.tensor_copy(out=s.cb[:, 1:2], in_=s.cb[:, 0:1]))
                    else:
                        ev[f"asum_{wname}_{i}"] = sDV.mark(vector.tensor_tensor(
                            out=s.cb[:, 1:2], in0=s.cb[:, 1:2], in1=s.cb[:, 0:1], op=ALU.add))
                self.on("vector", ared)

            def tot(tensor, wname=wname, ntile=ntile):
                tensor.wait_ge(sDV.h, ev[f"asum_{wname}_{ntile-1}"])
                tensor.wait_ge(sAC.h, rexp_ev(5, 7))
                ev[f"tot_{wname}"] = sPE.mark(tensor.matmul(
                    out=s.pC[0:1, 0:1], lhsT=s.ones128[:], rhs=s.cb[:, 1:2],
                    start=True, stop=True))
            self.on("tensor", tot)

            def sv(vector, wname=wname, nelem=nelem):
                vector.wait_ge(sPE.h, ev[f"tot_{wname}"])
                v = sDV.mark(vector.tensor_scalar(
                    out=s.scal[0:1, 0:1], in0=s.pC[0:1, 0:1], scalar1=1.0 / nelem,
                    scalar2=1e-5, op0=ALU.mult, op1=ALU.max))
                vector.wait_ge(sDV.h, v)
                ev[f"sval_{wname}"] = sDV.mark(
                    vector.reciprocal(out=s.scal[0:1, 1:2], in_=s.scal[0:1, 0:1]))
            self.on("vector", sv)

            def sbc(tensor, wname=wname):
                tensor.wait_ge(sDV.h, ev[f"sval_{wname}"])
                ev[f"sbc_{wname}"] = sPE.mark(tensor.matmul(
                    out=s.pC[:, 0:1], lhsT=s.ones1_f[:], rhs=s.scal[0:1, 1:2],
                    start=True, stop=True))
            self.on("tensor", sbc)

            def sce(vector, wname=wname, scol=scol):
                vector.wait_ge(sPE.h, ev[f"sbc_{wname}"])
                ev[f"scol_{wname}"] = sDV.mark(vector.tensor_copy(out=scol[:], in_=s.pC[:, 0:1]))
            self.on("vector", sce)

            for i in range(ntile):
                def qld(sync, wname=wname, i=i, src_w=src_w, ncols=ncols, ntile=ntile):
                    if i >= 3:
                        sync.wait_ge(sDV.h, ev[f"q_{wname}_{i-3}"])
                    if i < 3:
                        sync.wait_ge(sDV.h, ev[f"asum_{wname}_{l3(ntile, i)}"])
                    h2 = ncols // 2
                    self.dma(SQ[1 + i % 3], sync.dma_start(
                        out=s.wtile[:, (i % 3) * F:(i % 3) * F + h2],
                        in_=src_w[i * 128:(i + 1) * 128, 0:h2]), f"qld_{wname}_{i}")
                    self.dma(SQ[4 + i % 3], sync.dma_start(
                        out=s.wtile[:, (i % 3) * F + h2:(i % 3) * F + ncols],
                        in_=src_w[i * 128:(i + 1) * 128, h2:ncols]), f"qldR_{wname}_{i}")
                self.on("sync", qld)
                def wqst(sync, wname=wname, i=i - 2, dst_w=dst_w, ncols=ncols, nblk=nblk):
                    if i < 0:
                        return
                    sync.wait_ge(sAC.h, ev[f"tev_{wname}_{i}_{nblk-1}"])
                    if wname == "w1" and i < 3:
                        self.wt(sync, f"wst_wg_{l3(8, i)}")
                    if wname == "w2" and i < 3:
                        self.wt(sync, f"wqst_w1_{l3(32, i)}")
                    pitch = F if wname == "w1" else D
                    dst = bass.AP(dst_w.tensor, i * 128,
                                  [[pitch, 128], [128 * pitch, nblk], [1, 128]])
                    self.dma(SP[2 + i % 3], sync.dma_start(
                        out=dst, in_=s.stg8[:, (i % 2) * ncols:(i % 2) * ncols + ncols].rearrange(
                            "p (j n) -> p j n", j=nblk)), f"wqst_{wname}_{i}")
                self.on("sync", wqst)

                def qnt1(vector, wname=wname, i=i, ncols=ncols, scol=scol):
                    self.wt(vector, f"qld_{wname}_{i}")
                    self.wt(vector, f"qldR_{wname}_{i}")
                    if i == 0:
                        vector.wait_ge(sDV.h, ev[f"scol_{wname}"])
                    vector.tensor_scalar(
                        out=s.wtile[:, (i % 3) * F:(i % 3) * F + ncols],
                        in0=s.wtile[:, (i % 3) * F:(i % 3) * F + ncols],
                        scalar1=scol[:, 0:1], scalar2=MAGIC,
                        op0=ALU.mult, op1=ALU.add)
                    ev[f"q1_{wname}_{i}"] = sDV.mark(vector.tensor_scalar(
                        out=s.wtile[:, (i % 3) * F:(i % 3) * F + ncols],
                        in0=s.wtile[:, (i % 3) * F:(i % 3) * F + ncols],
                        scalar1=MAGIC, scalar2=1.0,
                        op0=ALU.subtract, op1=ALU.min))
                self.on("vector", qnt1)

                def qnt2(vector, wname=wname, i=i, ncols=ncols, nblk=nblk):
                    if i >= 3:
                        vector.wait_ge(sPE.h, ev[f"trp_{wname}_{i-3}_{nblk-1}"])
                    ev[f"q_{wname}_{i}"] = sDV.mark(vector.tensor_scalar(
                        out=s.wqtile[:, (i % 3) * ncols:(i % 3) * ncols + ncols],
                        in0=s.wtile[:, (i % 3) * F:(i % 3) * F + ncols],
                        scalar1=-1.0, scalar2=None, op0=ALU.max))
                self.on("vector", qnt2)

                nblk = ncols // 128
                for rr in range((nblk + 1) // 2):
                    def trp(tensor, wname=wname, i=i, rr=rr, nblk=nblk, ncols=ncols):
                        for half in range(2):
                            j = rr * 2 + half
                            if j >= nblk:
                                continue
                            if rr == 0 and half == 0:
                                tensor.wait_ge(sDV.h, ev[f"q_{wname}_{i}"])
                            pb = s.pT1 if j % 2 == 0 else s.pT2
                            if i > 0 or j >= 2:
                                key = (f"tev_{wname}_{i}_{j-2}" if j >= 2
                                       else f"tev_{wname}_{i-1}_{nblk - 2 + j}")
                                tensor.wait_ge(sAC.h, ev[key])
                            ev[f"trp_{wname}_{i}_{j}"] = sPE.mark(tensor.transpose(
                                out=pb[:],
                                in_=s.wqtile[:, (i % 3) * ncols + j * 128:(i % 3) * ncols + (j + 1) * 128],
                                identity=s.ident_bf[:]))
                    self.on("tensor", trp)

                    def tev(scalar, wname=wname, i=i, rr=rr, nblk=nblk, ncols=ncols):
                        for half in range(2):
                            j = rr * 2 + half
                            if j >= nblk:
                                continue
                            pb = s.pT1 if j % 2 == 0 else s.pT2
                            scalar.wait_ge(sPE.h, ev[f"trp_{wname}_{i}_{j}"])
                            if j == 0 and i >= 2:
                                self.wt(scalar, f"wqst_{wname}_{i-2}")
                            ev[f"tev_{wname}_{i}_{j}"] = sAC.mark(scalar.activation(
                                out=s.stg8[:, (i % 2) * ncols + j * 128:(i % 2) * ncols + (j + 1) * 128],
                                in_=pb[:], func=AF.Copy))
                    self.on("scalar", tev)

                if i == ntile - 1:
                    for it in (ntile - 2, ntile - 1):
                        def wqstt(sync, wname=wname, i=it, dst_w=dst_w, ncols=ncols, nblk=nblk):
                            sync.wait_ge(sAC.h, ev[f"tev_{wname}_{i}_{nblk-1}"])
                            pitch = F if wname == "w1" else D
                            dst = bass.AP(dst_w.tensor, i * 128,
                                          [[pitch, 128], [128 * pitch, nblk], [1, 128]])
                            self.dma(SP[2 + i % 3], sync.dma_start(
                                out=dst, in_=s.stg8[:, (i % 2) * ncols:(i % 2) * ncols + ncols].rearrange(
                                    "p (j n) -> p j n", j=nblk)), f"wqst_{wname}_{i}")
                        self.on("sync", wqstt)



        dvb0 = sDV.n

        def bbe_ev(c, dvb0=dvb0):
            return dvb0 + c + 1

        def bbm(tensor):
            tensor.wait_ge(sDV.h, ev[f"bce_{H-1}"])
            for c in range(10):
                pb = s.pA if c % 2 == 0 else s.pB
                if c >= 2:
                    tensor.wait_ge(sDV.h, bbe_ev(c - 2))
                rhs = (s.b1row[0:1, c * 512:(c + 1) * 512] if c < 8
                       else s.b2row[0:1, (c - 8) * 512:(c - 7) * 512])
                ev[f"bbm_{c}"] = sPE.mark(tensor.matmul(
                    out=pb[:], lhsT=s.ones1_f[:], rhs=rhs, start=True, stop=True))
        self.on("tensor", bbm)

        def bbe(vector):
            for c in range(10):
                pb = s.pA if c % 2 == 0 else s.pB
                vector.wait_ge(sPE.h, ev[f"bbm_{c}"])
                dst = (s.b1b[:, c * 512:(c + 1) * 512] if c < 8
                       else s.b2b[:, (c - 8) * 512:(c - 7) * 512])
                v = sDV.mark(vector.tensor_copy(out=dst, in_=pb[:]))
                ev[f"bbe_{c}"] = v
                assert v == bbe_ev(c), (v, c)
        self.on("vector", bbe)

    def p1_ln_quant(self, b):
        s, ev = self, self.ev
        SQ, SP = s.SQ, s.SP
        sPE, sDV, sAC, sPL = s.sPE, s.sDV, s.sAC, s.sPL

        tpcb = self.tap(f"p1cb_b{b}", [128, 20 * NTT], F32)

        for tt in range(NTT):
            def xld(sync, b=b, tt=tt):
                if tt >= 2:
                    sync.wait_ge(sDV.h, ev[f"xnt_{b}_{tt-2}"])
                self.dma(SQ[1 + tt % 2], sync.dma_start(
                    out=s.xt[:, (tt % 2) * D:(tt % 2) * D + D],
                    in_=s.x_in[b, tt * 128:(tt + 1) * 128, :]), f"xld_{b}_{tt}")
            self.on("sync", xld)
            def xq_w(sync, b=b, tt=tt - 1):
                if tt < 0:
                    return
                sync.wait_ge(sAC.h, ev[f"xqe_{b}_{tt}"])
                dst = s.d_xq[b, :, :, tt * 128:(tt + 1) * 128]
                src = s.xqs[:].rearrange("p (dc n) -> p dc n", dc=8)
                self.dma(SP[tt % 2], sync.dma_start(out=dst.transpose([1, 0, 2]), in_=src),
                         f"xqw_{b}_{tt}")
            self.on("sync", xq_w)

            def stats_a(gpsimd, b=b, tt=tt):
                self.wt(gpsimd, f"xld_{b}_{tt}")
                if tt >= 1:
                    gpsimd.wait_ge(sDV.h, ev[f"rden_{b}_{tt-1}"])
                ev[f"ss_{b}_{tt}"] = sPL.mark(gpsimd.tensor_tensor(
                    out=s.dump[:], in0=s.xt[:, (tt % 2) * D:(tt % 2) * D + D],
                    in1=s.xt[:, (tt % 2) * D:(tt % 2) * D + D], op=ALU.mult))
            self.on("gpsimd", stats_a)

            def stats_v(vector, b=b, tt=tt):
                self.wt(vector, f"xld_{b}_{tt}")
                v = sDV.mark(vector.tensor_reduce(out=s.cb[:, 0:1], in_=s.xt[:, (tt % 2) * D:(tt % 2) * D + D],
                                                  axis=AX.X, op=ALU.add))
                vector.wait_ge(sPL.h, ev[f"ss_{b}_{tt}"])
                v = sDV.mark(vector.tensor_reduce(out=s.cb[:, 1:2], in_=s.dump[:],
                                                  axis=AX.X, op=ALU.add))
                vector.wait_ge(sDV.h, v)
                v = sDV.mark(vector.tensor_scalar(out=s.cb[:, 2:3], in0=s.cb[:, 0:1],
                                                  scalar1=1.0 / D, scalar2=None, op0=ALU.mult))
                vector.wait_ge(sDV.h, v)
                v = sDV.mark(vector.tensor_scalar(out=s.cb[:, 3:4], in0=s.cb[:, 2:3],
                                                  scalar1=s.cb[:, 2:3], scalar2=None,
                                                  op0=ALU.mult))
                vector.wait_ge(sDV.h, v)
                ev[f"var_{b}_{tt}"] = sDV.mark(vector.scalar_tensor_tensor(
                    out=s.cb[:, 4:5], in0=s.cb[:, 1:2], scalar=1.0 / D, in1=s.cb[:, 3:4],
                    op0=ALU.mult, op1=ALU.subtract))
            self.on("vector", stats_v)

            def rstd_a(scalar, b=b, tt=tt):
                scalar.wait_ge(sDV.h, ev[f"var_{b}_{tt}"])
                ev[f"sqv_{b}_{tt}"] = sAC.mark(scalar.activation(
                    out=s.cb[:, 5:6], in_=s.cb[:, 4:5], func=AF.Sqrt, bias=s.epscol[:, 0:1]))
            self.on("scalar", rstd_a)

            def ln_ap(vector, b=b, tt=tt):
                vector.wait_ge(sAC.h, ev[f"sqv_{b}_{tt}"])
                v = sDV.mark(vector.reciprocal(out=s.cb[:, 6:7], in_=s.cb[:, 5:6]))
                vector.wait_ge(sDV.h, v)
                if tt >= 1:
                    vector.wait_ge(sAC.h, ev[f"xnc_{b}_{tt-1}"])
                    vector.wait_ge(sPL.h, ev[f"xnq_{b}_{tt-1}"])
                ev[f"xnt_{b}_{tt}"] = sDV.mark(vector.tensor_scalar(
                    out=s.xnt[:], in0=s.xt[:, (tt % 2) * D:(tt % 2) * D + D], scalar1=s.cb[:, 2:3],
                    scalar2=s.cb[:, 6:7], op0=ALU.subtract, op1=ALU.mult))
            self.on("vector", ln_ap)

            def xn_a(scalar, b=b, tt=tt):
                scalar.wait_ge(sDV.h, ev[f"xnt_{b}_{tt}"])
                if tt >= 1:
                    scalar.wait_ge(sPE.h, ev[f"xntr_{b}_{tt-1}"])
                ev[f"xnc_{b}_{tt}"] = sAC.mark(
                    scalar.activation(out=s.xnbt[:], in_=s.xnt[:], func=AF.Copy))
            self.on("scalar", xn_a)

            def xn_sq(gpsimd, b=b, tt=tt):
                gpsimd.wait_ge(sDV.h, ev[f"xnt_{b}_{tt}"])
                ev[f"xnq_{b}_{tt}"] = sPL.mark(gpsimd.tensor_tensor(
                    out=s.dump[:], in0=s.xnt[:], in1=s.xnt[:], op=ALU.mult))
            self.on("gpsimd", xn_sq)

            def q_cols(vector, b=b, tt=tt):
                vector.wait_ge(sPL.h, ev[f"xnq_{b}_{tt}"])
                v = sDV.mark(vector.tensor_reduce(out=s.cb[:, 8:9], in_=s.xnt[:], axis=AX.X,
                                                  op=ALU.max, apply_absolute_value=True))
                v = sDV.mark(vector.tensor_reduce(out=s.cb[:, 7:8], in_=s.dump[:],
                                                  axis=AX.X, op=ALU.add))
                vector.wait_ge(sDV.h, v)
                v = sDV.mark(vector.tensor_scalar(out=s.cb[:, 9:10], in0=s.cb[:, 7:8],
                                                  scalar1=1.0 / D, scalar2=EPS,
                                                  op0=ALU.mult, op1=ALU.add))
                ev[f"rden_{b}_{tt}"] = v
            self.on("vector", q_cols)

            def rsq_a(scalar, b=b, tt=tt):
                scalar.wait_ge(sDV.h, ev[f"rden_{b}_{tt}"])
                ev[f"rsq_{b}_{tt}"] = sAC.mark(scalar.activation(
                    out=s.cb[:, 10:11], in_=s.cb[:, 9:10], func=AF.Sqrt))
            self.on("scalar", rsq_a)

            def q_fin(vector, b=b, tt=tt):
                vector.wait_ge(sAC.h, ev[f"rsq_{b}_{tt}"])
                v = sDV.mark(vector.reciprocal(out=s.cb[:, 11:12], in_=s.cb[:, 10:11]))
                vector.wait_ge(sDV.h, v)
                v = sDV.mark(vector.tensor_scalar(out=s.cb[:, 12:13], in0=s.cb[:, 8:9],
                                                  scalar1=s.cb[:, 11:12], scalar2=1e-5,
                                                  op0=ALU.mult, op1=ALU.max))
                vector.wait_ge(sDV.h, v)
                v = sDV.mark(vector.reciprocal(out=s.cb[:, 13:14], in_=s.cb[:, 12:13]))
                vector.wait_ge(sDV.h, v)
                v = sDV.mark(vector.tensor_scalar(out=s.cb[:, 14:15], in0=s.cb[:, 13:14],
                                                  scalar1=127.0, scalar2=None, op0=ALU.mult))
                vector.wait_ge(sDV.h, v)
                v = sDV.mark(vector.tensor_scalar(out=s.cb[:, 15:16], in0=s.cb[:, 11:12],
                                                  scalar1=s.cb[:, 14:15], scalar2=None,
                                                  op0=ALU.mult))
                vector.wait_ge(sDV.h, v)
                v = sDV.mark(vector.tensor_scalar(out=s.cb[:, 16:17], in0=s.cb[:, 14:15],
                                                  scalar1=s.s1col[:, 0:1], scalar2=None,
                                                  op0=ALU.mult))
                vector.wait_ge(sDV.h, v)
                sDV.mark(vector.reciprocal(out=s.dq1c[:, b * NTT + tt:b * NTT + tt + 1],
                                           in_=s.cb[:, 16:17]))
                if tt >= 1:
                    vector.wait_ge(sPE.h, ev[f"xqtr_{b}_{tt-1}"])
                vector.tensor_scalar(out=s.rtmp[:], in0=s.xnt[:], scalar1=s.cb[:, 15:16],
                                     scalar2=MAGIC, op0=ALU.mult, op1=ALU.add)
                ev[f"p1q_{b}_{tt}"] = sDV.mark(vector.tensor_scalar(
                    out=s.xqbt[:], in0=s.rtmp[:], scalar1=MAGIC, scalar2=None,
                    op0=ALU.subtract))
                if getattr(s, "p1cb", None) is not None:
                    ev[f"p1cb_{b}_{tt}"] = sDV.mark(vector.tensor_copy(
                        out=s.p1cb[:][:, tt * 20:tt * 20 + 20], in_=s.cb[:, 0:20]))
            self.on("vector", q_fin)

            for rnd in range(8):
                def tr_p(tensor, b=b, tt=tt, rnd=rnd):
                    for half in range(2):
                        blk = rnd * 2 + half
                        pb = s.pT1 if blk % 2 == 0 else s.pT2
                        if blk == 0:
                            tensor.wait_ge(sAC.h, ev[f"xnc_{b}_{tt}"])
                        if blk == 8:
                            tensor.wait_ge(sDV.h, ev[f"p1q_{b}_{tt}"])
                        if tt > 0 or blk >= 2:
                            key = (f"trev_{b}_{tt}_{blk-2}" if blk >= 2
                                   else f"trev_{b}_{tt-1}_{14 + blk}")
                            tensor.wait_ge(sAC.h, ev[key])
                        src = (s.xnbt[:, blk * 128:(blk + 1) * 128] if blk < 8
                               else s.xqbt[:, (blk - 8) * 128:(blk - 7) * 128])
                        ev[f"tr_{b}_{tt}_{blk}"] = sPE.mark(tensor.transpose(
                            out=pb[:], in_=src, identity=s.ident_bf[:]))
                    if rnd == 7:
                        ev[f"xqtr_{b}_{tt}"] = sPE.n
                        ev[f"xntr_{b}_{tt}"] = sPE.n
                self.on("tensor", tr_p)

                def tr_e(scalar, b=b, tt=tt, rnd=rnd):
                    for half in range(2):
                        blk = rnd * 2 + half
                        pb = s.pT1 if blk % 2 == 0 else s.pT2
                        scalar.wait_ge(sPE.h, ev[f"tr_{b}_{tt}_{blk}"])
                        if blk < 8:
                            dst = s.xn_fm[:, blk * S + tt * 128:blk * S + (tt + 1) * 128]
                        else:
                            if blk == 8 and tt > 0:
                                self.wt(scalar, f"xqw_{b}_{tt-1}")
                            dst = s.xqs[:, (blk - 8) * 128:(blk - 7) * 128]
                        ev[f"trev_{b}_{tt}_{blk}"] = sAC.mark(scalar.activation(
                            out=dst, in_=pb[:], func=AF.Copy))
                    if rnd == 7:
                        ev[f"xqe_{b}_{tt}"] = sAC.n
                        ev[f"xne_{b}_{tt}"] = sAC.n
                self.on("scalar", tr_e)



        def xq_wl(sync, b=b, tt=NTT - 1):
            sync.wait_ge(sAC.h, ev[f"xqe_{b}_{tt}"])
            dst = s.d_xq[b, :, :, tt * 128:(tt + 1) * 128]
            src = s.xqs[:].rearrange("p (dc n) -> p dc n", dc=8)
            self.dma(SP[tt % 2], sync.dma_start(out=dst.transpose([1, 0, 2]), in_=src),
                     f"xqw_{b}_{tt}")
        self.on("sync", xq_wl)

        tp = self.tap(f"xn_fm_b{b}", [128, 8 * S], BF16)
        if tp is not None:
            def tapw(gpsimd, b=b, tp=tp):
                gpsimd.wait_ge(s.sAC.h, ev[f"xqe_{b}_{NTT-1}"])
                self.dma(s.SP[4], gpsimd.dma_start(out=tp, in_=s.xn_fm[:]))
            self.on("gpsimd", tapw)
        tpcb2 = self.tap_outs.get(f"p1cb_b{b}")
        if tpcb2 is not None:
            def tapw2(gpsimd, b=b, tpcb2=tpcb2):
                gpsimd.wait_ge(s.sDV.h, ev[f"p1cb_{b}_{NTT-1}"])
                self.dma(s.SP[4], gpsimd.dma_start(out=tpcb2, in_=s.p1cb[:]))
            self.on("gpsimd", tapw2)

    def p2_proj(self, b):
        s, ev = self, self.ev
        SQ, SP = s.SQ, s.SP
        sPE, sDV, sAC, sPL = s.sPE, s.sDV, s.sAC, s.sPL

        tpq1 = self.tap("wq1t", [D, F], BF16)
        if tpq1 is not None:
            def tapq1(gpsimd):
                self.wt(gpsimd, "wqst_w1_31")
                self.dma(SP[4], gpsimd.dma_start(out=tpq1, in_=s.d_wq1))
            self.on("gpsimd", tapq1)

        tpw = self.tap("winb", [D, 3 * D], BF16)
        if tpw is not None and b == 0:
            def tapwb(gpsimd):
                self.wt(gpsimd, "wst_win_7")
                self.dma(SP[4], gpsimd.dma_start(out=tpw, in_=s.d_winb))
            self.on("gpsimd", tapwb)

        ac2 = sAC.n

        def pje_ev(fs, tc):
            return ac2 + fs * 4 + tc + 1

        for fs in range(16):
            wcol_off = fs * 128 if fs < 8 else 2 * D + (fs - 8) * 128

            def wcl(sync, b=b, fs=fs, wcol_off=wcol_off):
                if fs >= 2:
                    sync.wait_ge(sPE.h, ev[f"pjm_{b}_{fs-2}_3"])
                src = bass.AP(s.d_winb.tensor, wcol_off,
                              [[3 * D, 128], [128 * 3 * D, 8], [1, 128]])
                self.dma(SQ[1 + fs % 2], sync.dma_start(
                    out=s.wcolb[:, (fs % 2) * 1024:(fs % 2) * 1024 + 1024].rearrange("p (dc n) -> p dc n", dc=8),
                    in_=src), f"wcl_{b}_{fs}")
            self.on("sync", wcl)

            def pj(tensor, b=b, fs=fs):
                self.wt(tensor, f"wcl_{b}_{fs}")
                for tc in range(4):
                    pb = s.pA if tc % 2 == 0 else s.pB
                    pv = (pje_ev(fs, tc - 2) if tc >= 2
                          else (pje_ev(fs - 1, 2 + tc) if fs >= 1 else None))
                    if pv is not None:
                        tensor.wait_ge(sAC.h, pv)
                    for dc in range(8):
                        ins = tensor.matmul(
                            out=pb[:], lhsT=s.wcolb[:, (fs % 2) * 1024 + dc * 128:(fs % 2) * 1024 + (dc + 1) * 128],
                            rhs=s.xn_fm[:, dc * S + tc * 512:dc * S + (tc + 1) * 512],
                            start=(dc == 0), stop=(dc == 7))
                    ev[f"pjm_{b}_{fs}_{tc}"] = sPE.mark(ins)
            self.on("tensor", pj)

            def pje(scalar, b=b, fs=fs):
                for tc in range(4):
                    pb = s.pA if tc % 2 == 0 else s.pB
                    scalar.wait_ge(sPE.h, ev[f"pjm_{b}_{fs}_{tc}"])
                    if fs < 8:
                        dst = s.k_fm[:, fs * S + tc * 512:fs * S + (tc + 1) * 512]
                        sc = 1.0
                    else:
                        dst = s.q_fm[:, (fs - 8) * S + tc * 512:(fs - 8) * S + (tc + 1) * 512]
                        sc = 0.125
                    v = sAC.mark(scalar.activation(
                        out=dst, in_=pb[:], func=AF.Copy, scale=sc))
                    ev[f"pje_{b}_{fs}_{tc}"] = v
                    assert v == pje_ev(fs, tc), (v, fs, tc)
            self.on("scalar", pje)

        def vones(vector, b=b):
            ev[f"vones_{b}"] = sDV.mark(vector.memset(s.v_tm[:], 1.0))
        self.on("vector", vones)

        def wvl(sync, b=b):
            src = bass.AP(s.d_winb.tensor, D,
                          [[3 * D, 128], [128 * 3 * D, 8], [1, 1024]])
            self.dma(SQ[3], sync.dma_start(
                out=s.wvb[:].rearrange("p (dc n) -> p dc n", dc=8), in_=src), f"wvl_{b}")
        self.on("sync", wvl)

        for tt in range(NTT):
            def vpj(tensor, b=b, tt=tt):
                if tt == 0:
                    self.wt(tensor, f"wvl_{b}")
                for vc in range(2):
                    pb = s.pC if vc == 0 else s.pD
                    if tt >= 1:
                        tensor.wait_ge(sDV.h, ev[f"vpe_{b}_{tt-1}_{vc}"])
                    for dc in range(8):
                        ins = tensor.matmul(
                            out=pb[:], lhsT=s.xn_fm[:, dc * S + tt * 128:dc * S + (tt + 1) * 128],
                            rhs=s.wvb[:, dc * 1024 + vc * 512:dc * 1024 + (vc + 1) * 512],
                            start=(dc == 0), stop=(dc == 7))
                    ev[f"vpm_{b}_{tt}_{vc}"] = sPE.mark(ins)
            self.on("tensor", vpj)

            def vpe(vector, b=b, tt=tt):
                if tt == 0:
                    vector.wait_ge(sDV.h, ev[f"vones_{b}"])
                for vc in range(2):
                    pb = s.pC if vc == 0 else s.pD
                    vector.wait_ge(sPE.h, ev[f"vpm_{b}_{tt}_{vc}"])
                    base = tt * 1040 + vc * 520
                    dst = s.v_tm[:, base:base + 520].rearrange(
                        "p (g c) -> p g c", g=8, c=65)[:, :, 0:64]
                    src = pb[:].rearrange("p (g c) -> p g c", g=8, c=64)
                    ev[f"vpe_{b}_{tt}_{vc}"] = sDV.mark(vector.tensor_copy(out=dst, in_=src))
            self.on("vector", vpe)

        tpk = self.tap(f"k_fm_b{b}", [128, 8 * S], BF16)
        tpq = self.tap(f"q_fm_b{b}", [128, 8 * S], BF16)
        tpv = self.tap(f"v_tm_b{b}", [128, NTT * 1040], BF16)
        if tpk is not None:
            def tapw(gpsimd, b=b):
                gpsimd.wait_ge(s.sAC.h, ev[f"pje_{b}_15_3"])
                gpsimd.wait_ge(s.sDV.h, ev[f"vpe_{b}_{NTT-1}_1"])
                self.dma(s.SP[4], gpsimd.dma_start(out=tpk, in_=s.k_fm[:]))
                self.dma(s.SP[4], gpsimd.dma_start(out=tpq, in_=s.q_fm[:]))
                self.dma(s.SP[4], gpsimd.dma_start(out=tpv, in_=s.v_tm[:]))
            self.on("gpsimd", tapw)

    def p3_attn(self, b):
        s, ev = self, self.ev
        SQ, SP = s.SQ, s.SP
        sPE, sDV, sAC, sPL = s.sPE, s.sDV, s.sAC, s.sPL

        SKW = 3968

        NG = 4 * H
        NS = 20
        qk_ev, pv_ev, rbm_ev = {}, {}, {}
        n = sPE.n
        for m in range(NG):
            if m >= 2:
                n += 1
                rbm_ev[m - 2] = n
            for step in range(4):
                for j in range(4):
                    n += 1
                    qk_ev[(m, 4 * step + j)] = n
                if m >= 1:
                    for j in range(4):
                        n += 1
                        pv_ev[(m - 1, 4 * step + j)] = n
        for step in range(4):
            for j in range(4):
                n += 1
                pv_ev[(NG - 1, 4 * step + j)] = n
        n += 1
        rbm_ev[NG - 2] = n
        n += 1
        rbm_ev[NG - 1] = n

        exp_ev, dc_ev = {}, {}
        n = sAC.n
        for m in range(NG):
            for kt in range(16):
                n += 1
                exp_ev[(m, kt)] = n
            if m >= 1:
                n += 1
                dc_ev[m - 1] = n
        n += 1
        dc_ev[NG - 1] = n

        pool_ev, rd_ev, ae_ev = {}, {}, {}
        n = sDV.n
        for m in range(NG):
            if m >= 2:
                n += 1
                rd_ev[m - 2] = n
                n += 1
                ae_ev[m - 2] = n
            for kt in range(16):
                n += 1
                pool_ev[(m, kt)] = n
        for g in (NG - 2, NG - 1):
            n += 1
            rd_ev[g] = n
            n += 1
            ae_ev[g] = n

        psq = [s.pA, s.pB, s.pC]

        def bank(m, kt):
            return (m + kt) % 3

        def hof(g):
            h, qc = g // 4, g % 4
            return (h % 2) * 64, (h // 2) * S, qc

        def pOx(g):
            return s.pO if g % 2 == 0 else s.pO2

        def pe_all(tensor, b=b):
            def qk(m, kt):
                hh, hc, qc = hof(m)
                T = 16 * m + kt
                if T >= 3:
                    pm, pk = (m, kt - 3) if kt >= 3 else (m - 1, 13 + kt)
                    tensor.wait_ge(sAC.h, exp_ev[(pm, pk)])
                v = sPE.mark(tensor.matmul(
                    out=psq[bank(m, kt)][:],
                    lhsT=s.k_fm[hh:hh + 64, hc + kt * 128:hc + (kt + 1) * 128],
                    rhs=s.q_fm[hh:hh + 64, hc + qc * 512:hc + (qc + 1) * 512],
                    start=True, stop=True))
                assert v == qk_ev[(m, kt)]

            def pv(g, kt):
                hh, hc, qc = hof(g)
                h = g // 4
                if kt == 0 and g >= 2:
                    tensor.wait_ge(sDV.h, ae_ev[g - 2])
                    tensor.wait_ge(sAC.h, dc_ev[g - 2])
                tensor.wait_ge(sDV.h, pool_ev[(g, kt)])
                v = sPE.mark(tensor.matmul(
                    out=pOx(g)[0:65, :],
                    lhsT=s.v_tm[:, kt * 1040 + h * 65:kt * 1040 + h * 65 + 65],
                    rhs=s.emul[:, ((16 * g + kt) % NS) * 512:((16 * g + kt) % NS) * 512 + 512],
                    start=(kt == 0), stop=(kt == 15)))
                assert v == pv_ev[(g, kt)]

            def rbm(g):
                tensor.wait_ge(sAC.h, dc_ev[g])
                if g >= 1:
                    tensor.wait_ge(sDV.h, rd_ev[g - 1])
                v = sPE.mark(tensor.matmul(
                    out=s.pD[0:64, :], lhsT=s.onesp64[64:65, :],
                    rhs=s.rdbuf[64:65, :], start=True, stop=True))
                assert v == rbm_ev[g]

            for m in range(NG):
                if m >= 2:
                    rbm(m - 2)
                for step in range(4):
                    for j in range(4):
                        qk(m, 4 * step + j)
                    if m >= 1:
                        for j in range(4):
                            pv(m - 1, 4 * step + j)
            for step in range(4):
                for j in range(4):
                    pv(NG - 1, 4 * step + j)
            rbm(NG - 2)
            rbm(NG - 1)
        self.on("tensor", pe_all)

        def act_all(scalar, b=b):
            def dcopy(g):
                scalar.wait_ge(sPE.h, pv_ev[(g, 15)])
                if g >= 1:
                    scalar.wait_ge(sPE.h, rbm_ev[g - 1])
                v = sAC.mark(scalar.activation(
                    out=s.rdbuf[64:65, :], in_=pOx(g)[64:65, :], func=AF.Copy))
                assert v == dc_ev[g]

            for m in range(NG):
                for kt in range(16):
                    scalar.wait_ge(sPE.h, qk_ev[(m, kt)])
                    T = 16 * m + kt
                    if T >= 3:
                        pm, pk = (m, kt - 3) if kt >= 3 else (m - 1, 13 + kt)
                        scalar.wait_ge(sDV.h, pool_ev[(pm, pk)])
                    v = sAC.mark(scalar.activation(
                        out=s.eraw[:, bank(m, kt) * 512:bank(m, kt) * 512 + 512],
                        in_=psq[bank(m, kt)][:], func=AF.Exp))
                    assert v == exp_ev[(m, kt)]
                if m >= 1:
                    dcopy(m - 1)
            dcopy(NG - 1)
        self.on("scalar", act_all)

        def sync_all(sync, b=b):
            for h in range(H):
                if h >= 3:
                    sync.wait_ge(sDV.h, pool_ev[(4 * (h - 3) + 3, 15)])
                src = bass.AP(s.d_expb.tensor, h * 128 * EXPW + 127,
                              [[EXPW - 1, 128], [1, SKW]])
                self.dma(SP[2 + h % 3], sync.dma_start(
                    out=s.skw[:, (h % 3) * SKW:(h % 3) * SKW + SKW], in_=src),
                    f"skw_{b}_{h}")
        self.on("sync", sync_all)

        def dve_all(vector, b=b):
            def fin(g):
                vector.wait_ge(sPE.h, rbm_ev[g])
                with s.nc.allow_low_precision("softmax 1/denom applied to bf16 weights"):
                    v = sDV.mark(vector.reciprocal(
                        out=s.rbs[:, (g % 2) * 512:(g % 2) * 512 + 512],
                        in_=s.pD[0:64, :]))
                assert v == rd_ev[g]
                vector.wait_ge(sDV.h, v)
                hh, hc, qc = hof(g)
                v = sDV.mark(vector.tensor_tensor(
                    out=s.a_fm[hh:hh + 64, hc + qc * 512:hc + (qc + 1) * 512],
                    in0=pOx(g)[0:64, :],
                    in1=s.rbs[:, (g % 2) * 512:(g % 2) * 512 + 512], op=ALU.mult))
                assert v == ae_ev[g]
                ev[f"ae_{b}_{g}"] = v

            for m in range(NG):
                if m >= 2:
                    fin(m - 2)
                h = m // 4
                for kt in range(16):
                    if m % 4 == 0 and kt == 0:
                        self.wt(vector, f"skw_{b}_{h}")
                    vector.wait_ge(sAC.h, exp_ev[(m, kt)])
                    T = 16 * m + kt
                    if T >= NS:
                        pg, pk = divmod(T - NS, 16)
                        vector.wait_ge(sPE.h, pv_ev[(pg, pk)])
                    j0 = 1920 + (m % 4) * 512 - kt * 128
                    v = sDV.mark(vector.tensor_tensor(
                        out=s.emul[:, (T % NS) * 512:(T % NS) * 512 + 512],
                        in0=s.eraw[:, bank(m, kt) * 512:bank(m, kt) * 512 + 512],
                        in1=s.skw[:, (h % 3) * SKW + j0:(h % 3) * SKW + j0 + 512],
                        op=ALU.mult))
                    assert v == pool_ev[(m, kt)]
            fin(NG - 2)
            fin(NG - 1)
        self.on("vector", dve_all)



        tpa = self.tap(f"a_fm_b{b}", [128, 8 * S], BF16)
        if tpa is not None:
            def tapw(gpsimd, b=b, tpa=tpa):
                gpsimd.wait_ge(s.sDV.h, ev[f"ae_{b}_{4 * H - 1}"])
                self.dma(s.SP[4], gpsimd.dma_start(out=tpa, in_=s.a_fm[:]))
            self.on("gpsimd", tapw)

    def p4_gate(self, b):
        s, ev = self, self.ev
        SQ, SP = s.SQ, s.SP
        sPE, sDV, sAC, sPL = s.sPE, s.sDV, s.sAC, s.sPL
        banks = [s.pA, s.pC, s.pB, s.pD]

        def wgl(sync, b=b):
            src = bass.AP(s.d_wgb.tensor, 0, [[2 * D, 128], [128 * 2 * D, 8], [1, 2 * D]])
            self.dma(SQ[3], sync.dma_start(
                out=s.wgs[:].rearrange("p (dc n) -> p dc n", dc=8), in_=src), f"wgl_{b}")
        self.on("sync", wgl)

        for tt in range(NTT):
            def gm(tensor, b=b, tt=tt):
                if tt == 0:
                    self.wt(tensor, f"wgl_{b}")
                for gc in range(4):
                    pb = banks[gc]
                    if tt >= 1:
                        if gc in (2, 3):
                            tensor.wait_ge(sAC.h, ev[f"sg_{b}_{tt-1}_{gc-2}"])
                        else:
                            tensor.wait_ge(sDV.h, ev[f"ab_{b}_{tt-1}_{gc}"])
                    for dc in range(8):
                        ins = tensor.matmul(
                            out=pb[:], lhsT=s.a_fm[:, dc * S + tt * 128:dc * S + (tt + 1) * 128],
                            rhs=s.wgs[:, dc * 2048 + gc * 512:dc * 2048 + (gc + 1) * 512],
                            start=(dc == 0), stop=False)
                    ins = tensor.matmul(
                        out=pb[:], lhsT=s.ones1_bf[:],
                        rhs=s.bgate_bf[0:1, gc * 512:(gc + 1) * 512], start=False, stop=True)
                    ev[f"gm_{b}_{tt}_{gc}"] = sPE.mark(ins)
            self.on("tensor", gm)

            for half in range(2):
                def sgst(scalar, b=b, tt=tt, half=half):
                    scalar.wait_ge(sPE.h, ev[f"gm_{b}_{tt}_{2 + half}"])
                    if tt >= 1 or half == 1:
                        key = f"ab_{b}_{tt}_{half-1}" if half == 1 else f"ab_{b}_{tt-1}_1"
                        scalar.wait_ge(sDV.h, ev[key])
                    ev[f"sg_{b}_{tt}_{half}"] = sAC.mark(scalar.activation(
                        out=s.sgb[:], in_=banks[2 + half][:], func=AF.Sigmoid))
                self.on("scalar", sgst)

                def abst(vector, b=b, tt=tt, half=half):
                    vector.wait_ge(sAC.h, ev[f"sg_{b}_{tt}_{half}"])
                    vector.wait_ge(sPE.h, ev[f"gm_{b}_{tt}_{half}"])
                    if tt >= 2:
                        self.wt(vector, f"abw_{b}_{tt-2}")
                    ev[f"ab_{b}_{tt}_{half}"] = sDV.mark(vector.tensor_tensor(
                        out=s.abb[:, (tt % 2) * 1024 + half * 512:(tt % 2) * 1024 + (half + 1) * 512],
                        in0=banks[half][:], in1=s.sgb[:], op=ALU.mult))
                self.on("vector", abst)

            def abw(sync, b=b, tt=tt):
                sync.wait_ge(sDV.h, ev[f"ab_{b}_{tt}_1"])
                self.dma(SP[tt % 2], sync.dma_start(
                    out=s.d_abr[b, tt * 128:(tt + 1) * 128, :],
                    in_=s.abb[:, (tt % 2) * 1024:(tt % 2) * 1024 + 1024]), f"abw_{b}_{tt}")
            self.on("sync", abw)

        tpab = self.tap(f"abr_b{b}", [S, D], F32)
        tpxq = self.tap(f"dxq_b{b}", [8, 128, S], BF16)
        if tpab is not None:
            def tapab(gpsimd, b=b, tpab=tpab, tpxq=tpxq):
                self.wt(gpsimd, f"abw_{b}_{NTT-1}")
                self.dma(SP[4], gpsimd.dma_start(out=tpab, in_=s.d_abr[b]))
                if tpxq is not None:
                    self.dma(SP[4], gpsimd.dma_start(out=tpxq, in_=s.d_xq[b]))
            self.on("gpsimd", tapab)

    def p56_ffn(self, b):
        s, ev = self, self.ev
        SQ, SP = s.SQ, s.SP
        sPE, sDV, sAC, sPL = s.sPE, s.sDV, s.sAC, s.sPL

        def wql(sync, b=b):
            if b == 0:
                self.wt(sync, "wqst_w1_31")
                self.wt(sync, "wqst_w2_7")
            src1 = bass.AP(s.d_wq1.tensor, 0, [[F, 128], [128 * F, 8], [1, F]])
            self.dma(SQ[4], sync.dma_start(
                out=s.wq1s[:].rearrange("p (dc n) -> p dc n", dc=8), in_=src1), f"wq1l_{b}")
            src2 = bass.AP(s.d_wq2.tensor, 0, [[D, 128], [128 * D, 32], [1, D]])
            self.dma(SQ[5], sync.dma_start(
                out=s.wq2s[:].rearrange("p (fc n) -> p fc n", fc=32), in_=src2), f"wq2l_{b}")
        self.on("sync", wql)

        def emit_tail(tt, part="all"):
            front = part in ("all", "front")
            back = part in ("all", "back")
            for fc8 in (range(8) if front else ()):
                def rnd_q(gpsimd, b=b, tt=tt, fc8=fc8):
                    if fc8 == 0:
                        gpsimd.wait_ge(sDV.h, ev[f"scq_{b}_{tt}"])
                    ev[f"hqp_{b}_{tt}_{fc8}"] = sPL.mark(gpsimd.tensor_scalar(
                        out=s.gbuf[:, fc8 * 512:(fc8 + 1) * 512],
                        in0=s.gbuf[:, fc8 * 512:(fc8 + 1) * 512],
                        scalar1=s.cb[:, 32 + tt % 2:33 + tt % 2], scalar2=MAGIC,
                        op0=ALU.mult, op1=ALU.add))
                self.on("gpsimd", rnd_q)

                def rnd_q2(scalar, b=b, tt=tt, fc8=fc8):
                    scalar.wait_ge(sPL.h, ev[f"hqp_{b}_{tt}_{fc8}"])
                    ev[f"hqc_{b}_{tt}_{fc8}"] = sAC.mark(scalar.activation(
                        out=s.hqt[:, (tt % 2) * F + fc8 * 512:(tt % 2) * F + (fc8 + 1) * 512],
                        in_=s.gbuf[:, fc8 * 512:(fc8 + 1) * 512],
                        func=AF.Identity, bias=s.negmag[:, 0:1]))
                self.on("scalar", rnd_q2)

            for fc8 in (range(8) if back else ()):
                for pr in range(2):
                    def htr(tensor, b=b, tt=tt, fc8=fc8, pr=pr):
                        if fc8 == 0 and pr == 0:
                            pass
                        if pr == 0:
                            tensor.wait_ge(sAC.h, ev[f"hqc_{b}_{tt}_{fc8}"])
                        for half in range(2):
                            fc = fc8 * 4 + pr * 2 + half
                            pb = s.pT1 if fc % 2 == 0 else s.pT2
                            if tt > 0 or fc >= 2:
                                key = (f"hte_{b}_{tt}_{fc-2}" if fc >= 2
                                       else f"hte_{b}_{tt-1}_{30 + fc}")
                                tensor.wait_ge(sAC.h, ev[key])
                            ev[f"htr_{b}_{tt}_{fc}"] = sPE.mark(tensor.transpose(
                                out=pb[:],
                                in_=s.hqt[:, (tt % 2) * F + fc * 128:(tt % 2) * F + (fc + 1) * 128],
                                identity=s.ident_bf[:]))
                    self.on("tensor", htr)

                    def hte(scalar, b=b, tt=tt, fc8=fc8, pr=pr):
                        for half in range(2):
                            fc = fc8 * 4 + pr * 2 + half
                            pb = s.pT1 if fc % 2 == 0 else s.pT2
                            scalar.wait_ge(sPE.h, ev[f"htr_{b}_{tt}_{fc}"])
                            ev[f"hte_{b}_{tt}_{fc}"] = sAC.mark(scalar.activation(
                                out=s.hqf[:, (tt % 2) * F + fc * 128:(tt % 2) * F + (fc + 1) * 128],
                                in_=pb[:], func=AF.Copy))
                    self.on("scalar", hte)

            if not back:
                return

            def f2m(tensor, b=b, tt=tt):
                if tt == 0:
                    self.wt(tensor, f"wq2l_{b}")
                for d2 in range(2):
                    pb = s.pC if d2 == 0 else s.pD
                    if tt >= 1:
                        tensor.wait_ge(sDV.h, ev[f"mg_{b}_{tt-1}_{d2}"])
                    for fc in range(32):
                        if d2 == 0:
                            tensor.wait_ge(sAC.h, ev[f"hte_{b}_{tt}_{fc}"])
                        ins = tensor.matmul(
                            out=pb[:],
                            lhsT=s.hqf[:, (tt % 2) * F + fc * 128:(tt % 2) * F + (fc + 1) * 128],
                            rhs=s.wq2s[:, fc * D + d2 * 512:fc * D + (d2 + 1) * 512],
                            start=(fc == 0), stop=(fc == 31))
                    ev[f"f2m_{b}_{tt}_{d2}"] = sPE.mark(ins)
            self.on("tensor", f2m)

            for d2 in range(2):
                def mg(vector, b=b, tt=tt, d2=d2):
                    vector.wait_ge(sPE.h, ev[f"f2m_{b}_{tt}_{d2}"])
                    vector.wait_ge(sPL.h, ev[f"xa_{b}_{tt}"])
                    if tt >= 2:
                        self.wt(vector, f"ow_{b}_{tt-2}")
                    pb = s.pC if d2 == 0 else s.pD
                    v = sDV.mark(vector.scalar_tensor_tensor(
                        out=s.ftmp[:, d2 * 512:(d2 + 1) * 512], in0=pb[:],
                        scalar=s.cb[:, 34 + tt % 2:35 + tt % 2],
                        in1=s.b2b[:, d2 * 512:(d2 + 1) * 512],
                        op0=ALU.mult, op1=ALU.add))
                    ev[f"mg_{b}_{tt}_{d2}"] = sDV.mark(vector.tensor_tensor(
                        out=s.xat[:, (tt % 2) * D + d2 * 512:(tt % 2) * D + (d2 + 1) * 512],
                        in0=s.ftmp[:, d2 * 512:(d2 + 1) * 512],
                        in1=s.xat[:, (tt % 2) * D + d2 * 512:(tt % 2) * D + (d2 + 1) * 512],
                        op=ALU.add))
                self.on("vector", mg)

            def ow(sync, b=b, tt=tt):
                sync.wait_ge(sDV.h, ev[f"mg_{b}_{tt}_0"])
                sync.wait_ge(sDV.h, ev[f"mg_{b}_{tt}_1"])
                self.dma(SP[tt % 2], sync.dma_start(
                    out=s.out[b, tt * 128:(tt + 1) * 128, :],
                    in_=s.xat[:, (tt % 2) * D:(tt % 2) * D + D]), f"ow_{b}_{tt}")
            self.on("sync", ow)

        def fin2(tt):
            def sq2(scalar, b=b, tt=tt):
                scalar.wait_ge(sDV.h, ev[f"rden2_{b}_{tt}"])
                ev[f"sq2_{b}_{tt}"] = sAC.mark(scalar.activation(
                    out=s.cb[:, 10:11], in_=s.cb[:, 9:10], func=AF.Sqrt))
            self.on("scalar", sq2)

            def quantB_b(vector, b=b, tt=tt):
                vector.wait_ge(sAC.h, ev[f"sq2_{b}_{tt}"])
                v = sDV.mark(vector.reciprocal(out=s.cb[:, 11:12], in_=s.cb[:, 10:11]))
                vector.wait_ge(sDV.h, v)
                v = sDV.mark(vector.tensor_scalar(out=s.cb[:, 22:23], in0=s.cb[:, 22:23],
                                                  scalar1=s.cb[:, 11:12], scalar2=None,
                                                  op0=ALU.mult))
                vector.wait_ge(sDV.h, v)
                p = tt % 2
                v = sDV.mark(vector.tensor_copy(out=s.cb[:, 34 + p:35 + p],
                                                in_=s.cb[:, 22:23]))
                vector.wait_ge(sDV.h, v)
                ev[f"scq_{b}_{tt}"] = v
            self.on("vector", quantB_b)

        for tt in range(NTT):
            def lds_a(sync, b=b, tt=tt):
                if tt >= 2:
                    sync.wait_ge(sPE.h, ev[f"f1m_{b}_{tt-2}_7"])
                    sync.wait_ge(sPL.h, ev[f"xa_{b}_{tt-2}"])
                src = s.d_xq[b, :, :, tt * 128:(tt + 1) * 128]
                self.dma(SQ[1 + tt % 2], sync.dma_start(
                    out=s.xqt[:, (tt % 2) * 1024:(tt % 2) * 1024 + 1024].rearrange("p (dc n) -> p dc n", dc=8),
                    in_=src.transpose([1, 0, 2])), f"xql_{b}_{tt}")
                self.dma(SQ[6 + tt % 2], sync.dma_start(
                    out=s.abrt[:, (tt % 2) * D:(tt % 2) * D + D],
                    in_=s.d_abr[b, tt * 128:(tt + 1) * 128, :]), f"abl_{b}_{tt}")
            self.on("sync", lds_a)

            if tt >= 2:
                emit_tail(tt - 2, "back")
            if tt >= 1:
                fin2(tt - 1)

            def lds(sync, b=b, tt=tt):
                if tt >= 2:
                    self.wt(sync, f"ow_{b}_{tt-2}")
                self.dma(SQ[3 - 3 * (tt % 2)], sync.dma_start(
                    out=s.xat[:, (tt % 2) * D:(tt % 2) * D + D],
                    in_=s.x_in[b, tt * 128:(tt + 1) * 128, :]), f"x2l_{b}_{tt}")
            self.on("sync", lds)

            def xadd(gpsimd, b=b, tt=tt):
                self.wt(gpsimd, f"x2l_{b}_{tt}")
                gpsimd.wait_ge(SQ[6 + tt % 2].h, ev[f"abl_{b}_{tt}"][1])
                ev[f"xa_{b}_{tt}"] = sPL.mark(gpsimd.tensor_tensor(
                    out=s.xat[:, (tt % 2) * D:(tt % 2) * D + D],
                    in0=s.xat[:, (tt % 2) * D:(tt % 2) * D + D],
                    in1=s.abrt[:, (tt % 2) * D:(tt % 2) * D + D], op=ALU.add))
            self.on("gpsimd", xadd)

            if tt >= 1:
                emit_tail(tt - 1, "front")

            for fc8 in range(8):
                def f1m(tensor, b=b, tt=tt, fc8=fc8):
                    if tt == 0 and fc8 == 0:
                        self.wt(tensor, f"wq1l_{b}")
                    if fc8 == 0:
                        self.wt(tensor, f"xql_{b}_{tt}")
                    pb = [s.pA, s.pB, s.pO, s.pO2][fc8 % 4]
                    key = (f"dq_{b}_{tt}_{fc8-4}" if fc8 >= 4
                           else (f"dq_{b}_{tt-1}_{4 + fc8}" if tt >= 1 else None))
                    if key is not None:
                        tensor.wait_ge(sDV.h, ev[key])
                    for dc in range(8):
                        ins = tensor.matmul(
                            out=pb[:], lhsT=s.xqt[:, (tt % 2) * 1024 + dc * 128:(tt % 2) * 1024 + (dc + 1) * 128],
                            rhs=s.wq1s[:, dc * F + fc8 * 512:dc * F + (fc8 + 1) * 512],
                            start=(dc == 0), stop=(dc == 7))
                    ev[f"f1m_{b}_{tt}_{fc8}"] = sPE.mark(ins)
                self.on("tensor", f1m)

                def dq(vector, b=b, tt=tt, fc8=fc8):
                    vector.wait_ge(sPE.h, ev[f"f1m_{b}_{tt}_{fc8}"])
                    if tt >= 1:
                        vector.wait_ge(sAC.h, ev[f"hqc_{b}_{tt-1}_{fc8}"])
                    ev[f"dq_{b}_{tt}_{fc8}"] = sDV.mark(vector.scalar_tensor_tensor(
                        out=s.gbuf[:, fc8 * 512:(fc8 + 1) * 512],
                        in0=[s.pA, s.pB, s.pO, s.pO2][fc8 % 4][:],
                        scalar=s.dq1c[:, b * NTT + tt:b * NTT + tt + 1],
                        in1=s.b1b[:, fc8 * 512:(fc8 + 1) * 512],
                        op0=ALU.mult, op1=ALU.add))
                self.on("vector", dq)

                def gl(scalar, b=b, tt=tt, fc8=fc8):
                    scalar.wait_ge(sDV.h, ev[f"dq_{b}_{tt}_{fc8}"])
                    ev[f"gl_{b}_{tt}_{fc8}"] = sAC.mark(scalar.activation(
                        out=s.gbuf[:, fc8 * 512:(fc8 + 1) * 512],
                        in_=s.gbuf[:, fc8 * 512:(fc8 + 1) * 512], func=AF.Gelu))
                self.on("scalar", gl)

                def glsq(gpsimd, b=b, tt=tt, fc8=fc8):
                    gpsimd.wait_ge(sAC.h, ev[f"gl_{b}_{tt}_{fc8}"])
                    if fc8 > 0:
                        gpsimd.wait_ge(sDV.h, ev[f"gsq_{b}_{tt}_{fc8-1}"])
                    ev[f"glsq_{b}_{tt}_{fc8}"] = sPL.mark(gpsimd.tensor_tensor(
                        out=s.gsqt[:], in0=s.gbuf[:, fc8 * 512:(fc8 + 1) * 512],
                        in1=s.gbuf[:, fc8 * 512:(fc8 + 1) * 512], op=ALU.mult))
                self.on("gpsimd", glsq)

                def gsq(vector, b=b, tt=tt, fc8=fc8):
                    vector.wait_ge(sPL.h, ev[f"glsq_{b}_{tt}_{fc8}"])
                    vector.tensor_reduce(out=s.cb[:, fc8:fc8 + 1], in_=s.gsqt[:],
                                         axis=AX.X, op=ALU.add)
                    ev[f"gsq_{b}_{tt}_{fc8}"] = sDV.mark(vector.tensor_reduce(
                        out=s.cb[:, 24 + fc8:25 + fc8],
                        in_=s.gbuf[:, fc8 * 512:(fc8 + 1) * 512], axis=AX.X,
                        op=ALU.max, apply_absolute_value=True))
                self.on("vector", gsq)

            def quantA(vector, b=b, tt=tt):
                vector.wait_ge(sDV.h, ev[f"gsq_{b}_{tt}_7"])
                v = sDV.mark(vector.tensor_reduce(out=s.cb[:, 17:18], in_=s.cb[:, 24:32],
                                                  axis=AX.X, op=ALU.max))
                ev[f"amax_{b}_{tt}"] = v
                v2 = sDV.mark(vector.tensor_reduce(out=s.cb[:, 8:9], in_=s.cb[:, 0:8],
                                                   axis=AX.X, op=ALU.add))
                vector.wait_ge(sDV.h, v2)
                v2 = sDV.mark(vector.tensor_scalar(out=s.cb[:, 9:10], in0=s.cb[:, 8:9],
                                                   scalar1=1.0 / F, scalar2=EPS,
                                                   op0=ALU.mult, op1=ALU.add))
                ev[f"rden2_{b}_{tt}"] = v2
            self.on("vector", quantA)

            def quantB_a(vector, b=b, tt=tt):
                vector.wait_ge(sDV.h, ev[f"amax_{b}_{tt}"])
                v = sDV.mark(vector.tensor_scalar(out=s.cb[:, 18:19], in0=s.cb[:, 17:18],
                                                  scalar1=1e-5, scalar2=None, op0=ALU.max))
                vector.wait_ge(sDV.h, v)
                v = sDV.mark(vector.reciprocal(out=s.cb[:, 19:20], in_=s.cb[:, 18:19]))
                vector.wait_ge(sDV.h, v)
                v = sDV.mark(vector.tensor_scalar(out=s.cb[:, 20:21], in0=s.cb[:, 19:20],
                                                  scalar1=127.0, scalar2=None, op0=ALU.mult))
                vector.wait_ge(sDV.h, v)
                v = sDV.mark(vector.tensor_scalar(out=s.cb[:, 21:22], in0=s.cb[:, 20:21],
                                                  scalar1=s.s2col[:, 0:1], scalar2=None,
                                                  op0=ALU.mult))
                vector.wait_ge(sDV.h, v)
                v = sDV.mark(vector.reciprocal(out=s.cb[:, 22:23], in_=s.cb[:, 21:22]))
                vector.wait_ge(sDV.h, v)
                p = tt % 2
                v = sDV.mark(vector.tensor_copy(out=s.cb[:, 32 + p:33 + p],
                                                in_=s.cb[:, 20:21]))
                vector.wait_ge(sDV.h, v)
            self.on("vector", quantB_a)

        fin2(NTT - 1)
        emit_tail(NTT - 2, "back")
        emit_tail(NTT - 1, "front")
        emit_tail(NTT - 1, "back")

    def finale(self):
        s = self

        def fin(gpsimd):
            for c in (*s.SP, *s.SQ):
                if c.n > 0:
                    gpsimd.wait_ge(c.h, c.n)
        self.on("gpsimd", fin)

        def fin2(sync):
            for c in (*s.SP, *s.SQ):
                if c.n > 0:
                    sync.wait_ge(c.h, c.n)
        self.on("sync", fin2)


def build_core_program(taps=(), stop_after=None):
    kb = KB(taps, stop_after=stop_after)
    nc = kb.build()
    return nc, kb


def kernel(L0_attn_x, L0_attn_w_in, L0_attn_w_gate, L0_attn_b_gate, L0_attn_amplitudes,
           L0_attn_sharpness, L0_attn_offsets, L0_ffn_w1, L0_ffn_b1, L0_ffn_w2, L0_ffn_b2):
    x = np.ascontiguousarray(L0_attn_x)
    nc, _ = build_core_program()
    in_maps = []
    for c in range(8):
        in_maps.append({
            "x": np.ascontiguousarray(x[2 * c:2 * c + 2]),
            "w_in": np.ascontiguousarray(L0_attn_w_in),
            "w_gate": np.ascontiguousarray(L0_attn_w_gate),
            "b_gate": np.ascontiguousarray(L0_attn_b_gate),
            "amp": np.ascontiguousarray(L0_attn_amplitudes),
            "sharp": np.ascontiguousarray(L0_attn_sharpness),
            "off": np.ascontiguousarray(L0_attn_offsets),
            "w1": np.ascontiguousarray(L0_ffn_w1),
            "b1": np.ascontiguousarray(L0_ffn_b1),
            "w2": np.ascontiguousarray(L0_ffn_w2),
            "b2": np.ascontiguousarray(L0_ffn_b2),
        })
    res = run_bass_kernel_spmd(nc, in_maps, core_ids=list(range(8)))
    return np.concatenate([r["out"] for r in res.results], axis=0)


# version 30
# speedup vs baseline: 1.0760x; 1.0372x over previous
import numpy as np
import ml_dtypes
import concourse.bass as bass
import concourse.mybir as mybir
from concourse.bass_utils import run_bass_kernel_spmd
from contextlib import ExitStack

F32 = mybir.dt.float32
BF16 = mybir.dt.bfloat16
AF = mybir.ActivationFunctionType
ALU = mybir.AluOpType
AX = mybir.AxisListType

B, S, D, H, NK, F = 2, 2048, 1024, 16, 21, 4096
TW = 4095
EXPW = 4224
MAGIC = 1.5 * 2.0 ** 23
EPS = 1e-5
NTT = S // 128
ENGS = ("sync", "tensor", "vector", "scalar", "gpsimd")


class Ctr:
    def __init__(self, nc, stack, name):
        self.h = stack.enter_context(nc.semaphore(name))
        self.n = 0

    def mark(self, instr, k=1):
        instr.then_inc(self.h, k)
        self.n += k
        return self.n


class _DummyInstr:
    def then_inc(self, sem, k=1, **kw):
        return self


class _DummyEng:
    def __getattr__(self, name):
        def f(*a, **k):
            return _DummyInstr()
        return f


class KB:
    def __init__(self, taps=(), stop_after=None):
        self.pending = None
        self.stop_after = stop_after
        self.taps = set(taps)
        self.tap_outs = {}
        nc = self.nc = bass.Bass("TRN2", target_bir_lowering=False)
        self.x_in = nc.dram_tensor("x", [B, S, D], F32, kind="ExternalInput").ap()
        self.w_in = nc.dram_tensor("w_in", [D, 3 * D], F32, kind="ExternalInput").ap()
        self.w_gate_in = nc.dram_tensor("w_gate", [D, 2 * D], F32, kind="ExternalInput").ap()
        self.b_gate_in = nc.dram_tensor("b_gate", [2 * D], F32, kind="ExternalInput").ap()
        self.amp_in = nc.dram_tensor("amp", [H, NK], F32, kind="ExternalInput").ap()
        self.sharp_in = nc.dram_tensor("sharp", [H, NK], F32, kind="ExternalInput").ap()
        self.off_in = nc.dram_tensor("off", [H, NK], F32, kind="ExternalInput").ap()
        self.w1_in = nc.dram_tensor("w1", [F, D], F32, kind="ExternalInput").ap()
        self.b1_in = nc.dram_tensor("b1", [F], F32, kind="ExternalInput").ap()
        self.w2_in = nc.dram_tensor("w2", [D, F], F32, kind="ExternalInput").ap()
        self.b2_in = nc.dram_tensor("b2", [D], F32, kind="ExternalInput").ap()
        self.out = nc.dram_tensor("out", [B, S, D], F32, kind="ExternalOutput").ap()

        self.d_expb = nc.dram_tensor("d_expb", [H, 128 * EXPW], BF16, kind="Internal").ap()
        self.d_winb = nc.dram_tensor("d_winb", [D, 3 * D], BF16, kind="Internal").ap()
        self.d_wgb = nc.dram_tensor("d_wgb", [D, 2 * D], BF16, kind="Internal").ap()
        self.d_wq1 = nc.dram_tensor("d_wq1", [D, F], BF16, kind="Internal").ap()
        self.d_wq2 = nc.dram_tensor("d_wq2", [F, D], BF16, kind="Internal").ap()
        self.d_xq = nc.dram_tensor("d_xq", [B, 8, 128, S], BF16, kind="Internal").ap()
        self.d_abr = nc.dram_tensor("d_abr", [B, S, D], F32, kind="Internal").ap()

        ident = np.eye(128)
        self.ident_bf_d = nc.inline_tensor(ident.astype(ml_dtypes.bfloat16), "identd_bf")
        sel = np.zeros((63, 65), dtype=np.float32)
        for p in range(63):
            sel[p, 32 * (p // 21)] = 1.0
        self.selG_d = nc.inline_tensor(sel, "selG_d")

    def tap(self, name, shape, dtype=F32):
        if name in self.taps and name not in self.tap_outs:
            self.tap_outs[name] = self.nc.dram_tensor(
                "tap_" + name, shape, dtype, kind="ExternalOutput").ap()
        return self.tap_outs.get(name)

    def on(self, engine_name, fn):
        if self.pending is not None:
            self.pending.append((engine_name, fn))
            fn(_DummyEng())
        else:
            getattr(self.block, engine_name)(fn)

    def grouped(self, phase_fn, *args):
        ctrs = (*self.SQ, *self.SP, self.sPE, self.sDV, self.sAC, self.sPL)
        snap = [(c, c.n) for c in ctrs]
        self.pending = []
        phase_fn(*args)
        pend = self.pending
        self.pending = None
        for c, v in snap:
            c.n = v
        for en in ENGS:
            fns = [f for e, f in pend if e == en]
            if fns:
                def sect(eng, fns=fns):
                    for f in fns:
                        f(eng)
                getattr(self.block, en)(sect)

    def dma(self, ctr, instr, key=None):
        instr.then_inc(ctr.h, 16)
        ctr.n += 16
        if key is not None:
            self.ev[key] = (ctr, ctr.n)
        return (ctr, ctr.n)

    def wt(self, eng, key):
        c, v = self.ev[key]
        eng.wait_ge(c.h, v)

    def bar(self):
        vals = [(c, c.n) for c in (*self.SQ, *self.SP, self.sPE, self.sDV, self.sAC, self.sPL)]
        for en in ENGS:
            def f(eng, vals=vals):
                for c, v in vals:
                    if v > 0:
                        eng.wait_ge(c.h, v)
            self.on(en, f)

    def build(self):
        nc = self.nc
        st = ExitStack()
        with st:
            s = self

            def sbufP(name, shape, dt, side=None):
                return st.enter_context(nc.sbuf_tensor(name, shape, dt, side=side))

            s.ident_bf = sbufP("ident_bf", [128, 128], BF16)
            s.ones1_bf = sbufP("ones1_bf", [1, 128], BF16)
            s.ones1_f = sbufP("ones1_f", [1, 128], F32)
            s.ones21 = sbufP("ones21", [NK, 1], F32)
            s.ones128 = sbufP("ones128", [128, 1], F32)
            s.onesp64 = sbufP("onesp64", [128, 64], BF16)
            s.bgate_bf = sbufP("bgate_bf", [1, 2 * D], BF16)
            s.b1b = sbufP("b1b", [128, F], BF16)
            s.b2b = sbufP("b2b", [128, D], BF16)
            s.s1col = sbufP("s1col", [128, 1], F32)
            s.s2col = sbufP("s2col", [128, 1], F32)
            s.dq1c = sbufP("dq1c", [128, B * NTT], F32)
            s.cb = sbufP("cb", [128, 40], F32)
            s.epscol = sbufP("epscol", [128, 1], F32)
            s.negmag = sbufP("negmag", [128, 1], F32)
            s.scal = sbufP("scal", [1, 8], F32)
            s.stg = sbufP("stg", [128, 2 * 128], BF16)
            s.p1cb = sbufP("p1cb_buf", [128, 20 * NTT], F32) \
                if any(t.startswith("p1cb") for t in self.taps) else None
            s.ftmp = sbufP("ftmp", [128, 2 * 512], F32)

            s.pA = st.enter_context(nc.psum_tensor("pA", [128, 512], F32))
            s.pB = st.enter_context(nc.psum_tensor("pB", [128, 512], F32))
            s.pC = st.enter_context(nc.psum_tensor("pC", [128, 512], F32))
            s.pD = st.enter_context(nc.psum_tensor("pD", [128, 512], F32))
            s.pO = st.enter_context(nc.psum_tensor("pO", [128, 512], F32))
            s.pO2 = st.enter_context(nc.psum_tensor("pO2", [128, 512], F32))
            s.pT1 = st.enter_context(nc.psum_tensor("pT1", [128, 512], BF16))
            s.pT2 = st.enter_context(nc.psum_tensor("pT2", [128, 512], BF16))

            s.SQ = [Ctr(nc, st, f"SQ{i}") for i in range(8)]
            s.SP = [Ctr(nc, st, f"SP{i}") for i in range(6)]
            s.sPE = Ctr(nc, st, "sPE")
            s.sDV = Ctr(nc, st, "sDV")
            s.sAC = Ctr(nc, st, "sAC")
            s.sPL = Ctr(nc, st, "sPL")

            s.block = st.enter_context(nc.Block())
            self.ev = {}

            stop = self.stop_after
            with ExitStack() as ph0:
                s.wtile = ph0.enter_context(nc.sbuf_tensor("wtile", [128, 3 * F], F32))
                s.wqtile = ph0.enter_context(nc.sbuf_tensor("wqtile", [128, 3 * F], BF16))
                s.stg8 = ph0.enter_context(nc.sbuf_tensor("stg8", [128, 2 * F], BF16))
                s.iotaf = ph0.enter_context(nc.sbuf_tensor("iotaf", [128, TW], F32))
                s.relrev = ph0.enter_context(nc.sbuf_tensor("relrev", [128, TW], F32))
                s.gwork = ph0.enter_context(nc.sbuf_tensor("gwork", [128, TW], F32))
                s.hkG = ph0.enter_context(nc.sbuf_tensor("hkG", [128, 18], F32))
                s.selG = ph0.enter_context(nc.sbuf_tensor("selG", [63, 65], F32))
                s.onesbc = ph0.enter_context(nc.sbuf_tensor("onesbc", [128, 128], BF16))
                s.rowexp = ph0.enter_context(nc.sbuf_tensor("rowexp", [65, TW], BF16))
                s.bgrow = ph0.enter_context(nc.sbuf_tensor("bgrow", [1, 2 * D], F32))
                s.b1row = ph0.enter_context(nc.sbuf_tensor("b1row", [1, F], F32))
                s.b2row = ph0.enter_context(nc.sbuf_tensor("b2row", [1, D], F32))
                s.bch = ph0.enter_context(nc.sbuf_tensor("bch", [128, TW], BF16))
                self.grouped(self.phase0)
            self.bar()

            for b in (range(B) if stop != "ph0" else []):
                with ExitStack() as gL1:
                    s.xn_fm = gL1.enter_context(
                        nc.sbuf_tensor(f"xn_fm_{b}", [128, 8 * S], BF16, side="left"))
                    s.a_fm = s.xn_fm
                    with ExitStack() as gR1:
                        s.k_fm = gR1.enter_context(
                            nc.sbuf_tensor(f"k_fm_{b}", [128, 8 * S], BF16, side="right"))
                        s.q_fm = gR1.enter_context(
                            nc.sbuf_tensor(f"q_fm_{b}", [128, 8 * S], BF16, side="right"))
                        s.v_tm = gR1.enter_context(
                            nc.sbuf_tensor(f"v_tm_{b}", [128, NTT * 1040], BF16, side="right"))
                        with ExitStack() as p1s:
                            s.xt = p1s.enter_context(nc.sbuf_tensor(f"xt_{b}", [128, 2 * D], F32))
                            s.xnt = p1s.enter_context(nc.sbuf_tensor(f"xnt_{b}", [128, D], F32))
                            s.xnbt = p1s.enter_context(nc.sbuf_tensor(f"xnbt_{b}", [128, D], BF16))
                            s.xqbt = p1s.enter_context(nc.sbuf_tensor(f"xqbt_{b}", [128, D], BF16))
                            s.xqs = p1s.enter_context(nc.sbuf_tensor(f"xqs_{b}", [128, 8 * 128], BF16))
                            s.dump = p1s.enter_context(nc.sbuf_tensor(f"dump_{b}", [128, D], F32))
                            s.rtmp = p1s.enter_context(nc.sbuf_tensor(f"rtmp_{b}", [128, D], F32))
                            self.grouped(self.p1_ln_quant, b)
                        self.bar()
                        if stop == "p1":
                            self.finale()
                            return nc
                        with ExitStack() as p2s:
                            s.wcolb = p2s.enter_context(
                                nc.sbuf_tensor(f"wcolb_{b}", [128, 2 * 1024], BF16))
                            s.wvb = p2s.enter_context(
                                nc.sbuf_tensor(f"wvb_{b}", [128, 2 * 8 * 512], BF16))
                            self.grouped(self.p2_proj, b)
                        self.bar()
                        if stop == "p2":
                            self.finale()
                            return nc
                        with ExitStack() as p3s:
                            s.skw = p3s.enter_context(
                                nc.sbuf_tensor(f"skw_{b}", [128, 3 * 3968], BF16, side="left"))
                            s.eraw = p3s.enter_context(
                                nc.sbuf_tensor(f"eraw_{b}", [128, 3 * 512], BF16, side="left"))
                            s.emul = p3s.enter_context(
                                nc.sbuf_tensor(f"emul_{b}", [128, 20 * 512], BF16, side="left"))
                            s.rdbuf = p3s.enter_context(
                                nc.sbuf_tensor(f"rdbuf_{b}", [128, 512], BF16, side="left"))
                            s.rbs = p3s.enter_context(
                                nc.sbuf_tensor(f"rbs_{b}", [64, 2 * 512], BF16, side="left"))
                            self.grouped(self.p3_attn, b)
                        self.bar()
                        if stop == "p3":
                            self.finale()
                            return nc
                        gR1.close()
                        with ExitStack() as p4s:
                            s.wgs = p4s.enter_context(
                                nc.sbuf_tensor(f"wgs_{b}", [128, 8 * 2048], BF16, side="right"))
                            s.sgb = p4s.enter_context(
                                nc.sbuf_tensor(f"sgb_{b}", [128, 512], F32, side="right"))
                            s.abb = p4s.enter_context(
                                nc.sbuf_tensor(f"abb_{b}", [128, 2 * 1024], F32, side="right"))
                            self.grouped(self.p4_gate, b)
                        self.bar()
                        if stop == "p4":
                            self.finale()
                            return nc
                with ExitStack() as p56s:
                    s.wq1s = p56s.enter_context(nc.sbuf_tensor(f"wq1s_{b}", [128, 8 * F], BF16))
                    s.wq2s = p56s.enter_context(nc.sbuf_tensor(f"wq2s_{b}", [128, 32 * D], BF16))
                    s.gbuf = p56s.enter_context(nc.sbuf_tensor(f"gbuf_{b}", [128, F], F32))
                    s.hqt = p56s.enter_context(nc.sbuf_tensor(f"hqt_{b}", [128, 2 * F], BF16))
                    s.gsqt = p56s.enter_context(nc.sbuf_tensor(f"gsqt_{b}", [128, 512], F32))
                    s.hqf = s.hqt
                    s.xqt = p56s.enter_context(nc.sbuf_tensor(f"xqt_{b}", [128, 2 * 1024], BF16))
                    s.abrt = p56s.enter_context(nc.sbuf_tensor(f"abrt_{b}", [128, 2 * D], F32))
                    s.xat = p56s.enter_context(nc.sbuf_tensor(f"xat_{b}", [128, 2 * D], F32))
                    self.grouped(self.p56_ffn, b)
                self.bar()
                if stop == "p56":
                    self.finale()
                    return nc
            self.finale()
        return nc

    def phase0(self):
        s, ev = self, self.ev
        SQ, SP = s.SQ, s.SP
        sPE, sDV, sAC, sPL = s.sPE, s.sDV, s.sAC, s.sPL

        def l3(n, i):
            return ((n - 1 - i) // 3) * 3 + i

        def loads(sync):
            for dst, src in [(s.ident_bf[:], s.ident_bf_d.ap()),
                             (s.selG[:], s.selG_d.ap()),
                             (s.b1row[:], s.b1_in[None, :]),
                             (s.b2row[:], s.b2_in[None, :]),
                             (s.bgrow[:], s.b_gate_in[None, :])]:
                self.dma(SQ[0], sync.dma_start(out=dst, in_=src))
            for g in range(6):
                nr = 63 if g < 5 else 21
                for k, t in enumerate((s.amp_in, s.sharp_in, s.off_in)):
                    flat = t.rearrange("h k -> (h k)")
                    self.dma(SQ[0], sync.dma_start(
                        out=s.hkG[0:nr, 3 * g + k:3 * g + k + 1],
                        in_=flat[g * 63:g * 63 + nr][:, None],
                        allow_slow_non_contiguous=True))
            ev["p0_loads"] = (SQ[0], SQ[0].n)
            ev["hkg"] = (SQ[0], SQ[0].n)
        self.on("sync", loads)

        def iota0(gpsimd):
            ev["iota"] = sPL.mark(gpsimd.iota(
                out=s.iotaf[:], pattern=[[1, TW]], base=0, channel_multiplier=0,
                allow_small_or_imprecise_dtypes=True))
        self.on("gpsimd", iota0)

        def consts(vector):
            self.wt(vector, "p0_loads")
            vector.memset(s.ones1_bf[:], 1.0)
            vector.memset(s.ones1_f[:], 1.0)
            vector.memset(s.ones128[:], 1.0)
            vector.memset(s.onesp64[64:65, :], 1.0)
            vector.memset(s.epscol[:], EPS)
            vector.memset(s.negmag[:], -MAGIC)
            vector.memset(s.onesbc[0:1, :], 1.0)
            vector.memset(s.onesbc[32:33, :], 1.0)
            vector.memset(s.onesbc[64:65, :], 1.0)
            vector.tensor_copy(out=s.bgate_bf[:], in_=s.bgrow[:])
            vector.wait_ge(sPL.h, ev["iota"])
            ev["relrev"] = sDV.mark(vector.tensor_scalar(
                out=s.relrev[:], in0=s.iotaf[:], scalar1=-1.0, scalar2=2047.0,
                op0=ALU.mult, op1=ALU.add))
        self.on("vector", consts)

        act0 = sAC.n
        dvb0 = sDV.n

        def rexp_ev(g, c, act0=act0):
            return act0 + 9 * g + 2 + c

        def bce_ev(h, c, dvb0=dvb0):
            return dvb0 + 6 + 12 + 8 * h + c + 1

        def hkg_neg(vector):
            self.wt(vector, "hkg")
            for g in range(6):
                c = 3 * g + 1
                ev[f"hkgn_{g}"] = sDV.mark(vector.scalar_tensor_tensor(
                    out=s.hkG[0:63, c:c + 1], in0=s.hkG[0:63, c:c + 1], scalar=-1.0,
                    in1=s.hkG[0:63, c:c + 1], op0=ALU.mult, op1=ALU.min))
        self.on("vector", hkg_neg)

        for g in range(6):
            nh = 3 if g < 5 else 1
            nr = nh * 21

            def u_step(vector, g=g, nr=nr):
                vector.wait_ge(sDV.h, ev[f"hkgn_{g}"])
                if g == 0:
                    vector.wait_ge(sDV.h, ev["relrev"])
                else:
                    vector.wait_ge(sPE.h, ev[f"tsum_{g-1}"])
                ev[f"u_{g}"] = sDV.mark(vector.tensor_scalar(
                    out=s.gwork[0:nr, :], in0=s.relrev[0:nr, :],
                    scalar1=s.hkG[0:nr, 3 * g + 2:3 * g + 3], scalar2=None,
                    op0=ALU.subtract))
            self.on("vector", u_step)

            def exp_step(scalar, g=g, nr=nr):
                scalar.wait_ge(sDV.h, ev[f"u_{g}"])
                scalar.activation(out=s.gwork[0:nr, :], in_=s.gwork[0:nr, :], func=AF.Square)
                ev[f"gexp_{g}"] = sAC.mark(scalar.activation(
                    out=s.gwork[0:nr, :], in_=s.gwork[0:nr, :], func=AF.Exp,
                    scale=s.hkG[0:nr, 3 * g + 1:3 * g + 2]))
            self.on("scalar", exp_step)

            def amp_step(vector, g=g, nr=nr):
                vector.wait_ge(sAC.h, ev[f"gexp_{g}"])
                ev[f"ge_{g}"] = sDV.mark(vector.tensor_scalar(
                    out=s.gwork[0:nr, :], in0=s.gwork[0:nr, :],
                    scalar1=s.hkG[0:nr, 3 * g:3 * g + 1], scalar2=None, op0=ALU.mult))
            self.on("vector", amp_step)

            def sum_step(tensor, g=g, nr=nr, nh=nh):
                tensor.wait_ge(sDV.h, ev[f"ge_{g}"])
                for c in range(8):
                    n = min(512, TW - c * 512)
                    if g > 0 or c > 0:
                        pv = rexp_ev(g, c - 1) if c > 0 else rexp_ev(g - 1, 7)
                        tensor.wait_ge(sAC.h, pv)
                    ev[f"tsum_{g}_{c}"] = sPE.mark(tensor.matmul(
                        out=s.pC[0:(32 * (nh - 1) + 1), 0:n], lhsT=s.selG[0:nr, 0:(32 * (nh - 1) + 1)],
                        rhs=s.gwork[0:nr, c * 512:c * 512 + n], start=True, stop=True))
                ev[f"tsum_{g}"] = sPE.n
            self.on("tensor", sum_step)

            def rexp_step(scalar, g=g, nh=nh):
                for c in range(8):
                    n = min(512, TW - c * 512)
                    scalar.wait_ge(sPE.h, ev[f"tsum_{g}_{c}"])
                    for m in range(nh):
                        ins = scalar.activation(
                            out=s.rowexp[32 * m:32 * m + 1, c * 512:c * 512 + n],
                            in_=s.pC[32 * m:32 * m + 1, 0:n], func=AF.Exp)
                    v = sAC.mark(ins)
                    ev[f"rexp_{g}_{c}"] = v
                    assert v == rexp_ev(g, c), (v, g, c)
            self.on("scalar", rexp_step)

        for h in range(H):
            g, m = divmod(h, 3)

            def bc_step(tensor, h=h, g=g, m=m):
                for c in range(8):
                    n = min(512, TW - c * 512)
                    tensor.wait_ge(sAC.h, rexp_ev(g, c))
                    pb = s.pA if c % 2 == 0 else s.pB
                    if h > 0 or c >= 2:
                        pv = bce_ev(h, c - 2) if c >= 2 else bce_ev(h - 1, 6 + c)
                        tensor.wait_ge(sDV.h, pv)
                    ev[f"bcm_{h}_{c}"] = sPE.mark(tensor.matmul(
                        out=pb[:, 0:n], lhsT=s.onesbc[32 * m:32 * m + 1, :],
                        rhs=s.rowexp[32 * m:32 * m + 1, c * 512:c * 512 + n],
                        start=True, stop=True, tile_position=(32 * m, 0)))
            self.on("tensor", bc_step)

            def bce_step(vector, h=h):
                for c in range(8):
                    n = min(512, TW - c * 512)
                    pb = s.pA if c % 2 == 0 else s.pB
                    vector.wait_ge(sPE.h, ev[f"bcm_{h}_{c}"])
                    if h > 0 and c == 0:
                        self.wt(vector, f"expw_{h-1}")
                    v = sDV.mark(vector.tensor_copy(
                        out=s.bch[:, c * 512:c * 512 + n], in_=pb[:, 0:n]))
                    ev[f"bce_{h}_{c}"] = v
                    assert v == bce_ev(h, c), (v, bce_ev(h, c), h, c)
                ev[f"bce_{h}"] = sDV.n
            self.on("vector", bce_step)

            def expw_step(sync, h=h):
                sync.wait_ge(sDV.h, ev[f"bce_{h}"])
                dst = bass.AP(s.d_expb.tensor, h * 128 * EXPW, [[EXPW, 128], [1, TW]])
                self.dma(SP[h % 2], sync.dma_start(out=dst, in_=s.bch[:, 0:TW]), f"expw_{h}")
            self.on("sync", expw_step)

        for wname, src_w, dst_w, ncols in (("win", s.w_in, s.d_winb, 3 * D),
                                           ("wg", s.w_gate_in, s.d_wgb, 2 * D)):
            for i in range(8):
                def wld(sync, wname=wname, i=i, src_w=src_w, ncols=ncols):
                    if i >= 3:
                        sync.wait_ge(sDV.h, ev[f"cast_{wname}_{i-3}"])
                    elif wname == "wg":
                        sync.wait_ge(sDV.h, ev[f"cast_win_{l3(8, i)}"])
                    h2 = ncols // 2
                    self.dma(SQ[1 + i % 3], sync.dma_start(
                        out=s.wtile[:, (i % 3) * F:(i % 3) * F + h2],
                        in_=src_w[i * 128:(i + 1) * 128, 0:h2]), f"wld_{wname}_{i}")
                    self.dma(SQ[4 + i % 3], sync.dma_start(
                        out=s.wtile[:, (i % 3) * F + h2:(i % 3) * F + ncols],
                        in_=src_w[i * 128:(i + 1) * 128, h2:ncols]), f"wldR_{wname}_{i}")
                self.on("sync", wld)

                def wcast(vector, wname=wname, i=i, ncols=ncols):
                    self.wt(vector, f"wld_{wname}_{i}")
                    self.wt(vector, f"wldR_{wname}_{i}")
                    if i >= 3:
                        self.wt(vector, f"wst_{wname}_{i-3}")
                    if wname == "wg" and i < 3:
                        self.wt(vector, f"wst_win_{l3(8, i)}")
                    ev[f"cast_{wname}_{i}"] = sDV.mark(vector.tensor_copy(
                        out=s.wqtile[:, (i % 3) * F:(i % 3) * F + ncols],
                        in_=s.wtile[:, (i % 3) * F:(i % 3) * F + ncols]))
                self.on("vector", wcast)

                def wst(sync, wname=wname, i=i - 2, dst_w=dst_w, ncols=ncols):
                    if i < 0:
                        return
                    sync.wait_ge(sDV.h, ev[f"cast_{wname}_{i}"])
                    self.dma(SP[2 + i % 3], sync.dma_start(
                        out=dst_w[i * 128:(i + 1) * 128, :],
                        in_=s.wqtile[:, (i % 3) * F:(i % 3) * F + ncols]),
                        f"wst_{wname}_{i}")
                self.on("sync", wst)
                if i == 7:
                    for it in (6, 7):
                        def wstt(sync, wname=wname, i=it, dst_w=dst_w, ncols=ncols):
                            sync.wait_ge(sDV.h, ev[f"cast_{wname}_{i}"])
                            self.dma(SP[2 + i % 3], sync.dma_start(
                                out=dst_w[i * 128:(i + 1) * 128, :],
                                in_=s.wqtile[:, (i % 3) * F:(i % 3) * F + ncols]),
                                f"wst_{wname}_{i}")
                        self.on("sync", wstt)

        for wname, src_w, dst_w, ntile, ncols, scol in (
                ("w1", s.w1_in, s.d_wq1, 32, D, s.s1col),
                ("w2", s.w2_in, s.d_wq2, 8, F, s.s2col)):
            nelem = float(ntile * 128 * ncols)
            nblk = ncols // 128
            for i in range(ntile):
                def ald(sync, wname=wname, i=i, src_w=src_w, ncols=ncols):
                    if i >= 3:
                        sync.wait_ge(sDV.h, ev[f"asum_{wname}_{i-3}"])
                    if i < 3 and wname == "w1":
                        sync.wait_ge(sDV.h, ev[f"cast_wg_{l3(8, i)}"])
                    if i < 3 and wname == "w2":
                        sync.wait_ge(sDV.h, ev[f"q_w1_{l3(32, i)}"])
                    h2 = ncols // 2
                    self.dma(SQ[1 + i % 3], sync.dma_start(
                        out=s.wtile[:, (i % 3) * F:(i % 3) * F + h2],
                        in_=src_w[i * 128:(i + 1) * 128, 0:h2]), f"ald_{wname}_{i}")
                    self.dma(SQ[4 + i % 3], sync.dma_start(
                        out=s.wtile[:, (i % 3) * F + h2:(i % 3) * F + ncols],
                        in_=src_w[i * 128:(i + 1) * 128, h2:ncols]), f"aldR_{wname}_{i}")
                self.on("sync", ald)

                def ared(vector, wname=wname, i=i, ncols=ncols):
                    self.wt(vector, f"ald_{wname}_{i}")
                    self.wt(vector, f"aldR_{wname}_{i}")
                    v = sDV.mark(vector.tensor_reduce(
                        out=s.cb[:, 0:1], in_=s.wtile[:, (i % 3) * F:(i % 3) * F + ncols],
                        axis=AX.X, op=ALU.add, apply_absolute_value=True))
                    vector.wait_ge(sDV.h, v)
                    if i == 0:
                        ev[f"asum_{wname}_{i}"] = sDV.mark(
                            vector.tensor_copy(out=s.cb[:, 1:2], in_=s.cb[:, 0:1]))
                    else:
                        ev[f"asum_{wname}_{i}"] = sDV.mark(vector.tensor_tensor(
                            out=s.cb[:, 1:2], in0=s.cb[:, 1:2], in1=s.cb[:, 0:1], op=ALU.add))
                self.on("vector", ared)

            def tot(tensor, wname=wname, ntile=ntile):
                tensor.wait_ge(sDV.h, ev[f"asum_{wname}_{ntile-1}"])
                tensor.wait_ge(sAC.h, rexp_ev(5, 7))
                ev[f"tot_{wname}"] = sPE.mark(tensor.matmul(
                    out=s.pC[0:1, 0:1], lhsT=s.ones128[:], rhs=s.cb[:, 1:2],
                    start=True, stop=True))
            self.on("tensor", tot)

            def sv(vector, wname=wname, nelem=nelem):
                vector.wait_ge(sPE.h, ev[f"tot_{wname}"])
                v = sDV.mark(vector.tensor_scalar(
                    out=s.scal[0:1, 0:1], in0=s.pC[0:1, 0:1], scalar1=1.0 / nelem,
                    scalar2=1e-5, op0=ALU.mult, op1=ALU.max))
                vector.wait_ge(sDV.h, v)
                ev[f"sval_{wname}"] = sDV.mark(
                    vector.reciprocal(out=s.scal[0:1, 1:2], in_=s.scal[0:1, 0:1]))
            self.on("vector", sv)

            def sbc(tensor, wname=wname):
                tensor.wait_ge(sDV.h, ev[f"sval_{wname}"])
                ev[f"sbc_{wname}"] = sPE.mark(tensor.matmul(
                    out=s.pC[:, 0:1], lhsT=s.ones1_f[:], rhs=s.scal[0:1, 1:2],
                    start=True, stop=True))
            self.on("tensor", sbc)

            def sce(vector, wname=wname, scol=scol):
                vector.wait_ge(sPE.h, ev[f"sbc_{wname}"])
                ev[f"scol_{wname}"] = sDV.mark(vector.tensor_copy(out=scol[:], in_=s.pC[:, 0:1]))
            self.on("vector", sce)

            for i in range(ntile):
                def qld(sync, wname=wname, i=i, src_w=src_w, ncols=ncols, ntile=ntile):
                    if i >= 3:
                        sync.wait_ge(sDV.h, ev[f"q_{wname}_{i-3}"])
                    if i < 3:
                        sync.wait_ge(sDV.h, ev[f"asum_{wname}_{l3(ntile, i)}"])
                    h2 = ncols // 2
                    self.dma(SQ[1 + i % 3], sync.dma_start(
                        out=s.wtile[:, (i % 3) * F:(i % 3) * F + h2],
                        in_=src_w[i * 128:(i + 1) * 128, 0:h2]), f"qld_{wname}_{i}")
                    self.dma(SQ[4 + i % 3], sync.dma_start(
                        out=s.wtile[:, (i % 3) * F + h2:(i % 3) * F + ncols],
                        in_=src_w[i * 128:(i + 1) * 128, h2:ncols]), f"qldR_{wname}_{i}")
                self.on("sync", qld)
                def wqst(sync, wname=wname, i=i - 2, dst_w=dst_w, ncols=ncols, nblk=nblk):
                    if i < 0:
                        return
                    sync.wait_ge(sAC.h, ev[f"tev_{wname}_{i}_{nblk-1}"])
                    if wname == "w1" and i < 3:
                        self.wt(sync, f"wst_wg_{l3(8, i)}")
                    if wname == "w2" and i < 3:
                        self.wt(sync, f"wqst_w1_{l3(32, i)}")
                    pitch = F if wname == "w1" else D
                    dst = bass.AP(dst_w.tensor, i * 128,
                                  [[pitch, 128], [128 * pitch, nblk], [1, 128]])
                    self.dma(SP[2 + i % 3], sync.dma_start(
                        out=dst, in_=s.stg8[:, (i % 2) * ncols:(i % 2) * ncols + ncols].rearrange(
                            "p (j n) -> p j n", j=nblk)), f"wqst_{wname}_{i}")
                self.on("sync", wqst)

                def qnt1(vector, wname=wname, i=i, ncols=ncols, scol=scol):
                    self.wt(vector, f"qld_{wname}_{i}")
                    self.wt(vector, f"qldR_{wname}_{i}")
                    if i == 0:
                        vector.wait_ge(sDV.h, ev[f"scol_{wname}"])
                    vector.tensor_scalar(
                        out=s.wtile[:, (i % 3) * F:(i % 3) * F + ncols],
                        in0=s.wtile[:, (i % 3) * F:(i % 3) * F + ncols],
                        scalar1=scol[:, 0:1], scalar2=MAGIC,
                        op0=ALU.mult, op1=ALU.add)
                    ev[f"q1_{wname}_{i}"] = sDV.mark(vector.tensor_scalar(
                        out=s.wtile[:, (i % 3) * F:(i % 3) * F + ncols],
                        in0=s.wtile[:, (i % 3) * F:(i % 3) * F + ncols],
                        scalar1=MAGIC, scalar2=1.0,
                        op0=ALU.subtract, op1=ALU.min))
                self.on("vector", qnt1)

                def qnt2(vector, wname=wname, i=i, ncols=ncols, nblk=nblk):
                    if i >= 3:
                        vector.wait_ge(sPE.h, ev[f"trp_{wname}_{i-3}_{nblk-1}"])
                    ev[f"q_{wname}_{i}"] = sDV.mark(vector.tensor_scalar(
                        out=s.wqtile[:, (i % 3) * ncols:(i % 3) * ncols + ncols],
                        in0=s.wtile[:, (i % 3) * F:(i % 3) * F + ncols],
                        scalar1=-1.0, scalar2=None, op0=ALU.max))
                self.on("vector", qnt2)

                nblk = ncols // 128
                for rr in range((nblk + 1) // 2):
                    def trp(tensor, wname=wname, i=i, rr=rr, nblk=nblk, ncols=ncols):
                        for half in range(2):
                            j = rr * 2 + half
                            if j >= nblk:
                                continue
                            if rr == 0 and half == 0:
                                tensor.wait_ge(sDV.h, ev[f"q_{wname}_{i}"])
                            pb = s.pT1 if j % 2 == 0 else s.pT2
                            if i > 0 or j >= 2:
                                key = (f"tev_{wname}_{i}_{j-2}" if j >= 2
                                       else f"tev_{wname}_{i-1}_{nblk - 2 + j}")
                                tensor.wait_ge(sAC.h, ev[key])
                            ev[f"trp_{wname}_{i}_{j}"] = sPE.mark(tensor.transpose(
                                out=pb[:, 0:128],
                                in_=s.wqtile[:, (i % 3) * ncols + j * 128:(i % 3) * ncols + (j + 1) * 128],
                                identity=s.ident_bf[:]))
                    self.on("tensor", trp)

                    def tev(scalar, wname=wname, i=i, rr=rr, nblk=nblk, ncols=ncols):
                        for half in range(2):
                            j = rr * 2 + half
                            if j >= nblk:
                                continue
                            pb = s.pT1 if j % 2 == 0 else s.pT2
                            scalar.wait_ge(sPE.h, ev[f"trp_{wname}_{i}_{j}"])
                            if j == 0 and i >= 2:
                                self.wt(scalar, f"wqst_{wname}_{i-2}")
                            ev[f"tev_{wname}_{i}_{j}"] = sAC.mark(scalar.activation(
                                out=s.stg8[:, (i % 2) * ncols + j * 128:(i % 2) * ncols + (j + 1) * 128],
                                in_=pb[:, 0:128], func=AF.Copy))
                    self.on("scalar", tev)

                if i == ntile - 1:
                    for it in (ntile - 2, ntile - 1):
                        def wqstt(sync, wname=wname, i=it, dst_w=dst_w, ncols=ncols, nblk=nblk):
                            sync.wait_ge(sAC.h, ev[f"tev_{wname}_{i}_{nblk-1}"])
                            pitch = F if wname == "w1" else D
                            dst = bass.AP(dst_w.tensor, i * 128,
                                          [[pitch, 128], [128 * pitch, nblk], [1, 128]])
                            self.dma(SP[2 + i % 3], sync.dma_start(
                                out=dst, in_=s.stg8[:, (i % 2) * ncols:(i % 2) * ncols + ncols].rearrange(
                                    "p (j n) -> p j n", j=nblk)), f"wqst_{wname}_{i}")
                        self.on("sync", wqstt)



        dvb0 = sDV.n

        def bbe_ev(c, dvb0=dvb0):
            return dvb0 + c + 1

        def bbm(tensor):
            tensor.wait_ge(sDV.h, ev[f"bce_{H-1}"])
            for c in range(10):
                pb = s.pA if c % 2 == 0 else s.pB
                if c >= 2:
                    tensor.wait_ge(sDV.h, bbe_ev(c - 2))
                rhs = (s.b1row[0:1, c * 512:(c + 1) * 512] if c < 8
                       else s.b2row[0:1, (c - 8) * 512:(c - 7) * 512])
                ev[f"bbm_{c}"] = sPE.mark(tensor.matmul(
                    out=pb[:], lhsT=s.ones1_f[:], rhs=rhs, start=True, stop=True))
        self.on("tensor", bbm)

        def bbe(vector):
            for c in range(10):
                pb = s.pA if c % 2 == 0 else s.pB
                vector.wait_ge(sPE.h, ev[f"bbm_{c}"])
                dst = (s.b1b[:, c * 512:(c + 1) * 512] if c < 8
                       else s.b2b[:, (c - 8) * 512:(c - 7) * 512])
                v = sDV.mark(vector.tensor_copy(out=dst, in_=pb[:]))
                ev[f"bbe_{c}"] = v
                assert v == bbe_ev(c), (v, c)
        self.on("vector", bbe)

    def p1_ln_quant(self, b):
        s, ev = self, self.ev
        SQ, SP = s.SQ, s.SP
        sPE, sDV, sAC, sPL = s.sPE, s.sDV, s.sAC, s.sPL

        tpcb = self.tap(f"p1cb_b{b}", [128, 20 * NTT], F32)

        for tt in range(NTT):
            def xld(sync, b=b, tt=tt):
                if tt >= 2:
                    sync.wait_ge(sDV.h, ev[f"xnt_{b}_{tt-2}"])
                self.dma(SQ[1 + tt % 2], sync.dma_start(
                    out=s.xt[:, (tt % 2) * D:(tt % 2) * D + D],
                    in_=s.x_in[b, tt * 128:(tt + 1) * 128, :]), f"xld_{b}_{tt}")
            self.on("sync", xld)
            def xq_w(sync, b=b, tt=tt - 1):
                if tt < 0:
                    return
                sync.wait_ge(sAC.h, ev[f"xqe_{b}_{tt}"])
                dst = s.d_xq[b, :, :, tt * 128:(tt + 1) * 128]
                src = s.xqs[:].rearrange("p (dc n) -> p dc n", dc=8)
                self.dma(SP[tt % 2], sync.dma_start(out=dst.transpose([1, 0, 2]), in_=src),
                         f"xqw_{b}_{tt}")
            self.on("sync", xq_w)

            def stats_a(gpsimd, b=b, tt=tt):
                self.wt(gpsimd, f"xld_{b}_{tt}")
                if tt >= 1:
                    gpsimd.wait_ge(sDV.h, ev[f"rden_{b}_{tt-1}"])
                ev[f"ss_{b}_{tt}"] = sPL.mark(gpsimd.tensor_tensor(
                    out=s.dump[:], in0=s.xt[:, (tt % 2) * D:(tt % 2) * D + D],
                    in1=s.xt[:, (tt % 2) * D:(tt % 2) * D + D], op=ALU.mult))
            self.on("gpsimd", stats_a)

            def stats_v(vector, b=b, tt=tt):
                self.wt(vector, f"xld_{b}_{tt}")
                v = sDV.mark(vector.tensor_reduce(out=s.cb[:, 0:1], in_=s.xt[:, (tt % 2) * D:(tt % 2) * D + D],
                                                  axis=AX.X, op=ALU.add))
                vector.wait_ge(sPL.h, ev[f"ss_{b}_{tt}"])
                v = sDV.mark(vector.tensor_reduce(out=s.cb[:, 1:2], in_=s.dump[:],
                                                  axis=AX.X, op=ALU.add))
                vector.wait_ge(sDV.h, v)
                v = sDV.mark(vector.tensor_scalar(out=s.cb[:, 2:3], in0=s.cb[:, 0:1],
                                                  scalar1=1.0 / D, scalar2=None, op0=ALU.mult))
                vector.wait_ge(sDV.h, v)
                v = sDV.mark(vector.tensor_scalar(out=s.cb[:, 3:4], in0=s.cb[:, 2:3],
                                                  scalar1=s.cb[:, 2:3], scalar2=None,
                                                  op0=ALU.mult))
                vector.wait_ge(sDV.h, v)
                ev[f"var_{b}_{tt}"] = sDV.mark(vector.scalar_tensor_tensor(
                    out=s.cb[:, 4:5], in0=s.cb[:, 1:2], scalar=1.0 / D, in1=s.cb[:, 3:4],
                    op0=ALU.mult, op1=ALU.subtract))
            self.on("vector", stats_v)

            def rstd_a(scalar, b=b, tt=tt):
                scalar.wait_ge(sDV.h, ev[f"var_{b}_{tt}"])
                ev[f"sqv_{b}_{tt}"] = sAC.mark(scalar.activation(
                    out=s.cb[:, 5:6], in_=s.cb[:, 4:5], func=AF.Sqrt, bias=s.epscol[:, 0:1]))
            self.on("scalar", rstd_a)

            def ln_ap(vector, b=b, tt=tt):
                vector.wait_ge(sAC.h, ev[f"sqv_{b}_{tt}"])
                v = sDV.mark(vector.reciprocal(out=s.cb[:, 6:7], in_=s.cb[:, 5:6]))
                vector.wait_ge(sDV.h, v)
                if tt >= 1:
                    vector.wait_ge(sAC.h, ev[f"xnc_{b}_{tt-1}"])
                    vector.wait_ge(sPL.h, ev[f"xnq_{b}_{tt-1}"])
                ev[f"xnt_{b}_{tt}"] = sDV.mark(vector.tensor_scalar(
                    out=s.xnt[:], in0=s.xt[:, (tt % 2) * D:(tt % 2) * D + D], scalar1=s.cb[:, 2:3],
                    scalar2=s.cb[:, 6:7], op0=ALU.subtract, op1=ALU.mult))
            self.on("vector", ln_ap)

            def xn_a(scalar, b=b, tt=tt):
                scalar.wait_ge(sDV.h, ev[f"xnt_{b}_{tt}"])
                if tt >= 1:
                    scalar.wait_ge(sPE.h, ev[f"xntr_{b}_{tt-1}"])
                ev[f"xnc_{b}_{tt}"] = sAC.mark(
                    scalar.activation(out=s.xnbt[:], in_=s.xnt[:], func=AF.Copy))
            self.on("scalar", xn_a)

            def xn_sq(gpsimd, b=b, tt=tt):
                gpsimd.wait_ge(sDV.h, ev[f"xnt_{b}_{tt}"])
                ev[f"xnq_{b}_{tt}"] = sPL.mark(gpsimd.tensor_tensor(
                    out=s.dump[:], in0=s.xnt[:], in1=s.xnt[:], op=ALU.mult))
            self.on("gpsimd", xn_sq)

            def q_cols(vector, b=b, tt=tt):
                vector.wait_ge(sPL.h, ev[f"xnq_{b}_{tt}"])
                v = sDV.mark(vector.tensor_reduce(out=s.cb[:, 8:9], in_=s.xnt[:], axis=AX.X,
                                                  op=ALU.max, apply_absolute_value=True))
                v = sDV.mark(vector.tensor_reduce(out=s.cb[:, 7:8], in_=s.dump[:],
                                                  axis=AX.X, op=ALU.add))
                vector.wait_ge(sDV.h, v)
                v = sDV.mark(vector.tensor_scalar(out=s.cb[:, 9:10], in0=s.cb[:, 7:8],
                                                  scalar1=1.0 / D, scalar2=EPS,
                                                  op0=ALU.mult, op1=ALU.add))
                ev[f"rden_{b}_{tt}"] = v
            self.on("vector", q_cols)

            def rsq_a(scalar, b=b, tt=tt):
                scalar.wait_ge(sDV.h, ev[f"rden_{b}_{tt}"])
                ev[f"rsq_{b}_{tt}"] = sAC.mark(scalar.activation(
                    out=s.cb[:, 10:11], in_=s.cb[:, 9:10], func=AF.Sqrt))
            self.on("scalar", rsq_a)

            def q_fin(vector, b=b, tt=tt):
                vector.wait_ge(sAC.h, ev[f"rsq_{b}_{tt}"])
                v = sDV.mark(vector.reciprocal(out=s.cb[:, 11:12], in_=s.cb[:, 10:11]))
                vector.wait_ge(sDV.h, v)
                v = sDV.mark(vector.tensor_scalar(out=s.cb[:, 12:13], in0=s.cb[:, 8:9],
                                                  scalar1=s.cb[:, 11:12], scalar2=1e-5,
                                                  op0=ALU.mult, op1=ALU.max))
                vector.wait_ge(sDV.h, v)
                v = sDV.mark(vector.reciprocal(out=s.cb[:, 13:14], in_=s.cb[:, 12:13]))
                vector.wait_ge(sDV.h, v)
                v = sDV.mark(vector.tensor_scalar(out=s.cb[:, 14:15], in0=s.cb[:, 13:14],
                                                  scalar1=127.0, scalar2=None, op0=ALU.mult))
                vector.wait_ge(sDV.h, v)
                v = sDV.mark(vector.tensor_scalar(out=s.cb[:, 15:16], in0=s.cb[:, 11:12],
                                                  scalar1=s.cb[:, 14:15], scalar2=None,
                                                  op0=ALU.mult))
                vector.wait_ge(sDV.h, v)
                v = sDV.mark(vector.tensor_scalar(out=s.cb[:, 16:17], in0=s.cb[:, 14:15],
                                                  scalar1=s.s1col[:, 0:1], scalar2=None,
                                                  op0=ALU.mult))
                vector.wait_ge(sDV.h, v)
                sDV.mark(vector.reciprocal(out=s.dq1c[:, b * NTT + tt:b * NTT + tt + 1],
                                           in_=s.cb[:, 16:17]))
                if tt >= 1:
                    vector.wait_ge(sPE.h, ev[f"xqtr_{b}_{tt-1}"])
                vector.tensor_scalar(out=s.rtmp[:], in0=s.xnt[:], scalar1=s.cb[:, 15:16],
                                     scalar2=MAGIC, op0=ALU.mult, op1=ALU.add)
                ev[f"p1q_{b}_{tt}"] = sDV.mark(vector.tensor_scalar(
                    out=s.xqbt[:], in0=s.rtmp[:], scalar1=MAGIC, scalar2=None,
                    op0=ALU.subtract))
                if getattr(s, "p1cb", None) is not None:
                    ev[f"p1cb_{b}_{tt}"] = sDV.mark(vector.tensor_copy(
                        out=s.p1cb[:][:, tt * 20:tt * 20 + 20], in_=s.cb[:, 0:20]))
            self.on("vector", q_fin)

            for rnd in range(8):
                def tr_p(tensor, b=b, tt=tt, rnd=rnd):
                    for half in range(2):
                        blk = rnd * 2 + half
                        pb = s.pT1 if blk % 2 == 0 else s.pT2
                        if blk == 0:
                            tensor.wait_ge(sAC.h, ev[f"xnc_{b}_{tt}"])
                        if blk == 8:
                            tensor.wait_ge(sDV.h, ev[f"p1q_{b}_{tt}"])
                        if tt > 0 or blk >= 2:
                            key = (f"trev_{b}_{tt}_{blk-2}" if blk >= 2
                                   else f"trev_{b}_{tt-1}_{14 + blk}")
                            tensor.wait_ge(sAC.h, ev[key])
                        src = (s.xnbt[:, blk * 128:(blk + 1) * 128] if blk < 8
                               else s.xqbt[:, (blk - 8) * 128:(blk - 7) * 128])
                        ev[f"tr_{b}_{tt}_{blk}"] = sPE.mark(tensor.transpose(
                            out=pb[:, 0:128], in_=src, identity=s.ident_bf[:]))
                    if rnd == 7:
                        ev[f"xqtr_{b}_{tt}"] = sPE.n
                        ev[f"xntr_{b}_{tt}"] = sPE.n
                self.on("tensor", tr_p)

                def tr_e(scalar, b=b, tt=tt, rnd=rnd):
                    for half in range(2):
                        blk = rnd * 2 + half
                        pb = s.pT1 if blk % 2 == 0 else s.pT2
                        scalar.wait_ge(sPE.h, ev[f"tr_{b}_{tt}_{blk}"])
                        if blk < 8:
                            dst = s.xn_fm[:, blk * S + tt * 128:blk * S + (tt + 1) * 128]
                        else:
                            if blk == 8 and tt > 0:
                                self.wt(scalar, f"xqw_{b}_{tt-1}")
                            dst = s.xqs[:, (blk - 8) * 128:(blk - 7) * 128]
                        ev[f"trev_{b}_{tt}_{blk}"] = sAC.mark(scalar.activation(
                            out=dst, in_=pb[:, 0:128], func=AF.Copy))
                    if rnd == 7:
                        ev[f"xqe_{b}_{tt}"] = sAC.n
                        ev[f"xne_{b}_{tt}"] = sAC.n
                self.on("scalar", tr_e)



        def xq_wl(sync, b=b, tt=NTT - 1):
            sync.wait_ge(sAC.h, ev[f"xqe_{b}_{tt}"])
            dst = s.d_xq[b, :, :, tt * 128:(tt + 1) * 128]
            src = s.xqs[:].rearrange("p (dc n) -> p dc n", dc=8)
            self.dma(SP[tt % 2], sync.dma_start(out=dst.transpose([1, 0, 2]), in_=src),
                     f"xqw_{b}_{tt}")
        self.on("sync", xq_wl)

        tp = self.tap(f"xn_fm_b{b}", [128, 8 * S], BF16)
        if tp is not None:
            def tapw(gpsimd, b=b, tp=tp):
                gpsimd.wait_ge(s.sAC.h, ev[f"xqe_{b}_{NTT-1}"])
                self.dma(s.SP[4], gpsimd.dma_start(out=tp, in_=s.xn_fm[:]))
            self.on("gpsimd", tapw)
        tpcb2 = self.tap_outs.get(f"p1cb_b{b}")
        if tpcb2 is not None:
            def tapw2(gpsimd, b=b, tpcb2=tpcb2):
                gpsimd.wait_ge(s.sDV.h, ev[f"p1cb_{b}_{NTT-1}"])
                self.dma(s.SP[4], gpsimd.dma_start(out=tpcb2, in_=s.p1cb[:]))
            self.on("gpsimd", tapw2)

    def p2_proj(self, b):
        s, ev = self, self.ev
        SQ, SP = s.SQ, s.SP
        sPE, sDV, sAC, sPL = s.sPE, s.sDV, s.sAC, s.sPL

        tpq1 = self.tap("wq1t", [D, F], BF16)
        if tpq1 is not None:
            def tapq1(gpsimd):
                self.wt(gpsimd, "wqst_w1_31")
                self.dma(SP[4], gpsimd.dma_start(out=tpq1, in_=s.d_wq1))
            self.on("gpsimd", tapq1)

        tpw = self.tap("winb", [D, 3 * D], BF16)
        if tpw is not None and b == 0:
            def tapwb(gpsimd):
                self.wt(gpsimd, "wst_win_7")
                self.dma(SP[4], gpsimd.dma_start(out=tpw, in_=s.d_winb))
            self.on("gpsimd", tapwb)

        ac2 = sAC.n

        def pje_ev(fs, tc):
            return ac2 + fs * 4 + tc + 1

        for fs in range(16):
            wcol_off = fs * 128 if fs < 8 else 2 * D + (fs - 8) * 128

            def wcl(sync, b=b, fs=fs, wcol_off=wcol_off):
                if fs >= 2:
                    sync.wait_ge(sPE.h, ev[f"pjm_{b}_{fs-2}_3"])
                src = bass.AP(s.d_winb.tensor, wcol_off,
                              [[3 * D, 128], [128 * 3 * D, 8], [1, 128]])
                self.dma(SQ[1 + fs % 2], sync.dma_start(
                    out=s.wcolb[:, (fs % 2) * 1024:(fs % 2) * 1024 + 1024].rearrange("p (dc n) -> p dc n", dc=8),
                    in_=src), f"wcl_{b}_{fs}")
            self.on("sync", wcl)

            def pj(tensor, b=b, fs=fs):
                self.wt(tensor, f"wcl_{b}_{fs}")
                for tc in range(4):
                    pb = s.pA if tc % 2 == 0 else s.pB
                    pv = (pje_ev(fs, tc - 2) if tc >= 2
                          else (pje_ev(fs - 1, 2 + tc) if fs >= 1 else None))
                    if pv is not None:
                        tensor.wait_ge(sAC.h, pv)
                    for dc in range(8):
                        ins = tensor.matmul(
                            out=pb[:], lhsT=s.wcolb[:, (fs % 2) * 1024 + dc * 128:(fs % 2) * 1024 + (dc + 1) * 128],
                            rhs=s.xn_fm[:, dc * S + tc * 512:dc * S + (tc + 1) * 512],
                            start=(dc == 0), stop=(dc == 7))
                    ev[f"pjm_{b}_{fs}_{tc}"] = sPE.mark(ins)
            self.on("tensor", pj)

            def pje(scalar, b=b, fs=fs):
                for tc in range(4):
                    pb = s.pA if tc % 2 == 0 else s.pB
                    scalar.wait_ge(sPE.h, ev[f"pjm_{b}_{fs}_{tc}"])
                    if fs < 8:
                        dst = s.k_fm[:, fs * S + tc * 512:fs * S + (tc + 1) * 512]
                        sc = 1.0
                    else:
                        dst = s.q_fm[:, (fs - 8) * S + tc * 512:(fs - 8) * S + (tc + 1) * 512]
                        sc = 0.125
                    v = sAC.mark(scalar.activation(
                        out=dst, in_=pb[:], func=AF.Copy, scale=sc))
                    ev[f"pje_{b}_{fs}_{tc}"] = v
                    assert v == pje_ev(fs, tc), (v, fs, tc)
            self.on("scalar", pje)

        def vones(vector, b=b):
            ev[f"vones_{b}"] = sDV.mark(vector.memset(s.v_tm[:], 1.0))
        self.on("vector", vones)

        def wvl(sync, b=b):
            src = bass.AP(s.d_winb.tensor, D,
                          [[3 * D, 128], [128 * 3 * D, 8], [1, 1024]])
            self.dma(SQ[3], sync.dma_start(
                out=s.wvb[:].rearrange("p (dc n) -> p dc n", dc=8), in_=src), f"wvl_{b}")
        self.on("sync", wvl)

        for tt in range(NTT):
            def vpj(tensor, b=b, tt=tt):
                if tt == 0:
                    self.wt(tensor, f"wvl_{b}")
                for vc in range(2):
                    pb = s.pC if vc == 0 else s.pD
                    if tt >= 1:
                        tensor.wait_ge(sDV.h, ev[f"vpe_{b}_{tt-1}_{vc}"])
                    for dc in range(8):
                        ins = tensor.matmul(
                            out=pb[:], lhsT=s.xn_fm[:, dc * S + tt * 128:dc * S + (tt + 1) * 128],
                            rhs=s.wvb[:, dc * 1024 + vc * 512:dc * 1024 + (vc + 1) * 512],
                            start=(dc == 0), stop=(dc == 7))
                    ev[f"vpm_{b}_{tt}_{vc}"] = sPE.mark(ins)
            self.on("tensor", vpj)

            def vpe(vector, b=b, tt=tt):
                if tt == 0:
                    vector.wait_ge(sDV.h, ev[f"vones_{b}"])
                for vc in range(2):
                    pb = s.pC if vc == 0 else s.pD
                    vector.wait_ge(sPE.h, ev[f"vpm_{b}_{tt}_{vc}"])
                    base = tt * 1040 + vc * 520
                    dst = s.v_tm[:, base:base + 520].rearrange(
                        "p (g c) -> p g c", g=8, c=65)[:, :, 0:64]
                    src = pb[:].rearrange("p (g c) -> p g c", g=8, c=64)
                    ev[f"vpe_{b}_{tt}_{vc}"] = sDV.mark(vector.tensor_copy(out=dst, in_=src))
            self.on("vector", vpe)

        tpk = self.tap(f"k_fm_b{b}", [128, 8 * S], BF16)
        tpq = self.tap(f"q_fm_b{b}", [128, 8 * S], BF16)
        tpv = self.tap(f"v_tm_b{b}", [128, NTT * 1040], BF16)
        if tpk is not None:
            def tapw(gpsimd, b=b):
                gpsimd.wait_ge(s.sAC.h, ev[f"pje_{b}_15_3"])
                gpsimd.wait_ge(s.sDV.h, ev[f"vpe_{b}_{NTT-1}_1"])
                self.dma(s.SP[4], gpsimd.dma_start(out=tpk, in_=s.k_fm[:]))
                self.dma(s.SP[4], gpsimd.dma_start(out=tpq, in_=s.q_fm[:]))
                self.dma(s.SP[4], gpsimd.dma_start(out=tpv, in_=s.v_tm[:]))
            self.on("gpsimd", tapw)

    def p3_attn(self, b):
        s, ev = self, self.ev
        SQ, SP = s.SQ, s.SP
        sPE, sDV, sAC, sPL = s.sPE, s.sDV, s.sAC, s.sPL

        SKW = 3968

        NG = 4 * H
        NS = 20
        qk_ev, pv_ev, rbm_ev = {}, {}, {}
        n = sPE.n
        for m in range(NG):
            if m >= 2:
                n += 1
                rbm_ev[m - 2] = n
            for step in range(4):
                for j in range(4):
                    n += 1
                    qk_ev[(m, 4 * step + j)] = n
                if m >= 1:
                    for j in range(4):
                        n += 1
                        pv_ev[(m - 1, 4 * step + j)] = n
        for step in range(4):
            for j in range(4):
                n += 1
                pv_ev[(NG - 1, 4 * step + j)] = n
        n += 1
        rbm_ev[NG - 2] = n
        n += 1
        rbm_ev[NG - 1] = n

        exp_ev, dc_ev = {}, {}
        n = sAC.n
        for m in range(NG):
            for kt in range(16):
                n += 1
                exp_ev[(m, kt)] = n
            if m >= 1:
                n += 1
                dc_ev[m - 1] = n
        n += 1
        dc_ev[NG - 1] = n

        pool_ev, rd_ev, ae_ev = {}, {}, {}
        n = sDV.n
        for m in range(NG):
            if m >= 2:
                n += 1
                rd_ev[m - 2] = n
                n += 1
                ae_ev[m - 2] = n
            for kt in range(16):
                n += 1
                pool_ev[(m, kt)] = n
        for g in (NG - 2, NG - 1):
            n += 1
            rd_ev[g] = n
            n += 1
            ae_ev[g] = n

        psq = [s.pA, s.pB, s.pC]

        def bank(m, kt):
            return (m + kt) % 3

        def hof(g):
            h, qc = g // 4, g % 4
            return (h % 2) * 64, (h // 2) * S, qc

        def pOx(g):
            return s.pO if g % 2 == 0 else s.pO2

        def pe_all(tensor, b=b):
            def qk(m, kt):
                hh, hc, qc = hof(m)
                T = 16 * m + kt
                if T >= 3:
                    pm, pk = (m, kt - 3) if kt >= 3 else (m - 1, 13 + kt)
                    tensor.wait_ge(sAC.h, exp_ev[(pm, pk)])
                v = sPE.mark(tensor.matmul(
                    out=psq[bank(m, kt)][:],
                    lhsT=s.k_fm[hh:hh + 64, hc + kt * 128:hc + (kt + 1) * 128],
                    rhs=s.q_fm[hh:hh + 64, hc + qc * 512:hc + (qc + 1) * 512],
                    start=True, stop=True))
                assert v == qk_ev[(m, kt)]

            def pv(g, kt):
                hh, hc, qc = hof(g)
                h = g // 4
                if kt == 0 and g >= 2:
                    tensor.wait_ge(sDV.h, ae_ev[g - 2])
                    tensor.wait_ge(sAC.h, dc_ev[g - 2])
                tensor.wait_ge(sDV.h, pool_ev[(g, kt)])
                v = sPE.mark(tensor.matmul(
                    out=pOx(g)[0:65, :],
                    lhsT=s.v_tm[:, kt * 1040 + h * 65:kt * 1040 + h * 65 + 65],
                    rhs=s.emul[:, ((16 * g + kt) % NS) * 512:((16 * g + kt) % NS) * 512 + 512],
                    start=(kt == 0), stop=(kt == 15)))
                assert v == pv_ev[(g, kt)]

            def rbm(g):
                tensor.wait_ge(sAC.h, dc_ev[g])
                if g >= 1:
                    tensor.wait_ge(sDV.h, rd_ev[g - 1])
                v = sPE.mark(tensor.matmul(
                    out=s.pD[0:64, :], lhsT=s.onesp64[64:65, :],
                    rhs=s.rdbuf[64:65, :], start=True, stop=True))
                assert v == rbm_ev[g]

            for m in range(NG):
                if m >= 2:
                    rbm(m - 2)
                for step in range(4):
                    for j in range(4):
                        qk(m, 4 * step + j)
                    if m >= 1:
                        for j in range(4):
                            pv(m - 1, 4 * step + j)
            for step in range(4):
                for j in range(4):
                    pv(NG - 1, 4 * step + j)
            rbm(NG - 2)
            rbm(NG - 1)
        self.on("tensor", pe_all)

        def act_all(scalar, b=b):
            def dcopy(g):
                scalar.wait_ge(sPE.h, pv_ev[(g, 15)])
                if g >= 1:
                    scalar.wait_ge(sPE.h, rbm_ev[g - 1])
                v = sAC.mark(scalar.activation(
                    out=s.rdbuf[64:65, :], in_=pOx(g)[64:65, :], func=AF.Copy))
                assert v == dc_ev[g]

            for m in range(NG):
                for kt in range(16):
                    scalar.wait_ge(sPE.h, qk_ev[(m, kt)])
                    T = 16 * m + kt
                    if T >= 3:
                        pm, pk = (m, kt - 3) if kt >= 3 else (m - 1, 13 + kt)
                        scalar.wait_ge(sDV.h, pool_ev[(pm, pk)])
                    v = sAC.mark(scalar.activation(
                        out=s.eraw[:, bank(m, kt) * 512:bank(m, kt) * 512 + 512],
                        in_=psq[bank(m, kt)][:], func=AF.Exp))
                    assert v == exp_ev[(m, kt)]
                if m >= 1:
                    dcopy(m - 1)
            dcopy(NG - 1)
        self.on("scalar", act_all)

        def sync_all(sync, b=b):
            for h in range(H):
                if h >= 3:
                    sync.wait_ge(sDV.h, pool_ev[(4 * (h - 3) + 3, 15)])
                src = bass.AP(s.d_expb.tensor, h * 128 * EXPW + 127,
                              [[EXPW - 1, 128], [1, SKW]])
                self.dma(SP[2 + h % 3], sync.dma_start(
                    out=s.skw[:, (h % 3) * SKW:(h % 3) * SKW + SKW], in_=src),
                    f"skw_{b}_{h}")
        self.on("sync", sync_all)

        def dve_all(vector, b=b):
            def fin(g):
                vector.wait_ge(sPE.h, rbm_ev[g])
                with s.nc.allow_low_precision("softmax 1/denom applied to bf16 weights"):
                    v = sDV.mark(vector.reciprocal(
                        out=s.rbs[:, (g % 2) * 512:(g % 2) * 512 + 512],
                        in_=s.pD[0:64, :]))
                assert v == rd_ev[g]
                vector.wait_ge(sDV.h, v)
                hh, hc, qc = hof(g)
                v = sDV.mark(vector.tensor_tensor(
                    out=s.a_fm[hh:hh + 64, hc + qc * 512:hc + (qc + 1) * 512],
                    in0=pOx(g)[0:64, :],
                    in1=s.rbs[:, (g % 2) * 512:(g % 2) * 512 + 512], op=ALU.mult))
                assert v == ae_ev[g]
                ev[f"ae_{b}_{g}"] = v

            for m in range(NG):
                if m >= 2:
                    fin(m - 2)
                h = m // 4
                for kt in range(16):
                    if m % 4 == 0 and kt == 0:
                        self.wt(vector, f"skw_{b}_{h}")
                    vector.wait_ge(sAC.h, exp_ev[(m, kt)])
                    T = 16 * m + kt
                    if T >= NS:
                        pg, pk = divmod(T - NS, 16)
                        vector.wait_ge(sPE.h, pv_ev[(pg, pk)])
                    j0 = 1920 + (m % 4) * 512 - kt * 128
                    v = sDV.mark(vector.tensor_tensor(
                        out=s.emul[:, (T % NS) * 512:(T % NS) * 512 + 512],
                        in0=s.eraw[:, bank(m, kt) * 512:bank(m, kt) * 512 + 512],
                        in1=s.skw[:, (h % 3) * SKW + j0:(h % 3) * SKW + j0 + 512],
                        op=ALU.mult))
                    assert v == pool_ev[(m, kt)]
            fin(NG - 2)
            fin(NG - 1)
        self.on("vector", dve_all)



        tpa = self.tap(f"a_fm_b{b}", [128, 8 * S], BF16)
        if tpa is not None:
            def tapw(gpsimd, b=b, tpa=tpa):
                gpsimd.wait_ge(s.sDV.h, ev[f"ae_{b}_{4 * H - 1}"])
                self.dma(s.SP[4], gpsimd.dma_start(out=tpa, in_=s.a_fm[:]))
            self.on("gpsimd", tapw)

    def p4_gate(self, b):
        s, ev = self, self.ev
        SQ, SP = s.SQ, s.SP
        sPE, sDV, sAC, sPL = s.sPE, s.sDV, s.sAC, s.sPL
        banks = [s.pA, s.pC, s.pB, s.pD]

        def wgl(sync, b=b):
            src = bass.AP(s.d_wgb.tensor, 0, [[2 * D, 128], [128 * 2 * D, 8], [1, 2 * D]])
            self.dma(SQ[3], sync.dma_start(
                out=s.wgs[:].rearrange("p (dc n) -> p dc n", dc=8), in_=src), f"wgl_{b}")
        self.on("sync", wgl)

        for tt in range(NTT):
            def gm(tensor, b=b, tt=tt):
                if tt == 0:
                    self.wt(tensor, f"wgl_{b}")
                for gc in range(4):
                    pb = banks[gc]
                    if tt >= 1:
                        if gc in (2, 3):
                            tensor.wait_ge(sAC.h, ev[f"sg_{b}_{tt-1}_{gc-2}"])
                        else:
                            tensor.wait_ge(sDV.h, ev[f"ab_{b}_{tt-1}_{gc}"])
                    for dc in range(8):
                        ins = tensor.matmul(
                            out=pb[:], lhsT=s.a_fm[:, dc * S + tt * 128:dc * S + (tt + 1) * 128],
                            rhs=s.wgs[:, dc * 2048 + gc * 512:dc * 2048 + (gc + 1) * 512],
                            start=(dc == 0), stop=False)
                    ins = tensor.matmul(
                        out=pb[:], lhsT=s.ones1_bf[:],
                        rhs=s.bgate_bf[0:1, gc * 512:(gc + 1) * 512], start=False, stop=True)
                    ev[f"gm_{b}_{tt}_{gc}"] = sPE.mark(ins)
            self.on("tensor", gm)

            for half in range(2):
                def sgst(scalar, b=b, tt=tt, half=half):
                    scalar.wait_ge(sPE.h, ev[f"gm_{b}_{tt}_{2 + half}"])
                    if tt >= 1 or half == 1:
                        key = f"ab_{b}_{tt}_{half-1}" if half == 1 else f"ab_{b}_{tt-1}_1"
                        scalar.wait_ge(sDV.h, ev[key])
                    ev[f"sg_{b}_{tt}_{half}"] = sAC.mark(scalar.activation(
                        out=s.sgb[:], in_=banks[2 + half][:], func=AF.Sigmoid))
                self.on("scalar", sgst)

                def abst(vector, b=b, tt=tt, half=half):
                    vector.wait_ge(sAC.h, ev[f"sg_{b}_{tt}_{half}"])
                    vector.wait_ge(sPE.h, ev[f"gm_{b}_{tt}_{half}"])
                    if tt >= 2:
                        self.wt(vector, f"abw_{b}_{tt-2}")
                    ev[f"ab_{b}_{tt}_{half}"] = sDV.mark(vector.tensor_tensor(
                        out=s.abb[:, (tt % 2) * 1024 + half * 512:(tt % 2) * 1024 + (half + 1) * 512],
                        in0=banks[half][:], in1=s.sgb[:], op=ALU.mult))
                self.on("vector", abst)

            def abw(sync, b=b, tt=tt):
                sync.wait_ge(sDV.h, ev[f"ab_{b}_{tt}_1"])
                self.dma(SP[tt % 2], sync.dma_start(
                    out=s.d_abr[b, tt * 128:(tt + 1) * 128, :],
                    in_=s.abb[:, (tt % 2) * 1024:(tt % 2) * 1024 + 1024]), f"abw_{b}_{tt}")
            self.on("sync", abw)

        tpab = self.tap(f"abr_b{b}", [S, D], F32)
        tpxq = self.tap(f"dxq_b{b}", [8, 128, S], BF16)
        if tpab is not None:
            def tapab(gpsimd, b=b, tpab=tpab, tpxq=tpxq):
                self.wt(gpsimd, f"abw_{b}_{NTT-1}")
                self.dma(SP[4], gpsimd.dma_start(out=tpab, in_=s.d_abr[b]))
                if tpxq is not None:
                    self.dma(SP[4], gpsimd.dma_start(out=tpxq, in_=s.d_xq[b]))
            self.on("gpsimd", tapab)

    def p56_ffn(self, b):
        s, ev = self, self.ev
        SQ, SP = s.SQ, s.SP
        sPE, sDV, sAC, sPL = s.sPE, s.sDV, s.sAC, s.sPL

        def wql(sync, b=b):
            if b == 0:
                self.wt(sync, "wqst_w1_31")
                self.wt(sync, "wqst_w2_7")
            src1 = bass.AP(s.d_wq1.tensor, 0, [[F, 128], [128 * F, 8], [1, F]])
            self.dma(SQ[4], sync.dma_start(
                out=s.wq1s[:].rearrange("p (dc n) -> p dc n", dc=8), in_=src1), f"wq1l_{b}")
            src2 = bass.AP(s.d_wq2.tensor, 0, [[D, 128], [128 * D, 32], [1, D]])
            self.dma(SQ[5], sync.dma_start(
                out=s.wq2s[:].rearrange("p (fc n) -> p fc n", fc=32), in_=src2), f"wq2l_{b}")
        self.on("sync", wql)

        def emit_tail(tt, part="all"):
            front = part in ("all", "front")
            back = part in ("all", "back")
            for fc8 in (range(8) if front else ()):
                def rnd_q(gpsimd, b=b, tt=tt, fc8=fc8):
                    if fc8 == 0:
                        gpsimd.wait_ge(sDV.h, ev[f"scq_{b}_{tt}"])
                    ev[f"hqp_{b}_{tt}_{fc8}"] = sPL.mark(gpsimd.tensor_scalar(
                        out=s.gbuf[:, fc8 * 512:(fc8 + 1) * 512],
                        in0=s.gbuf[:, fc8 * 512:(fc8 + 1) * 512],
                        scalar1=s.cb[:, 32 + tt % 2:33 + tt % 2], scalar2=MAGIC,
                        op0=ALU.mult, op1=ALU.add))
                self.on("gpsimd", rnd_q)

                def rnd_q2(scalar, b=b, tt=tt, fc8=fc8):
                    scalar.wait_ge(sPL.h, ev[f"hqp_{b}_{tt}_{fc8}"])
                    ev[f"hqc_{b}_{tt}_{fc8}"] = sAC.mark(scalar.activation(
                        out=s.hqt[:, (tt % 2) * F + fc8 * 512:(tt % 2) * F + (fc8 + 1) * 512],
                        in_=s.gbuf[:, fc8 * 512:(fc8 + 1) * 512],
                        func=AF.Identity, bias=s.negmag[:, 0:1]))
                self.on("scalar", rnd_q2)

            for fc8 in (range(8) if back else ()):
                def htr(tensor, b=b, tt=tt, fc8=fc8):
                    tensor.wait_ge(sAC.h, ev[f"hqc_{b}_{tt}_{fc8}"])
                    pb = s.pT1 if fc8 % 2 == 0 else s.pT2
                    if tt > 0 or fc8 >= 2:
                        key = (f"hte8_{b}_{tt}_{fc8-2}" if fc8 >= 2
                               else f"hte8_{b}_{tt-1}_{6 + fc8}")
                        tensor.wait_ge(sAC.h, ev[key])
                    for j in range(4):
                        fc = fc8 * 4 + j
                        ins = tensor.transpose(
                            out=pb[:, j * 128:(j + 1) * 128],
                            in_=s.hqt[:, (tt % 2) * F + fc * 128:(tt % 2) * F + (fc + 1) * 128],
                            identity=s.ident_bf[:])
                    ev[f"htr8_{b}_{tt}_{fc8}"] = sPE.mark(ins)
                self.on("tensor", htr)

                def hte(scalar, b=b, tt=tt, fc8=fc8):
                    scalar.wait_ge(sPE.h, ev[f"htr8_{b}_{tt}_{fc8}"])
                    pb = s.pT1 if fc8 % 2 == 0 else s.pT2
                    ev[f"hte8_{b}_{tt}_{fc8}"] = sAC.mark(scalar.activation(
                        out=s.hqf[:, (tt % 2) * F + fc8 * 512:(tt % 2) * F + (fc8 + 1) * 512],
                        in_=pb[:], func=AF.Copy))
                self.on("scalar", hte)

            if not back:
                return

            def f2m(tensor, b=b, tt=tt):
                if tt == 0:
                    self.wt(tensor, f"wq2l_{b}")
                for d2 in range(2):
                    pb = s.pC if d2 == 0 else s.pD
                    if tt >= 1:
                        tensor.wait_ge(sDV.h, ev[f"mg_{b}_{tt-1}_{d2}"])
                    for fc in range(32):
                        if d2 == 0 and fc % 4 == 0:
                            tensor.wait_ge(sAC.h, ev[f"hte8_{b}_{tt}_{fc // 4}"])
                        ins = tensor.matmul(
                            out=pb[:],
                            lhsT=s.hqf[:, (tt % 2) * F + fc * 128:(tt % 2) * F + (fc + 1) * 128],
                            rhs=s.wq2s[:, fc * D + d2 * 512:fc * D + (d2 + 1) * 512],
                            start=(fc == 0), stop=(fc == 31))
                    ev[f"f2m_{b}_{tt}_{d2}"] = sPE.mark(ins)
            self.on("tensor", f2m)

            for d2 in range(2):
                def mg(vector, b=b, tt=tt, d2=d2):
                    vector.wait_ge(sPE.h, ev[f"f2m_{b}_{tt}_{d2}"])
                    vector.wait_ge(sPL.h, ev[f"xa_{b}_{tt}"])
                    if tt >= 2:
                        self.wt(vector, f"ow_{b}_{tt-2}")
                    pb = s.pC if d2 == 0 else s.pD
                    v = sDV.mark(vector.scalar_tensor_tensor(
                        out=s.ftmp[:, d2 * 512:(d2 + 1) * 512], in0=pb[:],
                        scalar=s.cb[:, 34 + tt % 2:35 + tt % 2],
                        in1=s.b2b[:, d2 * 512:(d2 + 1) * 512],
                        op0=ALU.mult, op1=ALU.add))
                    ev[f"mg_{b}_{tt}_{d2}"] = sDV.mark(vector.tensor_tensor(
                        out=s.xat[:, (tt % 2) * D + d2 * 512:(tt % 2) * D + (d2 + 1) * 512],
                        in0=s.ftmp[:, d2 * 512:(d2 + 1) * 512],
                        in1=s.xat[:, (tt % 2) * D + d2 * 512:(tt % 2) * D + (d2 + 1) * 512],
                        op=ALU.add))
                self.on("vector", mg)

            def ow(sync, b=b, tt=tt):
                sync.wait_ge(sDV.h, ev[f"mg_{b}_{tt}_0"])
                sync.wait_ge(sDV.h, ev[f"mg_{b}_{tt}_1"])
                self.dma(SP[tt % 2], sync.dma_start(
                    out=s.out[b, tt * 128:(tt + 1) * 128, :],
                    in_=s.xat[:, (tt % 2) * D:(tt % 2) * D + D]), f"ow_{b}_{tt}")
            self.on("sync", ow)

        def fin2(tt):
            def sq2(scalar, b=b, tt=tt):
                scalar.wait_ge(sDV.h, ev[f"rden2_{b}_{tt}"])
                ev[f"sq2_{b}_{tt}"] = sAC.mark(scalar.activation(
                    out=s.cb[:, 10:11], in_=s.cb[:, 9:10], func=AF.Sqrt))
            self.on("scalar", sq2)

            def quantB_b(vector, b=b, tt=tt):
                vector.wait_ge(sAC.h, ev[f"sq2_{b}_{tt}"])
                v = sDV.mark(vector.reciprocal(out=s.cb[:, 11:12], in_=s.cb[:, 10:11]))
                vector.wait_ge(sDV.h, v)
                v = sDV.mark(vector.tensor_scalar(out=s.cb[:, 22:23], in0=s.cb[:, 22:23],
                                                  scalar1=s.cb[:, 11:12], scalar2=None,
                                                  op0=ALU.mult))
                vector.wait_ge(sDV.h, v)
                p = tt % 2
                v = sDV.mark(vector.tensor_copy(out=s.cb[:, 34 + p:35 + p],
                                                in_=s.cb[:, 22:23]))
                vector.wait_ge(sDV.h, v)
                ev[f"scq_{b}_{tt}"] = v
            self.on("vector", quantB_b)

        for tt in range(NTT):
            def lds_a(sync, b=b, tt=tt):
                if tt >= 2:
                    sync.wait_ge(sPE.h, ev[f"f1m_{b}_{tt-2}_7"])
                    sync.wait_ge(sPL.h, ev[f"xa_{b}_{tt-2}"])
                src = s.d_xq[b, :, :, tt * 128:(tt + 1) * 128]
                self.dma(SQ[1 + tt % 2], sync.dma_start(
                    out=s.xqt[:, (tt % 2) * 1024:(tt % 2) * 1024 + 1024].rearrange("p (dc n) -> p dc n", dc=8),
                    in_=src.transpose([1, 0, 2])), f"xql_{b}_{tt}")
                self.dma(SQ[6 + tt % 2], sync.dma_start(
                    out=s.abrt[:, (tt % 2) * D:(tt % 2) * D + D],
                    in_=s.d_abr[b, tt * 128:(tt + 1) * 128, :]), f"abl_{b}_{tt}")
            self.on("sync", lds_a)

            if tt >= 2:
                emit_tail(tt - 2, "back")
            if tt >= 1:
                fin2(tt - 1)

            def lds(sync, b=b, tt=tt):
                if tt >= 2:
                    self.wt(sync, f"ow_{b}_{tt-2}")
                self.dma(SQ[3 - 3 * (tt % 2)], sync.dma_start(
                    out=s.xat[:, (tt % 2) * D:(tt % 2) * D + D],
                    in_=s.x_in[b, tt * 128:(tt + 1) * 128, :]), f"x2l_{b}_{tt}")
            self.on("sync", lds)

            def xadd(gpsimd, b=b, tt=tt):
                self.wt(gpsimd, f"x2l_{b}_{tt}")
                gpsimd.wait_ge(SQ[6 + tt % 2].h, ev[f"abl_{b}_{tt}"][1])
                ev[f"xa_{b}_{tt}"] = sPL.mark(gpsimd.tensor_tensor(
                    out=s.xat[:, (tt % 2) * D:(tt % 2) * D + D],
                    in0=s.xat[:, (tt % 2) * D:(tt % 2) * D + D],
                    in1=s.abrt[:, (tt % 2) * D:(tt % 2) * D + D], op=ALU.add))
            self.on("gpsimd", xadd)

            if tt >= 1:
                emit_tail(tt - 1, "front")

            for fc8 in range(8):
                def f1m(tensor, b=b, tt=tt, fc8=fc8):
                    if tt == 0 and fc8 == 0:
                        self.wt(tensor, f"wq1l_{b}")
                    if fc8 == 0:
                        self.wt(tensor, f"xql_{b}_{tt}")
                    pb = [s.pA, s.pB, s.pO, s.pO2][fc8 % 4]
                    key = (f"dq_{b}_{tt}_{fc8-4}" if fc8 >= 4
                           else (f"dq_{b}_{tt-1}_{4 + fc8}" if tt >= 1 else None))
                    if key is not None:
                        tensor.wait_ge(sDV.h, ev[key])
                    for dc in range(8):
                        ins = tensor.matmul(
                            out=pb[:], lhsT=s.xqt[:, (tt % 2) * 1024 + dc * 128:(tt % 2) * 1024 + (dc + 1) * 128],
                            rhs=s.wq1s[:, dc * F + fc8 * 512:dc * F + (fc8 + 1) * 512],
                            start=(dc == 0), stop=(dc == 7))
                    ev[f"f1m_{b}_{tt}_{fc8}"] = sPE.mark(ins)
                self.on("tensor", f1m)

                def dq(vector, b=b, tt=tt, fc8=fc8):
                    vector.wait_ge(sPE.h, ev[f"f1m_{b}_{tt}_{fc8}"])
                    if tt >= 1:
                        vector.wait_ge(sAC.h, ev[f"hqc_{b}_{tt-1}_{fc8}"])
                    ev[f"dq_{b}_{tt}_{fc8}"] = sDV.mark(vector.scalar_tensor_tensor(
                        out=s.gbuf[:, fc8 * 512:(fc8 + 1) * 512],
                        in0=[s.pA, s.pB, s.pO, s.pO2][fc8 % 4][:],
                        scalar=s.dq1c[:, b * NTT + tt:b * NTT + tt + 1],
                        in1=s.b1b[:, fc8 * 512:(fc8 + 1) * 512],
                        op0=ALU.mult, op1=ALU.add))
                self.on("vector", dq)

                def gl(scalar, b=b, tt=tt, fc8=fc8):
                    scalar.wait_ge(sDV.h, ev[f"dq_{b}_{tt}_{fc8}"])
                    ev[f"gl_{b}_{tt}_{fc8}"] = sAC.mark(scalar.activation(
                        out=s.gbuf[:, fc8 * 512:(fc8 + 1) * 512],
                        in_=s.gbuf[:, fc8 * 512:(fc8 + 1) * 512], func=AF.Gelu))
                self.on("scalar", gl)

                def glsq(gpsimd, b=b, tt=tt, fc8=fc8):
                    gpsimd.wait_ge(sAC.h, ev[f"gl_{b}_{tt}_{fc8}"])
                    if fc8 > 0:
                        gpsimd.wait_ge(sDV.h, ev[f"gsq_{b}_{tt}_{fc8-1}"])
                    ev[f"glsq_{b}_{tt}_{fc8}"] = sPL.mark(gpsimd.tensor_tensor(
                        out=s.gsqt[:], in0=s.gbuf[:, fc8 * 512:(fc8 + 1) * 512],
                        in1=s.gbuf[:, fc8 * 512:(fc8 + 1) * 512], op=ALU.mult))
                self.on("gpsimd", glsq)

                def gsq(vector, b=b, tt=tt, fc8=fc8):
                    vector.wait_ge(sPL.h, ev[f"glsq_{b}_{tt}_{fc8}"])
                    vector.tensor_reduce(out=s.cb[:, fc8:fc8 + 1], in_=s.gsqt[:],
                                         axis=AX.X, op=ALU.add)
                    ev[f"gsq_{b}_{tt}_{fc8}"] = sDV.mark(vector.tensor_reduce(
                        out=s.cb[:, 24 + fc8:25 + fc8],
                        in_=s.gbuf[:, fc8 * 512:(fc8 + 1) * 512], axis=AX.X,
                        op=ALU.max, apply_absolute_value=True))
                self.on("vector", gsq)

            def quantA(vector, b=b, tt=tt):
                vector.wait_ge(sDV.h, ev[f"gsq_{b}_{tt}_7"])
                v = sDV.mark(vector.tensor_reduce(out=s.cb[:, 17:18], in_=s.cb[:, 24:32],
                                                  axis=AX.X, op=ALU.max))
                ev[f"amax_{b}_{tt}"] = v
                v2 = sDV.mark(vector.tensor_reduce(out=s.cb[:, 8:9], in_=s.cb[:, 0:8],
                                                   axis=AX.X, op=ALU.add))
                vector.wait_ge(sDV.h, v2)
                v2 = sDV.mark(vector.tensor_scalar(out=s.cb[:, 9:10], in0=s.cb[:, 8:9],
                                                   scalar1=1.0 / F, scalar2=EPS,
                                                   op0=ALU.mult, op1=ALU.add))
                ev[f"rden2_{b}_{tt}"] = v2
            self.on("vector", quantA)

            def quantB_a(vector, b=b, tt=tt):
                vector.wait_ge(sDV.h, ev[f"amax_{b}_{tt}"])
                v = sDV.mark(vector.tensor_scalar(out=s.cb[:, 18:19], in0=s.cb[:, 17:18],
                                                  scalar1=1e-5, scalar2=None, op0=ALU.max))
                vector.wait_ge(sDV.h, v)
                v = sDV.mark(vector.reciprocal(out=s.cb[:, 19:20], in_=s.cb[:, 18:19]))
                vector.wait_ge(sDV.h, v)
                v = sDV.mark(vector.tensor_scalar(out=s.cb[:, 20:21], in0=s.cb[:, 19:20],
                                                  scalar1=127.0, scalar2=None, op0=ALU.mult))
                vector.wait_ge(sDV.h, v)
                v = sDV.mark(vector.tensor_scalar(out=s.cb[:, 21:22], in0=s.cb[:, 20:21],
                                                  scalar1=s.s2col[:, 0:1], scalar2=None,
                                                  op0=ALU.mult))
                vector.wait_ge(sDV.h, v)
                v = sDV.mark(vector.reciprocal(out=s.cb[:, 22:23], in_=s.cb[:, 21:22]))
                vector.wait_ge(sDV.h, v)
                p = tt % 2
                v = sDV.mark(vector.tensor_copy(out=s.cb[:, 32 + p:33 + p],
                                                in_=s.cb[:, 20:21]))
                vector.wait_ge(sDV.h, v)
            self.on("vector", quantB_a)

        fin2(NTT - 1)
        emit_tail(NTT - 2, "back")
        emit_tail(NTT - 1, "front")
        emit_tail(NTT - 1, "back")

    def finale(self):
        s = self

        def fin(gpsimd):
            for c in (*s.SP, *s.SQ):
                if c.n > 0:
                    gpsimd.wait_ge(c.h, c.n)
        self.on("gpsimd", fin)

        def fin2(sync):
            for c in (*s.SP, *s.SQ):
                if c.n > 0:
                    sync.wait_ge(c.h, c.n)
        self.on("sync", fin2)


def build_core_program(taps=(), stop_after=None):
    kb = KB(taps, stop_after=stop_after)
    nc = kb.build()
    return nc, kb


def kernel(L0_attn_x, L0_attn_w_in, L0_attn_w_gate, L0_attn_b_gate, L0_attn_amplitudes,
           L0_attn_sharpness, L0_attn_offsets, L0_ffn_w1, L0_ffn_b1, L0_ffn_w2, L0_ffn_b2):
    x = np.ascontiguousarray(L0_attn_x)
    nc, _ = build_core_program()
    in_maps = []
    for c in range(8):
        in_maps.append({
            "x": np.ascontiguousarray(x[2 * c:2 * c + 2]),
            "w_in": np.ascontiguousarray(L0_attn_w_in),
            "w_gate": np.ascontiguousarray(L0_attn_w_gate),
            "b_gate": np.ascontiguousarray(L0_attn_b_gate),
            "amp": np.ascontiguousarray(L0_attn_amplitudes),
            "sharp": np.ascontiguousarray(L0_attn_sharpness),
            "off": np.ascontiguousarray(L0_attn_offsets),
            "w1": np.ascontiguousarray(L0_ffn_w1),
            "b1": np.ascontiguousarray(L0_ffn_b1),
            "w2": np.ascontiguousarray(L0_ffn_w2),
            "b2": np.ascontiguousarray(L0_ffn_b2),
        })
    res = run_bass_kernel_spmd(nc, in_maps, core_ids=list(range(8)))
    return np.concatenate([r["out"] for r in res.results], axis=0)


# version 31
# speedup vs baseline: 1.0839x; 1.0073x over previous
import numpy as np
import ml_dtypes
import concourse.bass as bass
import concourse.mybir as mybir
from concourse.bass_utils import run_bass_kernel_spmd
from contextlib import ExitStack

F32 = mybir.dt.float32
BF16 = mybir.dt.bfloat16
AF = mybir.ActivationFunctionType
ALU = mybir.AluOpType
AX = mybir.AxisListType

B, S, D, H, NK, F = 2, 2048, 1024, 16, 21, 4096
TW = 4095
EXPW = 4224
MAGIC = 1.5 * 2.0 ** 23
EPS = 1e-5
NTT = S // 128
ENGS = ("sync", "tensor", "vector", "scalar", "gpsimd")


class Ctr:
    def __init__(self, nc, stack, name):
        self.h = stack.enter_context(nc.semaphore(name))
        self.n = 0

    def mark(self, instr, k=1):
        instr.then_inc(self.h, k)
        self.n += k
        return self.n


class _DummyInstr:
    def then_inc(self, sem, k=1, **kw):
        return self


class _DummyEng:
    def __getattr__(self, name):
        def f(*a, **k):
            return _DummyInstr()
        return f


class KB:
    def __init__(self, taps=(), stop_after=None):
        self.pending = None
        self.stop_after = stop_after
        self.taps = set(taps)
        self.tap_outs = {}
        nc = self.nc = bass.Bass("TRN2", target_bir_lowering=False)
        self.x_in = nc.dram_tensor("x", [B, S, D], F32, kind="ExternalInput").ap()
        self.w_in = nc.dram_tensor("w_in", [D, 3 * D], F32, kind="ExternalInput").ap()
        self.w_gate_in = nc.dram_tensor("w_gate", [D, 2 * D], F32, kind="ExternalInput").ap()
        self.b_gate_in = nc.dram_tensor("b_gate", [2 * D], F32, kind="ExternalInput").ap()
        self.amp_in = nc.dram_tensor("amp", [H, NK], F32, kind="ExternalInput").ap()
        self.sharp_in = nc.dram_tensor("sharp", [H, NK], F32, kind="ExternalInput").ap()
        self.off_in = nc.dram_tensor("off", [H, NK], F32, kind="ExternalInput").ap()
        self.w1_in = nc.dram_tensor("w1", [F, D], F32, kind="ExternalInput").ap()
        self.b1_in = nc.dram_tensor("b1", [F], F32, kind="ExternalInput").ap()
        self.w2_in = nc.dram_tensor("w2", [D, F], F32, kind="ExternalInput").ap()
        self.b2_in = nc.dram_tensor("b2", [D], F32, kind="ExternalInput").ap()
        self.out = nc.dram_tensor("out", [B, S, D], F32, kind="ExternalOutput").ap()

        self.d_expb = nc.dram_tensor("d_expb", [H, 128 * EXPW], BF16, kind="Internal").ap()
        self.d_winb = nc.dram_tensor("d_winb", [D, 3 * D], BF16, kind="Internal").ap()
        self.d_wgb = nc.dram_tensor("d_wgb", [D, 2 * D], BF16, kind="Internal").ap()
        self.d_wq1 = nc.dram_tensor("d_wq1", [D, F], BF16, kind="Internal").ap()
        self.d_wq2 = nc.dram_tensor("d_wq2", [F, D], BF16, kind="Internal").ap()
        self.d_xq = nc.dram_tensor("d_xq", [B, 8, 128, S], BF16, kind="Internal").ap()
        self.d_abr = nc.dram_tensor("d_abr", [B, S, D], F32, kind="Internal").ap()

        ident = np.eye(128)
        self.ident_bf_d = nc.inline_tensor(ident.astype(ml_dtypes.bfloat16), "identd_bf")
        sel = np.zeros((63, 65), dtype=np.float32)
        for p in range(63):
            sel[p, 32 * (p // 21)] = 1.0
        self.selG_d = nc.inline_tensor(sel, "selG_d")

    def tap(self, name, shape, dtype=F32):
        if name in self.taps and name not in self.tap_outs:
            self.tap_outs[name] = self.nc.dram_tensor(
                "tap_" + name, shape, dtype, kind="ExternalOutput").ap()
        return self.tap_outs.get(name)

    def on(self, engine_name, fn):
        if self.pending is not None:
            self.pending.append((engine_name, fn))
            fn(_DummyEng())
        else:
            getattr(self.block, engine_name)(fn)

    def grouped(self, phase_fn, *args):
        ctrs = (*self.SQ, *self.SP, self.sPE, self.sDV, self.sAC, self.sPL)
        snap = [(c, c.n) for c in ctrs]
        self.pending = []
        phase_fn(*args)
        pend = self.pending
        self.pending = None
        for c, v in snap:
            c.n = v
        for en in ENGS:
            fns = [f for e, f in pend if e == en]
            if fns:
                def sect(eng, fns=fns):
                    for f in fns:
                        f(eng)
                getattr(self.block, en)(sect)

    def dma(self, ctr, instr, key=None):
        instr.then_inc(ctr.h, 16)
        ctr.n += 16
        if key is not None:
            self.ev[key] = (ctr, ctr.n)
        return (ctr, ctr.n)

    def wt(self, eng, key):
        c, v = self.ev[key]
        eng.wait_ge(c.h, v)

    def bar(self):
        vals = [(c, c.n) for c in (*self.SQ, *self.SP, self.sPE, self.sDV, self.sAC, self.sPL)]
        for en in ENGS:
            def f(eng, vals=vals):
                for c, v in vals:
                    if v > 0:
                        eng.wait_ge(c.h, v)
            self.on(en, f)

    def build(self):
        nc = self.nc
        st = ExitStack()
        with st:
            s = self

            def sbufP(name, shape, dt, side=None):
                return st.enter_context(nc.sbuf_tensor(name, shape, dt, side=side))

            s.ident_bf = sbufP("ident_bf", [128, 128], BF16)
            s.ones1_bf = sbufP("ones1_bf", [1, 128], BF16)
            s.ones1_f = sbufP("ones1_f", [1, 128], F32)
            s.ones21 = sbufP("ones21", [NK, 1], F32)
            s.ones128 = sbufP("ones128", [128, 1], F32)
            s.onesp64 = sbufP("onesp64", [128, 64], BF16)
            s.bgate_bf = sbufP("bgate_bf", [1, 2 * D], BF16)
            s.b1b = sbufP("b1b", [128, F], BF16)
            s.b2b = sbufP("b2b", [128, D], BF16)
            s.s1col = sbufP("s1col", [128, 1], F32)
            s.s2col = sbufP("s2col", [128, 1], F32)
            s.dq1c = sbufP("dq1c", [128, B * NTT], F32)
            s.cb = sbufP("cb", [128, 40], F32)
            s.epscol = sbufP("epscol", [128, 1], F32)
            s.negmag = sbufP("negmag", [128, 1], F32)
            s.scal = sbufP("scal", [1, 8], F32)
            s.stg = sbufP("stg", [128, 2 * 128], BF16)
            s.p1cb = sbufP("p1cb_buf", [128, 20 * NTT], F32) \
                if any(t.startswith("p1cb") for t in self.taps) else None
            s.ftmp = sbufP("ftmp", [128, 2 * 512], F32)

            s.pA = st.enter_context(nc.psum_tensor("pA", [128, 512], F32))
            s.pB = st.enter_context(nc.psum_tensor("pB", [128, 512], F32))
            s.pC = st.enter_context(nc.psum_tensor("pC", [128, 512], F32))
            s.pD = st.enter_context(nc.psum_tensor("pD", [128, 512], F32))
            s.pO = st.enter_context(nc.psum_tensor("pO", [128, 512], F32))
            s.pO2 = st.enter_context(nc.psum_tensor("pO2", [128, 512], F32))
            s.pT1 = st.enter_context(nc.psum_tensor("pT1", [128, 512], BF16))
            s.pT2 = st.enter_context(nc.psum_tensor("pT2", [128, 512], BF16))

            s.SQ = [Ctr(nc, st, f"SQ{i}") for i in range(8)]
            s.SP = [Ctr(nc, st, f"SP{i}") for i in range(6)]
            s.sPE = Ctr(nc, st, "sPE")
            s.sDV = Ctr(nc, st, "sDV")
            s.sAC = Ctr(nc, st, "sAC")
            s.sPL = Ctr(nc, st, "sPL")

            s.block = st.enter_context(nc.Block())
            self.ev = {}

            stop = self.stop_after
            with ExitStack() as ph0:
                s.wtile = ph0.enter_context(nc.sbuf_tensor("wtile", [128, 3 * F], F32))
                s.wqtile = ph0.enter_context(nc.sbuf_tensor("wqtile", [128, 3 * F], BF16))
                s.stg8 = ph0.enter_context(nc.sbuf_tensor("stg8", [128, 2 * F], BF16))
                s.iotaf = ph0.enter_context(nc.sbuf_tensor("iotaf", [128, TW], F32))
                s.relrev = ph0.enter_context(nc.sbuf_tensor("relrev", [128, TW], F32))
                s.gwork = ph0.enter_context(nc.sbuf_tensor("gwork", [128, TW], F32))
                s.hkG = ph0.enter_context(nc.sbuf_tensor("hkG", [128, 18], F32))
                s.selG = ph0.enter_context(nc.sbuf_tensor("selG", [63, 65], F32))
                s.onesbc = ph0.enter_context(nc.sbuf_tensor("onesbc", [128, 128], BF16))
                s.rowexp = ph0.enter_context(nc.sbuf_tensor("rowexp", [65, TW], BF16))
                s.bgrow = ph0.enter_context(nc.sbuf_tensor("bgrow", [1, 2 * D], F32))
                s.b1row = ph0.enter_context(nc.sbuf_tensor("b1row", [1, F], F32))
                s.b2row = ph0.enter_context(nc.sbuf_tensor("b2row", [1, D], F32))
                s.bch = ph0.enter_context(nc.sbuf_tensor("bch", [128, TW], BF16))
                self.grouped(self.phase0)
            self.bar()

            for b in (range(B) if stop != "ph0" else []):
                with ExitStack() as gL1:
                    s.xn_fm = gL1.enter_context(
                        nc.sbuf_tensor(f"xn_fm_{b}", [128, 8 * S], BF16, side="left"))
                    s.a_fm = s.xn_fm
                    with ExitStack() as gR1:
                        s.k_fm = gR1.enter_context(
                            nc.sbuf_tensor(f"k_fm_{b}", [128, 8 * S], BF16, side="right"))
                        s.q_fm = gR1.enter_context(
                            nc.sbuf_tensor(f"q_fm_{b}", [128, 8 * S], BF16, side="right"))
                        s.v_tm = gR1.enter_context(
                            nc.sbuf_tensor(f"v_tm_{b}", [128, NTT * 1040], BF16, side="right"))
                        with ExitStack() as p1s:
                            s.xt = p1s.enter_context(nc.sbuf_tensor(f"xt_{b}", [128, 2 * D], F32))
                            s.xnt = p1s.enter_context(nc.sbuf_tensor(f"xnt_{b}", [128, D], F32))
                            s.xnbt = p1s.enter_context(nc.sbuf_tensor(f"xnbt_{b}", [128, D], BF16))
                            s.xqbt = p1s.enter_context(nc.sbuf_tensor(f"xqbt_{b}", [128, D], BF16))
                            s.xqs = p1s.enter_context(nc.sbuf_tensor(f"xqs_{b}", [128, 8 * 128], BF16))
                            s.dump = p1s.enter_context(nc.sbuf_tensor(f"dump_{b}", [128, D], F32))
                            s.rtmp = p1s.enter_context(nc.sbuf_tensor(f"rtmp_{b}", [128, D], F32))
                            self.grouped(self.p1_ln_quant, b)
                        self.bar()
                        if stop == "p1":
                            self.finale()
                            return nc
                        with ExitStack() as p2s:
                            s.wcolb = p2s.enter_context(
                                nc.sbuf_tensor(f"wcolb_{b}", [128, 2 * 1024], BF16))
                            s.wvb = p2s.enter_context(
                                nc.sbuf_tensor(f"wvb_{b}", [128, 2 * 8 * 512], BF16))
                            self.grouped(self.p2_proj, b)
                        self.bar()
                        if stop == "p2":
                            self.finale()
                            return nc
                        with ExitStack() as p3s:
                            s.skw = p3s.enter_context(
                                nc.sbuf_tensor(f"skw_{b}", [128, 3 * 3968], BF16, side="left"))
                            s.eraw = p3s.enter_context(
                                nc.sbuf_tensor(f"eraw_{b}", [128, 3 * 512], BF16, side="left"))
                            s.emul = p3s.enter_context(
                                nc.sbuf_tensor(f"emul_{b}", [128, 20 * 512], BF16, side="left"))
                            s.rdbuf = p3s.enter_context(
                                nc.sbuf_tensor(f"rdbuf_{b}", [128, 512], BF16, side="left"))
                            s.rbs = p3s.enter_context(
                                nc.sbuf_tensor(f"rbs_{b}", [64, 2 * 512], BF16, side="left"))
                            self.grouped(self.p3_attn, b)
                        self.bar()
                        if stop == "p3":
                            self.finale()
                            return nc
                        gR1.close()
                        with ExitStack() as p4s:
                            s.wgs = p4s.enter_context(
                                nc.sbuf_tensor(f"wgs_{b}", [128, 8 * 2048], BF16, side="right"))
                            s.sgb = p4s.enter_context(
                                nc.sbuf_tensor(f"sgb_{b}", [128, 512], F32, side="right"))
                            s.abb = p4s.enter_context(
                                nc.sbuf_tensor(f"abb_{b}", [128, 2 * 1024], F32, side="right"))
                            self.grouped(self.p4_gate, b)
                        self.bar()
                        if stop == "p4":
                            self.finale()
                            return nc
                with ExitStack() as p56s:
                    s.wq1s = p56s.enter_context(nc.sbuf_tensor(f"wq1s_{b}", [128, 8 * F], BF16))
                    s.wq2s = p56s.enter_context(nc.sbuf_tensor(f"wq2s_{b}", [128, 32 * D], BF16))
                    s.gbuf = p56s.enter_context(nc.sbuf_tensor(f"gbuf_{b}", [128, F], F32))
                    s.hqt = p56s.enter_context(nc.sbuf_tensor(f"hqt_{b}", [128, 2 * F], BF16))
                    s.gsqt = p56s.enter_context(nc.sbuf_tensor(f"gsqt_{b}", [128, 512], F32))
                    s.hqf = s.hqt
                    s.xqt = p56s.enter_context(nc.sbuf_tensor(f"xqt_{b}", [128, 2 * 1024], BF16))
                    s.abrt = p56s.enter_context(nc.sbuf_tensor(f"abrt_{b}", [128, 2 * D], F32))
                    s.xat = p56s.enter_context(nc.sbuf_tensor(f"xat_{b}", [128, 2 * D], F32))
                    self.grouped(self.p56_ffn, b)
                self.bar()
                if stop == "p56":
                    self.finale()
                    return nc
            self.finale()
        return nc

    def phase0(self):
        s, ev = self, self.ev
        SQ, SP = s.SQ, s.SP
        sPE, sDV, sAC, sPL = s.sPE, s.sDV, s.sAC, s.sPL

        def l3(n, i):
            return ((n - 1 - i) // 3) * 3 + i

        def loads(sync):
            for dst, src in [(s.ident_bf[:], s.ident_bf_d.ap()),
                             (s.selG[:], s.selG_d.ap()),
                             (s.b1row[:], s.b1_in[None, :]),
                             (s.b2row[:], s.b2_in[None, :]),
                             (s.bgrow[:], s.b_gate_in[None, :])]:
                self.dma(SQ[0], sync.dma_start(out=dst, in_=src))
            for g in range(6):
                nr = 63 if g < 5 else 21
                for k, t in enumerate((s.amp_in, s.sharp_in, s.off_in)):
                    flat = t.rearrange("h k -> (h k)")
                    self.dma(SQ[0], sync.dma_start(
                        out=s.hkG[0:nr, 3 * g + k:3 * g + k + 1],
                        in_=flat[g * 63:g * 63 + nr][:, None],
                        allow_slow_non_contiguous=True))
            ev["p0_loads"] = (SQ[0], SQ[0].n)
            ev["hkg"] = (SQ[0], SQ[0].n)
        self.on("sync", loads)

        def iota0(gpsimd):
            ev["iota"] = sPL.mark(gpsimd.iota(
                out=s.iotaf[:], pattern=[[1, TW]], base=0, channel_multiplier=0,
                allow_small_or_imprecise_dtypes=True))
        self.on("gpsimd", iota0)

        def consts(vector):
            self.wt(vector, "p0_loads")
            vector.memset(s.ones1_bf[:], 1.0)
            vector.memset(s.ones1_f[:], 1.0)
            vector.memset(s.ones128[:], 1.0)
            vector.memset(s.onesp64[64:65, :], 1.0)
            vector.memset(s.epscol[:], EPS)
            vector.memset(s.negmag[:], -MAGIC)
            vector.memset(s.onesbc[0:1, :], 1.0)
            vector.memset(s.onesbc[32:33, :], 1.0)
            vector.memset(s.onesbc[64:65, :], 1.0)
            vector.tensor_copy(out=s.bgate_bf[:], in_=s.bgrow[:])
            vector.wait_ge(sPL.h, ev["iota"])
            ev["relrev"] = sDV.mark(vector.tensor_scalar(
                out=s.relrev[:], in0=s.iotaf[:], scalar1=-1.0, scalar2=2047.0,
                op0=ALU.mult, op1=ALU.add))
        self.on("vector", consts)

        act0 = sAC.n
        dvb0 = sDV.n

        def rexp_ev(g, c, act0=act0):
            return act0 + 9 * g + 2 + c

        def bce_ev(h, c, dvb0=dvb0):
            return dvb0 + 6 + 12 + 8 * h + c + 1

        def hkg_neg(vector):
            self.wt(vector, "hkg")
            for g in range(6):
                c = 3 * g + 1
                ev[f"hkgn_{g}"] = sDV.mark(vector.scalar_tensor_tensor(
                    out=s.hkG[0:63, c:c + 1], in0=s.hkG[0:63, c:c + 1], scalar=-1.0,
                    in1=s.hkG[0:63, c:c + 1], op0=ALU.mult, op1=ALU.min))
        self.on("vector", hkg_neg)

        for g in range(6):
            nh = 3 if g < 5 else 1
            nr = nh * 21

            def u_step(vector, g=g, nr=nr):
                vector.wait_ge(sDV.h, ev[f"hkgn_{g}"])
                if g == 0:
                    vector.wait_ge(sDV.h, ev["relrev"])
                else:
                    vector.wait_ge(sPE.h, ev[f"tsum_{g-1}"])
                ev[f"u_{g}"] = sDV.mark(vector.tensor_scalar(
                    out=s.gwork[0:nr, :], in0=s.relrev[0:nr, :],
                    scalar1=s.hkG[0:nr, 3 * g + 2:3 * g + 3], scalar2=None,
                    op0=ALU.subtract))
            self.on("vector", u_step)

            def exp_step(scalar, g=g, nr=nr):
                scalar.wait_ge(sDV.h, ev[f"u_{g}"])
                scalar.activation(out=s.gwork[0:nr, :], in_=s.gwork[0:nr, :], func=AF.Square)
                ev[f"gexp_{g}"] = sAC.mark(scalar.activation(
                    out=s.gwork[0:nr, :], in_=s.gwork[0:nr, :], func=AF.Exp,
                    scale=s.hkG[0:nr, 3 * g + 1:3 * g + 2]))
            self.on("scalar", exp_step)

            def amp_step(vector, g=g, nr=nr):
                vector.wait_ge(sAC.h, ev[f"gexp_{g}"])
                ev[f"ge_{g}"] = sDV.mark(vector.tensor_scalar(
                    out=s.gwork[0:nr, :], in0=s.gwork[0:nr, :],
                    scalar1=s.hkG[0:nr, 3 * g:3 * g + 1], scalar2=None, op0=ALU.mult))
            self.on("vector", amp_step)

            def sum_step(tensor, g=g, nr=nr, nh=nh):
                tensor.wait_ge(sDV.h, ev[f"ge_{g}"])
                for c in range(8):
                    n = min(512, TW - c * 512)
                    if g > 0 or c > 0:
                        pv = rexp_ev(g, c - 1) if c > 0 else rexp_ev(g - 1, 7)
                        tensor.wait_ge(sAC.h, pv)
                    ev[f"tsum_{g}_{c}"] = sPE.mark(tensor.matmul(
                        out=s.pC[0:(32 * (nh - 1) + 1), 0:n], lhsT=s.selG[0:nr, 0:(32 * (nh - 1) + 1)],
                        rhs=s.gwork[0:nr, c * 512:c * 512 + n], start=True, stop=True))
                ev[f"tsum_{g}"] = sPE.n
            self.on("tensor", sum_step)

            def rexp_step(scalar, g=g, nh=nh):
                for c in range(8):
                    n = min(512, TW - c * 512)
                    scalar.wait_ge(sPE.h, ev[f"tsum_{g}_{c}"])
                    for m in range(nh):
                        ins = scalar.activation(
                            out=s.rowexp[32 * m:32 * m + 1, c * 512:c * 512 + n],
                            in_=s.pC[32 * m:32 * m + 1, 0:n], func=AF.Exp)
                    v = sAC.mark(ins)
                    ev[f"rexp_{g}_{c}"] = v
                    assert v == rexp_ev(g, c), (v, g, c)
            self.on("scalar", rexp_step)

        for h in range(H):
            g, m = divmod(h, 3)

            def bc_step(tensor, h=h, g=g, m=m):
                for c in range(8):
                    n = min(512, TW - c * 512)
                    tensor.wait_ge(sAC.h, rexp_ev(g, c))
                    pb = s.pA if c % 2 == 0 else s.pB
                    if h > 0 or c >= 2:
                        pv = bce_ev(h, c - 2) if c >= 2 else bce_ev(h - 1, 6 + c)
                        tensor.wait_ge(sDV.h, pv)
                    ev[f"bcm_{h}_{c}"] = sPE.mark(tensor.matmul(
                        out=pb[:, 0:n], lhsT=s.onesbc[32 * m:32 * m + 1, :],
                        rhs=s.rowexp[32 * m:32 * m + 1, c * 512:c * 512 + n],
                        start=True, stop=True, tile_position=(32 * m, 0)))
            self.on("tensor", bc_step)

            def bce_step(vector, h=h):
                for c in range(8):
                    n = min(512, TW - c * 512)
                    pb = s.pA if c % 2 == 0 else s.pB
                    vector.wait_ge(sPE.h, ev[f"bcm_{h}_{c}"])
                    if h > 0 and c == 0:
                        self.wt(vector, f"expw_{h-1}")
                    v = sDV.mark(vector.tensor_copy(
                        out=s.bch[:, c * 512:c * 512 + n], in_=pb[:, 0:n]))
                    ev[f"bce_{h}_{c}"] = v
                    assert v == bce_ev(h, c), (v, bce_ev(h, c), h, c)
                ev[f"bce_{h}"] = sDV.n
            self.on("vector", bce_step)

            def expw_step(sync, h=h):
                sync.wait_ge(sDV.h, ev[f"bce_{h}"])
                dst = bass.AP(s.d_expb.tensor, h * 128 * EXPW, [[EXPW, 128], [1, TW]])
                self.dma(SP[h % 2], sync.dma_start(out=dst, in_=s.bch[:, 0:TW]), f"expw_{h}")
            self.on("sync", expw_step)

        for wname, src_w, dst_w, ncols in (("win", s.w_in, s.d_winb, 3 * D),
                                           ("wg", s.w_gate_in, s.d_wgb, 2 * D)):
            for i in range(8):
                def wld(sync, wname=wname, i=i, src_w=src_w, ncols=ncols):
                    if i >= 3:
                        sync.wait_ge(sDV.h, ev[f"cast_{wname}_{i-3}"])
                    elif wname == "wg":
                        sync.wait_ge(sDV.h, ev[f"cast_win_{l3(8, i)}"])
                    h2 = ncols // 2
                    self.dma(SQ[1 + i % 3], sync.dma_start(
                        out=s.wtile[:, (i % 3) * F:(i % 3) * F + h2],
                        in_=src_w[i * 128:(i + 1) * 128, 0:h2]), f"wld_{wname}_{i}")
                    self.dma(SQ[4 + i % 3], sync.dma_start(
                        out=s.wtile[:, (i % 3) * F + h2:(i % 3) * F + ncols],
                        in_=src_w[i * 128:(i + 1) * 128, h2:ncols]), f"wldR_{wname}_{i}")
                self.on("sync", wld)

                def wcast(vector, wname=wname, i=i, ncols=ncols):
                    self.wt(vector, f"wld_{wname}_{i}")
                    self.wt(vector, f"wldR_{wname}_{i}")
                    if i >= 3:
                        self.wt(vector, f"wst_{wname}_{i-3}")
                    if wname == "wg" and i < 3:
                        self.wt(vector, f"wst_win_{l3(8, i)}")
                    ev[f"cast_{wname}_{i}"] = sDV.mark(vector.tensor_copy(
                        out=s.wqtile[:, (i % 3) * F:(i % 3) * F + ncols],
                        in_=s.wtile[:, (i % 3) * F:(i % 3) * F + ncols]))
                self.on("vector", wcast)

                def wst(sync, wname=wname, i=i - 2, dst_w=dst_w, ncols=ncols):
                    if i < 0:
                        return
                    sync.wait_ge(sDV.h, ev[f"cast_{wname}_{i}"])
                    self.dma(SP[2 + i % 3], sync.dma_start(
                        out=dst_w[i * 128:(i + 1) * 128, :],
                        in_=s.wqtile[:, (i % 3) * F:(i % 3) * F + ncols]),
                        f"wst_{wname}_{i}")
                self.on("sync", wst)
                if i == 7:
                    for it in (6, 7):
                        def wstt(sync, wname=wname, i=it, dst_w=dst_w, ncols=ncols):
                            sync.wait_ge(sDV.h, ev[f"cast_{wname}_{i}"])
                            self.dma(SP[2 + i % 3], sync.dma_start(
                                out=dst_w[i * 128:(i + 1) * 128, :],
                                in_=s.wqtile[:, (i % 3) * F:(i % 3) * F + ncols]),
                                f"wst_{wname}_{i}")
                        self.on("sync", wstt)

        for wname, src_w, dst_w, ntile, ncols, scol in (
                ("w1", s.w1_in, s.d_wq1, 32, D, s.s1col),
                ("w2", s.w2_in, s.d_wq2, 8, F, s.s2col)):
            nelem = float(ntile * 128 * ncols)
            nblk = ncols // 128
            for i in range(ntile):
                def ald(sync, wname=wname, i=i, src_w=src_w, ncols=ncols):
                    if i >= 3:
                        sync.wait_ge(sDV.h, ev[f"asum_{wname}_{i-3}"])
                    if i < 3 and wname == "w1":
                        sync.wait_ge(sDV.h, ev[f"cast_wg_{l3(8, i)}"])
                    if i < 3 and wname == "w2":
                        sync.wait_ge(sDV.h, ev[f"q_w1_{l3(32, i)}"])
                    h2 = ncols // 2
                    self.dma(SQ[1 + i % 3], sync.dma_start(
                        out=s.wtile[:, (i % 3) * F:(i % 3) * F + h2],
                        in_=src_w[i * 128:(i + 1) * 128, 0:h2]), f"ald_{wname}_{i}")
                    self.dma(SQ[4 + i % 3], sync.dma_start(
                        out=s.wtile[:, (i % 3) * F + h2:(i % 3) * F + ncols],
                        in_=src_w[i * 128:(i + 1) * 128, h2:ncols]), f"aldR_{wname}_{i}")
                self.on("sync", ald)

                def ared(vector, wname=wname, i=i, ncols=ncols):
                    self.wt(vector, f"ald_{wname}_{i}")
                    self.wt(vector, f"aldR_{wname}_{i}")
                    v = sDV.mark(vector.tensor_reduce(
                        out=s.cb[:, 0:1], in_=s.wtile[:, (i % 3) * F:(i % 3) * F + ncols],
                        axis=AX.X, op=ALU.add, apply_absolute_value=True))
                    vector.wait_ge(sDV.h, v)
                    if i == 0:
                        ev[f"asum_{wname}_{i}"] = sDV.mark(
                            vector.tensor_copy(out=s.cb[:, 1:2], in_=s.cb[:, 0:1]))
                    else:
                        ev[f"asum_{wname}_{i}"] = sDV.mark(vector.tensor_tensor(
                            out=s.cb[:, 1:2], in0=s.cb[:, 1:2], in1=s.cb[:, 0:1], op=ALU.add))
                self.on("vector", ared)

            def tot(tensor, wname=wname, ntile=ntile):
                tensor.wait_ge(sDV.h, ev[f"asum_{wname}_{ntile-1}"])
                tensor.wait_ge(sAC.h, rexp_ev(5, 7))
                ev[f"tot_{wname}"] = sPE.mark(tensor.matmul(
                    out=s.pC[0:1, 0:1], lhsT=s.ones128[:], rhs=s.cb[:, 1:2],
                    start=True, stop=True))
            self.on("tensor", tot)

            def sv(vector, wname=wname, nelem=nelem):
                vector.wait_ge(sPE.h, ev[f"tot_{wname}"])
                v = sDV.mark(vector.tensor_scalar(
                    out=s.scal[0:1, 0:1], in0=s.pC[0:1, 0:1], scalar1=1.0 / nelem,
                    scalar2=1e-5, op0=ALU.mult, op1=ALU.max))
                vector.wait_ge(sDV.h, v)
                ev[f"sval_{wname}"] = sDV.mark(
                    vector.reciprocal(out=s.scal[0:1, 1:2], in_=s.scal[0:1, 0:1]))
            self.on("vector", sv)

            def sbc(tensor, wname=wname):
                tensor.wait_ge(sDV.h, ev[f"sval_{wname}"])
                ev[f"sbc_{wname}"] = sPE.mark(tensor.matmul(
                    out=s.pC[:, 0:1], lhsT=s.ones1_f[:], rhs=s.scal[0:1, 1:2],
                    start=True, stop=True))
            self.on("tensor", sbc)

            def sce(vector, wname=wname, scol=scol):
                vector.wait_ge(sPE.h, ev[f"sbc_{wname}"])
                ev[f"scol_{wname}"] = sDV.mark(vector.tensor_copy(out=scol[:], in_=s.pC[:, 0:1]))
            self.on("vector", sce)

            for i in range(ntile):
                def qld(sync, wname=wname, i=i, src_w=src_w, ncols=ncols, ntile=ntile):
                    if i >= 3:
                        sync.wait_ge(sDV.h, ev[f"q_{wname}_{i-3}"])
                    if i < 3:
                        sync.wait_ge(sDV.h, ev[f"asum_{wname}_{l3(ntile, i)}"])
                    h2 = ncols // 2
                    self.dma(SQ[1 + i % 3], sync.dma_start(
                        out=s.wtile[:, (i % 3) * F:(i % 3) * F + h2],
                        in_=src_w[i * 128:(i + 1) * 128, 0:h2]), f"qld_{wname}_{i}")
                    self.dma(SQ[4 + i % 3], sync.dma_start(
                        out=s.wtile[:, (i % 3) * F + h2:(i % 3) * F + ncols],
                        in_=src_w[i * 128:(i + 1) * 128, h2:ncols]), f"qldR_{wname}_{i}")
                self.on("sync", qld)
                def wqst(sync, wname=wname, i=i - 2, dst_w=dst_w, ncols=ncols, nblk=nblk):
                    if i < 0:
                        return
                    sync.wait_ge(sAC.h, ev[f"tev_{wname}_{i}_{nblk-1}"])
                    if wname == "w1" and i < 3:
                        self.wt(sync, f"wst_wg_{l3(8, i)}")
                    if wname == "w2" and i < 3:
                        self.wt(sync, f"wqst_w1_{l3(32, i)}")
                    pitch = F if wname == "w1" else D
                    dst = bass.AP(dst_w.tensor, i * 128,
                                  [[pitch, 128], [128 * pitch, nblk], [1, 128]])
                    self.dma(SP[2 + i % 3], sync.dma_start(
                        out=dst, in_=s.stg8[:, (i % 2) * ncols:(i % 2) * ncols + ncols].rearrange(
                            "p (j n) -> p j n", j=nblk)), f"wqst_{wname}_{i}")
                self.on("sync", wqst)

                def qnt1(vector, wname=wname, i=i, ncols=ncols, scol=scol):
                    self.wt(vector, f"qld_{wname}_{i}")
                    self.wt(vector, f"qldR_{wname}_{i}")
                    if i == 0:
                        vector.wait_ge(sDV.h, ev[f"scol_{wname}"])
                    vector.tensor_scalar(
                        out=s.wtile[:, (i % 3) * F:(i % 3) * F + ncols],
                        in0=s.wtile[:, (i % 3) * F:(i % 3) * F + ncols],
                        scalar1=scol[:, 0:1], scalar2=MAGIC,
                        op0=ALU.mult, op1=ALU.add)
                    ev[f"q1_{wname}_{i}"] = sDV.mark(vector.tensor_scalar(
                        out=s.wtile[:, (i % 3) * F:(i % 3) * F + ncols],
                        in0=s.wtile[:, (i % 3) * F:(i % 3) * F + ncols],
                        scalar1=MAGIC, scalar2=1.0,
                        op0=ALU.subtract, op1=ALU.min))
                self.on("vector", qnt1)

                def qnt2(vector, wname=wname, i=i, ncols=ncols, nblk=nblk):
                    if i >= 3:
                        vector.wait_ge(sPE.h, ev[f"trp_{wname}_{i-3}_{nblk-1}"])
                    ev[f"q_{wname}_{i}"] = sDV.mark(vector.tensor_scalar(
                        out=s.wqtile[:, (i % 3) * ncols:(i % 3) * ncols + ncols],
                        in0=s.wtile[:, (i % 3) * F:(i % 3) * F + ncols],
                        scalar1=-1.0, scalar2=None, op0=ALU.max))
                self.on("vector", qnt2)

                nblk = ncols // 128
                for rr in range((nblk + 1) // 2):
                    def trp(tensor, wname=wname, i=i, rr=rr, nblk=nblk, ncols=ncols):
                        for half in range(2):
                            j = rr * 2 + half
                            if j >= nblk:
                                continue
                            if rr == 0 and half == 0:
                                tensor.wait_ge(sDV.h, ev[f"q_{wname}_{i}"])
                            pb = s.pT1 if j % 2 == 0 else s.pT2
                            if i > 0 or j >= 2:
                                key = (f"tev_{wname}_{i}_{j-2}" if j >= 2
                                       else f"tev_{wname}_{i-1}_{nblk - 2 + j}")
                                tensor.wait_ge(sAC.h, ev[key])
                            ev[f"trp_{wname}_{i}_{j}"] = sPE.mark(tensor.transpose(
                                out=pb[:, 0:128],
                                in_=s.wqtile[:, (i % 3) * ncols + j * 128:(i % 3) * ncols + (j + 1) * 128],
                                identity=s.ident_bf[:]))
                    self.on("tensor", trp)

                    def tev(scalar, wname=wname, i=i, rr=rr, nblk=nblk, ncols=ncols):
                        for half in range(2):
                            j = rr * 2 + half
                            if j >= nblk:
                                continue
                            pb = s.pT1 if j % 2 == 0 else s.pT2
                            scalar.wait_ge(sPE.h, ev[f"trp_{wname}_{i}_{j}"])
                            if j == 0 and i >= 2:
                                self.wt(scalar, f"wqst_{wname}_{i-2}")
                            ev[f"tev_{wname}_{i}_{j}"] = sAC.mark(scalar.activation(
                                out=s.stg8[:, (i % 2) * ncols + j * 128:(i % 2) * ncols + (j + 1) * 128],
                                in_=pb[:, 0:128], func=AF.Copy))
                    self.on("scalar", tev)

                if i == ntile - 1:
                    for it in (ntile - 2, ntile - 1):
                        def wqstt(sync, wname=wname, i=it, dst_w=dst_w, ncols=ncols, nblk=nblk):
                            sync.wait_ge(sAC.h, ev[f"tev_{wname}_{i}_{nblk-1}"])
                            pitch = F if wname == "w1" else D
                            dst = bass.AP(dst_w.tensor, i * 128,
                                          [[pitch, 128], [128 * pitch, nblk], [1, 128]])
                            self.dma(SP[2 + i % 3], sync.dma_start(
                                out=dst, in_=s.stg8[:, (i % 2) * ncols:(i % 2) * ncols + ncols].rearrange(
                                    "p (j n) -> p j n", j=nblk)), f"wqst_{wname}_{i}")
                        self.on("sync", wqstt)



        dvb0 = sDV.n

        def bbe_ev(c, dvb0=dvb0):
            return dvb0 + c + 1

        def bbm(tensor):
            tensor.wait_ge(sDV.h, ev[f"bce_{H-1}"])
            for c in range(10):
                pb = s.pA if c % 2 == 0 else s.pB
                if c >= 2:
                    tensor.wait_ge(sDV.h, bbe_ev(c - 2))
                rhs = (s.b1row[0:1, c * 512:(c + 1) * 512] if c < 8
                       else s.b2row[0:1, (c - 8) * 512:(c - 7) * 512])
                ev[f"bbm_{c}"] = sPE.mark(tensor.matmul(
                    out=pb[:], lhsT=s.ones1_f[:], rhs=rhs, start=True, stop=True))
        self.on("tensor", bbm)

        def bbe(vector):
            for c in range(10):
                pb = s.pA if c % 2 == 0 else s.pB
                vector.wait_ge(sPE.h, ev[f"bbm_{c}"])
                dst = (s.b1b[:, c * 512:(c + 1) * 512] if c < 8
                       else s.b2b[:, (c - 8) * 512:(c - 7) * 512])
                v = sDV.mark(vector.tensor_copy(out=dst, in_=pb[:]))
                ev[f"bbe_{c}"] = v
                assert v == bbe_ev(c), (v, c)
        self.on("vector", bbe)

    def p1_ln_quant(self, b):
        s, ev = self, self.ev
        SQ, SP = s.SQ, s.SP
        sPE, sDV, sAC, sPL = s.sPE, s.sDV, s.sAC, s.sPL

        tpcb = self.tap(f"p1cb_b{b}", [128, 20 * NTT], F32)

        for tt in range(NTT):
            def xld(sync, b=b, tt=tt):
                if tt >= 2:
                    sync.wait_ge(sDV.h, ev[f"xnt_{b}_{tt-2}"])
                self.dma(SQ[1 + tt % 2], sync.dma_start(
                    out=s.xt[:, (tt % 2) * D:(tt % 2) * D + D],
                    in_=s.x_in[b, tt * 128:(tt + 1) * 128, :]), f"xld_{b}_{tt}")
            self.on("sync", xld)
            def xq_w(sync, b=b, tt=tt - 1):
                if tt < 0:
                    return
                sync.wait_ge(sAC.h, ev[f"xqe_{b}_{tt}"])
                dst = s.d_xq[b, :, :, tt * 128:(tt + 1) * 128]
                src = s.xqs[:].rearrange("p (dc n) -> p dc n", dc=8)
                self.dma(SP[tt % 2], sync.dma_start(out=dst.transpose([1, 0, 2]), in_=src),
                         f"xqw_{b}_{tt}")
            self.on("sync", xq_w)

            def stats_a(gpsimd, b=b, tt=tt):
                self.wt(gpsimd, f"xld_{b}_{tt}")
                if tt >= 1:
                    gpsimd.wait_ge(sDV.h, ev[f"rden_{b}_{tt-1}"])
                ev[f"ss_{b}_{tt}"] = sPL.mark(gpsimd.tensor_tensor(
                    out=s.dump[:], in0=s.xt[:, (tt % 2) * D:(tt % 2) * D + D],
                    in1=s.xt[:, (tt % 2) * D:(tt % 2) * D + D], op=ALU.mult))
            self.on("gpsimd", stats_a)

            def stats_v(vector, b=b, tt=tt):
                self.wt(vector, f"xld_{b}_{tt}")
                v = sDV.mark(vector.tensor_reduce(out=s.cb[:, 0:1], in_=s.xt[:, (tt % 2) * D:(tt % 2) * D + D],
                                                  axis=AX.X, op=ALU.add))
                vector.wait_ge(sPL.h, ev[f"ss_{b}_{tt}"])
                v = sDV.mark(vector.tensor_reduce(out=s.cb[:, 1:2], in_=s.dump[:],
                                                  axis=AX.X, op=ALU.add))
                vector.wait_ge(sDV.h, v)
                v = sDV.mark(vector.tensor_scalar(out=s.cb[:, 2:3], in0=s.cb[:, 0:1],
                                                  scalar1=1.0 / D, scalar2=None, op0=ALU.mult))
                vector.wait_ge(sDV.h, v)
                v = sDV.mark(vector.tensor_scalar(out=s.cb[:, 3:4], in0=s.cb[:, 2:3],
                                                  scalar1=s.cb[:, 2:3], scalar2=None,
                                                  op0=ALU.mult))
                vector.wait_ge(sDV.h, v)
                ev[f"var_{b}_{tt}"] = sDV.mark(vector.scalar_tensor_tensor(
                    out=s.cb[:, 4:5], in0=s.cb[:, 1:2], scalar=1.0 / D, in1=s.cb[:, 3:4],
                    op0=ALU.mult, op1=ALU.subtract))
            self.on("vector", stats_v)

            def rstd_a(scalar, b=b, tt=tt):
                scalar.wait_ge(sDV.h, ev[f"var_{b}_{tt}"])
                ev[f"sqv_{b}_{tt}"] = sAC.mark(scalar.activation(
                    out=s.cb[:, 5:6], in_=s.cb[:, 4:5], func=AF.Sqrt, bias=s.epscol[:, 0:1]))
            self.on("scalar", rstd_a)

            def ln_ap(vector, b=b, tt=tt):
                vector.wait_ge(sAC.h, ev[f"sqv_{b}_{tt}"])
                v = sDV.mark(vector.reciprocal(out=s.cb[:, 6:7], in_=s.cb[:, 5:6]))
                vector.wait_ge(sDV.h, v)
                if tt >= 1:
                    vector.wait_ge(sAC.h, ev[f"xnc_{b}_{tt-1}"])
                    vector.wait_ge(sPL.h, ev[f"xnq_{b}_{tt-1}"])
                ev[f"xnt_{b}_{tt}"] = sDV.mark(vector.tensor_scalar(
                    out=s.xnt[:], in0=s.xt[:, (tt % 2) * D:(tt % 2) * D + D], scalar1=s.cb[:, 2:3],
                    scalar2=s.cb[:, 6:7], op0=ALU.subtract, op1=ALU.mult))
            self.on("vector", ln_ap)

            def xn_a(scalar, b=b, tt=tt):
                scalar.wait_ge(sDV.h, ev[f"xnt_{b}_{tt}"])
                if tt >= 1:
                    scalar.wait_ge(sPE.h, ev[f"xntr_{b}_{tt-1}"])
                ev[f"xnc_{b}_{tt}"] = sAC.mark(
                    scalar.activation(out=s.xnbt[:], in_=s.xnt[:], func=AF.Copy))
            self.on("scalar", xn_a)

            def xn_sq(gpsimd, b=b, tt=tt):
                gpsimd.wait_ge(sDV.h, ev[f"xnt_{b}_{tt}"])
                ev[f"xnq_{b}_{tt}"] = sPL.mark(gpsimd.tensor_tensor(
                    out=s.dump[:], in0=s.xnt[:], in1=s.xnt[:], op=ALU.mult))
            self.on("gpsimd", xn_sq)

            def q_cols(vector, b=b, tt=tt):
                vector.wait_ge(sPL.h, ev[f"xnq_{b}_{tt}"])
                v = sDV.mark(vector.tensor_reduce(out=s.cb[:, 8:9], in_=s.xnt[:], axis=AX.X,
                                                  op=ALU.max, apply_absolute_value=True))
                v = sDV.mark(vector.tensor_reduce(out=s.cb[:, 7:8], in_=s.dump[:],
                                                  axis=AX.X, op=ALU.add))
                vector.wait_ge(sDV.h, v)
                v = sDV.mark(vector.tensor_scalar(out=s.cb[:, 9:10], in0=s.cb[:, 7:8],
                                                  scalar1=1.0 / D, scalar2=EPS,
                                                  op0=ALU.mult, op1=ALU.add))
                ev[f"rden_{b}_{tt}"] = v
            self.on("vector", q_cols)

            def rsq_a(scalar, b=b, tt=tt):
                scalar.wait_ge(sDV.h, ev[f"rden_{b}_{tt}"])
                ev[f"rsq_{b}_{tt}"] = sAC.mark(scalar.activation(
                    out=s.cb[:, 10:11], in_=s.cb[:, 9:10], func=AF.Sqrt))
            self.on("scalar", rsq_a)

            def q_fin(vector, b=b, tt=tt):
                vector.wait_ge(sAC.h, ev[f"rsq_{b}_{tt}"])
                v = sDV.mark(vector.reciprocal(out=s.cb[:, 11:12], in_=s.cb[:, 10:11]))
                vector.wait_ge(sDV.h, v)
                v = sDV.mark(vector.tensor_scalar(out=s.cb[:, 12:13], in0=s.cb[:, 8:9],
                                                  scalar1=s.cb[:, 11:12], scalar2=1e-5,
                                                  op0=ALU.mult, op1=ALU.max))
                vector.wait_ge(sDV.h, v)
                v = sDV.mark(vector.reciprocal(out=s.cb[:, 13:14], in_=s.cb[:, 12:13]))
                vector.wait_ge(sDV.h, v)
                v = sDV.mark(vector.tensor_scalar(out=s.cb[:, 14:15], in0=s.cb[:, 13:14],
                                                  scalar1=127.0, scalar2=None, op0=ALU.mult))
                vector.wait_ge(sDV.h, v)
                v = sDV.mark(vector.tensor_scalar(out=s.cb[:, 15:16], in0=s.cb[:, 11:12],
                                                  scalar1=s.cb[:, 14:15], scalar2=None,
                                                  op0=ALU.mult))
                vector.wait_ge(sDV.h, v)
                v = sDV.mark(vector.tensor_scalar(out=s.cb[:, 16:17], in0=s.cb[:, 14:15],
                                                  scalar1=s.s1col[:, 0:1], scalar2=None,
                                                  op0=ALU.mult))
                vector.wait_ge(sDV.h, v)
                sDV.mark(vector.reciprocal(out=s.dq1c[:, b * NTT + tt:b * NTT + tt + 1],
                                           in_=s.cb[:, 16:17]))
                if tt >= 1:
                    vector.wait_ge(sPE.h, ev[f"xqtr_{b}_{tt-1}"])
                vector.tensor_scalar(out=s.rtmp[:], in0=s.xnt[:], scalar1=s.cb[:, 15:16],
                                     scalar2=MAGIC, op0=ALU.mult, op1=ALU.add)
                ev[f"p1q_{b}_{tt}"] = sDV.mark(vector.tensor_scalar(
                    out=s.xqbt[:], in0=s.rtmp[:], scalar1=MAGIC, scalar2=None,
                    op0=ALU.subtract))
                if getattr(s, "p1cb", None) is not None:
                    ev[f"p1cb_{b}_{tt}"] = sDV.mark(vector.tensor_copy(
                        out=s.p1cb[:][:, tt * 20:tt * 20 + 20], in_=s.cb[:, 0:20]))
            self.on("vector", q_fin)

            for grp in range(4):
                def tr_p(tensor, b=b, tt=tt, grp=grp):
                    pb = s.pT1 if grp % 2 == 0 else s.pT2
                    if grp == 0:
                        tensor.wait_ge(sAC.h, ev[f"xnc_{b}_{tt}"])
                    if grp == 2:
                        tensor.wait_ge(sDV.h, ev[f"p1q_{b}_{tt}"])
                    if tt > 0 or grp >= 2:
                        key = (f"trev8_{b}_{tt}_{grp-2}" if grp >= 2
                               else f"trev8_{b}_{tt-1}_{2 + grp}")
                        tensor.wait_ge(sAC.h, ev[key])
                    for j in range(4):
                        blk = grp * 4 + j
                        src = (s.xnbt[:, blk * 128:(blk + 1) * 128] if blk < 8
                               else s.xqbt[:, (blk - 8) * 128:(blk - 7) * 128])
                        ins = tensor.transpose(
                            out=pb[:, j * 128:(j + 1) * 128], in_=src,
                            identity=s.ident_bf[:])
                    ev[f"tr8_{b}_{tt}_{grp}"] = sPE.mark(ins)
                    if grp == 3:
                        ev[f"xqtr_{b}_{tt}"] = sPE.n
                        ev[f"xntr_{b}_{tt}"] = sPE.n
                self.on("tensor", tr_p)

                def tr_e(scalar, b=b, tt=tt, grp=grp):
                    pb = s.pT1 if grp % 2 == 0 else s.pT2
                    scalar.wait_ge(sPE.h, ev[f"tr8_{b}_{tt}_{grp}"])
                    if grp < 2:
                        for j in range(4):
                            blk = grp * 4 + j
                            dst = s.xn_fm[:, blk * S + tt * 128:blk * S + (tt + 1) * 128]
                            v = sAC.mark(scalar.activation(
                                out=dst, in_=pb[:, j * 128:(j + 1) * 128], func=AF.Copy))
                    else:
                        if grp == 2 and tt > 0:
                            self.wt(scalar, f"xqw_{b}_{tt-1}")
                        v = sAC.mark(scalar.activation(
                            out=s.xqs[:, (grp - 2) * 512:(grp - 1) * 512],
                            in_=pb[:], func=AF.Copy))
                    ev[f"trev8_{b}_{tt}_{grp}"] = v
                    if grp == 3:
                        ev[f"xqe_{b}_{tt}"] = sAC.n
                        ev[f"xne_{b}_{tt}"] = sAC.n
                self.on("scalar", tr_e)



        def xq_wl(sync, b=b, tt=NTT - 1):
            sync.wait_ge(sAC.h, ev[f"xqe_{b}_{tt}"])
            dst = s.d_xq[b, :, :, tt * 128:(tt + 1) * 128]
            src = s.xqs[:].rearrange("p (dc n) -> p dc n", dc=8)
            self.dma(SP[tt % 2], sync.dma_start(out=dst.transpose([1, 0, 2]), in_=src),
                     f"xqw_{b}_{tt}")
        self.on("sync", xq_wl)

        tp = self.tap(f"xn_fm_b{b}", [128, 8 * S], BF16)
        if tp is not None:
            def tapw(gpsimd, b=b, tp=tp):
                gpsimd.wait_ge(s.sAC.h, ev[f"xqe_{b}_{NTT-1}"])
                self.dma(s.SP[4], gpsimd.dma_start(out=tp, in_=s.xn_fm[:]))
            self.on("gpsimd", tapw)
        tpcb2 = self.tap_outs.get(f"p1cb_b{b}")
        if tpcb2 is not None:
            def tapw2(gpsimd, b=b, tpcb2=tpcb2):
                gpsimd.wait_ge(s.sDV.h, ev[f"p1cb_{b}_{NTT-1}"])
                self.dma(s.SP[4], gpsimd.dma_start(out=tpcb2, in_=s.p1cb[:]))
            self.on("gpsimd", tapw2)

    def p2_proj(self, b):
        s, ev = self, self.ev
        SQ, SP = s.SQ, s.SP
        sPE, sDV, sAC, sPL = s.sPE, s.sDV, s.sAC, s.sPL

        tpq1 = self.tap("wq1t", [D, F], BF16)
        if tpq1 is not None:
            def tapq1(gpsimd):
                self.wt(gpsimd, "wqst_w1_31")
                self.dma(SP[4], gpsimd.dma_start(out=tpq1, in_=s.d_wq1))
            self.on("gpsimd", tapq1)

        tpw = self.tap("winb", [D, 3 * D], BF16)
        if tpw is not None and b == 0:
            def tapwb(gpsimd):
                self.wt(gpsimd, "wst_win_7")
                self.dma(SP[4], gpsimd.dma_start(out=tpw, in_=s.d_winb))
            self.on("gpsimd", tapwb)

        ac2 = sAC.n

        def pje_ev(fs, tc):
            return ac2 + fs * 4 + tc + 1

        for fs in range(16):
            wcol_off = fs * 128 if fs < 8 else 2 * D + (fs - 8) * 128

            def wcl(sync, b=b, fs=fs, wcol_off=wcol_off):
                if fs >= 2:
                    sync.wait_ge(sPE.h, ev[f"pjm_{b}_{fs-2}_3"])
                src = bass.AP(s.d_winb.tensor, wcol_off,
                              [[3 * D, 128], [128 * 3 * D, 8], [1, 128]])
                self.dma(SQ[1 + fs % 2], sync.dma_start(
                    out=s.wcolb[:, (fs % 2) * 1024:(fs % 2) * 1024 + 1024].rearrange("p (dc n) -> p dc n", dc=8),
                    in_=src), f"wcl_{b}_{fs}")
            self.on("sync", wcl)

            def pj(tensor, b=b, fs=fs):
                self.wt(tensor, f"wcl_{b}_{fs}")
                for tc in range(4):
                    pb = s.pA if tc % 2 == 0 else s.pB
                    pv = (pje_ev(fs, tc - 2) if tc >= 2
                          else (pje_ev(fs - 1, 2 + tc) if fs >= 1 else None))
                    if pv is not None:
                        tensor.wait_ge(sAC.h, pv)
                    for dc in range(8):
                        ins = tensor.matmul(
                            out=pb[:], lhsT=s.wcolb[:, (fs % 2) * 1024 + dc * 128:(fs % 2) * 1024 + (dc + 1) * 128],
                            rhs=s.xn_fm[:, dc * S + tc * 512:dc * S + (tc + 1) * 512],
                            start=(dc == 0), stop=(dc == 7))
                    ev[f"pjm_{b}_{fs}_{tc}"] = sPE.mark(ins)
            self.on("tensor", pj)

            def pje(scalar, b=b, fs=fs):
                for tc in range(4):
                    pb = s.pA if tc % 2 == 0 else s.pB
                    scalar.wait_ge(sPE.h, ev[f"pjm_{b}_{fs}_{tc}"])
                    if fs < 8:
                        dst = s.k_fm[:, fs * S + tc * 512:fs * S + (tc + 1) * 512]
                        sc = 1.0
                    else:
                        dst = s.q_fm[:, (fs - 8) * S + tc * 512:(fs - 8) * S + (tc + 1) * 512]
                        sc = 0.125
                    v = sAC.mark(scalar.activation(
                        out=dst, in_=pb[:], func=AF.Copy, scale=sc))
                    ev[f"pje_{b}_{fs}_{tc}"] = v
                    assert v == pje_ev(fs, tc), (v, fs, tc)
            self.on("scalar", pje)

        def vones(vector, b=b):
            ev[f"vones_{b}"] = sDV.mark(vector.memset(s.v_tm[:], 1.0))
        self.on("vector", vones)

        def wvl(sync, b=b):
            src = bass.AP(s.d_winb.tensor, D,
                          [[3 * D, 128], [128 * 3 * D, 8], [1, 1024]])
            self.dma(SQ[3], sync.dma_start(
                out=s.wvb[:].rearrange("p (dc n) -> p dc n", dc=8), in_=src), f"wvl_{b}")
        self.on("sync", wvl)

        for tt in range(NTT):
            def vpj(tensor, b=b, tt=tt):
                if tt == 0:
                    self.wt(tensor, f"wvl_{b}")
                for vc in range(2):
                    pb = s.pC if vc == 0 else s.pD
                    if tt >= 1:
                        tensor.wait_ge(sDV.h, ev[f"vpe_{b}_{tt-1}_{vc}"])
                    for dc in range(8):
                        ins = tensor.matmul(
                            out=pb[:], lhsT=s.xn_fm[:, dc * S + tt * 128:dc * S + (tt + 1) * 128],
                            rhs=s.wvb[:, dc * 1024 + vc * 512:dc * 1024 + (vc + 1) * 512],
                            start=(dc == 0), stop=(dc == 7))
                    ev[f"vpm_{b}_{tt}_{vc}"] = sPE.mark(ins)
            self.on("tensor", vpj)

            def vpe(vector, b=b, tt=tt):
                if tt == 0:
                    vector.wait_ge(sDV.h, ev[f"vones_{b}"])
                for vc in range(2):
                    pb = s.pC if vc == 0 else s.pD
                    vector.wait_ge(sPE.h, ev[f"vpm_{b}_{tt}_{vc}"])
                    base = tt * 1040 + vc * 520
                    dst = s.v_tm[:, base:base + 520].rearrange(
                        "p (g c) -> p g c", g=8, c=65)[:, :, 0:64]
                    src = pb[:].rearrange("p (g c) -> p g c", g=8, c=64)
                    ev[f"vpe_{b}_{tt}_{vc}"] = sDV.mark(vector.tensor_copy(out=dst, in_=src))
            self.on("vector", vpe)

        tpk = self.tap(f"k_fm_b{b}", [128, 8 * S], BF16)
        tpq = self.tap(f"q_fm_b{b}", [128, 8 * S], BF16)
        tpv = self.tap(f"v_tm_b{b}", [128, NTT * 1040], BF16)
        if tpk is not None:
            def tapw(gpsimd, b=b):
                gpsimd.wait_ge(s.sAC.h, ev[f"pje_{b}_15_3"])
                gpsimd.wait_ge(s.sDV.h, ev[f"vpe_{b}_{NTT-1}_1"])
                self.dma(s.SP[4], gpsimd.dma_start(out=tpk, in_=s.k_fm[:]))
                self.dma(s.SP[4], gpsimd.dma_start(out=tpq, in_=s.q_fm[:]))
                self.dma(s.SP[4], gpsimd.dma_start(out=tpv, in_=s.v_tm[:]))
            self.on("gpsimd", tapw)

    def p3_attn(self, b):
        s, ev = self, self.ev
        SQ, SP = s.SQ, s.SP
        sPE, sDV, sAC, sPL = s.sPE, s.sDV, s.sAC, s.sPL

        SKW = 3968

        NG = 4 * H
        NS = 20
        qk_ev, pv_ev, rbm_ev = {}, {}, {}
        n = sPE.n
        for m in range(NG):
            if m >= 2:
                n += 1
                rbm_ev[m - 2] = n
            for step in range(4):
                for j in range(4):
                    n += 1
                    qk_ev[(m, 4 * step + j)] = n
                if m >= 1:
                    for j in range(4):
                        n += 1
                        pv_ev[(m - 1, 4 * step + j)] = n
        for step in range(4):
            for j in range(4):
                n += 1
                pv_ev[(NG - 1, 4 * step + j)] = n
        n += 1
        rbm_ev[NG - 2] = n
        n += 1
        rbm_ev[NG - 1] = n

        exp_ev, dc_ev = {}, {}
        n = sAC.n
        for m in range(NG):
            for kt in range(16):
                n += 1
                exp_ev[(m, kt)] = n
            if m >= 1:
                n += 1
                dc_ev[m - 1] = n
        n += 1
        dc_ev[NG - 1] = n

        pool_ev, rd_ev, ae_ev = {}, {}, {}
        n = sDV.n
        for m in range(NG):
            if m >= 2:
                n += 1
                rd_ev[m - 2] = n
                n += 1
                ae_ev[m - 2] = n
            for kt in range(16):
                n += 1
                pool_ev[(m, kt)] = n
        for g in (NG - 2, NG - 1):
            n += 1
            rd_ev[g] = n
            n += 1
            ae_ev[g] = n

        psq = [s.pA, s.pB, s.pC]

        def bank(m, kt):
            return (m + kt) % 3

        def hof(g):
            h, qc = g // 4, g % 4
            return (h % 2) * 64, (h // 2) * S, qc

        def pOx(g):
            return s.pO if g % 2 == 0 else s.pO2

        def pe_all(tensor, b=b):
            def qk(m, kt):
                hh, hc, qc = hof(m)
                T = 16 * m + kt
                if T >= 3:
                    pm, pk = (m, kt - 3) if kt >= 3 else (m - 1, 13 + kt)
                    tensor.wait_ge(sAC.h, exp_ev[(pm, pk)])
                v = sPE.mark(tensor.matmul(
                    out=psq[bank(m, kt)][:],
                    lhsT=s.k_fm[hh:hh + 64, hc + kt * 128:hc + (kt + 1) * 128],
                    rhs=s.q_fm[hh:hh + 64, hc + qc * 512:hc + (qc + 1) * 512],
                    start=True, stop=True))
                assert v == qk_ev[(m, kt)]

            def pv(g, kt):
                hh, hc, qc = hof(g)
                h = g // 4
                if kt == 0 and g >= 2:
                    tensor.wait_ge(sDV.h, ae_ev[g - 2])
                    tensor.wait_ge(sAC.h, dc_ev[g - 2])
                tensor.wait_ge(sDV.h, pool_ev[(g, kt)])
                v = sPE.mark(tensor.matmul(
                    out=pOx(g)[0:65, :],
                    lhsT=s.v_tm[:, kt * 1040 + h * 65:kt * 1040 + h * 65 + 65],
                    rhs=s.emul[:, ((16 * g + kt) % NS) * 512:((16 * g + kt) % NS) * 512 + 512],
                    start=(kt == 0), stop=(kt == 15)))
                assert v == pv_ev[(g, kt)]

            def rbm(g):
                tensor.wait_ge(sAC.h, dc_ev[g])
                if g >= 1:
                    tensor.wait_ge(sDV.h, rd_ev[g - 1])
                v = sPE.mark(tensor.matmul(
                    out=s.pD[0:64, :], lhsT=s.onesp64[64:65, :],
                    rhs=s.rdbuf[64:65, :], start=True, stop=True))
                assert v == rbm_ev[g]

            for m in range(NG):
                if m >= 2:
                    rbm(m - 2)
                for step in range(4):
                    for j in range(4):
                        qk(m, 4 * step + j)
                    if m >= 1:
                        for j in range(4):
                            pv(m - 1, 4 * step + j)
            for step in range(4):
                for j in range(4):
                    pv(NG - 1, 4 * step + j)
            rbm(NG - 2)
            rbm(NG - 1)
        self.on("tensor", pe_all)

        def act_all(scalar, b=b):
            def dcopy(g):
                scalar.wait_ge(sPE.h, pv_ev[(g, 15)])
                if g >= 1:
                    scalar.wait_ge(sPE.h, rbm_ev[g - 1])
                v = sAC.mark(scalar.activation(
                    out=s.rdbuf[64:65, :], in_=pOx(g)[64:65, :], func=AF.Copy))
                assert v == dc_ev[g]

            for m in range(NG):
                for kt in range(16):
                    scalar.wait_ge(sPE.h, qk_ev[(m, kt)])
                    T = 16 * m + kt
                    if T >= 3:
                        pm, pk = (m, kt - 3) if kt >= 3 else (m - 1, 13 + kt)
                        scalar.wait_ge(sDV.h, pool_ev[(pm, pk)])
                    v = sAC.mark(scalar.activation(
                        out=s.eraw[:, bank(m, kt) * 512:bank(m, kt) * 512 + 512],
                        in_=psq[bank(m, kt)][:], func=AF.Exp))
                    assert v == exp_ev[(m, kt)]
                if m >= 1:
                    dcopy(m - 1)
            dcopy(NG - 1)
        self.on("scalar", act_all)

        def sync_all(sync, b=b):
            for h in range(H):
                if h >= 3:
                    sync.wait_ge(sDV.h, pool_ev[(4 * (h - 3) + 3, 15)])
                src = bass.AP(s.d_expb.tensor, h * 128 * EXPW + 127,
                              [[EXPW - 1, 128], [1, SKW]])
                self.dma(SP[2 + h % 3], sync.dma_start(
                    out=s.skw[:, (h % 3) * SKW:(h % 3) * SKW + SKW], in_=src),
                    f"skw_{b}_{h}")
        self.on("sync", sync_all)

        def dve_all(vector, b=b):
            def fin(g):
                vector.wait_ge(sPE.h, rbm_ev[g])
                with s.nc.allow_low_precision("softmax 1/denom applied to bf16 weights"):
                    v = sDV.mark(vector.reciprocal(
                        out=s.rbs[:, (g % 2) * 512:(g % 2) * 512 + 512],
                        in_=s.pD[0:64, :]))
                assert v == rd_ev[g]
                vector.wait_ge(sDV.h, v)
                hh, hc, qc = hof(g)
                v = sDV.mark(vector.tensor_tensor(
                    out=s.a_fm[hh:hh + 64, hc + qc * 512:hc + (qc + 1) * 512],
                    in0=pOx(g)[0:64, :],
                    in1=s.rbs[:, (g % 2) * 512:(g % 2) * 512 + 512], op=ALU.mult))
                assert v == ae_ev[g]
                ev[f"ae_{b}_{g}"] = v

            for m in range(NG):
                if m >= 2:
                    fin(m - 2)
                h = m // 4
                for kt in range(16):
                    if m % 4 == 0 and kt == 0:
                        self.wt(vector, f"skw_{b}_{h}")
                    vector.wait_ge(sAC.h, exp_ev[(m, kt)])
                    T = 16 * m + kt
                    if T >= NS:
                        pg, pk = divmod(T - NS, 16)
                        vector.wait_ge(sPE.h, pv_ev[(pg, pk)])
                    j0 = 1920 + (m % 4) * 512 - kt * 128
                    v = sDV.mark(vector.tensor_tensor(
                        out=s.emul[:, (T % NS) * 512:(T % NS) * 512 + 512],
                        in0=s.eraw[:, bank(m, kt) * 512:bank(m, kt) * 512 + 512],
                        in1=s.skw[:, (h % 3) * SKW + j0:(h % 3) * SKW + j0 + 512],
                        op=ALU.mult))
                    assert v == pool_ev[(m, kt)]
            fin(NG - 2)
            fin(NG - 1)
        self.on("vector", dve_all)



        tpa = self.tap(f"a_fm_b{b}", [128, 8 * S], BF16)
        if tpa is not None:
            def tapw(gpsimd, b=b, tpa=tpa):
                gpsimd.wait_ge(s.sDV.h, ev[f"ae_{b}_{4 * H - 1}"])
                self.dma(s.SP[4], gpsimd.dma_start(out=tpa, in_=s.a_fm[:]))
            self.on("gpsimd", tapw)

    def p4_gate(self, b):
        s, ev = self, self.ev
        SQ, SP = s.SQ, s.SP
        sPE, sDV, sAC, sPL = s.sPE, s.sDV, s.sAC, s.sPL
        banks = [s.pA, s.pC, s.pB, s.pD]

        def wgl(sync, b=b):
            src = bass.AP(s.d_wgb.tensor, 0, [[2 * D, 128], [128 * 2 * D, 8], [1, 2 * D]])
            self.dma(SQ[3], sync.dma_start(
                out=s.wgs[:].rearrange("p (dc n) -> p dc n", dc=8), in_=src), f"wgl_{b}")
        self.on("sync", wgl)

        for tt in range(NTT):
            def gm(tensor, b=b, tt=tt):
                if tt == 0:
                    self.wt(tensor, f"wgl_{b}")
                for gc in range(4):
                    pb = banks[gc]
                    if tt >= 1:
                        if gc in (2, 3):
                            tensor.wait_ge(sAC.h, ev[f"sg_{b}_{tt-1}_{gc-2}"])
                        else:
                            tensor.wait_ge(sDV.h, ev[f"ab_{b}_{tt-1}_{gc}"])
                    for dc in range(8):
                        ins = tensor.matmul(
                            out=pb[:], lhsT=s.a_fm[:, dc * S + tt * 128:dc * S + (tt + 1) * 128],
                            rhs=s.wgs[:, dc * 2048 + gc * 512:dc * 2048 + (gc + 1) * 512],
                            start=(dc == 0), stop=False)
                    ins = tensor.matmul(
                        out=pb[:], lhsT=s.ones1_bf[:],
                        rhs=s.bgate_bf[0:1, gc * 512:(gc + 1) * 512], start=False, stop=True)
                    ev[f"gm_{b}_{tt}_{gc}"] = sPE.mark(ins)
            self.on("tensor", gm)

            for half in range(2):
                def sgst(scalar, b=b, tt=tt, half=half):
                    scalar.wait_ge(sPE.h, ev[f"gm_{b}_{tt}_{2 + half}"])
                    if tt >= 1 or half == 1:
                        key = f"ab_{b}_{tt}_{half-1}" if half == 1 else f"ab_{b}_{tt-1}_1"
                        scalar.wait_ge(sDV.h, ev[key])
                    ev[f"sg_{b}_{tt}_{half}"] = sAC.mark(scalar.activation(
                        out=s.sgb[:], in_=banks[2 + half][:], func=AF.Sigmoid))
                self.on("scalar", sgst)

                def abst(vector, b=b, tt=tt, half=half):
                    vector.wait_ge(sAC.h, ev[f"sg_{b}_{tt}_{half}"])
                    vector.wait_ge(sPE.h, ev[f"gm_{b}_{tt}_{half}"])
                    if tt >= 2:
                        self.wt(vector, f"abw_{b}_{tt-2}")
                    ev[f"ab_{b}_{tt}_{half}"] = sDV.mark(vector.tensor_tensor(
                        out=s.abb[:, (tt % 2) * 1024 + half * 512:(tt % 2) * 1024 + (half + 1) * 512],
                        in0=banks[half][:], in1=s.sgb[:], op=ALU.mult))
                self.on("vector", abst)

            def abw(sync, b=b, tt=tt):
                sync.wait_ge(sDV.h, ev[f"ab_{b}_{tt}_1"])
                self.dma(SP[tt % 2], sync.dma_start(
                    out=s.d_abr[b, tt * 128:(tt + 1) * 128, :],
                    in_=s.abb[:, (tt % 2) * 1024:(tt % 2) * 1024 + 1024]), f"abw_{b}_{tt}")
            self.on("sync", abw)

        tpab = self.tap(f"abr_b{b}", [S, D], F32)
        tpxq = self.tap(f"dxq_b{b}", [8, 128, S], BF16)
        if tpab is not None:
            def tapab(gpsimd, b=b, tpab=tpab, tpxq=tpxq):
                self.wt(gpsimd, f"abw_{b}_{NTT-1}")
                self.dma(SP[4], gpsimd.dma_start(out=tpab, in_=s.d_abr[b]))
                if tpxq is not None:
                    self.dma(SP[4], gpsimd.dma_start(out=tpxq, in_=s.d_xq[b]))
            self.on("gpsimd", tapab)

    def p56_ffn(self, b):
        s, ev = self, self.ev
        SQ, SP = s.SQ, s.SP
        sPE, sDV, sAC, sPL = s.sPE, s.sDV, s.sAC, s.sPL

        def wql(sync, b=b):
            if b == 0:
                self.wt(sync, "wqst_w1_31")
                self.wt(sync, "wqst_w2_7")
            src1 = bass.AP(s.d_wq1.tensor, 0, [[F, 128], [128 * F, 8], [1, F]])
            self.dma(SQ[4], sync.dma_start(
                out=s.wq1s[:].rearrange("p (dc n) -> p dc n", dc=8), in_=src1), f"wq1l_{b}")
            src2 = bass.AP(s.d_wq2.tensor, 0, [[D, 128], [128 * D, 32], [1, D]])
            self.dma(SQ[5], sync.dma_start(
                out=s.wq2s[:].rearrange("p (fc n) -> p fc n", fc=32), in_=src2), f"wq2l_{b}")
        self.on("sync", wql)

        def emit_tail(tt, part="all"):
            front = part in ("all", "front")
            back = part in ("all", "back")
            for fc8 in (range(8) if front else ()):
                def rnd_q(gpsimd, b=b, tt=tt, fc8=fc8):
                    if fc8 == 0:
                        gpsimd.wait_ge(sDV.h, ev[f"scq_{b}_{tt}"])
                    ev[f"hqp_{b}_{tt}_{fc8}"] = sPL.mark(gpsimd.tensor_scalar(
                        out=s.gbuf[:, fc8 * 512:(fc8 + 1) * 512],
                        in0=s.gbuf[:, fc8 * 512:(fc8 + 1) * 512],
                        scalar1=s.cb[:, 32 + tt % 2:33 + tt % 2], scalar2=MAGIC,
                        op0=ALU.mult, op1=ALU.add))
                self.on("gpsimd", rnd_q)

                def rnd_q2(scalar, b=b, tt=tt, fc8=fc8):
                    scalar.wait_ge(sPL.h, ev[f"hqp_{b}_{tt}_{fc8}"])
                    ev[f"hqc_{b}_{tt}_{fc8}"] = sAC.mark(scalar.activation(
                        out=s.hqt[:, (tt % 2) * F + fc8 * 512:(tt % 2) * F + (fc8 + 1) * 512],
                        in_=s.gbuf[:, fc8 * 512:(fc8 + 1) * 512],
                        func=AF.Identity, bias=s.negmag[:, 0:1]))
                self.on("scalar", rnd_q2)

            for fc8 in (range(8) if back else ()):
                def htr(tensor, b=b, tt=tt, fc8=fc8):
                    tensor.wait_ge(sAC.h, ev[f"hqc_{b}_{tt}_{fc8}"])
                    pb = s.pT1 if fc8 % 2 == 0 else s.pT2
                    if tt > 0 or fc8 >= 2:
                        key = (f"hte8_{b}_{tt}_{fc8-2}" if fc8 >= 2
                               else f"hte8_{b}_{tt-1}_{6 + fc8}")
                        tensor.wait_ge(sAC.h, ev[key])
                    for j in range(4):
                        fc = fc8 * 4 + j
                        ins = tensor.transpose(
                            out=pb[:, j * 128:(j + 1) * 128],
                            in_=s.hqt[:, (tt % 2) * F + fc * 128:(tt % 2) * F + (fc + 1) * 128],
                            identity=s.ident_bf[:])
                    ev[f"htr8_{b}_{tt}_{fc8}"] = sPE.mark(ins)
                self.on("tensor", htr)

                def hte(scalar, b=b, tt=tt, fc8=fc8):
                    scalar.wait_ge(sPE.h, ev[f"htr8_{b}_{tt}_{fc8}"])
                    pb = s.pT1 if fc8 % 2 == 0 else s.pT2
                    ev[f"hte8_{b}_{tt}_{fc8}"] = sAC.mark(scalar.activation(
                        out=s.hqf[:, (tt % 2) * F + fc8 * 512:(tt % 2) * F + (fc8 + 1) * 512],
                        in_=pb[:], func=AF.Copy))
                self.on("scalar", hte)

            if not back:
                return

            def f2m(tensor, b=b, tt=tt):
                if tt == 0:
                    self.wt(tensor, f"wq2l_{b}")
                for d2 in range(2):
                    pb = s.pC if d2 == 0 else s.pD
                    if tt >= 1:
                        tensor.wait_ge(sDV.h, ev[f"mg_{b}_{tt-1}_{d2}"])
                    for fc in range(32):
                        if d2 == 0 and fc % 4 == 0:
                            tensor.wait_ge(sAC.h, ev[f"hte8_{b}_{tt}_{fc // 4}"])
                        ins = tensor.matmul(
                            out=pb[:],
                            lhsT=s.hqf[:, (tt % 2) * F + fc * 128:(tt % 2) * F + (fc + 1) * 128],
                            rhs=s.wq2s[:, fc * D + d2 * 512:fc * D + (d2 + 1) * 512],
                            start=(fc == 0), stop=(fc == 31))
                    ev[f"f2m_{b}_{tt}_{d2}"] = sPE.mark(ins)
            self.on("tensor", f2m)

            for d2 in range(2):
                def mg(vector, b=b, tt=tt, d2=d2):
                    vector.wait_ge(sPE.h, ev[f"f2m_{b}_{tt}_{d2}"])
                    vector.wait_ge(sPL.h, ev[f"xa_{b}_{tt}"])
                    if tt >= 2:
                        self.wt(vector, f"ow_{b}_{tt-2}")
                    pb = s.pC if d2 == 0 else s.pD
                    v = sDV.mark(vector.scalar_tensor_tensor(
                        out=s.ftmp[:, d2 * 512:(d2 + 1) * 512], in0=pb[:],
                        scalar=s.cb[:, 34 + tt % 2:35 + tt % 2],
                        in1=s.b2b[:, d2 * 512:(d2 + 1) * 512],
                        op0=ALU.mult, op1=ALU.add))
                    ev[f"mg_{b}_{tt}_{d2}"] = sDV.mark(vector.tensor_tensor(
                        out=s.xat[:, (tt % 2) * D + d2 * 512:(tt % 2) * D + (d2 + 1) * 512],
                        in0=s.ftmp[:, d2 * 512:(d2 + 1) * 512],
                        in1=s.xat[:, (tt % 2) * D + d2 * 512:(tt % 2) * D + (d2 + 1) * 512],
                        op=ALU.add))
                self.on("vector", mg)

            def ow(sync, b=b, tt=tt):
                sync.wait_ge(sDV.h, ev[f"mg_{b}_{tt}_0"])
                sync.wait_ge(sDV.h, ev[f"mg_{b}_{tt}_1"])
                self.dma(SP[tt % 2], sync.dma_start(
                    out=s.out[b, tt * 128:(tt + 1) * 128, :],
                    in_=s.xat[:, (tt % 2) * D:(tt % 2) * D + D]), f"ow_{b}_{tt}")
            self.on("sync", ow)

        def fin2(tt):
            def sq2(scalar, b=b, tt=tt):
                scalar.wait_ge(sDV.h, ev[f"rden2_{b}_{tt}"])
                ev[f"sq2_{b}_{tt}"] = sAC.mark(scalar.activation(
                    out=s.cb[:, 10:11], in_=s.cb[:, 9:10], func=AF.Sqrt))
            self.on("scalar", sq2)

            def quantB_b(vector, b=b, tt=tt):
                vector.wait_ge(sAC.h, ev[f"sq2_{b}_{tt}"])
                v = sDV.mark(vector.reciprocal(out=s.cb[:, 11:12], in_=s.cb[:, 10:11]))
                vector.wait_ge(sDV.h, v)
                v = sDV.mark(vector.tensor_scalar(out=s.cb[:, 22:23], in0=s.cb[:, 22:23],
                                                  scalar1=s.cb[:, 11:12], scalar2=None,
                                                  op0=ALU.mult))
                vector.wait_ge(sDV.h, v)
                p = tt % 2
                v = sDV.mark(vector.tensor_copy(out=s.cb[:, 34 + p:35 + p],
                                                in_=s.cb[:, 22:23]))
                vector.wait_ge(sDV.h, v)
                ev[f"scq_{b}_{tt}"] = v
            self.on("vector", quantB_b)

        for tt in range(NTT):
            def lds_a(sync, b=b, tt=tt):
                if tt >= 2:
                    sync.wait_ge(sPE.h, ev[f"f1m_{b}_{tt-2}_7"])
                    sync.wait_ge(sPL.h, ev[f"xa_{b}_{tt-2}"])
                src = s.d_xq[b, :, :, tt * 128:(tt + 1) * 128]
                self.dma(SQ[1 + tt % 2], sync.dma_start(
                    out=s.xqt[:, (tt % 2) * 1024:(tt % 2) * 1024 + 1024].rearrange("p (dc n) -> p dc n", dc=8),
                    in_=src.transpose([1, 0, 2])), f"xql_{b}_{tt}")
                self.dma(SQ[6 + tt % 2], sync.dma_start(
                    out=s.abrt[:, (tt % 2) * D:(tt % 2) * D + D],
                    in_=s.d_abr[b, tt * 128:(tt + 1) * 128, :]), f"abl_{b}_{tt}")
            self.on("sync", lds_a)

            if tt >= 2:
                emit_tail(tt - 2, "back")
            if tt >= 1:
                fin2(tt - 1)

            def lds(sync, b=b, tt=tt):
                if tt >= 2:
                    self.wt(sync, f"ow_{b}_{tt-2}")
                self.dma(SQ[3 - 3 * (tt % 2)], sync.dma_start(
                    out=s.xat[:, (tt % 2) * D:(tt % 2) * D + D],
                    in_=s.x_in[b, tt * 128:(tt + 1) * 128, :]), f"x2l_{b}_{tt}")
            self.on("sync", lds)

            def xadd(gpsimd, b=b, tt=tt):
                self.wt(gpsimd, f"x2l_{b}_{tt}")
                gpsimd.wait_ge(SQ[6 + tt % 2].h, ev[f"abl_{b}_{tt}"][1])
                ev[f"xa_{b}_{tt}"] = sPL.mark(gpsimd.tensor_tensor(
                    out=s.xat[:, (tt % 2) * D:(tt % 2) * D + D],
                    in0=s.xat[:, (tt % 2) * D:(tt % 2) * D + D],
                    in1=s.abrt[:, (tt % 2) * D:(tt % 2) * D + D], op=ALU.add))
            self.on("gpsimd", xadd)

            if tt >= 1:
                emit_tail(tt - 1, "front")

            for fc8 in range(8):
                def f1m(tensor, b=b, tt=tt, fc8=fc8):
                    if tt == 0 and fc8 == 0:
                        self.wt(tensor, f"wq1l_{b}")
                    if fc8 == 0:
                        self.wt(tensor, f"xql_{b}_{tt}")
                    pb = [s.pA, s.pB, s.pO, s.pO2][fc8 % 4]
                    key = (f"dq_{b}_{tt}_{fc8-4}" if fc8 >= 4
                           else (f"dq_{b}_{tt-1}_{4 + fc8}" if tt >= 1 else None))
                    if key is not None:
                        tensor.wait_ge(sDV.h, ev[key])
                    for dc in range(8):
                        ins = tensor.matmul(
                            out=pb[:], lhsT=s.xqt[:, (tt % 2) * 1024 + dc * 128:(tt % 2) * 1024 + (dc + 1) * 128],
                            rhs=s.wq1s[:, dc * F + fc8 * 512:dc * F + (fc8 + 1) * 512],
                            start=(dc == 0), stop=(dc == 7))
                    ev[f"f1m_{b}_{tt}_{fc8}"] = sPE.mark(ins)
                self.on("tensor", f1m)

                def dq(vector, b=b, tt=tt, fc8=fc8):
                    vector.wait_ge(sPE.h, ev[f"f1m_{b}_{tt}_{fc8}"])
                    if tt >= 1:
                        vector.wait_ge(sAC.h, ev[f"hqc_{b}_{tt-1}_{fc8}"])
                    ev[f"dq_{b}_{tt}_{fc8}"] = sDV.mark(vector.scalar_tensor_tensor(
                        out=s.gbuf[:, fc8 * 512:(fc8 + 1) * 512],
                        in0=[s.pA, s.pB, s.pO, s.pO2][fc8 % 4][:],
                        scalar=s.dq1c[:, b * NTT + tt:b * NTT + tt + 1],
                        in1=s.b1b[:, fc8 * 512:(fc8 + 1) * 512],
                        op0=ALU.mult, op1=ALU.add))
                self.on("vector", dq)

                def gl(scalar, b=b, tt=tt, fc8=fc8):
                    scalar.wait_ge(sDV.h, ev[f"dq_{b}_{tt}_{fc8}"])
                    ev[f"gl_{b}_{tt}_{fc8}"] = sAC.mark(scalar.activation(
                        out=s.gbuf[:, fc8 * 512:(fc8 + 1) * 512],
                        in_=s.gbuf[:, fc8 * 512:(fc8 + 1) * 512], func=AF.Gelu))
                self.on("scalar", gl)

                def glsq(gpsimd, b=b, tt=tt, fc8=fc8):
                    gpsimd.wait_ge(sAC.h, ev[f"gl_{b}_{tt}_{fc8}"])
                    if fc8 > 0:
                        gpsimd.wait_ge(sDV.h, ev[f"gsq_{b}_{tt}_{fc8-1}"])
                    ev[f"glsq_{b}_{tt}_{fc8}"] = sPL.mark(gpsimd.tensor_tensor(
                        out=s.gsqt[:], in0=s.gbuf[:, fc8 * 512:(fc8 + 1) * 512],
                        in1=s.gbuf[:, fc8 * 512:(fc8 + 1) * 512], op=ALU.mult))
                self.on("gpsimd", glsq)

                def gsq(vector, b=b, tt=tt, fc8=fc8):
                    vector.wait_ge(sPL.h, ev[f"glsq_{b}_{tt}_{fc8}"])
                    vector.tensor_reduce(out=s.cb[:, fc8:fc8 + 1], in_=s.gsqt[:],
                                         axis=AX.X, op=ALU.add)
                    ev[f"gsq_{b}_{tt}_{fc8}"] = sDV.mark(vector.tensor_reduce(
                        out=s.cb[:, 24 + fc8:25 + fc8],
                        in_=s.gbuf[:, fc8 * 512:(fc8 + 1) * 512], axis=AX.X,
                        op=ALU.max, apply_absolute_value=True))
                self.on("vector", gsq)

            def quantA(vector, b=b, tt=tt):
                vector.wait_ge(sDV.h, ev[f"gsq_{b}_{tt}_7"])
                v = sDV.mark(vector.tensor_reduce(out=s.cb[:, 17:18], in_=s.cb[:, 24:32],
                                                  axis=AX.X, op=ALU.max))
                ev[f"amax_{b}_{tt}"] = v
                v2 = sDV.mark(vector.tensor_reduce(out=s.cb[:, 8:9], in_=s.cb[:, 0:8],
                                                   axis=AX.X, op=ALU.add))
                vector.wait_ge(sDV.h, v2)
                v2 = sDV.mark(vector.tensor_scalar(out=s.cb[:, 9:10], in0=s.cb[:, 8:9],
                                                   scalar1=1.0 / F, scalar2=EPS,
                                                   op0=ALU.mult, op1=ALU.add))
                ev[f"rden2_{b}_{tt}"] = v2
            self.on("vector", quantA)

            def quantB_a(vector, b=b, tt=tt):
                vector.wait_ge(sDV.h, ev[f"amax_{b}_{tt}"])
                v = sDV.mark(vector.tensor_scalar(out=s.cb[:, 18:19], in0=s.cb[:, 17:18],
                                                  scalar1=1e-5, scalar2=None, op0=ALU.max))
                vector.wait_ge(sDV.h, v)
                v = sDV.mark(vector.reciprocal(out=s.cb[:, 19:20], in_=s.cb[:, 18:19]))
                vector.wait_ge(sDV.h, v)
                v = sDV.mark(vector.tensor_scalar(out=s.cb[:, 20:21], in0=s.cb[:, 19:20],
                                                  scalar1=127.0, scalar2=None, op0=ALU.mult))
                vector.wait_ge(sDV.h, v)
                v = sDV.mark(vector.tensor_scalar(out=s.cb[:, 21:22], in0=s.cb[:, 20:21],
                                                  scalar1=s.s2col[:, 0:1], scalar2=None,
                                                  op0=ALU.mult))
                vector.wait_ge(sDV.h, v)
                v = sDV.mark(vector.reciprocal(out=s.cb[:, 22:23], in_=s.cb[:, 21:22]))
                vector.wait_ge(sDV.h, v)
                p = tt % 2
                v = sDV.mark(vector.tensor_copy(out=s.cb[:, 32 + p:33 + p],
                                                in_=s.cb[:, 20:21]))
                vector.wait_ge(sDV.h, v)
            self.on("vector", quantB_a)

        fin2(NTT - 1)
        emit_tail(NTT - 2, "back")
        emit_tail(NTT - 1, "front")
        emit_tail(NTT - 1, "back")

    def finale(self):
        s = self

        def fin(gpsimd):
            for c in (*s.SP, *s.SQ):
                if c.n > 0:
                    gpsimd.wait_ge(c.h, c.n)
        self.on("gpsimd", fin)

        def fin2(sync):
            for c in (*s.SP, *s.SQ):
                if c.n > 0:
                    sync.wait_ge(c.h, c.n)
        self.on("sync", fin2)


def build_core_program(taps=(), stop_after=None):
    kb = KB(taps, stop_after=stop_after)
    nc = kb.build()
    return nc, kb


def kernel(L0_attn_x, L0_attn_w_in, L0_attn_w_gate, L0_attn_b_gate, L0_attn_amplitudes,
           L0_attn_sharpness, L0_attn_offsets, L0_ffn_w1, L0_ffn_b1, L0_ffn_w2, L0_ffn_b2):
    x = np.ascontiguousarray(L0_attn_x)
    nc, _ = build_core_program()
    in_maps = []
    for c in range(8):
        in_maps.append({
            "x": np.ascontiguousarray(x[2 * c:2 * c + 2]),
            "w_in": np.ascontiguousarray(L0_attn_w_in),
            "w_gate": np.ascontiguousarray(L0_attn_w_gate),
            "b_gate": np.ascontiguousarray(L0_attn_b_gate),
            "amp": np.ascontiguousarray(L0_attn_amplitudes),
            "sharp": np.ascontiguousarray(L0_attn_sharpness),
            "off": np.ascontiguousarray(L0_attn_offsets),
            "w1": np.ascontiguousarray(L0_ffn_w1),
            "b1": np.ascontiguousarray(L0_ffn_b1),
            "w2": np.ascontiguousarray(L0_ffn_w2),
            "b2": np.ascontiguousarray(L0_ffn_b2),
        })
    res = run_bass_kernel_spmd(nc, in_maps, core_ids=list(range(8)))
    return np.concatenate([r["out"] for r in res.results], axis=0)
